# version 01
# baseline (speedup 1.0000x reference)
import math
from contextlib import ExitStack

import numpy as np
import ml_dtypes

import concourse.bass as bass
import concourse.mybir as mybir
import concourse.library_config as libcfg
from concourse.bass_utils import run_bass_kernel_spmd

F32 = mybir.dt.float32
BF16 = mybir.dt.bfloat16
I16 = mybir.dt.int16
AX = mybir.AxisListType.X
ALU = mybir.AluOpType
ACT = mybir.ActivationFunctionType

D = 2048
S = 8192
NCORES = 8
T = S // NCORES
NT = T // 128
H = 16
NOPE, ROPE, VD = 128, 64, 128
QKD = NOPE + ROPE
QL, KVL = 768, 512
DFF = 8192
CAP = 384
SCALE = QKD ** (-0.5)
EPS_LN = 1e-5
EPS_RMS = 1e-6
MASK_NEG = -1.0e30
NKC = 8

KN_OFF = 0
KPE_OFF = KN_OFF + D * T
KN2_OFF = KPE_OFF + ROPE * T
V_OFF = KN2_OFF + 32
KVSH = V_OFF + H * T * VD


class Sem:
    def __init__(self, h):
        self.h = h
        self.v = 0


def _i(instr, s, n=1):
    instr.then_inc(s.h, n)
    s.v += n
    return s.v


class DQ:

    def __init__(self, nc, ctx, name, ns=8):
        self.sems = [Sem(ctx.enter_context(nc.semaphore(f"{name}{i}")))
                     for i in range(ns)]
        self.ns = ns
        self.n = 0

    def issue(self, eng, **kw):
        s = self.sems[self.n % self.ns]
        if s.v > 0:
            eng.wait_ge(s.h, s.v)
        eng.dma_start(**kw).then_inc(s.h, 16)
        s.v += 16
        self.n += 1
        return (s, s.v)

    @staticmethod
    def wait(eng, tok):
        eng.wait_ge(tok[0].h, tok[1])

    def wait_all(self, eng):
        for s in self.sems:
            if s.v > 0:
                eng.wait_ge(s.h, s.v)


def build_nc(stage=6):
    nc = bass.Bass("TRN2", target_bir_lowering=False, debug=False)

    x0_d = nc.dram_tensor("x0", [T, D], F32, kind="ExternalInput")
    w_dq_d = nc.dram_tensor("w_dq", [D, QL], F32, kind="ExternalInput")
    w_uq_d = nc.dram_tensor("w_uq", [QL, H * QKD], F32, kind="ExternalInput")
    w_dkv_d = nc.dram_tensor("w_dkv", [D, KVL], F32, kind="ExternalInput")
    w_kr_d = nc.dram_tensor("w_kr", [D, ROPE], F32, kind="ExternalInput")
    w_uk_d = nc.dram_tensor("w_uk", [KVL, H * NOPE], F32, kind="ExternalInput")
    w_uv_d = nc.dram_tensor("w_uv", [KVL, H * VD], F32, kind="ExternalInput")
    w_o_d = nc.dram_tensor("w_o", [H * VD, D], F32, kind="ExternalInput")
    g_q_d = nc.dram_tensor("g_q", [QL], F32, kind="ExternalInput")
    g_kv_d = nc.dram_tensor("g_kv", [KVL], F32, kind="ExternalInput")
    w1_d = nc.dram_tensor("w1", [D, DFF], F32, kind="ExternalInput")
    b1_d = nc.dram_tensor("b1", [DFF], F32, kind="ExternalInput")
    w2_d = nc.dram_tensor("w2", [DFF, D], F32, kind="ExternalInput")
    b2_d = nc.dram_tensor("b2", [D], F32, kind="ExternalInput")
    rw_d = nc.dram_tensor("rw", [D], F32, kind="ExternalInput")
    rb_d = nc.dram_tensor("rb", [1], F32, kind="ExternalInput")
    ropeC_d = nc.dram_tensor("ropeC", [128, T], F32, kind="ExternalInput")
    ropeS_d = nc.dram_tensor("ropeS", [128, T], F32, kind="ExternalInput")
    dmask_d = nc.dram_tensor("dmask", [128, 1024], F32, kind="ExternalInput")
    ident_d = nc.dram_tensor("ident", [128, 128], BF16, kind="ExternalInput")
    ones_d = nc.dram_tensor("ones", [128, 128], BF16, kind="ExternalInput")
    tok16_d = nc.dram_tensor("tok16", [1, T], I16, kind="ExternalInput")
    capio_d = nc.dram_tensor("capio", [1, CAP], F32, kind="ExternalInput")
    id32f_d = nc.dram_tensor("id32f", [32, 32], F32, kind="ExternalInput")

    y_d = nc.dram_tensor("y", [T, D], F32, kind="ExternalOutput")
    dbg = None
    if stage == 2:
        dbg = nc.dram_tensor("dbg", [KVSH], BF16, kind="ExternalOutput")
    elif stage == 3:
        dbg = nc.dram_tensor("dbg", [128, H * T], BF16, kind="ExternalOutput")
    elif stage == 4:
        dbg = nc.dram_tensor("dbg", [T, D], F32, kind="ExternalOutput")
    elif stage == 5:
        dbg = nc.dram_tensor("dbg", [8, T], F32, kind="ExternalOutput")

    agkv_in = nc.dram_tensor("agkv_in", [KVSH], BF16)
    agkv_out = nc.dram_tensor("agkv_out", [NCORES * KVSH], BF16)
    agsc_in = nc.dram_tensor("agsc_in", [T], F32)
    agsc_out = nc.dram_tensor("agsc_out", [NCORES * T], F32)
    h_dram = nc.dram_tensor("h_dram", [T, D], BF16)
    x1_dram = nc.dram_tensor("x1_dram", [T, D], F32)
    idx_dram = nc.dram_tensor("idx_dram", [1, CAP], I16)

    ctx = ExitStack()
    with ctx:
        dqQ = DQ(nc, ctx, "sQ", ns=8)
        dqG = DQ(nc, ctx, "sG", ns=8)
        sCC = Sem(ctx.enter_context(nc.semaphore("sCC")))
        sT = Sem(ctx.enter_context(nc.semaphore("sT")))
        sV = Sem(ctx.enter_context(nc.semaphore("sV")))
        sS = Sem(ctx.enter_context(nc.semaphore("sS")))
        sP = Sem(ctx.enter_context(nc.semaphore("sP")))
        blk = ctx.enter_context(nc.Block())

        TE, VE, SE, GE, SY = blk.tensor, blk.vector, blk.scalar, blk.gpsimd, blk.sync

        def fence():
            snap = (sT.v, sV.v, sS.v)

            def fg(g, snap=snap):
                dqQ.wait_all(g)
                dqG.wait_all(g)
                g.wait_ge(sT.h, snap[0])
                g.wait_ge(sV.h, snap[1])
                g.wait_ge(sS.h, snap[2])
                _i(g.drain(), sP)
            GE(fg)
            return sP.v

        ident = ctx.enter_context(nc.sbuf_tensor("ident_s", [128, 128], BF16))
        ones = ctx.enter_context(nc.sbuf_tensor("ones_s", [128, 128], BF16))
        ropeC = ctx.enter_context(nc.sbuf_tensor("ropeC_s", [128, T], F32))
        ropeS = ctx.enter_context(nc.sbuf_tensor("ropeS_s", [128, T], F32))
        dmask = ctx.enter_context(nc.sbuf_tensor("dmask_s", [128, 1024], F32))
        gq_sb = ctx.enter_context(nc.sbuf_tensor("gq_sb", [128, 6], F32))
        gkv_sb = ctx.enter_context(nc.sbuf_tensor("gkv_sb", [128, 4], F32))
        b1_sb = ctx.enter_context(nc.sbuf_tensor("b1_sb", [128, 64], F32))
        b2_sb = ctx.enter_context(nc.sbuf_tensor("b2_sb", [128, 16], F32))
        rb_sb = ctx.enter_context(nc.sbuf_tensor("rb_sb", [1, 1], F32))
        tok16 = ctx.enter_context(nc.sbuf_tensor("tok16_s", [16, T], I16))
        capio = ctx.enter_context(nc.sbuf_tensor("capio_s", [1, CAP], F32))
        id32f = ctx.enter_context(nc.sbuf_tensor("id32f_s", [32, 32], F32))

        psA = ctx.enter_context(nc.psum_tensor("psA", [128, 1024], F32))
        psB = ctx.enter_context(nc.psum_tensor("psB", [128, 1024], F32))
        ptA = ctx.enter_context(nc.psum_tensor("ptA", [128, 1024], BF16))
        ptB = ctx.enter_context(nc.psum_tensor("ptB", [128, 1024], BF16))
        psO = ctx.enter_context(nc.psum_tensor("psO", [128, 512], F32))
        psM = ctx.enter_context(nc.psum_tensor("psM", [128, 512], F32))

        tC0 = []

        def c0(sy):
            for dst, src in ((ident, ident_d), (ones, ones_d), (ropeC, ropeC_d),
                             (ropeS, ropeS_d), (dmask, dmask_d), (capio, capio_d),
                             (id32f, id32f_d)):
                tC0.append(dqQ.issue(sy, out=dst[:, :], in_=src[:, :]))
            for dst, src in ((gq_sb, g_q_d), (gkv_sb, g_kv_d),
                             (b1_sb, b1_d), (b2_sb, b2_d)):
                tC0.append(dqQ.issue(
                    sy, out=dst[:, :], in_=src.ap().rearrange("(c p) -> p c", p=128),
                    allow_slow_non_contiguous=True))
            tC0.append(dqQ.issue(sy, out=rb_sb[:, :],
                                 in_=rb_d.ap().rearrange("(o x) -> o x", o=1)))
            for b in range(16):
                tC0.append(dqQ.issue(sy, out=tok16[b:b + 1, :], in_=tok16_d[:, :]))
        SY(c0)
        tIdent = tC0

        def g0(g):
            g.load_library(libcfg.attnmlp)
        GE(g0)

        ofm_ctx = ExitStack()
        aqn = ofm_ctx.enter_context(nc.sbuf_tensor("aqn", [128, 6 * T], BF16, side="right"))
        o_fm = ofm_ctx.enter_context(nc.sbuf_tensor("o_fm", [128, H * T], BF16, side="right"))
        kn2x = ctx.enter_context(nc.sbuf_tensor("kn2x", [32, NCORES], F32))
        m2k16 = ctx.enter_context(nc.sbuf_tensor("m2k16", [32, 1], F32))
        rbb = ctx.enter_context(nc.sbuf_tensor("rbb", [128, 1], F32))

        p2 = ExitStack()
        x0b = [p2.enter_context(nc.sbuf_tensor(f"x0b{i}", [128, D], F32)) for i in range(2)]
        st6 = p2.enter_context(nc.sbuf_tensor("st6", [128, 24], F32))
        st2 = p2.enter_context(nc.sbuf_tensor("st2", [128, 2], F32))
        lrs = p2.enter_context(nc.sbuf_tensor("lrs", [128, 1], F32))
        lnm = p2.enter_context(nc.sbuf_tensor("lnm", [128, 1], F32))
        t_tm = p2.enter_context(nc.sbuf_tensor("t_tm", [128, D], BF16))
        t_fm = p2.enter_context(nc.sbuf_tensor("t_fm", [128, 16 * T], BF16))
        ckvn = p2.enter_context(nc.sbuf_tensor("ckvn", [128, 4 * T], BF16))
        sqtmp = p2.enter_context(nc.sbuf_tensor("sqtmp", [128, T], BF16))
        rmss = p2.enter_context(nc.sbuf_tensor("rmss", [1, T], F32))
        rmsb = p2.enter_context(nc.sbuf_tensor("rmsb", [128, T], F32))
        knch = p2.enter_context(nc.sbuf_tensor("knch", [128, T], BF16))
        knsq = p2.enter_context(nc.sbuf_tensor("knsq", [128, T], BF16))
        kn2row = p2.enter_context(nc.sbuf_tensor("kn2row", [1, 32], F32))
        kn2rb = p2.enter_context(nc.sbuf_tensor("kn2rb", [1, 32], BF16))
        vch = p2.enter_context(nc.sbuf_tensor("vch", [128, T], BF16))
        vtt = p2.enter_context(nc.sbuf_tensor("vtt", [128, T], BF16))
        kpe0 = p2.enter_context(nc.sbuf_tensor("kpe0", [64, T], F32))
        kpet = p2.enter_context(nc.sbuf_tensor("kpet", [64, T], F32))
        kper = p2.enter_context(nc.sbuf_tensor("kper", [64, T], BF16))
        kpesq = p2.enter_context(nc.sbuf_tensor("kpesq", [64, T], BF16))
        p2a = ExitStack()
        wdkv = p2a.enter_context(nc.sbuf_tensor("wdkv", [128, 16 * KVL], BF16))
        wkr = p2a.enter_context(nc.sbuf_tensor("wkr", [128, 16 * 2 * ROPE], BF16))
        wuk = p2a.enter_context(nc.sbuf_tensor("wuk", [128, 4 * H * NOPE], BF16))
        wuv = p2a.enter_context(nc.sbuf_tensor("wuv", [128, 4 * H * VD], BF16))

        tW = []

        def wld(g):
            tW.append(dqG.issue(g, out=wdkv[:, :].rearrange("p (c j) -> p c j", c=16),
                                in_=w_dkv_d.ap().rearrange("(c p) j -> p c j", p=128)))
            tW.append(dqG.issue(g, out=wuk[:, :].rearrange("p (c j) -> p c j", c=4),
                                in_=w_uk_d.ap().rearrange("(c p) j -> p c j", p=128)))
            tW.append(dqG.issue(g, out=wuv[:, :].rearrange("p (c j) -> p c j", c=4),
                                in_=w_uv_d.ap().rearrange("(c p) j -> p c j", p=128)))
            tW.append(dqG.issue(
                g, out=wkr[:, :].rearrange("p (c u) -> p c u", c=16)[:, :, 0:ROPE],
                in_=w_kr_d.ap().rearrange("(c p) j -> p c j", p=128)))
            tW.append(dqG.issue(
                g, out=wkr[:, :].rearrange("p (c u) -> p c u", c=16)[:, :, ROPE:ROPE + 32],
                in_=w_kr_d.ap().rearrange("(c p) j -> p c j", p=128)[:, :, 32:64]))
            tW.append(dqG.issue(
                g, out=wkr[:, :].rearrange("p (c u) -> p c u", c=16)[:, :, ROPE + 32:2 * ROPE],
                in_=w_kr_d.ap().rearrange("(c p) j -> p c j", p=128)[:, :, 0:32]))
        GE(wld)

        mX0free = [None, None]
        for b in range(NT):
            par = b % 2

            tkx = [None]

            def ldx(sy, b=b, par=par, w=mX0free[par]):
                if w is not None:
                    sy.wait_ge(sS.h, w)
                tkx[0] = dqQ.issue(sy, out=x0b[par][:, :],
                                   in_=x0_d[128 * b:128 * (b + 1), :])
            SY(ldx)
            mx = tkx[0]

            def lns(v, par=par, mx=mx, wprev=mX0free[1 - par]):
                DQ.wait(v, mx)
                if wprev is not None:
                    v.wait_ge(sS.h, wprev)
                for q in range(4):
                    v.bn_stats(st6[:, 6 * q:6 * q + 6], x0b[par][:, 512 * q:512 * (q + 1)])
                v.drain()
                v.bn_aggr(st2[:, :], st6[:, :])
                v.drain()
                v.tensor_scalar(lrs[:, :], st2[:, 1:2], EPS_LN, None, ALU.add)
                v.drain()
                _i(v.reciprocal(lrs[:, :], lrs[:, :]), sV)
            VE(lns)
            m1 = sV.v

            def lnr(sc, m1=m1):
                sc.wait_ge(sV.h, m1)
                _i(sc.activation(lrs[:, :], lrs[:, :], ACT.Sqrt), sS)
            SE(lnr)
            m2 = sS.v

            def lnn(v, m2=m2):
                v.wait_ge(sS.h, m2)
                v.tensor_mul(lnm[:, :], st2[:, 0:1], lrs[:, :])
                v.drain()
                _i(v.tensor_scalar(lnm[:, :], lnm[:, :], -1.0, None, ALU.mult), sV)
            VE(lnn)
            m3 = sV.v

            def lnap(sc, par=par, m3=m3):
                sc.wait_ge(sV.h, m3)
                _i(sc.activation(t_tm[:, :], x0b[par][:, :], ACT.Identity,
                                 bias=lnm[:, 0:1], scale=lrs[:, 0:1]), sS)
            SE(lnap)
            m4 = sS.v
            mX0free[par] = m4

            def ttr(t, m4=m4, b=b):
                t.wait_ge(sS.h, m4)
                if b == 0:
                    DQ.wait(t, tIdent[0])
                for half in range(2):
                    pt = ptA if half == 0 else ptB
                    for c in range(8):
                        cc = 8 * half + c
                        t.transpose(pt[:, 128 * c:128 * (c + 1)],
                                    t_tm[:, 128 * cc:128 * (cc + 1)], ident[:, :])
                    _i(t.drain(), sT)
            TE(ttr)
            mh = (sT.v - 1, sT.v)

            def tev(v, mh=mh, b=b):
                for half in range(2):
                    v.wait_ge(sT.h, mh[half])
                    _i(v.tensor_copy(
                        t_fm[:, :].rearrange("p (c m) -> p c m", c=16)
                        [:, 8 * half:8 * half + 8, 128 * b:128 * (b + 1)],
                        (ptA if half == 0 else ptB)[:, :].rearrange(
                            "p (c m) -> p c m", c=8)), sV)
            VE(tev)
        mTFM = sV.v

        def proj_rms(wsb, nchunk, gcol_sb, buf):
            vgate = sV.v
            sgate = sS.v
            mEv = []
            mSq = []
            for oc in range(nchunk):
                ps_full = psA if oc % 2 == 0 else psB
                def mm(t, oc=oc, ps_full=ps_full):
                    if oc == 0:
                        for tk in tW:
                            DQ.wait(t, tk)
                        t.wait_ge(sV.h, max(mTFM, vgate))
                    if oc >= 2:
                        t.wait_ge(sV.h, mEv[oc - 2])
                    for half in range(2):
                        ps = ps_full[:, 512 * half:512 * (half + 1)]
                        for ic in range(16):
                            t.matmul(ps,
                                     wsb[:, :].rearrange("p (c j) -> p c j", c=16)
                                     [:, ic, 128 * oc:128 * (oc + 1)],
                                     t_fm[:, :].rearrange("p (c m) -> p c m", c=16)
                                     [:, ic, 512 * half:512 * (half + 1)],
                                     start=(ic == 0), stop=(ic == 15))
                    _i(t.drain(), sT)
                TE(mm)
                mp = sT.v

                def ev(v, oc=oc, mp=mp, ps_full=ps_full):
                    v.wait_ge(sT.h, mp)
                    _i(v.tensor_copy(
                        buf[:, :].rearrange("p (c m) -> p c m", c=nchunk)[:, oc, :],
                        ps_full[:, :]), sV)
                VE(ev)
                mEv.append(sV.v)
                mr = sV.v

                def sq(sc, oc=oc, mr=mr):
                    sc.wait_ge(sV.h, mr)
                    if oc >= 1:
                        sc.wait_ge(sT.h, mSq[oc - 1])
                    _i(sc.activation(sqtmp[:, :],
                                     buf[:, :].rearrange("p (c m) -> p c m", c=nchunk)
                                     [:, oc, :], ACT.Square), sS)
                SE(sq)
                msq = sS.v

                def ssum(t, oc=oc, msq=msq, vgate=vgate):
                    t.wait_ge(sS.h, msq)
                    if oc == 0:
                        t.wait_ge(sV.h, vgate)
                    t.matmul(psM[0:1, 0:512], ones[:, 0:1], sqtmp[:, 0:512],
                             start=(oc == 0), stop=(oc == nchunk - 1))
                    t.matmul(psO[0:1, 0:512], ones[:, 0:1], sqtmp[:, 512:1024],
                             start=(oc == 0), stop=(oc == nchunk - 1))
                    _i(t.drain(), sT)
                TE(ssum)
                mSq.append(sT.v)
            msum = sT.v

            def rstd(v, msum=msum, nchunk=nchunk):
                v.wait_ge(sT.h, msum)
                v.tensor_scalar(rmss[0:1, 0:512], psM[0:1, 0:512],
                                1.0 / (nchunk * 128), EPS_RMS, ALU.mult, ALU.add)
                v.tensor_scalar(rmss[0:1, 512:1024], psO[0:1, 0:512],
                                1.0 / (nchunk * 128), EPS_RMS, ALU.mult, ALU.add)
                _i(v.drain(), sV)
            VE(rstd)
            mms = sV.v

            def rsq(sc, mms=mms):
                sc.wait_ge(sV.h, mms)
                _i(sc.activation(rmss[:, :], rmss[:, :], ACT.Sqrt), sS)
            SE(rsq)
            msq2 = sS.v

            def rrec(v, msq2=msq2):
                v.wait_ge(sS.h, msq2)
                _i(v.reciprocal(rmss[:, :], rmss[:, :]), sV)
            VE(rrec)
            mrec = sV.v

            def bcast(g, mrec=mrec):
                g.wait_ge(sV.h, mrec)
                _i(g.partition_broadcast(rmsb[:, :], rmss[0:1, :]), sP)
            GE(bcast)
            mb = sP.v

            for oc in range(nchunk):
                def nrm(v, oc=oc, mb=mb):
                    v.wait_ge(sP.h, mb)
                    _i(v.scalar_tensor_tensor(
                        buf[:, :].rearrange("p (c m) -> p c m", c=nchunk)[:, oc, :],
                        buf[:, :].rearrange("p (c m) -> p c m", c=nchunk)[:, oc, :],
                        gcol_sb[:, oc:oc + 1], rmsb[:, :],
                        ALU.mult, ALU.mult), sV)
                VE(nrm)
            def nrmd(v):
                _i(v.drain(), sV)
            VE(nrmd)
            return sV.v

        mCKV = proj_rms(wdkv, 4, gkv_sb, ckvn)

        tShard = []

        def z32(v):
            v.memset(kn2row[:, :], 0.0)
            _i(v.drain(), sV)
        VE(z32)
        mZ32 = sV.v

        mKNprev = [None, None]
        mKN2V = [None]
        tOutPrev = [None, None]
        for hh in range(H):
            def knmm(t, hh=hh, wprev=(mKNprev[0] if hh > 0 else None)):
                if hh == 0:
                    t.wait_ge(sV.h, mCKV)
                else:
                    t.wait_ge(sV.h, wprev)
                for half in range(2):
                    ps = psA[:, 512 * half:512 * (half + 1)]
                    for ic in range(4):
                        t.matmul(ps,
                                 wuk[:, :].rearrange("p (c j) -> p c j", c=4)
                                 [:, ic, 128 * hh:128 * (hh + 1)],
                                 ckvn[:, :].rearrange("p (c m) -> p c m", c=4)
                                 [:, ic, 512 * half:512 * (half + 1)],
                                 start=(ic == 0), stop=(ic == 3))
                _i(t.drain(), sT)
            TE(knmm)
            mkn = sT.v

            def knev(v, mkn=mkn, tprev=tOutPrev[0]):
                v.wait_ge(sT.h, mkn)
                if tprev is not None:
                    DQ.wait(v, tprev)
                _i(v.tensor_copy(knch[:, :], psA[:, :]), sV)
            VE(knev)
            mknb = sV.v

            mKNprev[0] = mknb

            def knq(sc, mknb=mknb):
                sc.wait_ge(sV.h, mknb)
                _i(sc.activation(knsq[:, :], knch[:, :], ACT.Square), sS)
            SE(knq)
            mknsq = sS.v

            def kn2mm(t, mknsq=mknsq, hh=hh, wv=(mKN2V[0] if hh > 0 else None)):
                t.wait_ge(sS.h, mknsq)
                if wv is not None:
                    t.wait_ge(sV.h, wv)
                t.matmul(psM[0:1, 0:512], ones[:, 0:1], knsq[:, 0:512],
                         start=True, stop=True)
                t.matmul(psO[0:1, 0:512], ones[:, 0:1], knsq[:, 512:1024],
                         start=True, stop=True)
                _i(t.drain(), sT)
            TE(kn2mm)
            mkn2 = sT.v

            def kn2red(v, mkn2=mkn2, hh=hh):
                v.wait_ge(sT.h, mkn2)
                if hh == 0:
                    v.wait_ge(sV.h, mZ32)
                v.reduce_max(kn2row[0:1, hh:hh + 1], psM[0:1, 0:512], axis=AX)
                v.reduce_max(lrs[0:1, 0:1], psO[0:1, 0:512], axis=AX)
                v.drain()
                _i(v.tensor_tensor(kn2row[0:1, hh:hh + 1], kn2row[0:1, hh:hh + 1],
                                   lrs[0:1, 0:1], ALU.max), sV)
            VE(kn2red)
            mKN2V[0] = sV.v

            tkn = [None]

            def knout(sy, hh=hh, mknb=mknb):
                sy.wait_ge(sV.h, mknb)
                tkn[0] = dqQ.issue(
                    sy,
                    out=agkv_in.ap()[KN_OFF + hh * 128 * T:KN_OFF + (hh + 1) * 128 * T]
                    .rearrange("(p m) -> p m", p=128),
                    in_=knch[:, :])
            SY(knout)
            tShard.append(tkn[0])
            tOutPrev[0] = tkn[0]

            def vmm(t, hh=hh, wprev=(mKNprev[1] if hh > 0 else None)):
                if wprev is not None:
                    t.wait_ge(sV.h, wprev)
                for half in range(2):
                    ps = psB[:, 512 * half:512 * (half + 1)]
                    for ic in range(4):
                        t.matmul(ps,
                                 wuv[:, :].rearrange("p (c j) -> p c j", c=4)
                                 [:, ic, 128 * hh:128 * (hh + 1)],
                                 ckvn[:, :].rearrange("p (c m) -> p c m", c=4)
                                 [:, ic, 512 * half:512 * (half + 1)],
                                 start=(ic == 0), stop=(ic == 3))
                _i(t.drain(), sT)
            TE(vmm)
            mv = sT.v

            def vev(v, mv=mv):
                v.wait_ge(sT.h, mv)
                _i(v.tensor_copy(vch[:, :], psB[:, :]), sV)
            VE(vev)
            mvb = sV.v
            mKNprev[1] = mvb

            def vtr(t, mvb=mvb):
                t.wait_ge(sV.h, mvb)
                for half in range(2):
                    pt = ptA if half == 0 else ptB
                    for bb in range(4):
                        b = 4 * half + bb
                        t.transpose(pt[:, 128 * bb:128 * (bb + 1)],
                                    vch[:, 128 * b:128 * (b + 1)], ident[:, :])
                    _i(t.drain(), sT)
            TE(vtr)
            mvt = (sT.v - 1, sT.v)

            def vtev(v, mvt=mvt, tprev=tOutPrev[1]):
                if tprev is not None:
                    DQ.wait(v, tprev)
                for half in range(2):
                    v.wait_ge(sT.h, mvt[half])
                    _i(v.tensor_copy(
                        vtt[:, :].rearrange("p (b m) -> p b m", b=8)
                        [:, 4 * half:4 * half + 4, :],
                        (ptA if half == 0 else ptB)[:, 0:512].rearrange(
                            "p (b m) -> p b m", b=4)), sV)
            VE(vtev)
            mvtt = sV.v

            tvo = [None]

            def vout(sy, hh=hh, mvtt=mvtt):
                sy.wait_ge(sV.h, mvtt)
                tvo[0] = dqQ.issue(
                    sy,
                    out=agkv_in.ap()[V_OFF + hh * T * VD:V_OFF + (hh + 1) * T * VD]
                    .rearrange("(b p v) -> p b v", b=8, p=128),
                    in_=vtt[:, :].rearrange("p (b v) -> p b v", b=8))
            SY(vout)
            tShard.append(tvo[0])
            tOutPrev[1] = tvo[0]

        def kpmm(t):
            for half in range(2):
                ps = psA[0:64, 512 * half:512 * (half + 1)]
                for ic in range(16):
                    t.matmul(ps,
                             wkr[:, :].rearrange("p (c u) -> p c u", c=16)[:, ic, 0:64],
                             t_fm[:, :].rearrange("p (c m) -> p c m", c=16)
                             [:, ic, 512 * half:512 * (half + 1)],
                             start=(ic == 0), stop=(ic == 15))
                ps2 = psB[0:64, 512 * half:512 * (half + 1)]
                for ic in range(16):
                    t.matmul(ps2,
                             wkr[:, :].rearrange("p (c u) -> p c u", c=16)[:, ic, 64:128],
                             t_fm[:, :].rearrange("p (c m) -> p c m", c=16)
                             [:, ic, 512 * half:512 * (half + 1)],
                             start=(ic == 0), stop=(ic == 15))
            _i(t.drain(), sT)
        TE(kpmm)
        mKP = sT.v

        def kprope(v, mKP=mKP):
            v.wait_ge(sT.h, mKP)
            v.tensor_tensor(kpe0[:, :], psA[0:64, :], ropeC[0:64, :], ALU.mult)
            v.tensor_tensor(kpet[:, :], psB[0:64, :], ropeS[0:64, :], ALU.mult)
            v.drain()
            _i(v.tensor_tensor(kper[:, :], kpe0[:, :], kpet[:, :], ALU.add), sV)
        VE(kprope)
        mKPR = sV.v

        def kpsq(sc, mKPR=mKPR):
            sc.wait_ge(sV.h, mKPR)
            _i(sc.activation(kpesq[:, :], kper[:, :], ACT.Square), sS)
        SE(kpsq)
        mKPSQ = sS.v

        def kpsum(t, mKPSQ=mKPSQ, wv=mKN2V[0]):
            t.wait_ge(sS.h, mKPSQ)
            t.wait_ge(sV.h, wv)
            t.matmul(psM[0:1, 0:512], ones[0:64, 0:1], kpesq[:, 0:512],
                     start=True, stop=True)
            t.matmul(psO[0:1, 0:512], ones[0:64, 0:1], kpesq[:, 512:1024],
                     start=True, stop=True)
            _i(t.drain(), sT)
        TE(kpsum)
        mKPS = sT.v

        def kpred(v, mKPS=mKPS):
            v.wait_ge(sT.h, mKPS)
            v.reduce_max(kn2row[0:1, 16:17], psM[0:1, 0:512], axis=AX)
            v.reduce_max(lrs[0:1, 0:1], psO[0:1, 0:512], axis=AX)
            v.drain()
            v.tensor_tensor(kn2row[0:1, 16:17], kn2row[0:1, 16:17], lrs[0:1, 0:1], ALU.max)
            v.drain()
            _i(v.tensor_copy(kn2rb[:, :], kn2row[:, :]), sV)
        VE(kpred)
        mKN2B = sV.v

        def kpout(sy, mKPR=mKPR, mKN2B=mKN2B):
            sy.wait_ge(sV.h, mKPR)
            tShard.append(dqQ.issue(
                sy,
                out=agkv_in.ap()[KPE_OFF:KPE_OFF + ROPE * T].rearrange("(p m) -> p m", p=64),
                in_=kper[:, :]))
            sy.wait_ge(sV.h, mKN2B)
            tShard.append(dqQ.issue(
                sy,
                out=agkv_in.ap()[KN2_OFF:KN2_OFF + 32].rearrange("(o x) -> o x", o=1),
                in_=kn2rb[:, :]))
        SY(kpout)

        p2a.close()
        wdq = p2.enter_context(nc.sbuf_tensor("wdq", [128, 16 * QL], BF16))

        def wdql(g, mKP=mKP):
            g.wait_ge(sT.h, mKP)
            tW.append(dqG.issue(g, out=wdq[:, :].rearrange("p (c j) -> p c j", c=16),
                                in_=w_dq_d.ap().rearrange("(c p) j -> p c j", p=128)))
        GE(wdql)

        def agkv(g):
            for tk in tShard:
                DQ.wait(g, tk)
            _i(g.collective_compute(
                "AllGather", ALU.bypass,
                replica_groups=[list(range(NCORES))],
                ins=[agkv_in.ap().opt()],
                outs=[agkv_out.ap().opt()]), sCC)
        GE(agkv)
        mAG = sCC.v

        mAQ = proj_rms(wdq, 6, gq_sb, aqn)

        if stage == 2:
            def s2(sy):
                sy.wait_ge(sCC.h, mAG)
                tk = dqQ.issue(sy, out=dbg.ap(), in_=agkv_in.ap())
                DQ.wait(sy, tk)
            SY(s2)

            def s2g(g, mAQ=mAQ):
                g.wait_ge(sV.h, mAQ)
                tk = dqG.issue(g, out=y_d[0:128, :], in_=aqn[:, 0:D])
                DQ.wait(g, tk)
            GE(s2g)
            p2.close()
            return nc

        p2.close()

        at = ExitStack()
        K_sb = at.enter_context(nc.sbuf_tensor("K_sb", [128, 2 * S], BF16))
        V_sb = at.enter_context(nc.sbuf_tensor("V_sb", [128, 2 * S], BF16))
        kpeA = at.enter_context(nc.sbuf_tensor("kpeA", [64, S], BF16))
        A_sb = at.enter_context(nc.sbuf_tensor("A_sb", [128, 2 * 1024], BF16))
        AT_sb = at.enter_context(nc.sbuf_tensor("AT_sb", [128, 2 * 512], BF16))
        Qn = at.enter_context(nc.sbuf_tensor("Qn", [128, 2 * T], BF16))
        Qp = at.enter_context(nc.sbuf_tensor("Qp", [64, 2 * T], BF16))
        peF = at.enter_context(nc.sbuf_tensor("peF", [64, 512], F32))
        peG = at.enter_context(nc.sbuf_tensor("peG", [64, 512], F32))
        Q2n = at.enter_context(nc.sbuf_tensor("Q2n", [128, T], BF16))
        Q2p = at.enter_context(nc.sbuf_tensor("Q2p", [64, T], BF16))
        wuqh = at.enter_context(nc.sbuf_tensor("wuqh", [128, 2 * 6 * 256], BF16))
        negB = at.enter_context(nc.sbuf_tensor("negB", [128, NT], F32))
        sigc = at.enter_context(nc.sbuf_tensor("sigc", [128, 2 * NT], F32))
        sigS = at.enter_context(nc.sbuf_tensor("sigS", [128, 1], F32))
        rSg = at.enter_context(nc.sbuf_tensor("rSg", [128, 1], F32))
        O_sb = at.enter_context(nc.sbuf_tensor("O_sb", [128, 128], BF16))
        m2krow = at.enter_context(nc.sbuf_tensor("m2krow", [1, 32], F32))
        m2ksc = at.enter_context(nc.sbuf_tensor("m2ksc", [128, 16], F32))

        def m2kload(g):
            g.wait_ge(sCC.h, mAG)
            g.wait_ge(sV.h, mAQ)
            tk = dqG.issue(
                g, out=kn2x[:, :],
                in_=agkv_out.ap()[0:NCORES * KVSH].rearrange("(r x) -> r x", x=KVSH)
                [:, KN2_OFF:KN2_OFF + 32].rearrange("r p -> p r"),
                allow_slow_non_contiguous=True)
            DQ.wait(g, tk)
            _i(g.drain(), sP)
        GE(m2kload)
        mK2X = sP.v

        def m2kred(v):
            v.wait_ge(sP.h, mK2X)
            _i(v.reduce_max(m2k16[:, :], kn2x[:, :], axis=AX), sV)
        VE(m2kred)
        mM2K0 = sV.v

        def m2ktr(t):
            t.wait_ge(sV.h, mM2K0)
            t.transpose(psM[0:1, 448:480], m2k16[0:32, 0:1], id32f[:, :])
            _i(t.drain(), sT)
        TE(m2ktr)
        mM2KT = sT.v

        def m2krow_f(v):
            v.wait_ge(sT.h, mM2KT)
            v.tensor_copy(m2krow[:, :], psM[0:1, 448:480])
            v.drain()
            _i(v.tensor_scalar(m2krow[0:1, 0:16], m2krow[0:1, 0:16],
                               m2krow[0:1, 16:17], SCALE * SCALE,
                               ALU.add, ALU.mult), sV)
        VE(m2krow_f)
        mM2KR = sV.v

        def m2kpb2(g):
            g.wait_ge(sV.h, mM2KR)
            _i(g.partition_broadcast(m2ksc[:, :], m2krow[0:1, 0:16]), sP)
        GE(m2kpb2)
        mM2KS = sP.v

        tKPE = []

        def kpeload(sy, mAQ=mAQ):
            sy.wait_ge(sCC.h, mAG)
            sy.wait_ge(sV.h, mAQ)
            for jj in range(NKC):
                tKPE.append(dqQ.issue(
                    sy,
                    out=kpeA[:, 1024 * jj:1024 * (jj + 1)].rearrange(
                        "p (r c) -> p r c", r=8),
                    in_=agkv_out.ap()[0:NCORES * KVSH]
                    .rearrange("(r x) -> r x", x=KVSH)
                    [:, KPE_OFF:KPE_OFF + ROPE * T].rearrange("r (p m) -> p r m", p=ROPE)
                    [:, :, 128 * jj:128 * (jj + 1)]))
        SY(kpeload)

        mHead = [None] * H
        mQMM = [None] * H
        tWQ = [None] * H
        tKV = [[None] * NKC for _ in range(H)]
        mExpL = [0] * H
        ccg = [0]
        mExp = {}
        mTrB = {}
        mATe = {}
        mOe = [0] * (NT + 2)
        mSgP = [0, 0]
        mSC2P = [0]

        for h in range(H):
            par = h % 2

            def wq(g, h=h, par=par):
                if h < 2:
                    g.wait_ge(sCC.h, mAG)
                    g.wait_ge(sV.h, mAQ)
                else:
                    g.wait_ge(sT.h, mQMM[h - 2])
                toks = []
                base = par * 1536
                w = wuqh[:, :].rearrange("p (q l u) -> p q l u", q=2, l=6)
                toks.append(dqG.issue(
                    g, out=w[:, par, :, 0:128],
                    in_=w_uq_d.ap().rearrange("(l p) j -> p l j", p=128)
                    [:, :, QKD * h:QKD * h + 128]))
                toks.append(dqG.issue(
                    g, out=w[:, par, :, 128:192],
                    in_=w_uq_d.ap().rearrange("(l p) j -> p l j", p=128)
                    [:, :, QKD * h + 128:QKD * h + 192]))
                toks.append(dqG.issue(
                    g, out=w[:, par, :, 192:224],
                    in_=w_uq_d.ap().rearrange("(l p) j -> p l j", p=128)
                    [:, :, QKD * h + 160:QKD * h + 192]))
                toks.append(dqG.issue(
                    g, out=w[:, par, :, 224:256],
                    in_=w_uq_d.ap().rearrange("(l p) j -> p l j", p=128)
                    [:, :, QKD * h + 128:QKD * h + 160]))
                tWQ[h] = toks
            GE(wq)

            def kv(sy, h=h, par=par, mAQ=mAQ):
                if h < 2:
                    sy.wait_ge(sCC.h, mAG)
                    sy.wait_ge(sV.h, mAQ)
                else:
                    sy.wait_ge(sT.h, mHead[h - 2])
                kview = agkv_out.ap()[0:NCORES * KVSH] \
                    .rearrange("(r x) -> r x", x=KVSH) \
                    [:, KN_OFF + h * 128 * T:KN_OFF + (h + 1) * 128 * T] \
                    .rearrange("r (p m) -> p r m", p=128)
                vview = agkv_out.ap()[0:NCORES * KVSH] \
                    .rearrange("(r x) -> r x", x=KVSH) \
                    [:, V_OFF + h * T * VD:V_OFF + (h + 1) * T * VD] \
                    .rearrange("r (b p v) -> p b r v", b=NT, p=128)
                for jj in range(NKC):
                    t1 = dqQ.issue(
                        sy,
                        out=K_sb[:, par * S + 1024 * jj:par * S + 1024 * (jj + 1)]
                        .rearrange("p (r c) -> p r c", r=8),
                        in_=kview[:, :, 128 * jj:128 * (jj + 1)])
                    t2 = dqQ.issue(
                        sy,
                        out=V_sb[:, par * S + 1024 * jj:par * S + 1024 * (jj + 1)]
                        .rearrange("p (r v) -> p r v", r=8),
                        in_=vview[:, jj, :, :])
                    tKV[h][jj] = (t1, t2)
            SY(kv)

            def qmm(t, h=h, par=par):
                for tk in tWQ[h]:
                    DQ.wait(t, tk)
                if h == 0:
                    t.wait_ge(sV.h, mAQ)
                else:
                    t.wait_ge(sS.h, mExpL[h - 1])
                    t.wait_ge(sV.h, mOe[7])
                w = wuqh[:, :].rearrange("p (q l u) -> p q l u", q=2, l=6)
                for half in range(2):
                    for l in range(6):
                        t.matmul(psA[:, 512 * half:512 * (half + 1)],
                                 w[:, par, l, 0:128],
                                 aqn[:, :].rearrange("p (c m) -> p c m", c=6)
                                 [:, l, 512 * half:512 * (half + 1)],
                                 start=(l == 0), stop=(l == 5))
                for half in range(2):
                    for l in range(6):
                        t.matmul(psB[0:64, 512 * half:512 * (half + 1)],
                                 w[:, par, l, 128:192],
                                 aqn[:, :].rearrange("p (c m) -> p c m", c=6)
                                 [:, l, 512 * half:512 * (half + 1)],
                                 start=(l == 0), stop=(l == 5))
                for half in range(2):
                    for l in range(6):
                        t.matmul(psO[0:64, 0:512] if half == 0 else psM[0:64, 0:512],
                                 w[:, par, l, 192:256],
                                 aqn[:, :].rearrange("p (c m) -> p c m", c=6)
                                 [:, l, 512 * half:512 * (half + 1)],
                                 start=(l == 0), stop=(l == 5))
                _i(t.drain(), sT)
            TE(qmm)
            mQMM[h] = sT.v

            def qev(v, h=h, par=par):
                v.wait_ge(sT.h, mQMM[h])
                v.tensor_copy(Qn[:, par * T:(par + 1) * T], psA[:, :])
                for half in range(2):
                    pesw = psO[0:64, 0:512] if half == 0 else psM[0:64, 0:512]
                    v.tensor_tensor(peF[:, :], psB[0:64, 512 * half:512 * (half + 1)],
                                    ropeC[0:64, 512 * half:512 * (half + 1)], ALU.mult)
                    v.tensor_tensor(peG[:, :], pesw,
                                    ropeS[0:64, 512 * half:512 * (half + 1)], ALU.mult)
                    v.drain()
                    v.tensor_tensor(Qp[:, par * T + 512 * half:par * T + 512 * (half + 1)],
                                    peF[:, :], peG[:, :], ALU.add)
                    v.drain()
                _i(v.drain(), sV)
            VE(qev)
            mQEV = sV.v

            def qsq(sc, h=h, par=par, mQEV=mQEV):
                sc.wait_ge(sV.h, mQEV)
                sc.activation(Q2n[:, :], Qn[:, par * T:(par + 1) * T], ACT.Square)
                _i(sc.activation(Q2p[:, :], Qp[:, par * T:(par + 1) * T], ACT.Square), sS)
            SE(qsq)
            mQSQ = sS.v

            def qn2mm(t, h=h, mQSQ=mQSQ):
                t.wait_ge(sS.h, mQSQ)
                for j in range(NT):
                    t.matmul(psM[:, 256 + j:256 + j + 1], Q2n[:, 128 * j:128 * (j + 1)],
                             ones[:, 0:1], start=True, stop=False)
                    t.matmul(psM[:, 256 + j:256 + j + 1], Q2p[:, 128 * j:128 * (j + 1)],
                             ones[0:64, 0:1], start=False, stop=True)
                _i(t.drain(), sT)
            TE(qn2mm)
            mQN2 = sT.v

            def bsqrt(sc, h=h, mQN2=mQN2):
                sc.wait_ge(sP.h, mM2KS)
                sc.wait_ge(sT.h, mQN2)
                _i(sc.activation(negB[:, :], psM[:, 256:256 + NT], ACT.Sqrt,
                                 scale=m2ksc[:, h:h + 1]), sS)
            SE(bsqrt)
            mBSQ = sS.v

            def bneg(v, mBSQ=mBSQ):
                v.wait_ge(sS.h, mBSQ)
                _i(v.tensor_scalar(negB[:, :], negB[:, :], -1.0, None, ALU.mult), sV)
            VE(bneg)
            mNEGB = sV.v

            mQnEv = mQEV
            cch = 0
            for j in range(NT):
                jpar = j % 2
                for jj in range(j + 1):
                    cc = ccg[0]
                    spar = cc % 2
                    apar = cc % 2
                    psX = psA if spar == 0 else psB
                    diag = (jj == j)

                    def smm(t, h=h, j=j, jj=jj, cc=cc, cch=cch, par=par, psX=psX,
                            mQnEv=mQnEv, mNEGB=mNEGB):
                        DQ.wait(t, tKV[h][jj][0])
                        DQ.wait(t, tKV[h][jj][1])
                        DQ.wait(t, tKPE[jj])
                        if cc >= 2 and (cc - 2) in mExp:
                            t.wait_ge(sS.h, mExp[cc - 2])
                        if cch == 0:
                            t.wait_ge(sV.h, mQnEv)
                        for half in range(2):
                            t.matmul(psX[:, 512 * half:512 * (half + 1)],
                                     Qn[:, par * T + 128 * j:par * T + 128 * (j + 1)],
                                     K_sb[:, par * S + 1024 * jj + 512 * half:
                                          par * S + 1024 * jj + 512 * (half + 1)],
                                     start=True, stop=False)
                        for half in range(2):
                            t.matmul(psX[:, 512 * half:512 * (half + 1)],
                                     Qp[:, par * T + 128 * j:par * T + 128 * (j + 1)],
                                     kpeA[:, 1024 * jj + 512 * half:
                                          1024 * jj + 512 * (half + 1)],
                                     start=False, stop=True)
                        _i(t.drain(), sT)
                    TE(smm)
                    mS = sT.v

                    if diag:
                        def dmk(v, mS=mS, psX=psX):
                            v.wait_ge(sT.h, mS)
                            _i(v.tensor_tensor(psX[:, :], psX[:, :], dmask[:, :],
                                               ALU.add), sV)
                        VE(dmk)
                        mMsk = sV.v
                    else:
                        mMsk = None

                    def expf(sc, cc=cc, j=j, jj=jj, jpar=jpar, apar=apar, psX=psX,
                             mS=mS, mMsk=mMsk, mNEGB=mNEGB, mSgPrev=mSgP[jpar]):
                        if mMsk is not None:
                            sc.wait_ge(sV.h, mMsk)
                        else:
                            sc.wait_ge(sT.h, mS)
                        sc.wait_ge(sV.h, mNEGB)
                        if jj == 0 and mSgPrev:
                            sc.wait_ge(sV.h, mSgPrev)
                        if cc >= 2 and (cc - 2) in mTrB:
                            sc.wait_ge(sT.h, mTrB[cc - 2])
                        _i(sc.activation(
                            A_sb[:, apar * 1024:(apar + 1) * 1024], psX[:, :],
                            ACT.Exp, scale=SCALE, bias=negB[:, j:j + 1],
                            accum_out=sigc[:, jpar * NT + jj:jpar * NT + jj + 1]), sS)
                    SE(expf)
                    mExp[cc] = sS.v
                    mExpL[h] = sS.v

                    for b in range(2):
                        bb = 2 * cc + b
                        tpar = bb % 2
                        ptX = ptA if tpar == 0 else ptB

                        def trb(t, cc=cc, b=b, bb=bb, apar=apar, ptX=ptX):
                            t.wait_ge(sS.h, mExp[cc])
                            if bb >= 2 and (bb - 2) in mATe:
                                t.wait_ge(sV.h, mATe[bb - 2])
                            for z in range(4):
                                t.transpose(ptX[:, 128 * z:128 * (z + 1)],
                                            A_sb[:, apar * 1024 + 512 * b + 128 * z:
                                                 apar * 1024 + 512 * b + 128 * (z + 1)],
                                            ident[:, :])
                            _i(t.drain(), sT)
                        TE(trb)
                        mTrB[cc] = sT.v
                        mTr = sT.v

                        def ate(v, bb=bb, mTr=mTr, tpar=tpar, ptX=ptX):
                            v.wait_ge(sT.h, mTr)
                            _i(v.tensor_copy(AT_sb[:, tpar * 512:(tpar + 1) * 512],
                                             ptX[:, 0:512]), sV)
                        VE(ate)
                        mATe[bb] = sV.v

                        def avm(t, h=h, j=j, jj=jj, b=b, bb=bb, cc=cc, par=par,
                                tpar=tpar):
                            t.wait_ge(sV.h, mATe[bb])
                            if jj == 0 and b == 0 and j >= 2:
                                t.wait_ge(sV.h, mOe[j - 2])
                            for z in range(4):
                                kt = 8 * jj + 4 * b + z
                                t.matmul(psO[:, 128 * (j % 2):128 * (j % 2 + 1)],
                                         AT_sb[:, tpar * 512 + 128 * z:
                                               tpar * 512 + 128 * (z + 1)],
                                         V_sb[:, par * S + 128 * kt:
                                              par * S + 128 * (kt + 1)],
                                         start=(jj == 0 and b == 0 and z == 0),
                                         stop=(jj == j and b == 1 and z == 3))
                            _i(t.drain(), sT)
                        TE(avm)
                    ccg[0] += 1
                    cch += 1
                mAVj = sT.v

                def sgm(v, j=j, jpar=jpar, mlast=mExpL[h]):
                    v.wait_ge(sS.h, mlast)
                    _i(v.reduce_sum(sigS[:, :], sigc[:, jpar * NT:jpar * NT + j + 1],
                                    axis=AX), sV)
                VE(sgm)
                mSg = sV.v
                mSgP[jpar] = sV.v

                def sln(sc, mSg=mSg):
                    sc.wait_ge(sV.h, mSg)
                    sc.activation(rSg[:, :], sigS[:, :], ACT.Ln)
                    sc.drain()
                    _i(sc.activation(rSg[:, :], rSg[:, :], ACT.Exp, scale=-1.0), sS)
                SE(sln)
                mRS = sS.v

                def oev(v, j=j, mAVj=mAVj, mRS=mRS):
                    v.wait_ge(sT.h, mAVj)
                    v.wait_ge(sS.h, mRS)
                    _i(v.tensor_scalar(O_sb[:, :],
                                       psO[:, 128 * (j % 2):128 * (j % 2 + 1)],
                                       rSg[:, 0:1], None, ALU.mult), sV)
                VE(oev)
                mOev = sV.v

                def otr(t, mOev=mOev):
                    t.wait_ge(sV.h, mOev)
                    t.transpose(ptA[:, 512:640], O_sb[:, :], ident[:, :])
                    _i(t.drain(), sT)
                TE(otr)
                mOtr = sT.v

                def ofe(v, h=h, j=j, mOtr=mOtr):
                    v.wait_ge(sT.h, mOtr)
                    _i(v.tensor_copy(o_fm[:, T * h + 128 * j:T * h + 128 * (j + 1)],
                                     ptA[:, 512:640]), sV)
                VE(ofe)
                mOe[j] = sV.v
            mHead[h] = sT.v

        mOFM = sV.v

        if stage == 3:
            def s3(g):
                g.wait_ge(sV.h, mOFM)
                tk = dqG.issue(g, out=dbg.ap().rearrange("(p m) -> p m", p=128)
                               if False else dbg[:, :], in_=o_fm[:, :])
                DQ.wait(g, tk)
            GE(s3)
            at.close()
            return nc

        at.close()
        mF4 = fence()

        sel_ctx = ExitStack()
        sc_tm = sel_ctx.enter_context(nc.sbuf_tensor("sc_tm", [128, NT], F32))
        sig_tm = sel_ctx.enter_context(nc.sbuf_tensor("sig_tm", [128, NT], F32))
        tau_t = sel_ctx.enter_context(nc.sbuf_tensor("tau_t", [1, 2], F32))
        tau_b = sel_ctx.enter_context(nc.sbuf_tensor("tau_b", [128, 1], F32))
        cnt_t = sel_ctx.enter_context(nc.sbuf_tensor("cnt_t", [1, 1], F32))
        sel16 = sel_ctx.enter_context(nc.sbuf_tensor("sel16", [16, CAP], I16))
        self32 = sel_ctx.enter_context(nc.sbuf_tensor("self32", [1, CAP], F32))
        scatf = sel_ctx.enter_context(nc.sbuf_tensor("scatf", [1, CAP], F32))
        cge = sel_ctx.enter_context(nc.sbuf_tensor("cge", [1, CAP], F32))
        scat16 = sel_ctx.enter_context(nc.sbuf_tensor("scat16", [1, CAP], I16))
        scat_b = sel_ctx.enter_context(nc.sbuf_tensor("scat_b", [128, CAP], I16))
        gidx = sel_ctx.enter_context(nc.sbuf_tensor("gidx", [128, CAP // 16], I16))
        dfm = sel_ctx.enter_context(nc.sbuf_tensor("dfm", [128, 16 * 1024], BF16))
        p4 = ExitStack()
        wo_sb = p4.enter_context(nc.sbuf_tensor("wo_sb", [128, 2 * D], BF16))
        rw_row = p4.enter_context(nc.sbuf_tensor("rw_row", [1, D], BF16))
        rwb = p4.enter_context(nc.sbuf_tensor("rwb", [128, D], BF16))
        ao_fm = p4.enter_context(nc.sbuf_tensor("ao_fm", [128, T], BF16))
        ao_tm = p4.enter_context(nc.sbuf_tensor("ao_tm", [128, NT * D], BF16))
        x0r = [p4.enter_context(nc.sbuf_tensor(f"x0r{i}", [128, D], F32)) for i in range(2)]
        x1t = [p4.enter_context(nc.sbuf_tensor(f"x1t{i}", [128, D], F32)) for i in range(2)]
        ht = [p4.enter_context(nc.sbuf_tensor(f"ht{i}", [128, D], BF16)) for i in range(2)]
        st6b = p4.enter_context(nc.sbuf_tensor("st6b", [128, 24], F32))
        st2b = p4.enter_context(nc.sbuf_tensor("st2b", [128, 2], F32))
        lrs2 = p4.enter_context(nc.sbuf_tensor("lrs2", [128, 1], F32))
        lnm2 = p4.enter_context(nc.sbuf_tensor("lnm2", [128, 1], F32))

        def rbpb(g):
            g.wait_ge(sP.h, mF4)
            DQ.wait(g, tC0[10])
            _i(g.partition_broadcast(rbb[:, :], rb_sb[0:1, 0:1]), sP)
            tok = dqG.issue(g, out=rw_row[:, :],
                            in_=rw_d.ap().rearrange("(o x) -> o x", o=1))
            DQ.wait(g, tok)
            _i(g.partition_broadcast(rwb[:, :], rw_row[0:1, :]), sP)
        GE(rbpb)
        mRBB = sP.v

        tWOC = [None] * 16
        mWOev = [0] * 16
        mAOtr = [0] * 16
        for oc in range(16):
            def wod(g, oc=oc):
                if oc < 2:
                    g.wait_ge(sP.h, mF4)
                else:
                    g.wait_ge(sT.h, mWOmm[oc - 2])
                tWOC[oc] = dqG.issue(
                    g, out=wo_sb[:, (oc % 2) * D:(oc % 2 + 1) * D]
                    .rearrange("p (c j) -> p c j", c=16),
                    in_=w_o_d.ap().rearrange("(c p) j -> p c j", p=128)
                    [:, :, 128 * oc:128 * (oc + 1)])
            GE(wod)

            def womm(t, oc=oc):
                DQ.wait(t, tWOC[oc])
                if oc == 0:
                    t.wait_ge(sV.h, mOFM)
                if oc >= 2:
                    t.wait_ge(sV.h, mWOev[oc - 2])
                psX = psA if oc % 2 == 0 else psB
                for half in range(2):
                    for ic in range(16):
                        t.matmul(psX[:, 512 * half:512 * (half + 1)],
                                 wo_sb[:, (oc % 2) * D:(oc % 2 + 1) * D]
                                 .rearrange("p (c j) -> p c j", c=16)[:, ic, :],
                                 o_fm[:, :].rearrange("p (c m) -> p c m", c=16)
                                 [:, ic, 512 * half:512 * (half + 1)],
                                 start=(ic == 0), stop=(ic == 15))
                _i(t.drain(), sT)
            TE(womm)
            if oc == 0:
                mWOmm = [0] * 16
            mWOmm[oc] = sT.v

            def woev(v, oc=oc, m=mWOmm[oc]):
                v.wait_ge(sT.h, m)
                if oc >= 2:
                    v.wait_ge(sT.h, mAOtr[oc - 2])
                psX = psA if oc % 2 == 0 else psB
                _i(v.tensor_copy(ao_fm[:, :], psX[:, :]), sV)
            VE(woev)
            mWOev[oc] = sV.v

            def aotr(t, oc=oc, m=mWOev[oc]):
                t.wait_ge(sV.h, m)
                for half in range(2):
                    ptX = ptA if half == 0 else ptB
                    for z in range(4):
                        j = 4 * half + z
                        t.transpose(ptX[:, 128 * z:128 * (z + 1)],
                                    ao_fm[:, 128 * j:128 * (j + 1)], ident[:, :])
                    _i(t.drain(), sT)
            TE(aotr)
            mAOtr[oc] = sT.v
            mtr2 = (sT.v - 1, sT.v)

            def aoev(v, oc=oc, mtr2=mtr2):
                for half in range(2):
                    v.wait_ge(sT.h, mtr2[half])
                    _i(v.tensor_copy(
                        ao_tm[:, :].rearrange("p (j m) -> p j m", j=NT)
                        [:, 4 * half:4 * half + 4, 128 * oc:128 * (oc + 1)],
                        (ptA if half == 0 else ptB)[:, 0:512]
                        .rearrange("p (j m) -> p j m", j=4)), sV)
            VE(aoev)
        mAOTM = sV.v
        ofm_ctx.close()
        sel_tr = ExitStack()
        sc_all = sel_tr.enter_context(nc.sbuf_tensor("sc_all", [128, 64], F32, side="right"))
        kthtmp = sel_tr.enter_context(nc.sbuf_tensor("kthtmp", [128, 64], F32, side="right"))
        selmask = sel_tr.enter_context(nc.sbuf_tensor("selmask", [1, T], F32, side="right"))
        sc_own = sel_tr.enter_context(nc.sbuf_tensor("sc_own", [1, T], F32, side="right"))
        pref_a = sel_tr.enter_context(nc.sbuf_tensor("pref_a", [1, T], F32, side="right"))
        pref_b = sel_tr.enter_context(nc.sbuf_tensor("pref_b", [1, T], F32, side="right"))
        ranks = sel_tr.enter_context(nc.sbuf_tensor("ranks", [1, T], F32, side="right"))
        r16b = sel_tr.enter_context(nc.sbuf_tensor("r16b", [16, T], I16, side="right"))

        tX1S = [None] * NT
        tHS = [None] * NT
        mX1f = [0, 0]
        mHtf = [0, 0]
        for j in range(NT):
            par = j % 2
            tx0 = [None]

            def ldx0(sy, j=j, par=par, w=mX1f[par]):
                if w:
                    sy.wait_ge(sV.h, w)
                else:
                    sy.wait_ge(sP.h, mF4)
                tx0[0] = dqQ.issue(sy, out=x0r[par][:, :],
                                   in_=x0_d[128 * j:128 * (j + 1), :])
            SY(ldx0)
            tx0 = tx0[0]

            def ln2a(v, j=j, mAOTM=mAOTM):
                v.wait_ge(sV.h, mAOTM)
                tl = ao_tm[:, :].rearrange("p (j m) -> p j m", j=NT)[:, j, :]
                for q in range(4):
                    v.bn_stats(st6b[:, 6 * q:6 * q + 6], tl[:, 512 * q:512 * (q + 1)])
                v.drain()
                v.bn_aggr(st2b[:, :], st6b[:, :])
                v.drain()
                v.tensor_scalar(lrs2[:, :], st2b[:, 1:2], EPS_LN, None, ALU.add)
                v.drain()
                _i(v.reciprocal(lrs2[:, :], lrs2[:, :]), sV)
            VE(ln2a)
            m1 = sV.v

            def ln2b(sc, m1=m1):
                sc.wait_ge(sV.h, m1)
                _i(sc.activation(lrs2[:, :], lrs2[:, :], ACT.Sqrt), sS)
            SE(ln2b)
            m2 = sS.v

            def ln2c(v, j=j, par=par, m2=m2, tx0=tx0):
                v.wait_ge(sS.h, m2)
                v.tensor_mul(lnm2[:, :], st2b[:, 0:1], lrs2[:, :])
                v.drain()
                DQ.wait(v, tx0)
                tl = ao_tm[:, :].rearrange("p (j m) -> p j m", j=NT)[:, j, :]
                v.scalar_tensor_tensor(x1t[par][:, :], tl, lrs2[:, 0:1], x0r[par][:, :],
                                       ALU.mult, ALU.add)
                v.drain()
                _i(v.tensor_scalar(x1t[par][:, :], x1t[par][:, :], lnm2[:, 0:1],
                                   None, ALU.subtract), sV)
            VE(ln2c)
            mX1 = sV.v
            mX1f[par] = mX1

            def spx1(sy, j=j, par=par, mX1=mX1):
                sy.wait_ge(sV.h, mX1)
                tX1S[j] = dqQ.issue(sy, out=x1_dram[128 * j:128 * (j + 1), :],
                                    in_=x1t[par][:, :])
            SY(spx1)

            def ln3a(v, j=j, par=par, mX1=mX1):
                v.drain()
                for q in range(4):
                    v.bn_stats(st6b[:, 6 * q:6 * q + 6],
                               x1t[par][:, 512 * q:512 * (q + 1)])
                v.drain()
                v.bn_aggr(st2b[:, :], st6b[:, :])
                v.drain()
                v.tensor_scalar(lrs2[:, :], st2b[:, 1:2], EPS_LN, None, ALU.add)
                v.drain()
                _i(v.reciprocal(lrs2[:, :], lrs2[:, :]), sV)
            VE(ln3a)
            m3 = sV.v

            def ln3b(sc, m3=m3):
                sc.wait_ge(sV.h, m3)
                _i(sc.activation(lrs2[:, :], lrs2[:, :], ACT.Sqrt), sS)
            SE(ln3b)
            m4 = sS.v

            def ln3c(v, m4=m4):
                v.wait_ge(sS.h, m4)
                v.tensor_mul(lnm2[:, :], st2b[:, 0:1], lrs2[:, :])
                v.drain()
                _i(v.tensor_scalar(lnm2[:, :], lnm2[:, :], -1.0, None, ALU.mult), sV)
            VE(ln3c)
            m5 = sV.v

            def ln3d(sc, j=j, par=par, m5=m5, whf=mHtf[par],
                     tprev=(tHS[j - 2] if j >= 2 else None)):
                sc.wait_ge(sV.h, m5)
                if whf:
                    sc.wait_ge(sV.h, whf)
                if tprev is not None:
                    DQ.wait(sc, tprev)
                _i(sc.activation(ht[par][:, :], x1t[par][:, :], ACT.Identity,
                                 bias=lnm2[:, 0:1], scale=lrs2[:, 0:1]), sS)
            SE(ln3d)
            m6 = sS.v

            def sth(sy, j=j, par=par, m6=m6):
                sy.wait_ge(sS.h, m6)
                tHS[j] = dqQ.issue(sy, out=h_dram[128 * j:128 * (j + 1), :],
                                   in_=ht[par][:, :])
            SY(sth)

            def scj(v, j=j, par=par, m6=m6):
                v.wait_ge(sS.h, m6)
                DQ.wait(v, tX1S[j])
                v.tensor_tensor(x1t[par][:, :], ht[par][:, :], rwb[:, :], ALU.mult)
                v.drain()
                _i(v.reduce_sum(sc_tm[:, j:j + 1], x1t[par][:, :], axis=AX), sV)
            VE(scj)
            mHtf[par] = sV.v
        mSCT = sV.v

        def scf(v, mSCT=mSCT):
            v.wait_ge(sV.h, mSCT)
            v.wait_ge(sP.h, mRBB)
            _i(v.tensor_scalar(sc_tm[:, :], sc_tm[:, :], rbb[:, 0:1], None, ALU.add),
               sV)
        VE(scf)
        mSCF = sV.v

        def sgf(sc, mSCF=mSCF):
            sc.wait_ge(sV.h, mSCF)
            _i(sc.activation(sig_tm[:, :], sc_tm[:, :], ACT.Sigmoid), sS)
        SE(sgf)
        mSIG = sS.v

        if stage == 4:
            def s4(sy):
                for j in range(NT):
                    DQ.wait(sy, tX1S[j])
                tk = dqQ.issue(sy, out=dbg[:, :], in_=x1_dram[:, :])
                DQ.wait(sy, tk)
            SY(s4)
            p4.close()
            return nc

        tSC = [None]

        def scout(sy, mSCF=mSCF):
            sy.wait_ge(sV.h, mSCF)
            tSC[0] = dqQ.issue(sy, out=agsc_in.ap().rearrange("(j p) -> p j", p=128),
                               in_=sc_tm[:, :], allow_slow_non_contiguous=True)
        SY(scout)

        def agsc(g):
            DQ.wait(g, tSC[0])
            _i(g.collective_compute(
                "AllGather", ALU.bypass,
                replica_groups=[list(range(NCORES))],
                ins=[agsc_in.ap().opt()],
                outs=[agsc_out.ap().opt()]), sCC)
        GE(agsc)
        mAGS = sCC.v

        tSCA = [None]
        tSCO = [None]

        def scload(sy):
            sy.wait_ge(sCC.h, mAGS)
            tSCA[0] = dqQ.issue(sy, out=sc_all[:, :],
                                in_=agsc_out.ap().rearrange("(p f) -> p f", p=128))
            tSCO[0] = dqQ.issue(sy, out=sc_own[:, :],
                                in_=agsc_in.ap().rearrange("(o x) -> o x", o=1))
        SY(scload)

        mKth = [sP.v + 2 * (r + 1) for r in range(5)]
        mKmask = [sV.v + r + 1 for r in range(4)]

        def kthc(g):
            DQ.wait(g, tSCA[0])
            g.load_library(libcfg.attn)
            for rnd in range(5):
                kk = 510 if rnd < 4 else 3
                if rnd > 0:
                    g.wait_ge(sV.h, mKmask[rnd - 1])
                _i(g.kth_largest(tau_t[:, :], sc_all[:, :], n_per_lane=64, k=kk,
                                 quantile=1e-7), sP)
                g.wait_ge(sP.h, sP.v)
                _i(g.partition_broadcast(tau_b[:, :], tau_t[0:1, 0:1]), sP)
                g.wait_ge(sP.h, sP.v)
        GE(kthc)
        assert sP.v == mKth[4]

        def kmsk(v):
            for rnd in range(4):
                v.wait_ge(sP.h, mKth[rnd])
                v.tensor_scalar(kthtmp[:, :], sc_all[:, :], tau_b[:, 0:1], None,
                                ALU.is_ge)
                v.drain()
                v.tensor_scalar(kthtmp[:, :], kthtmp[:, :], -2.0e30, None, ALU.mult)
                v.drain()
                v.tensor_tensor(sc_all[:, :], sc_all[:, :], kthtmp[:, :], ALU.add)
                _i(v.drain(), sV)
        VE(kmsk)
        assert sV.v == mKmask[3]

        def selv(v):
            v.wait_ge(sP.h, mKth[4])
            v.drain()
            DQ.wait(v, tSCO[0])
            v.tensor_scalar(selmask[:, :], sc_own[:, :], tau_t[0:1, 0:1], None,
                            ALU.is_ge)
            v.drain()
            v.reduce_sum(cnt_t[:, :], selmask[:, :], axis=AX)
            v.tensor_copy(pref_a[:, :], selmask[:, :])
            v.drain()
            src_, dst_ = pref_a, pref_b
            for sh in [1, 2, 4, 8, 16, 32, 64, 128, 256, 512]:
                v.tensor_copy(dst_[:, 0:sh], src_[:, 0:sh])
                v.tensor_tensor(dst_[:, sh:T], src_[:, sh:T], src_[:, 0:T - sh],
                                ALU.add)
                v.drain()
                src_, dst_ = dst_, src_
            v.tensor_sub(ranks[:, :], src_[:, :], selmask[:, :])
            v.drain()
            v.tensor_scalar(pref_b[:, :], ranks[:, :], float(CAP), None, ALU.is_lt)
            v.drain()
            v.tensor_mul(pref_b[:, :], pref_b[:, :], selmask[:, :])
            v.tensor_scalar(pref_a[:, :], ranks[:, :], 1.0, None, ALU.add)
            v.drain()
            v.tensor_mul(pref_a[:, :], pref_a[:, :], pref_b[:, :])
            v.drain()
            v.tensor_scalar(pref_a[:, :], pref_a[:, :], 1.0, None, ALU.subtract)
            v.memset(r16b[:, :], -1)
            v.drain()
            _i(v.tensor_copy(r16b[0:1, :], pref_a[:, :]), sV)
        VE(selv)
        mRANKS = sV.v

        def comp(g):
            g.wait_ge(sV.h, mRANKS)
            g.load_library(libcfg.local_scatter)
            _i(g.local_scatter(sel16[:, :], tok16[:, :], r16b[:, :], channels=16,
                               num_elems=CAP, num_idxs=T), sP)
            g.wait_ge(sP.h, sP.v)
            t1 = dqG.issue(g, out=idx_dram[:, :], in_=sel16[0:1, :])
            DQ.wait(g, t1)
            g.memset(gidx[:, :], 0)
            g.drain()
            toks = []
            for b in range(8):
                toks.append(dqG.issue(
                    g, out=gidx[16 * b:16 * (b + 1), :],
                    in_=idx_dram.ap().rearrange("o (f p) -> (o p) f", p=16),
                    allow_slow_non_contiguous=True))
            for tk in toks:
                DQ.wait(g, tk)
            _i(g.drain(), sP)
        GE(comp)
        mGIDX = sP.v

        def selw(v):
            v.wait_ge(sP.h, mGIDX)
            v.tensor_copy(self32[:, :], sel16[0:1, :])
            v.drain()
            v.tensor_scalar(cge[:, :], capio[:, :], cnt_t[0:1, 0:1], None, ALU.is_ge)
            v.tensor_scalar(scatf[:, :], self32[:, :], 1.0, None, ALU.add)
            v.drain()
            v.tensor_mul(scatf[:, :], scatf[:, :], cge[:, :])
            v.drain()
            v.tensor_sub(scatf[:, :], self32[:, :], scatf[:, :])
            v.drain()
            _i(v.tensor_copy(scat16[:, :], scatf[:, :]), sV)
        VE(selw)
        mSCAT = sV.v

        if stage == 5:
            def s5(sy):
                sy.wait_ge(sV.h, mSCAT)
                t1 = dqQ.issue(sy, out=dbg[0:1, :], in_=sc_own[:, :])
                t2 = dqQ.issue(sy, out=dbg[1:2, :], in_=selmask[:, :])
                t3 = dqQ.issue(sy, out=dbg[2:3, :], in_=ranks[:, :])
                t4 = dqQ.issue(sy, out=dbg[3:4, 0:2], in_=tau_t[:, :])
                t5 = dqQ.issue(sy, out=dbg[3:4, 2:3], in_=cnt_t[:, :])
                t6 = dqQ.issue(sy, out=dbg[4:5, 0:CAP], in_=scatf[:, :])
                for tk in (t1, t2, t3, t4, t5, t6):
                    DQ.wait(sy, tk)
            SY(s5)
            p4.close()
            return nc

        p4.close()
        sel_tr.close()
        mF5 = fence()

        f5 = ExitStack()
        hsel = f5.enter_context(nc.sbuf_tensor("hsel", [128, 16 * CAP], BF16))
        g_sb = f5.enter_context(nc.sbuf_tensor("g_sb", [128, 64 * CAP], BF16))
        w1c = [f5.enter_context(nc.sbuf_tensor(f"w1c{i}", [128, 16 * 512], BF16))
               for i in range(2)]
        w2c = [f5.enter_context(nc.sbuf_tensor(f"w2c{i}", [128, 64 * 128], BF16))
               for i in range(2)]
        wd2 = f5.enter_context(nc.sbuf_tensor("wd2", [128, 16 * CAP], BF16))

        def hgat(g):
            g.wait_ge(sP.h, mF5)
            g.load_library(libcfg.attnmlp)
            g.wait_ge(sP.h, mGIDX)
            for j in range(NT):
                DQ.wait(g, tHS[j])
            g.wait_ge(sS.h, mSIG)
            g.wait_ge(sV.h, mSCAT)
            s = dqG.sems[dqG.n % dqG.ns]
            if s.v > 0:
                g.wait_ge(s.h, s.v)
            g.dma_gather(hsel[:, :].rearrange("p (c i) -> p c i", c=16),
                         h_dram[:, :], gidx[:, :], num_idxs=CAP, num_idxs_reg=CAP,
                         elem_size=D, transpose=True).then_inc(s.h, 16)
            s.v += 16
            dqG.n += 1
            g.wait_ge(s.h, s.v)
            _i(g.drain(), sP)
        GE(hgat)
        mHSEL = sP.v

        mW1mm = [0] * 16
        mGelQ = [0] * 64
        tW1 = [None] * 16
        for dc in range(16):
            def w1l(g, dc=dc):
                if dc < 2:
                    g.wait_ge(sP.h, mF5)
                    g.wait_ge(sV.h, mSCAT)
                else:
                    g.wait_ge(sT.h, mW1mm[dc - 2])
                tW1[dc] = dqG.issue(
                    g, out=w1c[dc % 2][:, :].rearrange("p (c j) -> p c j", c=16),
                    in_=w1_d.ap().rearrange("(c p) j -> p c j", p=128)
                    [:, :, 512 * dc:512 * (dc + 1)])
            GE(w1l)
            for q in range(4):
                qq = 4 * dc + q

                def d1mm(t, dc=dc, q=q, qq=qq):
                    if q == 0:
                        DQ.wait(t, tW1[dc])
                    if qq == 0:
                        t.wait_ge(sP.h, mHSEL)
                    if qq >= 2:
                        t.wait_ge(sS.h, mGelQ[qq - 2])
                    psX = psA if qq % 2 == 0 else psB
                    for ic in range(16):
                        t.matmul(psX[:, 0:CAP],
                                 w1c[dc % 2][:, :].rearrange("p (c j) -> p c j", c=16)
                                 [:, ic, 128 * q:128 * (q + 1)],
                                 hsel[:, :].rearrange("p (c i) -> p c i", c=16)
                                 [:, ic, :],
                                 start=(ic == 0), stop=(ic == 15))
                    _i(t.drain(), sT)
                TE(d1mm)
                md1 = sT.v

                def gel(sc, qq=qq, md1=md1):
                    sc.wait_ge(sT.h, md1)
                    psX = psA if qq % 2 == 0 else psB
                    _i(sc.activation(
                        g_sb[:, :].rearrange("p (c i) -> p c i", c=64)[:, qq, :],
                        psX[:, 0:CAP], ACT.Gelu_apprx_tanh,
                        bias=b1_sb[:, qq:qq + 1]), sS)
                SE(gel)
                mGelQ[qq] = sS.v
            mW1mm[dc] = sT.v
        mGELL = sS.v

        mW2mm = [0] * 16
        mW2ev = [0] * 16
        tW2 = [None] * 16
        for oc in range(16):
            def w2l(g, oc=oc):
                if oc < 2:
                    g.wait_ge(sP.h, mF5)
                else:
                    g.wait_ge(sT.h, mW2mm[oc - 2])
                tW2[oc] = dqG.issue(
                    g, out=w2c[oc % 2][:, :].rearrange("p (c j) -> p c j", c=64),
                    in_=w2_d.ap().rearrange("(c p) j -> p c j", p=128)
                    [:, :, 128 * oc:128 * (oc + 1)])
            GE(w2l)

            def d2mm(t, oc=oc):
                DQ.wait(t, tW2[oc])
                t.wait_ge(sS.h, mGELL)
                if oc >= 2:
                    t.wait_ge(sV.h, mW2ev[oc - 2])
                psX = psA if oc % 2 == 0 else psB
                for d in range(64):
                    t.matmul(psX[:, 0:CAP],
                             w2c[oc % 2][:, :].rearrange("p (c j) -> p c j", c=64)
                             [:, d, :],
                             g_sb[:, :].rearrange("p (c i) -> p c i", c=64)[:, d, :],
                             start=(d == 0), stop=(d == 63))
                _i(t.drain(), sT)
            TE(d2mm)
            mW2mm[oc] = sT.v

            def d2ev(v, oc=oc, m=mW2mm[oc]):
                v.wait_ge(sT.h, m)
                psX = psA if oc % 2 == 0 else psB
                _i(v.tensor_scalar(
                    wd2[:, :].rearrange("p (c i) -> p c i", c=16)[:, oc, :],
                    psX[:, 0:CAP], b2_sb[:, oc:oc + 1], None, ALU.add), sV)
            VE(d2ev)
            mW2ev[oc] = sV.v
        mWD2 = sV.v

        def dsc(g):
            g.wait_ge(sV.h, mWD2)
            _i(g.partition_broadcast(scat_b[:, :], scat16[0:1, :]), sP)
            g.wait_ge(sP.h, sP.v)
            g.load_library(libcfg.local_scatter)
            for fc in range(16):
                _i(g.local_scatter(
                    dfm[:, :].rearrange("p (c m) -> p c m", c=16)[:, fc, :],
                    wd2[:, :].rearrange("p (c i) -> p c i", c=16)[:, fc, :],
                    scat_b[:, :], channels=128, num_elems=1024, num_idxs=CAP), sP)
            _i(g.drain(), sP)
        GE(dsc)
        mDFM = sP.v

        f5.close()
        fin = ExitStack()
        dtm = [fin.enter_context(nc.sbuf_tensor(f"dtm{i}", [128, D], BF16))
               for i in range(2)]
        x1r = [fin.enter_context(nc.sbuf_tensor(f"x1r{i}", [128, D], F32))
               for i in range(2)]
        hr = [fin.enter_context(nc.sbuf_tensor(f"hr{i}", [128, D], BF16))
              for i in range(2)]
        ft = fin.enter_context(nc.sbuf_tensor("ft", [128, D], F32))
        yt = [fin.enter_context(nc.sbuf_tensor(f"yt{i}", [128, D], F32))
              for i in range(2)]
        st6c = fin.enter_context(nc.sbuf_tensor("st6c", [128, 24], F32))
        st2c = fin.enter_context(nc.sbuf_tensor("st2c", [128, 2], F32))
        lrs3 = fin.enter_context(nc.sbuf_tensor("lrs3", [128, 1], F32))
        lnm3 = fin.enter_context(nc.sbuf_tensor("lnm3", [128, 1], F32))

        tY = [None] * NT
        mDT = [0, 0]
        mYW = [0, 0]
        gDb = {}
        gbc = [0]
        for j in range(NT):
            par = j % 2
            tld = [None, None]

            def ldf(sy, j=j, par=par, wx=mYW[par]):
                sy.wait_ge(sP.h, mDFM)
                if wx:
                    sy.wait_ge(sV.h, wx)
                DQ.wait(sy, tX1S[j])
                tld[0] = dqQ.issue(sy, out=x1r[par][:, :],
                                   in_=x1_dram[128 * j:128 * (j + 1), :])
                tld[1] = dqQ.issue(sy, out=hr[par][:, :],
                                   in_=h_dram[128 * j:128 * (j + 1), :])
            SY(ldf)
            tldx, tldh = tld

            for b in range(4):
                gb = gbc[0]

                def dtr(t, j=j, b=b, gb=gb, wv=(mDT[par] if b == 0 else 0),
                        wpt=gDb.get(gb - 2)):
                    if b == 0:
                        t.wait_ge(sP.h, mDFM)
                    if wv:
                        t.wait_ge(sV.h, wv)
                    if wpt:
                        t.wait_ge(sV.h, wpt)
                    ptX = ptA if b % 2 == 0 else ptB
                    for z in range(4):
                        fc = 4 * b + z
                        t.transpose(ptX[:, 128 * z:128 * (z + 1)],
                                    dfm[:, :].rearrange("p (c m) -> p c m", c=16)
                                    [:, fc, 128 * j:128 * (j + 1)], ident[:, :])
                    _i(t.drain(), sT)
                TE(dtr)
                mtr = sT.v

                def dte(v, par=par, b=b, mtr=mtr):
                    v.wait_ge(sT.h, mtr)
                    ptX = ptA if b % 2 == 0 else ptB
                    _i(v.tensor_copy(
                        dtm[par][:, 512 * b:512 * (b + 1)], ptX[:, 0:512]), sV)
                VE(dte)
                gDb[gb] = sV.v
                gbc[0] += 1
            mDT[par] = sV.v
            mDTj = sV.v

            def ffo(v, j=j, par=par, mDTj=mDTj, tldx=tldx, tldh=tldh):
                v.wait_ge(sV.h, mDTj)
                DQ.wait(v, tldh)
                v.scalar_tensor_tensor(ft[:, :], dtm[par][:, :], sig_tm[:, j:j + 1],
                                       hr[par][:, :], ALU.mult, ALU.add)
                v.drain()
                for q in range(4):
                    v.bn_stats(st6c[:, 6 * q:6 * q + 6], ft[:, 512 * q:512 * (q + 1)])
                v.drain()
                v.bn_aggr(st2c[:, :], st6c[:, :])
                v.drain()
                v.tensor_scalar(lrs3[:, :], st2c[:, 1:2], EPS_LN, None, ALU.add)
                v.drain()
                _i(v.reciprocal(lrs3[:, :], lrs3[:, :]), sV)
            VE(ffo)
            m1 = sV.v

            def fsq(sc, m1=m1):
                sc.wait_ge(sV.h, m1)
                _i(sc.activation(lrs3[:, :], lrs3[:, :], ACT.Sqrt), sS)
            SE(fsq)
            m2 = sS.v

            def fy(v, j=j, par=par, m2=m2, tldx=tldx,
                   typ=(tY[j - 2] if j >= 2 else None)):
                v.wait_ge(sS.h, m2)
                if typ is not None:
                    DQ.wait(v, typ)
                v.tensor_mul(lnm3[:, :], st2c[:, 0:1], lrs3[:, :])
                v.drain()
                DQ.wait(v, tldx)
                v.scalar_tensor_tensor(yt[par][:, :], ft[:, :], lrs3[:, 0:1],
                                       x1r[par][:, :], ALU.mult, ALU.add)
                v.drain()
                _i(v.tensor_scalar(yt[par][:, :], yt[par][:, :], lnm3[:, 0:1],
                                   None, ALU.subtract), sV)
            VE(fy)
            mYW[par] = sV.v
            mY = sV.v

            def sty(sy, j=j, par=par, mY=mY):
                sy.wait_ge(sV.h, mY)
                tY[j] = dqQ.issue(sy, out=y_d[128 * j:128 * (j + 1), :],
                                  in_=yt[par][:, :])
            SY(sty)

        def fwait(sy):
            for j in range(NT):
                DQ.wait(sy, tY[j])
        SY(fwait)

        fin.close()
        sel_ctx.close()
        return nc
    return nc



def _consts(core):
    pos = (np.arange(T) * NCORES + core).astype(np.float64)
    inv = 1.0 / (10000.0 ** (np.arange(0, ROPE, 2, dtype=np.float64) / ROPE))
    fr = np.outer(inv, pos)
    cos = np.cos(fr)
    sin = np.sin(fr)
    ropeCh = np.tile(cos, (4, 1)).astype(np.float32)
    ropeSh = np.concatenate([np.tile(-sin, 1), sin], axis=0)
    ropeSh = np.tile(np.concatenate([-sin, sin], axis=0), (2, 1)).astype(np.float32)
    r = np.arange(128)[:, None]
    c = np.tile(np.arange(128)[None, :], (1, 8)).reshape(1, 1024)
    rr = np.repeat(np.arange(8), 128).reshape(1, 1024)
    allow = (c < r) | ((c == r) & (rr <= core))
    dm = np.where(allow, 0.0, MASK_NEG).astype(np.float32)
    return ropeCh, ropeSh, dm


def kernel(**inputs):
    nc = build_nc(stage=6)
    return _run(nc, inputs, stage=6)["y"]


def _run(nc, inputs, stage):
    mybir.codegen_inst_isa_subclasses(nc)
    x = np.ascontiguousarray(np.asarray(inputs["L0_attn_x"], dtype=np.float32)[0])
    ident = np.eye(128, dtype=ml_dtypes.bfloat16)
    onesm = np.ones((128, 128), dtype=ml_dtypes.bfloat16)
    tok = np.arange(T, dtype=np.int16).reshape(1, T)
    cap = np.arange(CAP, dtype=np.float32).reshape(1, CAP)
    common = {
        "w_dq": np.asarray(inputs["L0_attn_w_dq"], np.float32),
        "w_uq": np.asarray(inputs["L0_attn_w_uq"], np.float32),
        "w_dkv": np.asarray(inputs["L0_attn_w_dkv"], np.float32),
        "w_kr": np.asarray(inputs["L0_attn_w_kr"], np.float32),
        "w_uk": np.asarray(inputs["L0_attn_w_uk"], np.float32),
        "w_uv": np.asarray(inputs["L0_attn_w_uv"], np.float32),
        "w_o": np.asarray(inputs["L0_attn_w_o"], np.float32),
        "g_q": np.asarray(inputs["L0_attn_g_q"], np.float32),
        "g_kv": np.asarray(inputs["L0_attn_g_kv"], np.float32),
        "w1": np.asarray(inputs["L0_ffn_w1"], np.float32),
        "b1": np.asarray(inputs["L0_ffn_b1"], np.float32),
        "w2": np.asarray(inputs["L0_ffn_w2"], np.float32),
        "b2": np.asarray(inputs["L0_ffn_b2"], np.float32),
        "rw": np.asarray(inputs["L0_ffn_router_weight"], np.float32),
        "rb": np.asarray(inputs["L0_ffn_router_bias"], np.float32),
        "ident": ident, "ones": onesm, "tok16": tok, "capio": cap,
        "id32f": np.eye(32, dtype=np.float32),
    }
    in_maps = []
    for i in range(NCORES):
        rc, rs, dm = _consts(i)
        m = dict(common)
        m["x0"] = np.ascontiguousarray(x[i::NCORES])
        m["ropeC"] = rc
        m["ropeS"] = rs
        m["dmask"] = dm
        in_maps.append(m)
    res = run_bass_kernel_spmd(nc, in_maps, core_ids=list(range(NCORES)))
    if stage == 6:
        out = np.empty((1, S, D), np.float32)
        for i in range(NCORES):
            out[0, i::NCORES, :] = res.results[i]["y"]
        return {"y": out}
    return {"res": res}


# version 03
# speedup vs baseline: 1.6368x; 1.6368x over previous
import math
from contextlib import ExitStack

import numpy as np
import ml_dtypes

import concourse.bass as bass
import concourse.mybir as mybir
import concourse.library_config as libcfg
from concourse.bass_utils import run_bass_kernel_spmd

F32 = mybir.dt.float32
BF16 = mybir.dt.bfloat16
I16 = mybir.dt.int16
AX = mybir.AxisListType.X
ALU = mybir.AluOpType
ACT = mybir.ActivationFunctionType

D = 2048
S = 8192
NCORES = 8
T = S // NCORES
NT = T // 128
H = 16
NOPE, ROPE, VD = 128, 64, 128
QKD = NOPE + ROPE
QL, KVL = 768, 512
DFF = 8192
CAP = 384
SCALE = QKD ** (-0.5)
EPS_LN = 1e-5
EPS_RMS = 1e-6
MASK_NEG = -1.0e30
NKC = 8
KSEL = 2048

KN_OFF = 0
KPE_OFF = KN_OFF + D * T
KN2_OFF = KPE_OFF + ROPE * T
V_OFF = KN2_OFF + 32
KVSH = V_OFF + H * T * VD


class Sem:
    def __init__(self, h):
        self.h = h
        self.v = 0


def _i(instr, s, n=1):
    instr.then_inc(s.h, n)
    s.v += n
    return s.v


class DQ:

    def __init__(self, nc, ctx, name, ns=8):
        self.sems = [Sem(ctx.enter_context(nc.semaphore(f"{name}{i}")))
                     for i in range(ns)]
        self.ns = ns
        self.n = 0

    def issue(self, eng, **kw):
        s = self.sems[self.n % self.ns]
        if s.v > 0:
            eng.wait_ge(s.h, s.v)
        eng.dma_start(**kw).then_inc(s.h, 16)
        s.v += 16
        self.n += 1
        return (s, s.v)

    @staticmethod
    def wait(eng, tok):
        eng.wait_ge(tok[0].h, tok[1])

    def wait_all(self, eng):
        for s in self.sems:
            if s.v > 0:
                eng.wait_ge(s.h, s.v)


def build_nc(stage=6):
    nc = bass.Bass("TRN2", target_bir_lowering=False, debug=False)

    x0_d = nc.dram_tensor("x0", [T, D], F32, kind="ExternalInput")
    w_dq_d = nc.dram_tensor("w_dq", [D, QL], F32, kind="ExternalInput")
    w_uq_d = nc.dram_tensor("w_uq", [QL, H * QKD], F32, kind="ExternalInput")
    w_dkv_d = nc.dram_tensor("w_dkv", [D, KVL], F32, kind="ExternalInput")
    w_kr_d = nc.dram_tensor("w_kr", [D, ROPE], F32, kind="ExternalInput")
    w_uk_d = nc.dram_tensor("w_uk", [KVL, H * NOPE], F32, kind="ExternalInput")
    w_uv_d = nc.dram_tensor("w_uv", [KVL, H * VD], F32, kind="ExternalInput")
    w_o_d = nc.dram_tensor("w_o", [H * VD, D], F32, kind="ExternalInput")
    g_q_d = nc.dram_tensor("g_q", [QL], F32, kind="ExternalInput")
    g_kv_d = nc.dram_tensor("g_kv", [KVL], F32, kind="ExternalInput")
    w1_d = nc.dram_tensor("w1", [D, DFF], F32, kind="ExternalInput")
    b1_d = nc.dram_tensor("b1", [DFF], F32, kind="ExternalInput")
    w2_d = nc.dram_tensor("w2", [DFF, D], F32, kind="ExternalInput")
    b2_d = nc.dram_tensor("b2", [D], F32, kind="ExternalInput")
    rw_d = nc.dram_tensor("rw", [D], F32, kind="ExternalInput")
    rb_d = nc.dram_tensor("rb", [1], F32, kind="ExternalInput")
    ropeC_d = nc.dram_tensor("ropeC", [128, T], F32, kind="ExternalInput")
    ropeS_d = nc.dram_tensor("ropeS", [128, T], F32, kind="ExternalInput")
    dmask_d = nc.dram_tensor("dmask", [128, 1024], F32, kind="ExternalInput")
    ident_d = nc.dram_tensor("ident", [128, 128], BF16, kind="ExternalInput")
    ones_d = nc.dram_tensor("ones", [128, 128], BF16, kind="ExternalInput")
    tok16_d = nc.dram_tensor("tok16", [1, T], I16, kind="ExternalInput")
    capio_d = nc.dram_tensor("capio", [1, CAP], F32, kind="ExternalInput")
    id32f_d = nc.dram_tensor("id32f", [32, 32], F32, kind="ExternalInput")
    onesf_d = nc.dram_tensor("onesf", [128, 128], F32, kind="ExternalInput")

    y_d = nc.dram_tensor("y", [T, D], F32, kind="ExternalOutput")
    dbg = None
    if stage == 2:
        dbg = nc.dram_tensor("dbg", [KVSH], BF16, kind="ExternalOutput")
    elif stage == 3:
        dbg = nc.dram_tensor("dbg", [128, H * T], BF16, kind="ExternalOutput")
    elif stage == 4:
        dbg = nc.dram_tensor("dbg", [T, D], F32, kind="ExternalOutput")
    elif stage == 5:
        dbg = nc.dram_tensor("dbg", [8, T], F32, kind="ExternalOutput")

    agkv_in = nc.dram_tensor("agkv_in", [KVSH], BF16)
    agkv_out = nc.dram_tensor("agkv_out", [NCORES * KVSH], BF16)
    agsc_in = nc.dram_tensor("agsc_in", [T], F32)
    agsc_out = nc.dram_tensor("agsc_out", [NCORES * T], F32)
    h_dram = nc.dram_tensor("h_dram", [T, D], BF16)
    x1_dram = nc.dram_tensor("x1_dram", [T, D], F32)
    idx_dram = nc.dram_tensor("idx_dram", [1, CAP], I16)

    ctx = ExitStack()
    with ctx:
        dqQ = DQ(nc, ctx, "sQ", ns=8)
        dqG = DQ(nc, ctx, "sG", ns=8)
        sCC = Sem(ctx.enter_context(nc.semaphore("sCC")))
        sT = Sem(ctx.enter_context(nc.semaphore("sT")))
        sV = Sem(ctx.enter_context(nc.semaphore("sV")))
        sS = Sem(ctx.enter_context(nc.semaphore("sS")))
        sP = Sem(ctx.enter_context(nc.semaphore("sP")))
        blk = ctx.enter_context(nc.Block())

        TE, VE, SE, GE, SY = blk.tensor, blk.vector, blk.scalar, blk.gpsimd, blk.sync

        def fence():
            snap = (sT.v, sV.v, sS.v)

            def fg(g, snap=snap):
                dqQ.wait_all(g)
                dqG.wait_all(g)
                g.wait_ge(sT.h, snap[0])
                g.wait_ge(sV.h, snap[1])
                g.wait_ge(sS.h, snap[2])
                _i(g.drain(), sP)
            GE(fg)
            return sP.v

        ident = ctx.enter_context(nc.sbuf_tensor("ident_s", [128, 128], BF16))
        ones = ctx.enter_context(nc.sbuf_tensor("ones_s", [128, 128], BF16))
        ropeC = ctx.enter_context(nc.sbuf_tensor("ropeC_s", [128, T], F32))
        ropeS = ctx.enter_context(nc.sbuf_tensor("ropeS_s", [128, T], F32))
        dmask = ctx.enter_context(nc.sbuf_tensor("dmask_s", [128, 1024], F32))
        gq_sb = ctx.enter_context(nc.sbuf_tensor("gq_sb", [128, 6], F32))
        gkv_sb = ctx.enter_context(nc.sbuf_tensor("gkv_sb", [128, 4], F32))
        b1_sb = ctx.enter_context(nc.sbuf_tensor("b1_sb", [128, 64], F32))
        b2_sb = ctx.enter_context(nc.sbuf_tensor("b2_sb", [128, 16], F32))
        rb_sb = ctx.enter_context(nc.sbuf_tensor("rb_sb", [1, 1], F32))
        tok16 = ctx.enter_context(nc.sbuf_tensor("tok16_s", [16, T], I16))
        capio = ctx.enter_context(nc.sbuf_tensor("capio_s", [1, CAP], F32))
        id32f = ctx.enter_context(nc.sbuf_tensor("id32f_s", [32, 32], F32))
        onesf = ctx.enter_context(nc.sbuf_tensor("onesf_s", [128, 128], F32))

        psA = ctx.enter_context(nc.psum_tensor("psA", [128, 1024], F32))
        psB = ctx.enter_context(nc.psum_tensor("psB", [128, 1024], F32))
        ptA = ctx.enter_context(nc.psum_tensor("ptA", [128, 1024], BF16))
        ptB = ctx.enter_context(nc.psum_tensor("ptB", [128, 1024], BF16))
        psO = ctx.enter_context(nc.psum_tensor("psO", [128, 512], F32))
        psM = ctx.enter_context(nc.psum_tensor("psM", [128, 512], F32))

        tC0 = []

        def c0(sy):
            for dst, src in ((ident, ident_d), (ones, ones_d), (ropeC, ropeC_d),
                             (ropeS, ropeS_d), (dmask, dmask_d), (capio, capio_d),
                             (id32f, id32f_d), (onesf, onesf_d)):
                tC0.append(dqQ.issue(sy, out=dst[:, :], in_=src[:, :]))
            for dst, src in ((gq_sb, g_q_d), (gkv_sb, g_kv_d),
                             (b1_sb, b1_d), (b2_sb, b2_d)):
                tC0.append(dqQ.issue(
                    sy, out=dst[:, :], in_=src.ap().rearrange("(c p) -> p c", p=128),
                    allow_slow_non_contiguous=True))
            tC0.append(dqQ.issue(sy, out=rb_sb[:, :],
                                 in_=rb_d.ap().rearrange("(o x) -> o x", o=1)))
            for b in range(16):
                tC0.append(dqQ.issue(sy, out=tok16[b:b + 1, :], in_=tok16_d[:, :]))
        SY(c0)
        tIdent = tC0

        def g0(g):
            g.load_library(libcfg.attnmlp)
        GE(g0)

        ofm_ctx = ExitStack()
        aqn = ofm_ctx.enter_context(nc.sbuf_tensor("aqn", [128, 6 * T], BF16, side="right"))
        o_fm = ofm_ctx.enter_context(nc.sbuf_tensor("o_fm", [128, H * T], BF16, side="right"))
        kn2x = ctx.enter_context(nc.sbuf_tensor("kn2x", [32, NCORES], F32))
        m2k16 = ctx.enter_context(nc.sbuf_tensor("m2k16", [32, 1], F32))
        rbb = ctx.enter_context(nc.sbuf_tensor("rbb", [128, 1], F32))

        p2 = ExitStack()
        x0b = [p2.enter_context(nc.sbuf_tensor(f"x0b{i}", [128, D], F32)) for i in range(2)]
        st6 = p2.enter_context(nc.sbuf_tensor("st6", [128, 24], F32))
        st2 = p2.enter_context(nc.sbuf_tensor("st2", [128, 2], F32))
        lrs = p2.enter_context(nc.sbuf_tensor("lrs", [128, 1], F32))
        lnm = p2.enter_context(nc.sbuf_tensor("lnm", [128, 1], F32))
        t_tm = p2.enter_context(nc.sbuf_tensor("t_tm", [128, D], BF16))
        t_fm = p2.enter_context(nc.sbuf_tensor("t_fm", [128, 16 * T], BF16))
        ckvn = p2.enter_context(nc.sbuf_tensor("ckvn", [128, 4 * T], BF16))
        sqtmp = p2.enter_context(nc.sbuf_tensor("sqtmp", [128, T], BF16))
        rmss = p2.enter_context(nc.sbuf_tensor("rmss", [1, T], F32))
        rmsb = p2.enter_context(nc.sbuf_tensor("rmsb", [128, T], F32))
        knch = p2.enter_context(nc.sbuf_tensor("knch", [128, T], BF16))
        knsq = p2.enter_context(nc.sbuf_tensor("knsq", [128, T], BF16))
        kn2row = p2.enter_context(nc.sbuf_tensor("kn2row", [1, 32], F32))
        kn2rb = p2.enter_context(nc.sbuf_tensor("kn2rb", [1, 32], BF16))
        vch = p2.enter_context(nc.sbuf_tensor("vch", [128, T], BF16))
        vtt = p2.enter_context(nc.sbuf_tensor("vtt", [128, T], BF16))
        kpe0 = p2.enter_context(nc.sbuf_tensor("kpe0", [64, T], F32))
        kpet = p2.enter_context(nc.sbuf_tensor("kpet", [64, T], F32))
        kper = p2.enter_context(nc.sbuf_tensor("kper", [64, T], BF16))
        kpesq = p2.enter_context(nc.sbuf_tensor("kpesq", [64, T], BF16))
        p2a = ExitStack()
        wdkv = p2a.enter_context(nc.sbuf_tensor("wdkv", [128, 16 * KVL], BF16))
        wkr = p2a.enter_context(nc.sbuf_tensor("wkr", [128, 16 * 2 * ROPE], BF16))
        wuk = p2a.enter_context(nc.sbuf_tensor("wuk", [128, 4 * H * NOPE], BF16))
        wuv = p2a.enter_context(nc.sbuf_tensor("wuv", [128, 4 * H * VD], BF16))

        tW = []

        def wld(g):
            tW.append(dqG.issue(g, out=wdkv[:, :].rearrange("p (c j) -> p c j", c=16),
                                in_=w_dkv_d.ap().rearrange("(c p) j -> p c j", p=128)))
            tW.append(dqG.issue(g, out=wuk[:, :].rearrange("p (c j) -> p c j", c=4),
                                in_=w_uk_d.ap().rearrange("(c p) j -> p c j", p=128)))
            tW.append(dqG.issue(g, out=wuv[:, :].rearrange("p (c j) -> p c j", c=4),
                                in_=w_uv_d.ap().rearrange("(c p) j -> p c j", p=128)))
            tW.append(dqG.issue(
                g, out=wkr[:, :].rearrange("p (c u) -> p c u", c=16)[:, :, 0:ROPE],
                in_=w_kr_d.ap().rearrange("(c p) j -> p c j", p=128)))
            tW.append(dqG.issue(
                g, out=wkr[:, :].rearrange("p (c u) -> p c u", c=16)[:, :, ROPE:ROPE + 32],
                in_=w_kr_d.ap().rearrange("(c p) j -> p c j", p=128)[:, :, 32:64]))
            tW.append(dqG.issue(
                g, out=wkr[:, :].rearrange("p (c u) -> p c u", c=16)[:, :, ROPE + 32:2 * ROPE],
                in_=w_kr_d.ap().rearrange("(c p) j -> p c j", p=128)[:, :, 0:32]))
        GE(wld)

        mX0free = [None, None]
        for b in range(NT):
            par = b % 2

            tkx = [None]

            def ldx(sy, b=b, par=par, w=mX0free[par]):
                if w is not None:
                    sy.wait_ge(sS.h, w)
                tkx[0] = dqQ.issue(sy, out=x0b[par][:, :],
                                   in_=x0_d[128 * b:128 * (b + 1), :])
            SY(ldx)
            mx = tkx[0]

            def lns(v, par=par, mx=mx, wprev=mX0free[1 - par]):
                DQ.wait(v, mx)
                if wprev is not None:
                    v.wait_ge(sS.h, wprev)
                for q in range(4):
                    v.bn_stats(st6[:, 6 * q:6 * q + 6], x0b[par][:, 512 * q:512 * (q + 1)])
                v.drain()
                v.bn_aggr(st2[:, :], st6[:, :])
                v.drain()
                v.tensor_scalar(lrs[:, :], st2[:, 1:2], EPS_LN, None, ALU.add)
                v.drain()
                _i(v.reciprocal(lrs[:, :], lrs[:, :]), sV)
            VE(lns)
            m1 = sV.v

            def lnr(sc, m1=m1):
                sc.wait_ge(sV.h, m1)
                _i(sc.activation(lrs[:, :], lrs[:, :], ACT.Sqrt), sS)
            SE(lnr)
            m2 = sS.v

            def lnn(v, m2=m2):
                v.wait_ge(sS.h, m2)
                v.tensor_mul(lnm[:, :], st2[:, 0:1], lrs[:, :])
                v.drain()
                _i(v.tensor_scalar(lnm[:, :], lnm[:, :], -1.0, None, ALU.mult), sV)
            VE(lnn)
            m3 = sV.v

            def lnap(sc, par=par, m3=m3):
                sc.wait_ge(sV.h, m3)
                _i(sc.activation(t_tm[:, :], x0b[par][:, :], ACT.Identity,
                                 bias=lnm[:, 0:1], scale=lrs[:, 0:1]), sS)
            SE(lnap)
            m4 = sS.v
            mX0free[par] = m4

            def ttr(t, m4=m4, b=b):
                t.wait_ge(sS.h, m4)
                if b == 0:
                    DQ.wait(t, tIdent[0])
                for half in range(2):
                    pt = ptA if half == 0 else ptB
                    for c in range(8):
                        cc = 8 * half + c
                        t.transpose(pt[:, 128 * c:128 * (c + 1)],
                                    t_tm[:, 128 * cc:128 * (cc + 1)], ident[:, :])
                    _i(t.drain(), sT)
            TE(ttr)
            mh = (sT.v - 1, sT.v)

            def tev(v, mh=mh, b=b):
                for half in range(2):
                    v.wait_ge(sT.h, mh[half])
                    _i(v.tensor_copy(
                        t_fm[:, :].rearrange("p (c m) -> p c m", c=16)
                        [:, 8 * half:8 * half + 8, 128 * b:128 * (b + 1)],
                        (ptA if half == 0 else ptB)[:, :].rearrange(
                            "p (c m) -> p c m", c=8)), sV)
            VE(tev)
        mTFM = sV.v

        def proj_rms(wsb, nchunk, gcol_sb, buf):
            vgate = sV.v
            sgate = sS.v
            mEv = []
            mSq = []
            for oc in range(nchunk):
                ps_full = psA if oc % 2 == 0 else psB
                def mm(t, oc=oc, ps_full=ps_full):
                    if oc == 0:
                        for tk in tW:
                            DQ.wait(t, tk)
                        t.wait_ge(sV.h, max(mTFM, vgate))
                    if oc >= 2:
                        t.wait_ge(sV.h, mEv[oc - 2])
                    for half in range(2):
                        ps = ps_full[:, 512 * half:512 * (half + 1)]
                        for ic in range(16):
                            t.matmul(ps,
                                     wsb[:, :].rearrange("p (c j) -> p c j", c=16)
                                     [:, ic, 128 * oc:128 * (oc + 1)],
                                     t_fm[:, :].rearrange("p (c m) -> p c m", c=16)
                                     [:, ic, 512 * half:512 * (half + 1)],
                                     start=(ic == 0), stop=(ic == 15))
                    _i(t.drain(), sT)
                TE(mm)
                mp = sT.v

                def ev(v, oc=oc, mp=mp, ps_full=ps_full):
                    v.wait_ge(sT.h, mp)
                    _i(v.tensor_copy(
                        buf[:, :].rearrange("p (c m) -> p c m", c=nchunk)[:, oc, :],
                        ps_full[:, :]), sV)
                VE(ev)
                mEv.append(sV.v)
                mr = sV.v

                def sq(sc, oc=oc, mr=mr):
                    sc.wait_ge(sV.h, mr)
                    if oc >= 1:
                        sc.wait_ge(sT.h, mSq[oc - 1])
                    _i(sc.activation(sqtmp[:, :],
                                     buf[:, :].rearrange("p (c m) -> p c m", c=nchunk)
                                     [:, oc, :], ACT.Square), sS)
                SE(sq)
                msq = sS.v

                def ssum(t, oc=oc, msq=msq, vgate=vgate):
                    t.wait_ge(sS.h, msq)
                    if oc == 0:
                        t.wait_ge(sV.h, vgate)
                    t.matmul(psM[0:1, 0:512], ones[:, 0:1], sqtmp[:, 0:512],
                             start=(oc == 0), stop=(oc == nchunk - 1))
                    t.matmul(psO[0:1, 0:512], ones[:, 0:1], sqtmp[:, 512:1024],
                             start=(oc == 0), stop=(oc == nchunk - 1))
                    _i(t.drain(), sT)
                TE(ssum)
                mSq.append(sT.v)
            msum = sT.v

            def rstd(v, msum=msum, nchunk=nchunk):
                v.wait_ge(sT.h, msum)
                v.tensor_scalar(rmss[0:1, 0:512], psM[0:1, 0:512],
                                1.0 / (nchunk * 128), EPS_RMS, ALU.mult, ALU.add)
                v.tensor_scalar(rmss[0:1, 512:1024], psO[0:1, 0:512],
                                1.0 / (nchunk * 128), EPS_RMS, ALU.mult, ALU.add)
                _i(v.drain(), sV)
            VE(rstd)
            mms = sV.v

            def rsq(sc, mms=mms):
                sc.wait_ge(sV.h, mms)
                _i(sc.activation(rmss[:, :], rmss[:, :], ACT.Sqrt), sS)
            SE(rsq)
            msq2 = sS.v

            def rrec(v, msq2=msq2):
                v.wait_ge(sS.h, msq2)
                _i(v.reciprocal(rmss[:, :], rmss[:, :]), sV)
            VE(rrec)
            mrec = sV.v

            def bcast(g, mrec=mrec):
                g.wait_ge(sV.h, mrec)
                _i(g.partition_broadcast(rmsb[:, :], rmss[0:1, :]), sP)
            GE(bcast)
            mb = sP.v

            for oc in range(nchunk):
                def nrm(v, oc=oc, mb=mb):
                    v.wait_ge(sP.h, mb)
                    _i(v.scalar_tensor_tensor(
                        buf[:, :].rearrange("p (c m) -> p c m", c=nchunk)[:, oc, :],
                        buf[:, :].rearrange("p (c m) -> p c m", c=nchunk)[:, oc, :],
                        gcol_sb[:, oc:oc + 1], rmsb[:, :],
                        ALU.mult, ALU.mult), sV)
                VE(nrm)
            def nrmd(v):
                _i(v.drain(), sV)
            VE(nrmd)
            return sV.v

        mCKV = proj_rms(wdkv, 4, gkv_sb, ckvn)

        tShard = []

        def z32(v):
            v.memset(kn2row[:, :], 0.0)
            _i(v.drain(), sV)
        VE(z32)
        mZ32 = sV.v

        mKNprev = [None, None]
        mKN2V = [None]
        tOutPrev = [None, None]
        for hh in range(H):
            def knmm(t, hh=hh, wprev=(mKNprev[0] if hh > 0 else None)):
                if hh == 0:
                    t.wait_ge(sV.h, mCKV)
                else:
                    t.wait_ge(sV.h, wprev)
                for half in range(2):
                    ps = psA[:, 512 * half:512 * (half + 1)]
                    for ic in range(4):
                        t.matmul(ps,
                                 wuk[:, :].rearrange("p (c j) -> p c j", c=4)
                                 [:, ic, 128 * hh:128 * (hh + 1)],
                                 ckvn[:, :].rearrange("p (c m) -> p c m", c=4)
                                 [:, ic, 512 * half:512 * (half + 1)],
                                 start=(ic == 0), stop=(ic == 3))
                _i(t.drain(), sT)
            TE(knmm)
            mkn = sT.v

            def knev(v, mkn=mkn, tprev=tOutPrev[0]):
                v.wait_ge(sT.h, mkn)
                if tprev is not None:
                    DQ.wait(v, tprev)
                _i(v.tensor_copy(knch[:, :], psA[:, :]), sV)
            VE(knev)
            mknb = sV.v

            mKNprev[0] = mknb

            def knq(sc, mknb=mknb):
                sc.wait_ge(sV.h, mknb)
                _i(sc.activation(knsq[:, :], knch[:, :], ACT.Square), sS)
            SE(knq)
            mknsq = sS.v

            def kn2mm(t, mknsq=mknsq, hh=hh, wv=(mKN2V[0] if hh > 0 else None)):
                t.wait_ge(sS.h, mknsq)
                if wv is not None:
                    t.wait_ge(sV.h, wv)
                t.matmul(psM[0:1, 0:512], ones[:, 0:1], knsq[:, 0:512],
                         start=True, stop=True)
                t.matmul(psO[0:1, 0:512], ones[:, 0:1], knsq[:, 512:1024],
                         start=True, stop=True)
                _i(t.drain(), sT)
            TE(kn2mm)
            mkn2 = sT.v

            def kn2red(v, mkn2=mkn2, hh=hh):
                v.wait_ge(sT.h, mkn2)
                if hh == 0:
                    v.wait_ge(sV.h, mZ32)
                v.reduce_max(kn2row[0:1, hh:hh + 1], psM[0:1, 0:512], axis=AX)
                v.reduce_max(lrs[0:1, 0:1], psO[0:1, 0:512], axis=AX)
                v.drain()
                _i(v.tensor_tensor(kn2row[0:1, hh:hh + 1], kn2row[0:1, hh:hh + 1],
                                   lrs[0:1, 0:1], ALU.max), sV)
            VE(kn2red)
            mKN2V[0] = sV.v

            tkn = [None]

            def knout(sy, hh=hh, mknb=mknb):
                sy.wait_ge(sV.h, mknb)
                tkn[0] = dqQ.issue(
                    sy,
                    out=agkv_in.ap()[KN_OFF + hh * 128 * T:KN_OFF + (hh + 1) * 128 * T]
                    .rearrange("(p m) -> p m", p=128),
                    in_=knch[:, :])
            SY(knout)
            tShard.append(tkn[0])
            tOutPrev[0] = tkn[0]

            def vmm(t, hh=hh, wprev=(mKNprev[1] if hh > 0 else None)):
                if wprev is not None:
                    t.wait_ge(sV.h, wprev)
                for half in range(2):
                    ps = psB[:, 512 * half:512 * (half + 1)]
                    for ic in range(4):
                        t.matmul(ps,
                                 wuv[:, :].rearrange("p (c j) -> p c j", c=4)
                                 [:, ic, 128 * hh:128 * (hh + 1)],
                                 ckvn[:, :].rearrange("p (c m) -> p c m", c=4)
                                 [:, ic, 512 * half:512 * (half + 1)],
                                 start=(ic == 0), stop=(ic == 3))
                _i(t.drain(), sT)
            TE(vmm)
            mv = sT.v

            def vev(v, mv=mv):
                v.wait_ge(sT.h, mv)
                _i(v.tensor_copy(vch[:, :], psB[:, :]), sV)
            VE(vev)
            mvb = sV.v
            mKNprev[1] = mvb

            def vtr(t, mvb=mvb):
                t.wait_ge(sV.h, mvb)
                for half in range(2):
                    pt = ptA if half == 0 else ptB
                    for bb in range(4):
                        b = 4 * half + bb
                        t.transpose(pt[:, 128 * bb:128 * (bb + 1)],
                                    vch[:, 128 * b:128 * (b + 1)], ident[:, :])
                    _i(t.drain(), sT)
            TE(vtr)
            mvt = (sT.v - 1, sT.v)

            def vtev(v, mvt=mvt, tprev=tOutPrev[1]):
                if tprev is not None:
                    DQ.wait(v, tprev)
                for half in range(2):
                    v.wait_ge(sT.h, mvt[half])
                    _i(v.tensor_copy(
                        vtt[:, :].rearrange("p (b m) -> p b m", b=8)
                        [:, 4 * half:4 * half + 4, :],
                        (ptA if half == 0 else ptB)[:, 0:512].rearrange(
                            "p (b m) -> p b m", b=4)), sV)
            VE(vtev)
            mvtt = sV.v

            tvo = [None]

            def vout(sy, hh=hh, mvtt=mvtt):
                sy.wait_ge(sV.h, mvtt)
                tvo[0] = dqQ.issue(
                    sy,
                    out=agkv_in.ap()[V_OFF + hh * T * VD:V_OFF + (hh + 1) * T * VD]
                    .rearrange("(b p v) -> p b v", b=8, p=128),
                    in_=vtt[:, :].rearrange("p (b v) -> p b v", b=8))
            SY(vout)
            tShard.append(tvo[0])
            tOutPrev[1] = tvo[0]

        def kpmm(t):
            for half in range(2):
                ps = psA[0:64, 512 * half:512 * (half + 1)]
                for ic in range(16):
                    t.matmul(ps,
                             wkr[:, :].rearrange("p (c u) -> p c u", c=16)[:, ic, 0:64],
                             t_fm[:, :].rearrange("p (c m) -> p c m", c=16)
                             [:, ic, 512 * half:512 * (half + 1)],
                             start=(ic == 0), stop=(ic == 15))
                ps2 = psB[0:64, 512 * half:512 * (half + 1)]
                for ic in range(16):
                    t.matmul(ps2,
                             wkr[:, :].rearrange("p (c u) -> p c u", c=16)[:, ic, 64:128],
                             t_fm[:, :].rearrange("p (c m) -> p c m", c=16)
                             [:, ic, 512 * half:512 * (half + 1)],
                             start=(ic == 0), stop=(ic == 15))
            _i(t.drain(), sT)
        TE(kpmm)
        mKP = sT.v

        def kprope(v, mKP=mKP):
            v.wait_ge(sT.h, mKP)
            v.tensor_tensor(kpe0[:, :], psA[0:64, :], ropeC[0:64, :], ALU.mult)
            v.tensor_tensor(kpet[:, :], psB[0:64, :], ropeS[0:64, :], ALU.mult)
            v.drain()
            _i(v.tensor_tensor(kper[:, :], kpe0[:, :], kpet[:, :], ALU.add), sV)
        VE(kprope)
        mKPR = sV.v

        def kpsq(sc, mKPR=mKPR):
            sc.wait_ge(sV.h, mKPR)
            _i(sc.activation(kpesq[:, :], kper[:, :], ACT.Square), sS)
        SE(kpsq)
        mKPSQ = sS.v

        def kpsum(t, mKPSQ=mKPSQ, wv=mKN2V[0]):
            t.wait_ge(sS.h, mKPSQ)
            t.wait_ge(sV.h, wv)
            t.matmul(psM[0:1, 0:512], ones[0:64, 0:1], kpesq[:, 0:512],
                     start=True, stop=True)
            t.matmul(psO[0:1, 0:512], ones[0:64, 0:1], kpesq[:, 512:1024],
                     start=True, stop=True)
            _i(t.drain(), sT)
        TE(kpsum)
        mKPS = sT.v

        def kpred(v, mKPS=mKPS):
            v.wait_ge(sT.h, mKPS)
            v.reduce_max(kn2row[0:1, 16:17], psM[0:1, 0:512], axis=AX)
            v.reduce_max(lrs[0:1, 0:1], psO[0:1, 0:512], axis=AX)
            v.drain()
            v.tensor_tensor(kn2row[0:1, 16:17], kn2row[0:1, 16:17], lrs[0:1, 0:1], ALU.max)
            v.drain()
            _i(v.tensor_copy(kn2rb[:, :], kn2row[:, :]), sV)
        VE(kpred)
        mKN2B = sV.v

        def kpout(sy, mKPR=mKPR, mKN2B=mKN2B):
            sy.wait_ge(sV.h, mKPR)
            tShard.append(dqQ.issue(
                sy,
                out=agkv_in.ap()[KPE_OFF:KPE_OFF + ROPE * T].rearrange("(p m) -> p m", p=64),
                in_=kper[:, :]))
            sy.wait_ge(sV.h, mKN2B)
            tShard.append(dqQ.issue(
                sy,
                out=agkv_in.ap()[KN2_OFF:KN2_OFF + 32].rearrange("(o x) -> o x", o=1),
                in_=kn2rb[:, :]))
        SY(kpout)

        p2a.close()
        wdq = p2.enter_context(nc.sbuf_tensor("wdq", [128, 16 * QL], BF16))

        def wdql(g, mKP=mKP):
            g.wait_ge(sT.h, mKP)
            tW.append(dqG.issue(g, out=wdq[:, :].rearrange("p (c j) -> p c j", c=16),
                                in_=w_dq_d.ap().rearrange("(c p) j -> p c j", p=128)))
        GE(wdql)

        def agkv(g):
            for tk in tShard:
                DQ.wait(g, tk)
            _i(g.collective_compute(
                "AllGather", ALU.bypass,
                replica_groups=[list(range(NCORES))],
                ins=[agkv_in.ap().opt()],
                outs=[agkv_out.ap().opt()]), sCC)
        GE(agkv)
        mAG = sCC.v

        mAQ = proj_rms(wdq, 6, gq_sb, aqn)

        if stage == 2:
            def s2(sy):
                sy.wait_ge(sCC.h, mAG)
                tk = dqQ.issue(sy, out=dbg.ap(), in_=agkv_in.ap())
                DQ.wait(sy, tk)
            SY(s2)

            def s2g(g, mAQ=mAQ):
                g.wait_ge(sV.h, mAQ)
                tk = dqG.issue(g, out=y_d[0:128, :], in_=aqn[:, 0:D])
                DQ.wait(g, tk)
            GE(s2g)
            p2.close()
            return nc

        p2.close()

        at = ExitStack()
        K_sb = at.enter_context(nc.sbuf_tensor("K_sb", [128, 2 * S], BF16))
        V_sb = at.enter_context(nc.sbuf_tensor("V_sb", [128, 2 * S], BF16))
        kpeA = at.enter_context(nc.sbuf_tensor("kpeA", [64, S], BF16))
        A_sb = at.enter_context(nc.sbuf_tensor("A_sb", [128, 2 * 1024], BF16))
        AT_sb = at.enter_context(nc.sbuf_tensor("AT_sb", [128, 2 * 512], BF16))
        Qn = at.enter_context(nc.sbuf_tensor("Qn", [128, 2 * T], BF16))
        Qp = at.enter_context(nc.sbuf_tensor("Qp", [64, 2 * T], BF16))
        peF = at.enter_context(nc.sbuf_tensor("peF", [64, 512], F32))
        peG = at.enter_context(nc.sbuf_tensor("peG", [64, 512], F32))
        Q2n = at.enter_context(nc.sbuf_tensor("Q2n", [128, T], BF16))
        Q2p = at.enter_context(nc.sbuf_tensor("Q2p", [64, T], BF16))
        wuqh = at.enter_context(nc.sbuf_tensor("wuqh", [128, 2 * 6 * 256], BF16))
        negB = at.enter_context(nc.sbuf_tensor("negB", [128, NT], F32))
        sigc = at.enter_context(nc.sbuf_tensor("sigc", [128, 2 * NT], F32))
        sigS = at.enter_context(nc.sbuf_tensor("sigS", [128, 1], F32))
        rSg = at.enter_context(nc.sbuf_tensor("rSg", [128, 1], F32))
        O_sb = at.enter_context(nc.sbuf_tensor("O_sb", [128, 128], BF16))
        m2krow = at.enter_context(nc.sbuf_tensor("m2krow", [1, 32], F32))
        m2ksc = at.enter_context(nc.sbuf_tensor("m2ksc", [128, 16], F32))

        def m2kload(g):
            g.wait_ge(sCC.h, mAG)
            g.wait_ge(sV.h, mAQ)
            tk = dqG.issue(
                g, out=kn2x[:, :],
                in_=agkv_out.ap()[0:NCORES * KVSH].rearrange("(r x) -> r x", x=KVSH)
                [:, KN2_OFF:KN2_OFF + 32].rearrange("r p -> p r"),
                allow_slow_non_contiguous=True)
            DQ.wait(g, tk)
            _i(g.drain(), sP)
        GE(m2kload)
        mK2X = sP.v

        def m2kred(v):
            v.wait_ge(sP.h, mK2X)
            _i(v.reduce_max(m2k16[:, :], kn2x[:, :], axis=AX), sV)
        VE(m2kred)
        mM2K0 = sV.v

        def m2ktr(t):
            t.wait_ge(sV.h, mM2K0)
            t.transpose(psM[0:1, 448:480], m2k16[0:32, 0:1], id32f[:, :])
            _i(t.drain(), sT)
        TE(m2ktr)
        mM2KT = sT.v

        def m2krow_f(v):
            v.wait_ge(sT.h, mM2KT)
            v.tensor_copy(m2krow[:, :], psM[0:1, 448:480])
            v.drain()
            _i(v.tensor_scalar(m2krow[0:1, 0:16], m2krow[0:1, 0:16],
                               m2krow[0:1, 16:17], SCALE * SCALE,
                               ALU.add, ALU.mult), sV)
        VE(m2krow_f)
        mM2KR = sV.v

        def m2kpb2(g):
            g.wait_ge(sV.h, mM2KR)
            _i(g.partition_broadcast(m2ksc[:, :], m2krow[0:1, 0:16]), sP)
        GE(m2kpb2)
        mM2KS = sP.v

        tKPE = []

        def kpeload(sy, mAQ=mAQ):
            sy.wait_ge(sCC.h, mAG)
            sy.wait_ge(sV.h, mAQ)
            for jj in range(NKC):
                tKPE.append(dqQ.issue(
                    sy,
                    out=kpeA[:, 1024 * jj:1024 * (jj + 1)].rearrange(
                        "p (r c) -> p r c", r=8),
                    in_=agkv_out.ap()[0:NCORES * KVSH]
                    .rearrange("(r x) -> r x", x=KVSH)
                    [:, KPE_OFF:KPE_OFF + ROPE * T].rearrange("r (p m) -> p r m", p=ROPE)
                    [:, :, 128 * jj:128 * (jj + 1)]))
        SY(kpeload)

        mHead = [None] * H
        mQMM = [None] * H
        tWQ = [None] * H
        tKV = [[None] * NKC for _ in range(H)]
        mExpL = [0] * H
        ccg = [0]
        mExp = {}
        mTrB = {}
        mATe = {}
        mOe = [0] * (NT + 2)
        mSgP = [0, 0]
        mSC2P = [0]

        for h in range(H):
            par = h % 2

            def wq(g, h=h, par=par):
                if h < 2:
                    g.wait_ge(sCC.h, mAG)
                    g.wait_ge(sV.h, mAQ)
                else:
                    g.wait_ge(sT.h, mQMM[h - 2])
                toks = []
                base = par * 1536
                w = wuqh[:, :].rearrange("p (q l u) -> p q l u", q=2, l=6)
                toks.append(dqG.issue(
                    g, out=w[:, par, :, 0:128],
                    in_=w_uq_d.ap().rearrange("(l p) j -> p l j", p=128)
                    [:, :, QKD * h:QKD * h + 128]))
                toks.append(dqG.issue(
                    g, out=w[:, par, :, 128:192],
                    in_=w_uq_d.ap().rearrange("(l p) j -> p l j", p=128)
                    [:, :, QKD * h + 128:QKD * h + 192]))
                toks.append(dqG.issue(
                    g, out=w[:, par, :, 192:224],
                    in_=w_uq_d.ap().rearrange("(l p) j -> p l j", p=128)
                    [:, :, QKD * h + 160:QKD * h + 192]))
                toks.append(dqG.issue(
                    g, out=w[:, par, :, 224:256],
                    in_=w_uq_d.ap().rearrange("(l p) j -> p l j", p=128)
                    [:, :, QKD * h + 128:QKD * h + 160]))
                tWQ[h] = toks
            GE(wq)

            def kv(sy, h=h, par=par, mAQ=mAQ):
                if h < 2:
                    sy.wait_ge(sCC.h, mAG)
                    sy.wait_ge(sV.h, mAQ)
                else:
                    sy.wait_ge(sT.h, mHead[h - 2])
                kview = agkv_out.ap()[0:NCORES * KVSH] \
                    .rearrange("(r x) -> r x", x=KVSH) \
                    [:, KN_OFF + h * 128 * T:KN_OFF + (h + 1) * 128 * T] \
                    .rearrange("r (p m) -> p r m", p=128)
                vview = agkv_out.ap()[0:NCORES * KVSH] \
                    .rearrange("(r x) -> r x", x=KVSH) \
                    [:, V_OFF + h * T * VD:V_OFF + (h + 1) * T * VD] \
                    .rearrange("r (b p v) -> p b r v", b=NT, p=128)
                for jj in range(NKC):
                    t1 = dqQ.issue(
                        sy,
                        out=K_sb[:, par * S + 1024 * jj:par * S + 1024 * (jj + 1)]
                        .rearrange("p (r c) -> p r c", r=8),
                        in_=kview[:, :, 128 * jj:128 * (jj + 1)])
                    t2 = dqQ.issue(
                        sy,
                        out=V_sb[:, par * S + 1024 * jj:par * S + 1024 * (jj + 1)]
                        .rearrange("p (r v) -> p r v", r=8),
                        in_=vview[:, jj, :, :])
                    tKV[h][jj] = (t1, t2)
            SY(kv)

            def qmm(t, h=h, par=par):
                for tk in tWQ[h]:
                    DQ.wait(t, tk)
                if h == 0:
                    t.wait_ge(sV.h, mAQ)
                else:
                    t.wait_ge(sS.h, mExpL[h - 1])
                    t.wait_ge(sV.h, mOe[7])
                w = wuqh[:, :].rearrange("p (q l u) -> p q l u", q=2, l=6)
                for half in range(2):
                    for l in range(6):
                        t.matmul(psA[:, 512 * half:512 * (half + 1)],
                                 w[:, par, l, 0:128],
                                 aqn[:, :].rearrange("p (c m) -> p c m", c=6)
                                 [:, l, 512 * half:512 * (half + 1)],
                                 start=(l == 0), stop=(l == 5))
                for half in range(2):
                    for l in range(6):
                        t.matmul(psB[0:64, 512 * half:512 * (half + 1)],
                                 w[:, par, l, 128:192],
                                 aqn[:, :].rearrange("p (c m) -> p c m", c=6)
                                 [:, l, 512 * half:512 * (half + 1)],
                                 start=(l == 0), stop=(l == 5))
                for half in range(2):
                    for l in range(6):
                        t.matmul(psO[0:64, 0:512] if half == 0 else psM[0:64, 0:512],
                                 w[:, par, l, 192:256],
                                 aqn[:, :].rearrange("p (c m) -> p c m", c=6)
                                 [:, l, 512 * half:512 * (half + 1)],
                                 start=(l == 0), stop=(l == 5))
                _i(t.drain(), sT)
            TE(qmm)
            mQMM[h] = sT.v

            def qev(v, h=h, par=par):
                v.wait_ge(sT.h, mQMM[h])
                v.tensor_copy(Qn[:, par * T:(par + 1) * T], psA[:, :])
                for half in range(2):
                    pesw = psO[0:64, 0:512] if half == 0 else psM[0:64, 0:512]
                    v.tensor_tensor(peF[:, :], psB[0:64, 512 * half:512 * (half + 1)],
                                    ropeC[0:64, 512 * half:512 * (half + 1)], ALU.mult)
                    v.tensor_tensor(peG[:, :], pesw,
                                    ropeS[0:64, 512 * half:512 * (half + 1)], ALU.mult)
                    v.drain()
                    v.tensor_tensor(Qp[:, par * T + 512 * half:par * T + 512 * (half + 1)],
                                    peF[:, :], peG[:, :], ALU.add)
                    v.drain()
                _i(v.drain(), sV)
            VE(qev)
            mQEV = sV.v

            def qsq(sc, h=h, par=par, mQEV=mQEV):
                sc.wait_ge(sV.h, mQEV)
                sc.activation(Q2n[:, :], Qn[:, par * T:(par + 1) * T], ACT.Square)
                _i(sc.activation(Q2p[:, :], Qp[:, par * T:(par + 1) * T], ACT.Square), sS)
            SE(qsq)
            mQSQ = sS.v

            def qn2mm(t, h=h, mQSQ=mQSQ):
                t.wait_ge(sS.h, mQSQ)
                for j in range(NT):
                    t.matmul(psM[:, 256 + j:256 + j + 1], Q2n[:, 128 * j:128 * (j + 1)],
                             ones[:, 0:1], start=True, stop=False)
                    t.matmul(psM[:, 256 + j:256 + j + 1], Q2p[:, 128 * j:128 * (j + 1)],
                             ones[0:64, 0:1], start=False, stop=True)
                _i(t.drain(), sT)
            TE(qn2mm)
            mQN2 = sT.v

            def bsqrt(sc, h=h, mQN2=mQN2):
                sc.wait_ge(sP.h, mM2KS)
                sc.wait_ge(sT.h, mQN2)
                _i(sc.activation(negB[:, :], psM[:, 256:256 + NT], ACT.Sqrt,
                                 scale=m2ksc[:, h:h + 1]), sS)
            SE(bsqrt)
            mBSQ = sS.v

            def bneg(v, mBSQ=mBSQ):
                v.wait_ge(sS.h, mBSQ)
                _i(v.tensor_scalar(negB[:, :], negB[:, :], -1.0, None, ALU.mult), sV)
            VE(bneg)
            mNEGB = sV.v

            mQnEv = mQEV
            cch = 0
            pend = [None]

            def s_seg(h, j, jj, cc, cch, par, psX):
                def smm(t, h=h, j=j, jj=jj, cc=cc, cch=cch, par=par, psX=psX,
                        mQnEv=mQnEv, wexp=(mExp.get(cc - 2, 0))):
                    DQ.wait(t, tKV[h][jj][0])
                    DQ.wait(t, tKV[h][jj][1])
                    DQ.wait(t, tKPE[jj])
                    if wexp:
                        t.wait_ge(sS.h, wexp)
                    if cch == 0:
                        t.wait_ge(sV.h, mQnEv)
                    for half in range(2):
                        t.matmul(psX[:, 512 * half:512 * (half + 1)],
                                 Qn[:, par * T + 128 * j:par * T + 128 * (j + 1)],
                                 K_sb[:, par * S + 1024 * jj + 512 * half:
                                      par * S + 1024 * jj + 512 * (half + 1)],
                                 start=True, stop=False)
                    for half in range(2):
                        ins = t.matmul(psX[:, 512 * half:512 * (half + 1)],
                                       Qp[:, par * T + 128 * j:par * T + 128 * (j + 1)],
                                       kpeA[:, 1024 * jj + 512 * half:
                                            1024 * jj + 512 * (half + 1)],
                                       start=False, stop=True)
                        if half == 1:
                            _i(ins, sT)
                TE(smm)
                return sT.v

            def mid_seg(h, j, jj, cc, jpar, apar, psX, mS, diag):
                if diag:
                    def dmk(v, mS=mS, psX=psX):
                        v.wait_ge(sT.h, mS)
                        _i(v.tensor_tensor(psX[:, :], psX[:, :], dmask[:, :],
                                           ALU.add), sV)
                    VE(dmk)
                    mMsk = sV.v
                else:
                    mMsk = None

                def expf(sc, cc=cc, j=j, jj=jj, jpar=jpar, apar=apar, psX=psX,
                         mS=mS, mMsk=mMsk, mNEGB=mNEGB, mSgPrev=mSgP[jpar],
                         wtr=(mTrB.get(cc - 2, 0))):
                    if mMsk is not None:
                        sc.wait_ge(sV.h, mMsk)
                    else:
                        sc.wait_ge(sT.h, mS)
                    sc.wait_ge(sV.h, mNEGB)
                    if wtr:
                        sc.wait_ge(sT.h, wtr)
                    if jj == 0 and mSgPrev:
                        sc.wait_ge(sV.h, mSgPrev)
                    _i(sc.activation(
                        A_sb[:, apar * 1024:(apar + 1) * 1024], psX[:, :],
                        ACT.Exp, scale=SCALE, bias=negB[:, j:j + 1],
                        accum_out=sigc[:, jpar * NT + jj:jpar * NT + jj + 1]), sS)
                SE(expf)
                mExp[cc] = sS.v
                return sS.v

            def tail_seg(h, j, jj, cc, par, apar, mE, last_of_j):
                for b in range(2):
                    bb = 2 * cc + b
                    tpar = bb % 2
                    ptX = ptA if tpar == 0 else ptB

                    def trb(t, cc=cc, b=b, bb=bb, apar=apar, ptX=ptX, mE=mE,
                            wev=(mATe.get(bb - 2, 0))):
                        t.wait_ge(sS.h, mE)
                        if wev:
                            t.wait_ge(sV.h, wev)
                        for z in range(4):
                            ins = t.transpose(ptX[:, 128 * z:128 * (z + 1)],
                                              A_sb[:, apar * 1024 + 512 * b + 128 * z:
                                                   apar * 1024 + 512 * b + 128 * (z + 1)],
                                              ident[:, :])
                            if z == 3:
                                _i(ins, sT)
                    TE(trb)
                    mTrB[cc] = sT.v
                    mTr = sT.v

                    def ate(v, bb=bb, mTr=mTr, tpar=tpar, ptX=ptX):
                        v.wait_ge(sT.h, mTr)
                        _i(v.tensor_copy(AT_sb[:, tpar * 512:(tpar + 1) * 512],
                                         ptX[:, 0:512]), sV)
                    VE(ate)
                    mATe[bb] = sV.v

                    def avm(t, h=h, j=j, jj=jj, b=b, bb=bb, cc=cc, par=par,
                            tpar=tpar, woe=(mOe[j - 2] if (jj == 0 and b == 0 and j >= 2) else 0)):
                        t.wait_ge(sV.h, mATe[bb])
                        if woe:
                            t.wait_ge(sV.h, woe)
                        for z in range(4):
                            kt = 8 * jj + 4 * b + z
                            ins = t.matmul(psO[:, 128 * (j % 2):128 * (j % 2 + 1)],
                                           AT_sb[:, tpar * 512 + 128 * z:
                                                 tpar * 512 + 128 * (z + 1)],
                                           V_sb[:, par * S + 128 * kt:
                                                par * S + 128 * (kt + 1)],
                                           start=(jj == 0 and b == 0 and z == 0),
                                           stop=(jj == j and b == 1 and z == 3))
                            if z == 3:
                                _i(ins, sT)
                    TE(avm)
                if not last_of_j:
                    return
                mAVj = sT.v
                jpar = j % 2

                def sgm(v, j=j, jpar=jpar, mE=mE):
                    v.wait_ge(sS.h, mE)
                    _i(v.reduce_sum(sigS[:, :], sigc[:, jpar * NT:jpar * NT + j + 1],
                                    axis=AX), sV)
                VE(sgm)
                mSg = sV.v
                mSgP[jpar] = sV.v

                def sln(sc, mSg=mSg):
                    sc.wait_ge(sV.h, mSg)
                    sc.activation(rSg[:, :], sigS[:, :], ACT.Ln)
                    sc.drain()
                    _i(sc.activation(rSg[:, :], rSg[:, :], ACT.Exp, scale=-1.0), sS)
                SE(sln)
                mRS = sS.v

                def oev(v, j=j, mAVj=mAVj, mRS=mRS):
                    v.wait_ge(sT.h, mAVj)
                    v.wait_ge(sS.h, mRS)
                    _i(v.tensor_scalar(O_sb[:, :],
                                       psO[:, 128 * (j % 2):128 * (j % 2 + 1)],
                                       rSg[:, 0:1], None, ALU.mult), sV)
                VE(oev)
                mOev = sV.v

                def otr(t, mOev=mOev):
                    t.wait_ge(sV.h, mOev)
                    _i(t.transpose(ptA[:, 512:640], O_sb[:, :], ident[:, :]), sT)
                TE(otr)
                mOtr = sT.v

                def ofe(v, h=h, j=j, mOtr=mOtr):
                    v.wait_ge(sT.h, mOtr)
                    _i(v.tensor_copy(o_fm[:, T * h + 128 * j:T * h + 128 * (j + 1)],
                                     ptA[:, 512:640]), sV)
                VE(ofe)
                mOe[j] = sV.v

            for j in range(NT):
                jpar = j % 2
                for jj in range(j + 1):
                    cc = ccg[0]
                    spar = cc % 2
                    apar = cc % 2
                    psX = psA if spar == 0 else psB
                    diag = (jj == j)
                    mS = s_seg(h, j, jj, cc, cch, par, psX)
                    if pend[0] is not None:
                        pend[0]()
                    mE = mid_seg(h, j, jj, cc, jpar, apar, psX, mS, diag)
                    mExpL[h] = mE
                    pend[0] = (lambda h=h, j=j, jj=jj, cc=cc, par=par, apar=apar,
                               mE=mE, lj=(jj == j):
                               tail_seg(h, j, jj, cc, par, apar, mE, lj))
                    ccg[0] += 1
                    cch += 1
            pend[0]()
            pend[0] = None
            mHead[h] = sT.v

        mOFM = sV.v

        if stage == 3:
            def s3(g):
                g.wait_ge(sV.h, mOFM)
                tk = dqG.issue(g, out=dbg.ap().rearrange("(p m) -> p m", p=128)
                               if False else dbg[:, :], in_=o_fm[:, :])
                DQ.wait(g, tk)
            GE(s3)
            at.close()
            return nc

        at.close()
        mF4 = fence()

        sel_ctx = ExitStack()
        sc_tm = sel_ctx.enter_context(nc.sbuf_tensor("sc_tm", [128, NT], F32))
        sig_tm = sel_ctx.enter_context(nc.sbuf_tensor("sig_tm", [128, NT], F32))
        tau_t = sel_ctx.enter_context(nc.sbuf_tensor("tau_t", [1, 2], F32))
        tau_b = sel_ctx.enter_context(nc.sbuf_tensor("tau_b", [128, 1], F32))
        cnt_t = sel_ctx.enter_context(nc.sbuf_tensor("cnt_t", [1, 1], F32))
        sel16 = sel_ctx.enter_context(nc.sbuf_tensor("sel16", [16, CAP], I16))
        self32 = sel_ctx.enter_context(nc.sbuf_tensor("self32", [1, CAP], F32))
        scatf = sel_ctx.enter_context(nc.sbuf_tensor("scatf", [1, CAP], F32))
        cge = sel_ctx.enter_context(nc.sbuf_tensor("cge", [1, CAP], F32))
        scat16 = sel_ctx.enter_context(nc.sbuf_tensor("scat16", [1, CAP], I16))
        scat_b = sel_ctx.enter_context(nc.sbuf_tensor("scat_b", [128, CAP], I16))
        gidx = sel_ctx.enter_context(nc.sbuf_tensor("gidx", [128, CAP // 16], I16))
        dfm = sel_ctx.enter_context(nc.sbuf_tensor("dfm", [128, 16 * 1024], BF16))
        p4 = ExitStack()
        wo_sb = p4.enter_context(nc.sbuf_tensor("wo_sb", [128, 2 * D], BF16))
        rw_row = p4.enter_context(nc.sbuf_tensor("rw_row", [1, D], BF16))
        rwb = p4.enter_context(nc.sbuf_tensor("rwb", [128, D], BF16))
        ao_fm = p4.enter_context(nc.sbuf_tensor("ao_fm", [128, T], BF16))
        ao_tm = p4.enter_context(nc.sbuf_tensor("ao_tm", [128, NT * D], BF16))
        x0r = [p4.enter_context(nc.sbuf_tensor(f"x0r{i}", [128, D], F32)) for i in range(2)]
        x1t = [p4.enter_context(nc.sbuf_tensor(f"x1t{i}", [128, D], F32)) for i in range(2)]
        ht = [p4.enter_context(nc.sbuf_tensor(f"ht{i}", [128, D], BF16)) for i in range(2)]
        st6b = p4.enter_context(nc.sbuf_tensor("st6b", [128, 24], F32))
        st2b = p4.enter_context(nc.sbuf_tensor("st2b", [128, 2], F32))
        lrs2 = p4.enter_context(nc.sbuf_tensor("lrs2", [128, 1], F32))
        lnm2 = p4.enter_context(nc.sbuf_tensor("lnm2", [128, 1], F32))

        def rbpb(g):
            g.wait_ge(sP.h, mF4)
            DQ.wait(g, tC0[10])
            _i(g.partition_broadcast(rbb[:, :], rb_sb[0:1, 0:1]), sP)
            tok = dqG.issue(g, out=rw_row[:, :],
                            in_=rw_d.ap().rearrange("(o x) -> o x", o=1))
            DQ.wait(g, tok)
            _i(g.partition_broadcast(rwb[:, :], rw_row[0:1, :]), sP)
        GE(rbpb)
        mRBB = sP.v

        tWOC = [None] * 16
        mWOev = [0] * 16
        mAOtr = [0] * 16
        for oc in range(16):
            def wod(g, oc=oc):
                if oc < 2:
                    g.wait_ge(sP.h, mF4)
                else:
                    g.wait_ge(sT.h, mWOmm[oc - 2])
                tWOC[oc] = dqG.issue(
                    g, out=wo_sb[:, (oc % 2) * D:(oc % 2 + 1) * D]
                    .rearrange("p (c j) -> p c j", c=16),
                    in_=w_o_d.ap().rearrange("(c p) j -> p c j", p=128)
                    [:, :, 128 * oc:128 * (oc + 1)])
            GE(wod)

            def womm(t, oc=oc):
                DQ.wait(t, tWOC[oc])
                if oc == 0:
                    t.wait_ge(sV.h, mOFM)
                if oc >= 2:
                    t.wait_ge(sV.h, mWOev[oc - 2])
                psX = psA if oc % 2 == 0 else psB
                for half in range(2):
                    for ic in range(16):
                        t.matmul(psX[:, 512 * half:512 * (half + 1)],
                                 wo_sb[:, (oc % 2) * D:(oc % 2 + 1) * D]
                                 .rearrange("p (c j) -> p c j", c=16)[:, ic, :],
                                 o_fm[:, :].rearrange("p (c m) -> p c m", c=16)
                                 [:, ic, 512 * half:512 * (half + 1)],
                                 start=(ic == 0), stop=(ic == 15))
                _i(t.drain(), sT)
            TE(womm)
            if oc == 0:
                mWOmm = [0] * 16
            mWOmm[oc] = sT.v

            def woev(v, oc=oc, m=mWOmm[oc]):
                v.wait_ge(sT.h, m)
                if oc >= 2:
                    v.wait_ge(sT.h, mAOtr[oc - 2])
                psX = psA if oc % 2 == 0 else psB
                _i(v.tensor_copy(ao_fm[:, :], psX[:, :]), sV)
            VE(woev)
            mWOev[oc] = sV.v

            def aotr(t, oc=oc, m=mWOev[oc]):
                t.wait_ge(sV.h, m)
                for half in range(2):
                    ptX = ptA if half == 0 else ptB
                    for z in range(4):
                        j = 4 * half + z
                        t.transpose(ptX[:, 128 * z:128 * (z + 1)],
                                    ao_fm[:, 128 * j:128 * (j + 1)], ident[:, :])
                    _i(t.drain(), sT)
            TE(aotr)
            mAOtr[oc] = sT.v
            mtr2 = (sT.v - 1, sT.v)

            def aoev(v, oc=oc, mtr2=mtr2):
                for half in range(2):
                    v.wait_ge(sT.h, mtr2[half])
                    _i(v.tensor_copy(
                        ao_tm[:, :].rearrange("p (j m) -> p j m", j=NT)
                        [:, 4 * half:4 * half + 4, 128 * oc:128 * (oc + 1)],
                        (ptA if half == 0 else ptB)[:, 0:512]
                        .rearrange("p (j m) -> p j m", j=4)), sV)
            VE(aoev)
        mAOTM = sV.v
        ofm_ctx.close()
        sel_tr = ExitStack()
        sc_all = sel_tr.enter_context(nc.sbuf_tensor("sc_all", [128, 64], F32, side="right"))
        kthtmp = sel_tr.enter_context(nc.sbuf_tensor("kthtmp", [128, 64], F32, side="right"))
        selmask = sel_tr.enter_context(nc.sbuf_tensor("selmask", [1, T], F32, side="right"))
        sc_own = sel_tr.enter_context(nc.sbuf_tensor("sc_own", [1, T], F32, side="right"))
        pref_a = sel_tr.enter_context(nc.sbuf_tensor("pref_a", [1, T], F32, side="right"))
        pref_b = sel_tr.enter_context(nc.sbuf_tensor("pref_b", [1, T], F32, side="right"))
        ranks = sel_tr.enter_context(nc.sbuf_tensor("ranks", [1, T], F32, side="right"))
        r16b = sel_tr.enter_context(nc.sbuf_tensor("r16b", [16, T], I16, side="right"))
        bs_lo = sel_tr.enter_context(nc.sbuf_tensor("bs_lo", [128, 1], F32, side="right"))
        bs_hi = sel_tr.enter_context(nc.sbuf_tensor("bs_hi", [128, 1], F32, side="right"))
        bs_mid = sel_tr.enter_context(nc.sbuf_tensor("bs_mid", [128, 1], F32, side="right"))
        bs_fl = sel_tr.enter_context(nc.sbuf_tensor("bs_fl", [128, 1], F32, side="right"))
        bs_pc = sel_tr.enter_context(nc.sbuf_tensor("bs_pc", [128, 1], F32, side="right"))

        tX1S = [None] * NT
        tHS = [None] * NT
        mX1f = [0, 0]
        mHtf = [0, 0]
        for j in range(NT):
            par = j % 2
            tx0 = [None]

            def ldx0(sy, j=j, par=par, w=mX1f[par]):
                if w:
                    sy.wait_ge(sV.h, w)
                else:
                    sy.wait_ge(sP.h, mF4)
                tx0[0] = dqQ.issue(sy, out=x0r[par][:, :],
                                   in_=x0_d[128 * j:128 * (j + 1), :])
            SY(ldx0)
            tx0 = tx0[0]

            def ln2a(v, j=j, mAOTM=mAOTM):
                v.wait_ge(sV.h, mAOTM)
                tl = ao_tm[:, :].rearrange("p (j m) -> p j m", j=NT)[:, j, :]
                for q in range(4):
                    v.bn_stats(st6b[:, 6 * q:6 * q + 6], tl[:, 512 * q:512 * (q + 1)])
                v.drain()
                v.bn_aggr(st2b[:, :], st6b[:, :])
                v.drain()
                v.tensor_scalar(lrs2[:, :], st2b[:, 1:2], EPS_LN, None, ALU.add)
                v.drain()
                _i(v.reciprocal(lrs2[:, :], lrs2[:, :]), sV)
            VE(ln2a)
            m1 = sV.v

            def ln2b(sc, m1=m1):
                sc.wait_ge(sV.h, m1)
                _i(sc.activation(lrs2[:, :], lrs2[:, :], ACT.Sqrt), sS)
            SE(ln2b)
            m2 = sS.v

            def ln2c(v, j=j, par=par, m2=m2, tx0=tx0):
                v.wait_ge(sS.h, m2)
                v.tensor_mul(lnm2[:, :], st2b[:, 0:1], lrs2[:, :])
                v.drain()
                DQ.wait(v, tx0)
                tl = ao_tm[:, :].rearrange("p (j m) -> p j m", j=NT)[:, j, :]
                v.scalar_tensor_tensor(x1t[par][:, :], tl, lrs2[:, 0:1], x0r[par][:, :],
                                       ALU.mult, ALU.add)
                v.drain()
                _i(v.tensor_scalar(x1t[par][:, :], x1t[par][:, :], lnm2[:, 0:1],
                                   None, ALU.subtract), sV)
            VE(ln2c)
            mX1 = sV.v
            mX1f[par] = mX1

            def spx1(sy, j=j, par=par, mX1=mX1):
                sy.wait_ge(sV.h, mX1)
                tX1S[j] = dqQ.issue(sy, out=x1_dram[128 * j:128 * (j + 1), :],
                                    in_=x1t[par][:, :])
            SY(spx1)

            def ln3a(v, j=j, par=par, mX1=mX1):
                v.drain()
                for q in range(4):
                    v.bn_stats(st6b[:, 6 * q:6 * q + 6],
                               x1t[par][:, 512 * q:512 * (q + 1)])
                v.drain()
                v.bn_aggr(st2b[:, :], st6b[:, :])
                v.drain()
                v.tensor_scalar(lrs2[:, :], st2b[:, 1:2], EPS_LN, None, ALU.add)
                v.drain()
                _i(v.reciprocal(lrs2[:, :], lrs2[:, :]), sV)
            VE(ln3a)
            m3 = sV.v

            def ln3b(sc, m3=m3):
                sc.wait_ge(sV.h, m3)
                _i(sc.activation(lrs2[:, :], lrs2[:, :], ACT.Sqrt), sS)
            SE(ln3b)
            m4 = sS.v

            def ln3c(v, m4=m4):
                v.wait_ge(sS.h, m4)
                v.tensor_mul(lnm2[:, :], st2b[:, 0:1], lrs2[:, :])
                v.drain()
                _i(v.tensor_scalar(lnm2[:, :], lnm2[:, :], -1.0, None, ALU.mult), sV)
            VE(ln3c)
            m5 = sV.v

            def ln3d(sc, j=j, par=par, m5=m5, whf=mHtf[par],
                     tprev=(tHS[j - 2] if j >= 2 else None)):
                sc.wait_ge(sV.h, m5)
                if whf:
                    sc.wait_ge(sV.h, whf)
                if tprev is not None:
                    DQ.wait(sc, tprev)
                _i(sc.activation(ht[par][:, :], x1t[par][:, :], ACT.Identity,
                                 bias=lnm2[:, 0:1], scale=lrs2[:, 0:1]), sS)
            SE(ln3d)
            m6 = sS.v

            def sth(sy, j=j, par=par, m6=m6):
                sy.wait_ge(sS.h, m6)
                tHS[j] = dqQ.issue(sy, out=h_dram[128 * j:128 * (j + 1), :],
                                   in_=ht[par][:, :])
            SY(sth)

            def scj(v, j=j, par=par, m6=m6):
                v.wait_ge(sS.h, m6)
                DQ.wait(v, tX1S[j])
                v.tensor_tensor(x1t[par][:, :], ht[par][:, :], rwb[:, :], ALU.mult)
                v.drain()
                _i(v.reduce_sum(sc_tm[:, j:j + 1], x1t[par][:, :], axis=AX), sV)
            VE(scj)
            mHtf[par] = sV.v
        mSCT = sV.v

        def scf(v, mSCT=mSCT):
            v.wait_ge(sV.h, mSCT)
            v.wait_ge(sP.h, mRBB)
            _i(v.tensor_scalar(sc_tm[:, :], sc_tm[:, :], rbb[:, 0:1], None, ALU.add),
               sV)
        VE(scf)
        mSCF = sV.v

        def sgf(sc, mSCF=mSCF):
            sc.wait_ge(sV.h, mSCF)
            _i(sc.activation(sig_tm[:, :], sc_tm[:, :], ACT.Sigmoid), sS)
        SE(sgf)
        mSIG = sS.v

        if stage == 4:
            def s4(sy):
                for j in range(NT):
                    DQ.wait(sy, tX1S[j])
                tk = dqQ.issue(sy, out=dbg[:, :], in_=x1_dram[:, :])
                DQ.wait(sy, tk)
            SY(s4)
            p4.close()
            sel_tr.close()
            sel_ctx.close()
            return nc

        tSC = [None]

        def scout(sy, mSCF=mSCF):
            sy.wait_ge(sV.h, mSCF)
            tSC[0] = dqQ.issue(sy, out=agsc_in.ap().rearrange("(j p) -> p j", p=128),
                               in_=sc_tm[:, :], allow_slow_non_contiguous=True)
        SY(scout)

        def agsc(g):
            DQ.wait(g, tSC[0])
            _i(g.collective_compute(
                "AllGather", ALU.bypass,
                replica_groups=[list(range(NCORES))],
                ins=[agsc_in.ap().opt()],
                outs=[agsc_out.ap().opt()]), sCC)
        GE(agsc)
        mAGS = sCC.v

        tSCA = [None]
        tSCO = [None]

        def scload(sy):
            sy.wait_ge(sCC.h, mAGS)
            tSCA[0] = dqQ.issue(sy, out=sc_all[:, :],
                                in_=agsc_out.ap().rearrange("(p f) -> p f", p=128))
            tSCO[0] = dqQ.issue(sy, out=sc_own[:, :],
                                in_=agsc_in.ap().rearrange("(o x) -> o x", o=1))
        SY(scload)

        N_BS = 56

        def bsinit(v):
            DQ.wait(v, tSCA[0])
            _i(v.reduce_max(bs_lo[:, :], sc_all[:, :], axis=AX,
                            apply_absolute_value=True), sV)
        VE(bsinit)
        mBS0 = sV.v

        def bssum(t, mBS0=mBS0):
            t.wait_ge(sV.h, mBS0)
            _i(t.matmul(psM[:, 320:321], onesf[:, :], bs_lo[:, :],
                        start=True, stop=True), sT)
        TE(bssum)
        mBS1 = sT.v

        def bsrng(v, mBS1=mBS1):
            v.wait_ge(sT.h, mBS1)
            v.tensor_scalar(bs_hi[:, :], psM[:, 320:321], 1.0, None, ALU.add)
            v.tensor_scalar(bs_lo[:, :], psM[:, 320:321], -1.0, -1.0,
                            ALU.mult, ALU.add)
            v.drain()
            v.tensor_add(bs_mid[:, :], bs_lo[:, :], bs_hi[:, :])
            v.drain()
            v.tensor_scalar(bs_mid[:, :], bs_mid[:, :], 0.5, None, ALU.mult)
            v.drain()
            v.tensor_scalar(kthtmp[:, :], sc_all[:, :], bs_mid[:, 0:1], None,
                            ALU.is_ge)
            v.drain()
            _i(v.reduce_sum(bs_pc[:, :], kthtmp[:, :], axis=AX), sV)
        VE(bsrng)
        mBSit = sV.v

        for it in range(N_BS):
            def bcnt(t, m=mBSit):
                t.wait_ge(sV.h, m)
                _i(t.matmul(psM[:, 320:321], onesf[:, :], bs_pc[:, :],
                            start=True, stop=True), sT)
            TE(bcnt)
            mPE = sT.v

            def bupd(v, mPE=mPE, last=(it == N_BS - 1)):
                v.wait_ge(sT.h, mPE)
                v.tensor_scalar(bs_fl[:, :], psM[:, 320:321], float(KSEL), None,
                                ALU.is_ge)
                v.drain()
                v.tensor_sub(kthtmp[:, 0:1], bs_mid[:, :], bs_lo[:, :])
                v.tensor_sub(kthtmp[:, 1:2], bs_mid[:, :], bs_hi[:, :])
                v.tensor_scalar(kthtmp[:, 2:3], bs_fl[:, :], -1.0, 1.0,
                                ALU.mult, ALU.add)
                v.drain()
                v.scalar_tensor_tensor(bs_lo[:, :], kthtmp[:, 0:1], bs_fl[:, 0:1],
                                       bs_lo[:, :], ALU.mult, ALU.add)
                v.scalar_tensor_tensor(bs_hi[:, :], kthtmp[:, 1:2], kthtmp[:, 2:3],
                                       bs_hi[:, :], ALU.mult, ALU.add)
                v.drain()
                if not last:
                    v.tensor_add(bs_mid[:, :], bs_lo[:, :], bs_hi[:, :])
                    v.drain()
                    v.tensor_scalar(bs_mid[:, :], bs_mid[:, :], 0.5, None, ALU.mult)
                    v.drain()
                    v.tensor_scalar(kthtmp[:, :], sc_all[:, :], bs_mid[:, 0:1], None,
                                    ALU.is_ge)
                    v.drain()
                    _i(v.reduce_sum(bs_pc[:, :], kthtmp[:, :], axis=AX), sV)
                else:
                    v.tensor_copy(tau_t[0:1, 0:1], bs_lo[0:1, 0:1])
                    _i(v.tensor_copy(tau_t[0:1, 1:2], bs_hi[0:1, 0:1]), sV)
            VE(bupd)
            mBSit = sV.v
        mKth4 = mBSit

        def selv(v):
            v.wait_ge(sV.h, mKth4)
            v.drain()
            DQ.wait(v, tSCO[0])
            v.tensor_scalar(selmask[:, :], sc_own[:, :], tau_t[0:1, 0:1], None,
                            ALU.is_ge)
            v.drain()
            v.reduce_sum(cnt_t[:, :], selmask[:, :], axis=AX)
            v.tensor_copy(pref_a[:, :], selmask[:, :])
            v.drain()
            src_, dst_ = pref_a, pref_b
            for sh in [1, 2, 4, 8, 16, 32, 64, 128, 256, 512]:
                v.tensor_copy(dst_[:, 0:sh], src_[:, 0:sh])
                v.tensor_tensor(dst_[:, sh:T], src_[:, sh:T], src_[:, 0:T - sh],
                                ALU.add)
                v.drain()
                src_, dst_ = dst_, src_
            v.tensor_sub(ranks[:, :], src_[:, :], selmask[:, :])
            v.drain()
            v.tensor_scalar(pref_b[:, :], ranks[:, :], float(CAP), None, ALU.is_lt)
            v.drain()
            v.tensor_mul(pref_b[:, :], pref_b[:, :], selmask[:, :])
            v.tensor_scalar(pref_a[:, :], ranks[:, :], 1.0, None, ALU.add)
            v.drain()
            v.tensor_mul(pref_a[:, :], pref_a[:, :], pref_b[:, :])
            v.drain()
            v.tensor_scalar(pref_a[:, :], pref_a[:, :], 1.0, None, ALU.subtract)
            v.memset(r16b[:, :], -1)
            v.drain()
            _i(v.tensor_copy(r16b[0:1, :], pref_a[:, :]), sV)
        VE(selv)
        mRANKS = sV.v

        def comp(g):
            g.wait_ge(sV.h, mRANKS)
            g.load_library(libcfg.local_scatter)
            _i(g.local_scatter(sel16[:, :], tok16[:, :], r16b[:, :], channels=16,
                               num_elems=CAP, num_idxs=T), sP)
            g.wait_ge(sP.h, sP.v)
            t1 = dqG.issue(g, out=idx_dram[:, :], in_=sel16[0:1, :])
            DQ.wait(g, t1)
            g.memset(gidx[:, :], 0)
            g.drain()
            toks = []
            for b in range(8):
                toks.append(dqG.issue(
                    g, out=gidx[16 * b:16 * (b + 1), :],
                    in_=idx_dram.ap().rearrange("o (f p) -> (o p) f", p=16),
                    allow_slow_non_contiguous=True))
            for tk in toks:
                DQ.wait(g, tk)
            _i(g.drain(), sP)
        GE(comp)
        mGIDX = sP.v

        def selw(v):
            v.wait_ge(sP.h, mGIDX)
            v.tensor_copy(self32[:, :], sel16[0:1, :])
            v.drain()
            v.tensor_scalar(cge[:, :], capio[:, :], cnt_t[0:1, 0:1], None, ALU.is_ge)
            v.tensor_scalar(scatf[:, :], self32[:, :], 1.0, None, ALU.add)
            v.drain()
            v.tensor_mul(scatf[:, :], scatf[:, :], cge[:, :])
            v.drain()
            v.tensor_sub(scatf[:, :], self32[:, :], scatf[:, :])
            v.drain()
            _i(v.tensor_copy(scat16[:, :], scatf[:, :]), sV)
        VE(selw)
        mSCAT = sV.v

        if stage == 5:
            def s5(sy):
                sy.wait_ge(sV.h, mSCAT)
                t1 = dqQ.issue(sy, out=dbg[0:1, :], in_=sc_own[:, :])
                t2 = dqQ.issue(sy, out=dbg[1:2, :], in_=selmask[:, :])
                t3 = dqQ.issue(sy, out=dbg[2:3, :], in_=ranks[:, :])
                t4 = dqQ.issue(sy, out=dbg[3:4, 0:2], in_=tau_t[:, :])
                t5 = dqQ.issue(sy, out=dbg[3:4, 2:3], in_=cnt_t[:, :])
                t6 = dqQ.issue(sy, out=dbg[4:5, 0:CAP], in_=scatf[:, :])
                for tk in (t1, t2, t3, t4, t5, t6):
                    DQ.wait(sy, tk)
            SY(s5)
            p4.close()
            sel_tr.close()
            sel_ctx.close()
            return nc

        p4.close()
        sel_tr.close()
        mF5 = fence()

        f5 = ExitStack()
        hsel = f5.enter_context(nc.sbuf_tensor("hsel", [128, 16 * CAP], BF16))
        g_sb = f5.enter_context(nc.sbuf_tensor("g_sb", [128, 64 * CAP], BF16))
        w1c = [f5.enter_context(nc.sbuf_tensor(f"w1c{i}", [128, 16 * 512], BF16))
               for i in range(2)]
        w2c = [f5.enter_context(nc.sbuf_tensor(f"w2c{i}", [128, 64 * 128], BF16))
               for i in range(2)]
        wd2 = f5.enter_context(nc.sbuf_tensor("wd2", [128, 16 * CAP], BF16))

        def hgat(g):
            g.wait_ge(sP.h, mF5)
            g.load_library(libcfg.attnmlp)
            g.wait_ge(sP.h, mGIDX)
            for j in range(NT):
                DQ.wait(g, tHS[j])
            g.wait_ge(sS.h, mSIG)
            g.wait_ge(sV.h, mSCAT)
            s = dqG.sems[dqG.n % dqG.ns]
            if s.v > 0:
                g.wait_ge(s.h, s.v)
            g.dma_gather(hsel[:, :].rearrange("p (c i) -> p c i", c=16),
                         h_dram[:, :], gidx[:, :], num_idxs=CAP, num_idxs_reg=CAP,
                         elem_size=D, transpose=True).then_inc(s.h, 16)
            s.v += 16
            dqG.n += 1
            g.wait_ge(s.h, s.v)
            _i(g.drain(), sP)
        GE(hgat)
        mHSEL = sP.v

        mW1mm = [0] * 16
        mGelQ = [0] * 64
        tW1 = [None] * 16
        for dc in range(16):
            def w1l(g, dc=dc):
                if dc < 2:
                    g.wait_ge(sP.h, mF5)
                    g.wait_ge(sV.h, mSCAT)
                else:
                    g.wait_ge(sT.h, mW1mm[dc - 2])
                tW1[dc] = dqG.issue(
                    g, out=w1c[dc % 2][:, :].rearrange("p (c j) -> p c j", c=16),
                    in_=w1_d.ap().rearrange("(c p) j -> p c j", p=128)
                    [:, :, 512 * dc:512 * (dc + 1)])
            GE(w1l)
            for q in range(4):
                qq = 4 * dc + q

                def d1mm(t, dc=dc, q=q, qq=qq):
                    if q == 0:
                        DQ.wait(t, tW1[dc])
                    if qq == 0:
                        t.wait_ge(sP.h, mHSEL)
                    if qq >= 2:
                        t.wait_ge(sS.h, mGelQ[qq - 2])
                    psX = psA if qq % 2 == 0 else psB
                    for ic in range(16):
                        t.matmul(psX[:, 0:CAP],
                                 w1c[dc % 2][:, :].rearrange("p (c j) -> p c j", c=16)
                                 [:, ic, 128 * q:128 * (q + 1)],
                                 hsel[:, :].rearrange("p (c i) -> p c i", c=16)
                                 [:, ic, :],
                                 start=(ic == 0), stop=(ic == 15))
                    _i(t.drain(), sT)
                TE(d1mm)
                md1 = sT.v

                def gel(sc, qq=qq, md1=md1):
                    sc.wait_ge(sT.h, md1)
                    psX = psA if qq % 2 == 0 else psB
                    _i(sc.activation(
                        g_sb[:, :].rearrange("p (c i) -> p c i", c=64)[:, qq, :],
                        psX[:, 0:CAP], ACT.Gelu_apprx_tanh,
                        bias=b1_sb[:, qq:qq + 1]), sS)
                SE(gel)
                mGelQ[qq] = sS.v
            mW1mm[dc] = sT.v
        mGELL = sS.v

        mW2mm = [0] * 16
        mW2ev = [0] * 16
        tW2 = [None] * 16
        for oc in range(16):
            def w2l(g, oc=oc):
                if oc < 2:
                    g.wait_ge(sP.h, mF5)
                else:
                    g.wait_ge(sT.h, mW2mm[oc - 2])
                tW2[oc] = dqG.issue(
                    g, out=w2c[oc % 2][:, :].rearrange("p (c j) -> p c j", c=64),
                    in_=w2_d.ap().rearrange("(c p) j -> p c j", p=128)
                    [:, :, 128 * oc:128 * (oc + 1)])
            GE(w2l)

            def d2mm(t, oc=oc):
                DQ.wait(t, tW2[oc])
                t.wait_ge(sS.h, mGELL)
                if oc >= 2:
                    t.wait_ge(sV.h, mW2ev[oc - 2])
                psX = psA if oc % 2 == 0 else psB
                for d in range(64):
                    t.matmul(psX[:, 0:CAP],
                             w2c[oc % 2][:, :].rearrange("p (c j) -> p c j", c=64)
                             [:, d, :],
                             g_sb[:, :].rearrange("p (c i) -> p c i", c=64)[:, d, :],
                             start=(d == 0), stop=(d == 63))
                _i(t.drain(), sT)
            TE(d2mm)
            mW2mm[oc] = sT.v

            def d2ev(v, oc=oc, m=mW2mm[oc]):
                v.wait_ge(sT.h, m)
                psX = psA if oc % 2 == 0 else psB
                _i(v.tensor_scalar(
                    wd2[:, :].rearrange("p (c i) -> p c i", c=16)[:, oc, :],
                    psX[:, 0:CAP], b2_sb[:, oc:oc + 1], None, ALU.add), sV)
            VE(d2ev)
            mW2ev[oc] = sV.v
        mWD2 = sV.v

        def dsc(g):
            g.wait_ge(sV.h, mWD2)
            _i(g.partition_broadcast(scat_b[:, :], scat16[0:1, :]), sP)
            g.wait_ge(sP.h, sP.v)
            g.load_library(libcfg.local_scatter)
            for fc in range(16):
                _i(g.local_scatter(
                    dfm[:, :].rearrange("p (c m) -> p c m", c=16)[:, fc, :],
                    wd2[:, :].rearrange("p (c i) -> p c i", c=16)[:, fc, :],
                    scat_b[:, :], channels=128, num_elems=1024, num_idxs=CAP), sP)
            _i(g.drain(), sP)
        GE(dsc)
        mDFM = sP.v

        f5.close()
        fin = ExitStack()
        dtm = [fin.enter_context(nc.sbuf_tensor(f"dtm{i}", [128, D], BF16))
               for i in range(2)]
        x1r = [fin.enter_context(nc.sbuf_tensor(f"x1r{i}", [128, D], F32))
               for i in range(2)]
        hr = [fin.enter_context(nc.sbuf_tensor(f"hr{i}", [128, D], BF16))
              for i in range(2)]
        ft = fin.enter_context(nc.sbuf_tensor("ft", [128, D], F32))
        yt = [fin.enter_context(nc.sbuf_tensor(f"yt{i}", [128, D], F32))
              for i in range(2)]
        st6c = fin.enter_context(nc.sbuf_tensor("st6c", [128, 24], F32))
        st2c = fin.enter_context(nc.sbuf_tensor("st2c", [128, 2], F32))
        lrs3 = fin.enter_context(nc.sbuf_tensor("lrs3", [128, 1], F32))
        lnm3 = fin.enter_context(nc.sbuf_tensor("lnm3", [128, 1], F32))

        tY = [None] * NT
        mDT = [0, 0]
        mYW = [0, 0]
        gDb = {}
        gbc = [0]
        for j in range(NT):
            par = j % 2
            tld = [None, None]

            def ldf(sy, j=j, par=par, wx=mYW[par]):
                sy.wait_ge(sP.h, mDFM)
                if wx:
                    sy.wait_ge(sV.h, wx)
                DQ.wait(sy, tX1S[j])
                tld[0] = dqQ.issue(sy, out=x1r[par][:, :],
                                   in_=x1_dram[128 * j:128 * (j + 1), :])
                tld[1] = dqQ.issue(sy, out=hr[par][:, :],
                                   in_=h_dram[128 * j:128 * (j + 1), :])
            SY(ldf)
            tldx, tldh = tld

            for b in range(4):
                gb = gbc[0]

                def dtr(t, j=j, b=b, gb=gb, wv=(mDT[par] if b == 0 else 0),
                        wpt=gDb.get(gb - 2)):
                    if b == 0:
                        t.wait_ge(sP.h, mDFM)
                    if wv:
                        t.wait_ge(sV.h, wv)
                    if wpt:
                        t.wait_ge(sV.h, wpt)
                    ptX = ptA if b % 2 == 0 else ptB
                    for z in range(4):
                        fc = 4 * b + z
                        t.transpose(ptX[:, 128 * z:128 * (z + 1)],
                                    dfm[:, :].rearrange("p (c m) -> p c m", c=16)
                                    [:, fc, 128 * j:128 * (j + 1)], ident[:, :])
                    _i(t.drain(), sT)
                TE(dtr)
                mtr = sT.v

                def dte(v, par=par, b=b, mtr=mtr):
                    v.wait_ge(sT.h, mtr)
                    ptX = ptA if b % 2 == 0 else ptB
                    _i(v.tensor_copy(
                        dtm[par][:, 512 * b:512 * (b + 1)], ptX[:, 0:512]), sV)
                VE(dte)
                gDb[gb] = sV.v
                gbc[0] += 1
            mDT[par] = sV.v
            mDTj = sV.v

            def ffo(v, j=j, par=par, mDTj=mDTj, tldx=tldx, tldh=tldh):
                v.wait_ge(sV.h, mDTj)
                DQ.wait(v, tldh)
                v.scalar_tensor_tensor(ft[:, :], dtm[par][:, :], sig_tm[:, j:j + 1],
                                       hr[par][:, :], ALU.mult, ALU.add)
                v.drain()
                for q in range(4):
                    v.bn_stats(st6c[:, 6 * q:6 * q + 6], ft[:, 512 * q:512 * (q + 1)])
                v.drain()
                v.bn_aggr(st2c[:, :], st6c[:, :])
                v.drain()
                v.tensor_scalar(lrs3[:, :], st2c[:, 1:2], EPS_LN, None, ALU.add)
                v.drain()
                _i(v.reciprocal(lrs3[:, :], lrs3[:, :]), sV)
            VE(ffo)
            m1 = sV.v

            def fsq(sc, m1=m1):
                sc.wait_ge(sV.h, m1)
                _i(sc.activation(lrs3[:, :], lrs3[:, :], ACT.Sqrt), sS)
            SE(fsq)
            m2 = sS.v

            def fy(v, j=j, par=par, m2=m2, tldx=tldx,
                   typ=(tY[j - 2] if j >= 2 else None)):
                v.wait_ge(sS.h, m2)
                if typ is not None:
                    DQ.wait(v, typ)
                v.tensor_mul(lnm3[:, :], st2c[:, 0:1], lrs3[:, :])
                v.drain()
                DQ.wait(v, tldx)
                v.scalar_tensor_tensor(yt[par][:, :], ft[:, :], lrs3[:, 0:1],
                                       x1r[par][:, :], ALU.mult, ALU.add)
                v.drain()
                _i(v.tensor_scalar(yt[par][:, :], yt[par][:, :], lnm3[:, 0:1],
                                   None, ALU.subtract), sV)
            VE(fy)
            mYW[par] = sV.v
            mY = sV.v

            def sty(sy, j=j, par=par, mY=mY):
                sy.wait_ge(sV.h, mY)
                tY[j] = dqQ.issue(sy, out=y_d[128 * j:128 * (j + 1), :],
                                  in_=yt[par][:, :])
            SY(sty)

        def fwait(sy):
            for j in range(NT):
                DQ.wait(sy, tY[j])
        SY(fwait)

        fin.close()
        sel_ctx.close()
        return nc
    return nc



def _consts(core):
    pos = (np.arange(T) * NCORES + core).astype(np.float64)
    inv = 1.0 / (10000.0 ** (np.arange(0, ROPE, 2, dtype=np.float64) / ROPE))
    fr = np.outer(inv, pos)
    cos = np.cos(fr)
    sin = np.sin(fr)
    ropeCh = np.tile(cos, (4, 1)).astype(np.float32)
    ropeSh = np.concatenate([np.tile(-sin, 1), sin], axis=0)
    ropeSh = np.tile(np.concatenate([-sin, sin], axis=0), (2, 1)).astype(np.float32)
    r = np.arange(128)[:, None]
    c = np.tile(np.arange(128)[None, :], (1, 8)).reshape(1, 1024)
    rr = np.repeat(np.arange(8), 128).reshape(1, 1024)
    allow = (c < r) | ((c == r) & (rr <= core))
    dm = np.where(allow, 0.0, MASK_NEG).astype(np.float32)
    return ropeCh, ropeSh, dm


def kernel(**inputs):
    nc = build_nc(stage=6)
    return _run(nc, inputs, stage=6)["y"]


def _run(nc, inputs, stage):
    mybir.codegen_inst_isa_subclasses(nc)
    x = np.ascontiguousarray(np.asarray(inputs["L0_attn_x"], dtype=np.float32)[0])
    ident = np.eye(128, dtype=ml_dtypes.bfloat16)
    onesm = np.ones((128, 128), dtype=ml_dtypes.bfloat16)
    tok = np.arange(T, dtype=np.int16).reshape(1, T)
    cap = np.arange(CAP, dtype=np.float32).reshape(1, CAP)
    common = {
        "w_dq": np.asarray(inputs["L0_attn_w_dq"], np.float32),
        "w_uq": np.asarray(inputs["L0_attn_w_uq"], np.float32),
        "w_dkv": np.asarray(inputs["L0_attn_w_dkv"], np.float32),
        "w_kr": np.asarray(inputs["L0_attn_w_kr"], np.float32),
        "w_uk": np.asarray(inputs["L0_attn_w_uk"], np.float32),
        "w_uv": np.asarray(inputs["L0_attn_w_uv"], np.float32),
        "w_o": np.asarray(inputs["L0_attn_w_o"], np.float32),
        "g_q": np.asarray(inputs["L0_attn_g_q"], np.float32),
        "g_kv": np.asarray(inputs["L0_attn_g_kv"], np.float32),
        "w1": np.asarray(inputs["L0_ffn_w1"], np.float32),
        "b1": np.asarray(inputs["L0_ffn_b1"], np.float32),
        "w2": np.asarray(inputs["L0_ffn_w2"], np.float32),
        "b2": np.asarray(inputs["L0_ffn_b2"], np.float32),
        "rw": np.asarray(inputs["L0_ffn_router_weight"], np.float32),
        "rb": np.asarray(inputs["L0_ffn_router_bias"], np.float32),
        "ident": ident, "ones": onesm, "tok16": tok, "capio": cap,
        "id32f": np.eye(32, dtype=np.float32),
        "onesf": np.ones((128, 128), dtype=np.float32),
    }
    in_maps = []
    for i in range(NCORES):
        rc, rs, dm = _consts(i)
        m = dict(common)
        m["x0"] = np.ascontiguousarray(x[i::NCORES])
        m["ropeC"] = rc
        m["ropeS"] = rs
        m["dmask"] = dm
        in_maps.append(m)
    res = run_bass_kernel_spmd(nc, in_maps, core_ids=list(range(NCORES)))
    if stage == 6:
        out = np.empty((1, S, D), np.float32)
        for i in range(NCORES):
            out[0, i::NCORES, :] = res.results[i]["y"]
        return {"y": out}
    return {"res": res}


# version 04
# speedup vs baseline: 1.6597x; 1.0140x over previous
import math
from contextlib import ExitStack

import numpy as np
import ml_dtypes

import concourse.bass as bass
import concourse.mybir as mybir
import concourse.library_config as libcfg
from concourse.bass_utils import run_bass_kernel_spmd

F32 = mybir.dt.float32
BF16 = mybir.dt.bfloat16
I16 = mybir.dt.int16
AX = mybir.AxisListType.X
ALU = mybir.AluOpType
ACT = mybir.ActivationFunctionType

D = 2048
S = 8192
NCORES = 8
T = S // NCORES
NT = T // 128
H = 16
NOPE, ROPE, VD = 128, 64, 128
QKD = NOPE + ROPE
QL, KVL = 768, 512
DFF = 8192
CAP = 384
SCALE = QKD ** (-0.5)
EPS_LN = 1e-5
EPS_RMS = 1e-6
MASK_NEG = -1.0e30
NKC = 8
KSEL = 2048

KN_OFF = 0
KPE_OFF = KN_OFF + D * T
KN2_OFF = KPE_OFF + ROPE * T
V_OFF = KN2_OFF + 32
KVSH = V_OFF + H * T * VD


class Sem:
    def __init__(self, h):
        self.h = h
        self.v = 0


def _i(instr, s, n=1):
    instr.then_inc(s.h, n)
    s.v += n
    return s.v


class DQ:

    def __init__(self, nc, ctx, name, ns=8):
        self.sems = [Sem(ctx.enter_context(nc.semaphore(f"{name}{i}")))
                     for i in range(ns)]
        self.ns = ns
        self.n = 0

    def issue(self, eng, **kw):
        s = self.sems[self.n % self.ns]
        if s.v > 0:
            eng.wait_ge(s.h, s.v)
        eng.dma_start(**kw).then_inc(s.h, 16)
        s.v += 16
        self.n += 1
        return (s, s.v)

    @staticmethod
    def wait(eng, tok):
        eng.wait_ge(tok[0].h, tok[1])

    def wait_all(self, eng):
        for s in self.sems:
            if s.v > 0:
                eng.wait_ge(s.h, s.v)


def build_nc(stage=6):
    nc = bass.Bass("TRN2", target_bir_lowering=False, debug=False)

    x0_d = nc.dram_tensor("x0", [T, D], F32, kind="ExternalInput")
    w_dq_d = nc.dram_tensor("w_dq", [D, QL], F32, kind="ExternalInput")
    w_uq_d = nc.dram_tensor("w_uq", [QL, H * QKD], F32, kind="ExternalInput")
    w_dkv_d = nc.dram_tensor("w_dkv", [D, KVL], F32, kind="ExternalInput")
    w_kr_d = nc.dram_tensor("w_kr", [D, ROPE], F32, kind="ExternalInput")
    w_uk_d = nc.dram_tensor("w_uk", [KVL, H * NOPE], F32, kind="ExternalInput")
    w_uv_d = nc.dram_tensor("w_uv", [KVL, H * VD], F32, kind="ExternalInput")
    w_o_d = nc.dram_tensor("w_o", [H * VD, D], F32, kind="ExternalInput")
    g_q_d = nc.dram_tensor("g_q", [QL], F32, kind="ExternalInput")
    g_kv_d = nc.dram_tensor("g_kv", [KVL], F32, kind="ExternalInput")
    w1_d = nc.dram_tensor("w1", [D, DFF], F32, kind="ExternalInput")
    b1_d = nc.dram_tensor("b1", [DFF], F32, kind="ExternalInput")
    w2_d = nc.dram_tensor("w2", [DFF, D], F32, kind="ExternalInput")
    b2_d = nc.dram_tensor("b2", [D], F32, kind="ExternalInput")
    rw_d = nc.dram_tensor("rw", [D], F32, kind="ExternalInput")
    rb_d = nc.dram_tensor("rb", [1], F32, kind="ExternalInput")
    ropeC_d = nc.dram_tensor("ropeC", [128, T], F32, kind="ExternalInput")
    ropeS_d = nc.dram_tensor("ropeS", [128, T], F32, kind="ExternalInput")
    dmask_d = nc.dram_tensor("dmask", [128, 1024], F32, kind="ExternalInput")
    ident_d = nc.dram_tensor("ident", [128, 128], BF16, kind="ExternalInput")
    ones_d = nc.dram_tensor("ones", [128, 128], BF16, kind="ExternalInput")
    tok16_d = nc.dram_tensor("tok16", [1, T], I16, kind="ExternalInput")
    capio_d = nc.dram_tensor("capio", [1, CAP], F32, kind="ExternalInput")
    id32f_d = nc.dram_tensor("id32f", [32, 32], F32, kind="ExternalInput")
    onesf_d = nc.dram_tensor("onesf", [128, 128], F32, kind="ExternalInput")

    y_d = nc.dram_tensor("y", [T, D], F32, kind="ExternalOutput")
    dbg = None
    if stage == 2:
        dbg = nc.dram_tensor("dbg", [KVSH], BF16, kind="ExternalOutput")
    elif stage == 3:
        dbg = nc.dram_tensor("dbg", [128, H * T], BF16, kind="ExternalOutput")
    elif stage == 4:
        dbg = nc.dram_tensor("dbg", [T, D], F32, kind="ExternalOutput")
    elif stage == 5:
        dbg = nc.dram_tensor("dbg", [8, T], F32, kind="ExternalOutput")

    agkv_in = nc.dram_tensor("agkv_in", [KVSH], BF16)
    agkv_out = nc.dram_tensor("agkv_out", [NCORES * KVSH], BF16)
    agsc_in = nc.dram_tensor("agsc_in", [T], F32)
    agsc_out = nc.dram_tensor("agsc_out", [NCORES * T], F32)
    h_dram = nc.dram_tensor("h_dram", [T, D], BF16)
    x1_dram = nc.dram_tensor("x1_dram", [T, D], F32)
    idx_dram = nc.dram_tensor("idx_dram", [1, CAP], I16)

    ctx = ExitStack()
    with ctx:
        dqQ = DQ(nc, ctx, "sQ", ns=8)
        dqG = DQ(nc, ctx, "sG", ns=8)
        sCC = Sem(ctx.enter_context(nc.semaphore("sCC")))
        sT = Sem(ctx.enter_context(nc.semaphore("sT")))
        sV = Sem(ctx.enter_context(nc.semaphore("sV")))
        sS = Sem(ctx.enter_context(nc.semaphore("sS")))
        sP = Sem(ctx.enter_context(nc.semaphore("sP")))
        blk = ctx.enter_context(nc.Block())

        TE, VE, SE, GE, SY = blk.tensor, blk.vector, blk.scalar, blk.gpsimd, blk.sync

        def fence():
            snap = (sT.v, sV.v, sS.v)

            def fg(g, snap=snap):
                dqQ.wait_all(g)
                dqG.wait_all(g)
                g.wait_ge(sT.h, snap[0])
                g.wait_ge(sV.h, snap[1])
                g.wait_ge(sS.h, snap[2])
                _i(g.drain(), sP)
            GE(fg)
            return sP.v

        ident = ctx.enter_context(nc.sbuf_tensor("ident_s", [128, 128], BF16))
        ones = ctx.enter_context(nc.sbuf_tensor("ones_s", [128, 128], BF16))
        ropeC = ctx.enter_context(nc.sbuf_tensor("ropeC_s", [128, T], F32))
        ropeS = ctx.enter_context(nc.sbuf_tensor("ropeS_s", [128, T], F32))
        dmask = ctx.enter_context(nc.sbuf_tensor("dmask_s", [128, 1024], F32))
        gq_sb = ctx.enter_context(nc.sbuf_tensor("gq_sb", [128, 6], F32))
        gkv_sb = ctx.enter_context(nc.sbuf_tensor("gkv_sb", [128, 4], F32))
        b1_sb = ctx.enter_context(nc.sbuf_tensor("b1_sb", [128, 64], F32))
        b2_sb = ctx.enter_context(nc.sbuf_tensor("b2_sb", [128, 16], F32))
        rb_sb = ctx.enter_context(nc.sbuf_tensor("rb_sb", [1, 1], F32))
        tok16 = ctx.enter_context(nc.sbuf_tensor("tok16_s", [16, T], I16))
        capio = ctx.enter_context(nc.sbuf_tensor("capio_s", [1, CAP], F32))
        id32f = ctx.enter_context(nc.sbuf_tensor("id32f_s", [32, 32], F32))
        onesf = ctx.enter_context(nc.sbuf_tensor("onesf_s", [128, 128], F32))

        psA = ctx.enter_context(nc.psum_tensor("psA", [128, 1024], F32))
        psB = ctx.enter_context(nc.psum_tensor("psB", [128, 1024], F32))
        ptA = ctx.enter_context(nc.psum_tensor("ptA", [128, 1024], BF16))
        ptB = ctx.enter_context(nc.psum_tensor("ptB", [128, 1024], BF16))
        psO = ctx.enter_context(nc.psum_tensor("psO", [128, 512], F32))
        psM = ctx.enter_context(nc.psum_tensor("psM", [128, 512], F32))

        tC0 = []

        def c0(sy):
            for dst, src in ((ident, ident_d), (ones, ones_d), (ropeC, ropeC_d),
                             (ropeS, ropeS_d), (dmask, dmask_d), (capio, capio_d),
                             (id32f, id32f_d), (onesf, onesf_d)):
                tC0.append(dqQ.issue(sy, out=dst[:, :], in_=src[:, :]))
            for dst, src in ((gq_sb, g_q_d), (gkv_sb, g_kv_d),
                             (b1_sb, b1_d), (b2_sb, b2_d)):
                tC0.append(dqQ.issue(
                    sy, out=dst[:, :], in_=src.ap().rearrange("(c p) -> p c", p=128),
                    allow_slow_non_contiguous=True))
            tC0.append(dqQ.issue(sy, out=rb_sb[:, :],
                                 in_=rb_d.ap().rearrange("(o x) -> o x", o=1)))
            for b in range(16):
                tC0.append(dqQ.issue(sy, out=tok16[b:b + 1, :], in_=tok16_d[:, :]))
        SY(c0)
        tIdent = tC0

        def g0(g):
            g.load_library(libcfg.attnmlp)
        GE(g0)

        ofm_ctx = ExitStack()
        aqn = ofm_ctx.enter_context(nc.sbuf_tensor("aqn", [128, 6 * T], BF16, side="right"))
        o_fm = ofm_ctx.enter_context(nc.sbuf_tensor("o_fm", [128, H * T], BF16, side="right"))
        kn2x = ctx.enter_context(nc.sbuf_tensor("kn2x", [32, NCORES], F32))
        m2k16 = ctx.enter_context(nc.sbuf_tensor("m2k16", [32, 1], F32))
        rbb = ctx.enter_context(nc.sbuf_tensor("rbb", [128, 1], F32))

        p2 = ExitStack()
        x0b = [p2.enter_context(nc.sbuf_tensor(f"x0b{i}", [128, D], F32)) for i in range(2)]
        st6 = p2.enter_context(nc.sbuf_tensor("st6", [128, 24], F32))
        st2 = p2.enter_context(nc.sbuf_tensor("st2", [128, 2], F32))
        lrs = p2.enter_context(nc.sbuf_tensor("lrs", [128, 1], F32))
        lnm = p2.enter_context(nc.sbuf_tensor("lnm", [128, 1], F32))
        t_tm = p2.enter_context(nc.sbuf_tensor("t_tm", [128, D], BF16))
        t_fm = p2.enter_context(nc.sbuf_tensor("t_fm", [128, 16 * T], BF16))
        ckvn = p2.enter_context(nc.sbuf_tensor("ckvn", [128, 4 * T], BF16))
        sqtmp = p2.enter_context(nc.sbuf_tensor("sqtmp", [128, T], BF16))
        rmss = p2.enter_context(nc.sbuf_tensor("rmss", [1, T], F32))
        rmsb = p2.enter_context(nc.sbuf_tensor("rmsb", [128, T], F32))
        knch = p2.enter_context(nc.sbuf_tensor("knch", [128, T], BF16))
        knsq = p2.enter_context(nc.sbuf_tensor("knsq", [128, T], BF16))
        kn2row = p2.enter_context(nc.sbuf_tensor("kn2row", [1, 32], F32))
        kn2rb = p2.enter_context(nc.sbuf_tensor("kn2rb", [1, 32], BF16))
        vch = p2.enter_context(nc.sbuf_tensor("vch", [128, T], BF16))
        vtt = p2.enter_context(nc.sbuf_tensor("vtt", [128, T], BF16))
        kpe0 = p2.enter_context(nc.sbuf_tensor("kpe0", [64, T], F32))
        kpet = p2.enter_context(nc.sbuf_tensor("kpet", [64, T], F32))
        kper = p2.enter_context(nc.sbuf_tensor("kper", [64, T], BF16))
        kpesq = p2.enter_context(nc.sbuf_tensor("kpesq", [64, T], BF16))
        p2a = ExitStack()
        wdkv = p2a.enter_context(nc.sbuf_tensor("wdkv", [128, 16 * KVL], BF16))
        wkr = p2a.enter_context(nc.sbuf_tensor("wkr", [128, 16 * 2 * ROPE], BF16))
        wuk = p2a.enter_context(nc.sbuf_tensor("wuk", [128, 4 * H * NOPE], BF16))
        wuv = p2a.enter_context(nc.sbuf_tensor("wuv", [128, 4 * H * VD], BF16))

        tW = []

        def wld(g):
            tW.append(dqG.issue(g, out=wdkv[:, :].rearrange("p (c j) -> p c j", c=16),
                                in_=w_dkv_d.ap().rearrange("(c p) j -> p c j", p=128)))
            tW.append(dqG.issue(g, out=wuk[:, :].rearrange("p (c j) -> p c j", c=4),
                                in_=w_uk_d.ap().rearrange("(c p) j -> p c j", p=128)))
            tW.append(dqG.issue(g, out=wuv[:, :].rearrange("p (c j) -> p c j", c=4),
                                in_=w_uv_d.ap().rearrange("(c p) j -> p c j", p=128)))
            tW.append(dqG.issue(
                g, out=wkr[:, :].rearrange("p (c u) -> p c u", c=16)[:, :, 0:ROPE],
                in_=w_kr_d.ap().rearrange("(c p) j -> p c j", p=128)))
            tW.append(dqG.issue(
                g, out=wkr[:, :].rearrange("p (c u) -> p c u", c=16)[:, :, ROPE:ROPE + 32],
                in_=w_kr_d.ap().rearrange("(c p) j -> p c j", p=128)[:, :, 32:64]))
            tW.append(dqG.issue(
                g, out=wkr[:, :].rearrange("p (c u) -> p c u", c=16)[:, :, ROPE + 32:2 * ROPE],
                in_=w_kr_d.ap().rearrange("(c p) j -> p c j", p=128)[:, :, 0:32]))
        GE(wld)

        mX0free = [None, None]
        for b in range(NT):
            par = b % 2

            tkx = [None]

            def ldx(sy, b=b, par=par, w=mX0free[par]):
                if w is not None:
                    sy.wait_ge(sS.h, w)
                tkx[0] = dqQ.issue(sy, out=x0b[par][:, :],
                                   in_=x0_d[128 * b:128 * (b + 1), :])
            SY(ldx)
            mx = tkx[0]

            def lns(v, par=par, mx=mx, wprev=mX0free[1 - par]):
                DQ.wait(v, mx)
                if wprev is not None:
                    v.wait_ge(sS.h, wprev)
                for q in range(4):
                    v.bn_stats(st6[:, 6 * q:6 * q + 6], x0b[par][:, 512 * q:512 * (q + 1)])
                v.drain()
                v.bn_aggr(st2[:, :], st6[:, :])
                v.drain()
                v.tensor_scalar(lrs[:, :], st2[:, 1:2], EPS_LN, None, ALU.add)
                v.drain()
                _i(v.reciprocal(lrs[:, :], lrs[:, :]), sV)
            VE(lns)
            m1 = sV.v

            def lnr(sc, m1=m1):
                sc.wait_ge(sV.h, m1)
                _i(sc.activation(lrs[:, :], lrs[:, :], ACT.Sqrt), sS)
            SE(lnr)
            m2 = sS.v

            def lnn(v, m2=m2):
                v.wait_ge(sS.h, m2)
                v.tensor_mul(lnm[:, :], st2[:, 0:1], lrs[:, :])
                v.drain()
                _i(v.tensor_scalar(lnm[:, :], lnm[:, :], -1.0, None, ALU.mult), sV)
            VE(lnn)
            m3 = sV.v

            def lnap(sc, par=par, m3=m3):
                sc.wait_ge(sV.h, m3)
                _i(sc.activation(t_tm[:, :], x0b[par][:, :], ACT.Identity,
                                 bias=lnm[:, 0:1], scale=lrs[:, 0:1]), sS)
            SE(lnap)
            m4 = sS.v
            mX0free[par] = m4

            def ttr(t, m4=m4, b=b):
                t.wait_ge(sS.h, m4)
                if b == 0:
                    DQ.wait(t, tIdent[0])
                for half in range(2):
                    pt = ptA if half == 0 else ptB
                    for c in range(8):
                        cc = 8 * half + c
                        t.transpose(pt[:, 128 * c:128 * (c + 1)],
                                    t_tm[:, 128 * cc:128 * (cc + 1)], ident[:, :])
                    _i(t.drain(), sT)
            TE(ttr)
            mh = (sT.v - 1, sT.v)

            def tev(v, mh=mh, b=b):
                for half in range(2):
                    v.wait_ge(sT.h, mh[half])
                    _i(v.tensor_copy(
                        t_fm[:, :].rearrange("p (c m) -> p c m", c=16)
                        [:, 8 * half:8 * half + 8, 128 * b:128 * (b + 1)],
                        (ptA if half == 0 else ptB)[:, :].rearrange(
                            "p (c m) -> p c m", c=8)), sV)
            VE(tev)
        mTFM = sV.v

        def proj_rms(wsb, nchunk, gcol_sb, buf):
            vgate = sV.v
            sgate = sS.v
            mEv = []
            mSq = []
            for oc in range(nchunk):
                ps_full = psA if oc % 2 == 0 else psB
                def mm(t, oc=oc, ps_full=ps_full):
                    if oc == 0:
                        for tk in tW:
                            DQ.wait(t, tk)
                        t.wait_ge(sV.h, max(mTFM, vgate))
                    if oc >= 2:
                        t.wait_ge(sV.h, mEv[oc - 2])
                    for half in range(2):
                        ps = ps_full[:, 512 * half:512 * (half + 1)]
                        for ic in range(16):
                            t.matmul(ps,
                                     wsb[:, :].rearrange("p (c j) -> p c j", c=16)
                                     [:, ic, 128 * oc:128 * (oc + 1)],
                                     t_fm[:, :].rearrange("p (c m) -> p c m", c=16)
                                     [:, ic, 512 * half:512 * (half + 1)],
                                     start=(ic == 0), stop=(ic == 15))
                    _i(t.drain(), sT)
                TE(mm)
                mp = sT.v

                def ev(v, oc=oc, mp=mp, ps_full=ps_full):
                    v.wait_ge(sT.h, mp)
                    _i(v.tensor_copy(
                        buf[:, :].rearrange("p (c m) -> p c m", c=nchunk)[:, oc, :],
                        ps_full[:, :]), sV)
                VE(ev)
                mEv.append(sV.v)
                mr = sV.v

                def sq(sc, oc=oc, mr=mr):
                    sc.wait_ge(sV.h, mr)
                    if oc >= 1:
                        sc.wait_ge(sT.h, mSq[oc - 1])
                    _i(sc.activation(sqtmp[:, :],
                                     buf[:, :].rearrange("p (c m) -> p c m", c=nchunk)
                                     [:, oc, :], ACT.Square), sS)
                SE(sq)
                msq = sS.v

                def ssum(t, oc=oc, msq=msq, vgate=vgate):
                    t.wait_ge(sS.h, msq)
                    if oc == 0:
                        t.wait_ge(sV.h, vgate)
                    t.matmul(psM[0:1, 0:512], ones[:, 0:1], sqtmp[:, 0:512],
                             start=(oc == 0), stop=(oc == nchunk - 1))
                    t.matmul(psO[0:1, 0:512], ones[:, 0:1], sqtmp[:, 512:1024],
                             start=(oc == 0), stop=(oc == nchunk - 1))
                    _i(t.drain(), sT)
                TE(ssum)
                mSq.append(sT.v)
            msum = sT.v

            def rstd(v, msum=msum, nchunk=nchunk):
                v.wait_ge(sT.h, msum)
                v.tensor_scalar(rmss[0:1, 0:512], psM[0:1, 0:512],
                                1.0 / (nchunk * 128), EPS_RMS, ALU.mult, ALU.add)
                v.tensor_scalar(rmss[0:1, 512:1024], psO[0:1, 0:512],
                                1.0 / (nchunk * 128), EPS_RMS, ALU.mult, ALU.add)
                _i(v.drain(), sV)
            VE(rstd)
            mms = sV.v

            def rsq(sc, mms=mms):
                sc.wait_ge(sV.h, mms)
                _i(sc.activation(rmss[:, :], rmss[:, :], ACT.Sqrt), sS)
            SE(rsq)
            msq2 = sS.v

            def rrec(v, msq2=msq2):
                v.wait_ge(sS.h, msq2)
                _i(v.reciprocal(rmss[:, :], rmss[:, :]), sV)
            VE(rrec)
            mrec = sV.v

            def bcast(g, mrec=mrec):
                g.wait_ge(sV.h, mrec)
                _i(g.partition_broadcast(rmsb[:, :], rmss[0:1, :]), sP)
            GE(bcast)
            mb = sP.v

            for oc in range(nchunk):
                def nrm(v, oc=oc, mb=mb):
                    v.wait_ge(sP.h, mb)
                    _i(v.scalar_tensor_tensor(
                        buf[:, :].rearrange("p (c m) -> p c m", c=nchunk)[:, oc, :],
                        buf[:, :].rearrange("p (c m) -> p c m", c=nchunk)[:, oc, :],
                        gcol_sb[:, oc:oc + 1], rmsb[:, :],
                        ALU.mult, ALU.mult), sV)
                VE(nrm)
            def nrmd(v):
                _i(v.drain(), sV)
            VE(nrmd)
            return sV.v

        mCKV = proj_rms(wdkv, 4, gkv_sb, ckvn)

        tShard = []

        def z32(v):
            v.memset(kn2row[:, :], 0.0)
            _i(v.drain(), sV)
        VE(z32)
        mZ32 = sV.v

        mKNprev = [None, None]
        mKN2V = [None]
        tOutPrev = [None, None]
        for hh in range(H):
            def knmm(t, hh=hh, wprev=(mKNprev[0] if hh > 0 else None)):
                if hh == 0:
                    t.wait_ge(sV.h, mCKV)
                else:
                    t.wait_ge(sV.h, wprev)
                for half in range(2):
                    ps = psA[:, 512 * half:512 * (half + 1)]
                    for ic in range(4):
                        t.matmul(ps,
                                 wuk[:, :].rearrange("p (c j) -> p c j", c=4)
                                 [:, ic, 128 * hh:128 * (hh + 1)],
                                 ckvn[:, :].rearrange("p (c m) -> p c m", c=4)
                                 [:, ic, 512 * half:512 * (half + 1)],
                                 start=(ic == 0), stop=(ic == 3))
                _i(t.drain(), sT)
            TE(knmm)
            mkn = sT.v

            def knev(v, mkn=mkn, tprev=tOutPrev[0]):
                v.wait_ge(sT.h, mkn)
                if tprev is not None:
                    DQ.wait(v, tprev)
                _i(v.tensor_copy(knch[:, :], psA[:, :]), sV)
            VE(knev)
            mknb = sV.v

            mKNprev[0] = mknb

            def knq(sc, mknb=mknb):
                sc.wait_ge(sV.h, mknb)
                _i(sc.activation(knsq[:, :], knch[:, :], ACT.Square), sS)
            SE(knq)
            mknsq = sS.v

            def kn2mm(t, mknsq=mknsq, hh=hh, wv=(mKN2V[0] if hh > 0 else None)):
                t.wait_ge(sS.h, mknsq)
                if wv is not None:
                    t.wait_ge(sV.h, wv)
                t.matmul(psM[0:1, 0:512], ones[:, 0:1], knsq[:, 0:512],
                         start=True, stop=True)
                t.matmul(psO[0:1, 0:512], ones[:, 0:1], knsq[:, 512:1024],
                         start=True, stop=True)
                _i(t.drain(), sT)
            TE(kn2mm)
            mkn2 = sT.v

            def kn2red(v, mkn2=mkn2, hh=hh):
                v.wait_ge(sT.h, mkn2)
                if hh == 0:
                    v.wait_ge(sV.h, mZ32)
                v.reduce_max(kn2row[0:1, hh:hh + 1], psM[0:1, 0:512], axis=AX)
                v.reduce_max(lrs[0:1, 0:1], psO[0:1, 0:512], axis=AX)
                v.drain()
                _i(v.tensor_tensor(kn2row[0:1, hh:hh + 1], kn2row[0:1, hh:hh + 1],
                                   lrs[0:1, 0:1], ALU.max), sV)
            VE(kn2red)
            mKN2V[0] = sV.v

            tkn = [None]

            def knout(sy, hh=hh, mknb=mknb):
                sy.wait_ge(sV.h, mknb)
                tkn[0] = dqQ.issue(
                    sy,
                    out=agkv_in.ap()[KN_OFF + hh * 128 * T:KN_OFF + (hh + 1) * 128 * T]
                    .rearrange("(p m) -> p m", p=128),
                    in_=knch[:, :])
            SY(knout)
            tShard.append(tkn[0])
            tOutPrev[0] = tkn[0]

            def vmm(t, hh=hh, wprev=(mKNprev[1] if hh > 0 else None)):
                if wprev is not None:
                    t.wait_ge(sV.h, wprev)
                for half in range(2):
                    ps = psB[:, 512 * half:512 * (half + 1)]
                    for ic in range(4):
                        t.matmul(ps,
                                 wuv[:, :].rearrange("p (c j) -> p c j", c=4)
                                 [:, ic, 128 * hh:128 * (hh + 1)],
                                 ckvn[:, :].rearrange("p (c m) -> p c m", c=4)
                                 [:, ic, 512 * half:512 * (half + 1)],
                                 start=(ic == 0), stop=(ic == 3))
                _i(t.drain(), sT)
            TE(vmm)
            mv = sT.v

            def vev(v, mv=mv):
                v.wait_ge(sT.h, mv)
                _i(v.tensor_copy(vch[:, :], psB[:, :]), sV)
            VE(vev)
            mvb = sV.v
            mKNprev[1] = mvb

            def vtr(t, mvb=mvb):
                t.wait_ge(sV.h, mvb)
                for half in range(2):
                    pt = ptA if half == 0 else ptB
                    for bb in range(4):
                        b = 4 * half + bb
                        t.transpose(pt[:, 128 * bb:128 * (bb + 1)],
                                    vch[:, 128 * b:128 * (b + 1)], ident[:, :])
                    _i(t.drain(), sT)
            TE(vtr)
            mvt = (sT.v - 1, sT.v)

            def vtev(v, mvt=mvt, tprev=tOutPrev[1]):
                if tprev is not None:
                    DQ.wait(v, tprev)
                for half in range(2):
                    v.wait_ge(sT.h, mvt[half])
                    _i(v.tensor_copy(
                        vtt[:, :].rearrange("p (b m) -> p b m", b=8)
                        [:, 4 * half:4 * half + 4, :],
                        (ptA if half == 0 else ptB)[:, 0:512].rearrange(
                            "p (b m) -> p b m", b=4)), sV)
            VE(vtev)
            mvtt = sV.v

            tvo = [None]

            def vout(sy, hh=hh, mvtt=mvtt):
                sy.wait_ge(sV.h, mvtt)
                tvo[0] = dqQ.issue(
                    sy,
                    out=agkv_in.ap()[V_OFF + hh * T * VD:V_OFF + (hh + 1) * T * VD]
                    .rearrange("(b p v) -> p b v", b=8, p=128),
                    in_=vtt[:, :].rearrange("p (b v) -> p b v", b=8))
            SY(vout)
            tShard.append(tvo[0])
            tOutPrev[1] = tvo[0]

        def kpmm(t):
            for half in range(2):
                ps = psA[0:64, 512 * half:512 * (half + 1)]
                for ic in range(16):
                    t.matmul(ps,
                             wkr[:, :].rearrange("p (c u) -> p c u", c=16)[:, ic, 0:64],
                             t_fm[:, :].rearrange("p (c m) -> p c m", c=16)
                             [:, ic, 512 * half:512 * (half + 1)],
                             start=(ic == 0), stop=(ic == 15))
                ps2 = psB[0:64, 512 * half:512 * (half + 1)]
                for ic in range(16):
                    t.matmul(ps2,
                             wkr[:, :].rearrange("p (c u) -> p c u", c=16)[:, ic, 64:128],
                             t_fm[:, :].rearrange("p (c m) -> p c m", c=16)
                             [:, ic, 512 * half:512 * (half + 1)],
                             start=(ic == 0), stop=(ic == 15))
            _i(t.drain(), sT)
        TE(kpmm)
        mKP = sT.v

        def kprope(v, mKP=mKP):
            v.wait_ge(sT.h, mKP)
            v.tensor_tensor(kpe0[:, :], psA[0:64, :], ropeC[0:64, :], ALU.mult)
            v.tensor_tensor(kpet[:, :], psB[0:64, :], ropeS[0:64, :], ALU.mult)
            v.drain()
            _i(v.tensor_tensor(kper[:, :], kpe0[:, :], kpet[:, :], ALU.add), sV)
        VE(kprope)
        mKPR = sV.v

        def kpsq(sc, mKPR=mKPR):
            sc.wait_ge(sV.h, mKPR)
            _i(sc.activation(kpesq[:, :], kper[:, :], ACT.Square), sS)
        SE(kpsq)
        mKPSQ = sS.v

        def kpsum(t, mKPSQ=mKPSQ, wv=mKN2V[0]):
            t.wait_ge(sS.h, mKPSQ)
            t.wait_ge(sV.h, wv)
            t.matmul(psM[0:1, 0:512], ones[0:64, 0:1], kpesq[:, 0:512],
                     start=True, stop=True)
            t.matmul(psO[0:1, 0:512], ones[0:64, 0:1], kpesq[:, 512:1024],
                     start=True, stop=True)
            _i(t.drain(), sT)
        TE(kpsum)
        mKPS = sT.v

        def kpred(v, mKPS=mKPS):
            v.wait_ge(sT.h, mKPS)
            v.reduce_max(kn2row[0:1, 16:17], psM[0:1, 0:512], axis=AX)
            v.reduce_max(lrs[0:1, 0:1], psO[0:1, 0:512], axis=AX)
            v.drain()
            v.tensor_tensor(kn2row[0:1, 16:17], kn2row[0:1, 16:17], lrs[0:1, 0:1], ALU.max)
            v.drain()
            _i(v.tensor_copy(kn2rb[:, :], kn2row[:, :]), sV)
        VE(kpred)
        mKN2B = sV.v

        def kpout(sy, mKPR=mKPR, mKN2B=mKN2B):
            sy.wait_ge(sV.h, mKPR)
            tShard.append(dqQ.issue(
                sy,
                out=agkv_in.ap()[KPE_OFF:KPE_OFF + ROPE * T].rearrange("(p m) -> p m", p=64),
                in_=kper[:, :]))
            sy.wait_ge(sV.h, mKN2B)
            tShard.append(dqQ.issue(
                sy,
                out=agkv_in.ap()[KN2_OFF:KN2_OFF + 32].rearrange("(o x) -> o x", o=1),
                in_=kn2rb[:, :]))
        SY(kpout)

        p2a.close()
        wdq = p2.enter_context(nc.sbuf_tensor("wdq", [128, 16 * QL], BF16))

        def wdql(g, mKP=mKP):
            g.wait_ge(sT.h, mKP)
            tW.append(dqG.issue(g, out=wdq[:, :].rearrange("p (c j) -> p c j", c=16),
                                in_=w_dq_d.ap().rearrange("(c p) j -> p c j", p=128)))
        GE(wdql)

        def agkv(g):
            for tk in tShard:
                DQ.wait(g, tk)
            _i(g.collective_compute(
                "AllGather", ALU.bypass,
                replica_groups=[list(range(NCORES))],
                ins=[agkv_in.ap().opt()],
                outs=[agkv_out.ap().opt()]), sCC)
        GE(agkv)
        mAG = sCC.v

        mAQ = proj_rms(wdq, 6, gq_sb, aqn)

        if stage == 2:
            def s2(sy):
                sy.wait_ge(sCC.h, mAG)
                tk = dqQ.issue(sy, out=dbg.ap(), in_=agkv_in.ap())
                DQ.wait(sy, tk)
            SY(s2)

            def s2g(g, mAQ=mAQ):
                g.wait_ge(sV.h, mAQ)
                tk = dqG.issue(g, out=y_d[0:128, :], in_=aqn[:, 0:D])
                DQ.wait(g, tk)
            GE(s2g)
            p2.close()
            return nc

        p2.close()

        at = ExitStack()
        K_sb = at.enter_context(nc.sbuf_tensor("K_sb", [128, 2 * S], BF16))
        V_sb = at.enter_context(nc.sbuf_tensor("V_sb", [128, 2 * S], BF16))
        kpeA = at.enter_context(nc.sbuf_tensor("kpeA", [64, S], BF16))
        A_sb = at.enter_context(nc.sbuf_tensor("A_sb", [128, 2 * 1024], BF16))
        AT_sb = at.enter_context(nc.sbuf_tensor("AT_sb", [128, 2 * 512], BF16))
        Qn = at.enter_context(nc.sbuf_tensor("Qn", [128, 2 * T], BF16))
        Qp = at.enter_context(nc.sbuf_tensor("Qp", [64, 2 * T], BF16))
        peF = at.enter_context(nc.sbuf_tensor("peF", [64, 512], F32))
        peG = at.enter_context(nc.sbuf_tensor("peG", [64, 512], F32))
        Q2n = at.enter_context(nc.sbuf_tensor("Q2n", [128, T], BF16))
        Q2p = at.enter_context(nc.sbuf_tensor("Q2p", [64, T], BF16))
        wuqh = at.enter_context(nc.sbuf_tensor("wuqh", [128, 2 * 6 * 256], BF16))
        negB = at.enter_context(nc.sbuf_tensor("negB", [128, NT], F32))
        sigc = at.enter_context(nc.sbuf_tensor("sigc", [128, 4 * NT], F32))
        sigS = at.enter_context(nc.sbuf_tensor("sigS", [128, 1], F32))
        rSg = at.enter_context(nc.sbuf_tensor("rSg", [128, 1], F32))
        O_sb = at.enter_context(nc.sbuf_tensor("O_sb", [128, 128], BF16))
        m2krow = at.enter_context(nc.sbuf_tensor("m2krow", [1, 32], F32))
        m2ksc = at.enter_context(nc.sbuf_tensor("m2ksc", [128, 16], F32))

        def m2kload(g):
            g.wait_ge(sCC.h, mAG)
            g.wait_ge(sV.h, mAQ)
            tk = dqG.issue(
                g, out=kn2x[:, :],
                in_=agkv_out.ap()[0:NCORES * KVSH].rearrange("(r x) -> r x", x=KVSH)
                [:, KN2_OFF:KN2_OFF + 32].rearrange("r p -> p r"),
                allow_slow_non_contiguous=True)
            DQ.wait(g, tk)
            _i(g.drain(), sP)
        GE(m2kload)
        mK2X = sP.v

        def m2kred(v):
            v.wait_ge(sP.h, mK2X)
            _i(v.reduce_max(m2k16[:, :], kn2x[:, :], axis=AX), sV)
        VE(m2kred)
        mM2K0 = sV.v

        def m2ktr(t):
            t.wait_ge(sV.h, mM2K0)
            t.transpose(psM[0:1, 448:480], m2k16[0:32, 0:1], id32f[:, :])
            _i(t.drain(), sT)
        TE(m2ktr)
        mM2KT = sT.v

        def m2krow_f(v):
            v.wait_ge(sT.h, mM2KT)
            v.tensor_copy(m2krow[:, :], psM[0:1, 448:480])
            v.drain()
            _i(v.tensor_scalar(m2krow[0:1, 0:16], m2krow[0:1, 0:16],
                               m2krow[0:1, 16:17], SCALE * SCALE,
                               ALU.add, ALU.mult), sV)
        VE(m2krow_f)
        mM2KR = sV.v

        def m2kpb2(g):
            g.wait_ge(sV.h, mM2KR)
            _i(g.partition_broadcast(m2ksc[:, :], m2krow[0:1, 0:16]), sP)
        GE(m2kpb2)
        mM2KS = sP.v

        tKPE = []

        def kpeload(sy, mAQ=mAQ):
            sy.wait_ge(sCC.h, mAG)
            sy.wait_ge(sV.h, mAQ)
            for jj in range(NKC):
                tKPE.append(dqQ.issue(
                    sy,
                    out=kpeA[:, 1024 * jj:1024 * (jj + 1)].rearrange(
                        "p (r c) -> p r c", r=8),
                    in_=agkv_out.ap()[0:NCORES * KVSH]
                    .rearrange("(r x) -> r x", x=KVSH)
                    [:, KPE_OFF:KPE_OFF + ROPE * T].rearrange("r (p m) -> p r m", p=ROPE)
                    [:, :, 128 * jj:128 * (jj + 1)]))
        SY(kpeload)

        mHead = [None] * H
        mQMM = [None] * H
        tWQ = [None] * H
        tKV = [[None] * NKC for _ in range(H)]
        mExpL = [0] * H
        ccg = [0]
        mExp = {}
        mTrB = {}
        mATe = {}
        mOe = [0] * (NT + 2)
        mSgP = [0, 0]
        mSC2P = [0]

        for h in range(H):
            par = h % 2

            def wq(g, h=h, par=par):
                if h < 2:
                    g.wait_ge(sCC.h, mAG)
                    g.wait_ge(sV.h, mAQ)
                else:
                    g.wait_ge(sT.h, mQMM[h - 2])
                toks = []
                base = par * 1536
                w = wuqh[:, :].rearrange("p (q l u) -> p q l u", q=2, l=6)
                toks.append(dqG.issue(
                    g, out=w[:, par, :, 0:128],
                    in_=w_uq_d.ap().rearrange("(l p) j -> p l j", p=128)
                    [:, :, QKD * h:QKD * h + 128]))
                toks.append(dqG.issue(
                    g, out=w[:, par, :, 128:192],
                    in_=w_uq_d.ap().rearrange("(l p) j -> p l j", p=128)
                    [:, :, QKD * h + 128:QKD * h + 192]))
                toks.append(dqG.issue(
                    g, out=w[:, par, :, 192:224],
                    in_=w_uq_d.ap().rearrange("(l p) j -> p l j", p=128)
                    [:, :, QKD * h + 160:QKD * h + 192]))
                toks.append(dqG.issue(
                    g, out=w[:, par, :, 224:256],
                    in_=w_uq_d.ap().rearrange("(l p) j -> p l j", p=128)
                    [:, :, QKD * h + 128:QKD * h + 160]))
                tWQ[h] = toks
            GE(wq)

            def kv(sy, h=h, par=par, mAQ=mAQ):
                if h < 2:
                    sy.wait_ge(sCC.h, mAG)
                    sy.wait_ge(sV.h, mAQ)
                else:
                    sy.wait_ge(sT.h, mHead[h - 2])
                kview = agkv_out.ap()[0:NCORES * KVSH] \
                    .rearrange("(r x) -> r x", x=KVSH) \
                    [:, KN_OFF + h * 128 * T:KN_OFF + (h + 1) * 128 * T] \
                    .rearrange("r (p m) -> p r m", p=128)
                vview = agkv_out.ap()[0:NCORES * KVSH] \
                    .rearrange("(r x) -> r x", x=KVSH) \
                    [:, V_OFF + h * T * VD:V_OFF + (h + 1) * T * VD] \
                    .rearrange("r (b p v) -> p b r v", b=NT, p=128)
                for jj in range(NKC):
                    t1 = dqQ.issue(
                        sy,
                        out=K_sb[:, par * S + 1024 * jj:par * S + 1024 * (jj + 1)]
                        .rearrange("p (r c) -> p r c", r=8),
                        in_=kview[:, :, 128 * jj:128 * (jj + 1)])
                    t2 = dqQ.issue(
                        sy,
                        out=V_sb[:, par * S + 1024 * jj:par * S + 1024 * (jj + 1)]
                        .rearrange("p (r v) -> p r v", r=8),
                        in_=vview[:, jj, :, :])
                    tKV[h][jj] = (t1, t2)
            SY(kv)

            def qmm(t, h=h, par=par):
                for tk in tWQ[h]:
                    DQ.wait(t, tk)
                if h == 0:
                    t.wait_ge(sV.h, mAQ)
                else:
                    t.wait_ge(sS.h, mExpL[h - 1])
                    t.wait_ge(sV.h, mOe[7])
                w = wuqh[:, :].rearrange("p (q l u) -> p q l u", q=2, l=6)
                for half in range(2):
                    for l in range(6):
                        t.matmul(psA[:, 512 * half:512 * (half + 1)],
                                 w[:, par, l, 0:128],
                                 aqn[:, :].rearrange("p (c m) -> p c m", c=6)
                                 [:, l, 512 * half:512 * (half + 1)],
                                 start=(l == 0), stop=(l == 5))
                for half in range(2):
                    for l in range(6):
                        t.matmul(psB[0:64, 512 * half:512 * (half + 1)],
                                 w[:, par, l, 128:192],
                                 aqn[:, :].rearrange("p (c m) -> p c m", c=6)
                                 [:, l, 512 * half:512 * (half + 1)],
                                 start=(l == 0), stop=(l == 5))
                for half in range(2):
                    for l in range(6):
                        t.matmul(psO[0:64, 0:512] if half == 0 else psM[0:64, 0:512],
                                 w[:, par, l, 192:256],
                                 aqn[:, :].rearrange("p (c m) -> p c m", c=6)
                                 [:, l, 512 * half:512 * (half + 1)],
                                 start=(l == 0), stop=(l == 5))
                _i(t.drain(), sT)
            TE(qmm)
            mQMM[h] = sT.v

            def qev(v, h=h, par=par):
                v.wait_ge(sT.h, mQMM[h])
                v.tensor_copy(Qn[:, par * T:(par + 1) * T], psA[:, :])
                for half in range(2):
                    pesw = psO[0:64, 0:512] if half == 0 else psM[0:64, 0:512]
                    v.tensor_tensor(peF[:, :], psB[0:64, 512 * half:512 * (half + 1)],
                                    ropeC[0:64, 512 * half:512 * (half + 1)], ALU.mult)
                    v.tensor_tensor(peG[:, :], pesw,
                                    ropeS[0:64, 512 * half:512 * (half + 1)], ALU.mult)
                    v.drain()
                    v.tensor_tensor(Qp[:, par * T + 512 * half:par * T + 512 * (half + 1)],
                                    peF[:, :], peG[:, :], ALU.add)
                    v.drain()
                _i(v.drain(), sV)
            VE(qev)
            mQEV = sV.v

            def qsq(sc, h=h, par=par, mQEV=mQEV):
                sc.wait_ge(sV.h, mQEV)
                sc.activation(Q2n[:, :], Qn[:, par * T:(par + 1) * T], ACT.Square)
                _i(sc.activation(Q2p[:, :], Qp[:, par * T:(par + 1) * T], ACT.Square), sS)
            SE(qsq)
            mQSQ = sS.v

            def qn2mm(t, h=h, mQSQ=mQSQ):
                t.wait_ge(sS.h, mQSQ)
                for j in range(NT):
                    t.matmul(psM[:, 256 + j:256 + j + 1], Q2n[:, 128 * j:128 * (j + 1)],
                             ones[:, 0:1], start=True, stop=False)
                    t.matmul(psM[:, 256 + j:256 + j + 1], Q2p[:, 128 * j:128 * (j + 1)],
                             ones[0:64, 0:1], start=False, stop=True)
                _i(t.drain(), sT)
            TE(qn2mm)
            mQN2 = sT.v

            def bsqrt(sc, h=h, mQN2=mQN2):
                sc.wait_ge(sP.h, mM2KS)
                sc.wait_ge(sT.h, mQN2)
                _i(sc.activation(negB[:, :], psM[:, 256:256 + NT], ACT.Sqrt,
                                 scale=m2ksc[:, h:h + 1]), sS)
            SE(bsqrt)
            mBSQ = sS.v

            def bneg(v, mBSQ=mBSQ):
                v.wait_ge(sS.h, mBSQ)
                _i(v.tensor_scalar(negB[:, :], negB[:, :], -1.0, None, ALU.mult), sV)
            VE(bneg)
            mNEGB = sV.v

            mQnEv = mQEV
            cch = 0
            pend = [None]

            def s_seg(h, j, jj, cc, cch, par, psX):
                def smm(t, h=h, j=j, jj=jj, cc=cc, cch=cch, par=par, psX=psX,
                        mQnEv=mQnEv, wexp=(mExp.get(cc - 2))):
                    DQ.wait(t, tKV[h][jj][0])
                    DQ.wait(t, tKV[h][jj][1])
                    DQ.wait(t, tKPE[jj])
                    if wexp:
                        t.wait_ge(sS.h, wexp[1])
                    if cch == 0:
                        t.wait_ge(sV.h, mQnEv)
                    for half in range(2):
                        t.matmul(psX[:, 512 * half:512 * (half + 1)],
                                 Qn[:, par * T + 128 * j:par * T + 128 * (j + 1)],
                                 K_sb[:, par * S + 1024 * jj + 512 * half:
                                      par * S + 1024 * jj + 512 * (half + 1)],
                                 start=True, stop=False)
                        _i(t.matmul(psX[:, 512 * half:512 * (half + 1)],
                                    Qp[:, par * T + 128 * j:par * T + 128 * (j + 1)],
                                    kpeA[:, 1024 * jj + 512 * half:
                                         1024 * jj + 512 * (half + 1)],
                                    start=False, stop=True), sT)
                TE(smm)
                return (sT.v - 1, sT.v)

            def mid_seg(h, j, jj, cc, jpar, apar, psX, mS, diag):
                if diag:
                    def dmk(v, mS=mS, psX=psX):
                        for half in range(2):
                            v.wait_ge(sT.h, mS[half])
                            _i(v.tensor_tensor(psX[:, 512 * half:512 * (half + 1)],
                                               psX[:, 512 * half:512 * (half + 1)],
                                               dmask[:, 512 * half:512 * (half + 1)],
                                               ALU.add), sV)
                    VE(dmk)
                    mMsk = (sV.v - 1, sV.v)
                else:
                    mMsk = None

                def expf(sc, cc=cc, j=j, jj=jj, jpar=jpar, apar=apar, psX=psX,
                         mS=mS, mMsk=mMsk, mNEGB=mNEGB, mSgPrev=mSgP[jpar],
                         wtr=(mTrB.get(cc - 2, 0))):
                    for half in range(2):
                        if mMsk is not None:
                            sc.wait_ge(sV.h, mMsk[half])
                        else:
                            sc.wait_ge(sT.h, mS[half])
                        if half == 0:
                            sc.wait_ge(sV.h, mNEGB)
                            if wtr:
                                sc.wait_ge(sT.h, wtr)
                            if jj == 0 and mSgPrev:
                                sc.wait_ge(sV.h, mSgPrev)
                        _i(sc.activation(
                            A_sb[:, apar * 1024 + 512 * half:
                                 apar * 1024 + 512 * (half + 1)],
                            psX[:, 512 * half:512 * (half + 1)],
                            ACT.Exp, scale=SCALE, bias=negB[:, j:j + 1],
                            accum_out=sigc[:, jpar * 2 * NT + 2 * jj + half:
                                           jpar * 2 * NT + 2 * jj + half + 1]), sS)
                SE(expf)
                mExp[cc] = (sS.v - 1, sS.v)
                return (sS.v - 1, sS.v)

            def tail_seg(h, j, jj, cc, par, apar, mE, last_of_j):
                for b in range(2):
                    bb = 2 * cc + b
                    tpar = bb % 2
                    ptX = ptA if tpar == 0 else ptB

                    def trb(t, cc=cc, b=b, bb=bb, apar=apar, ptX=ptX, mE=mE,
                            wev=(mATe.get(bb - 2, 0))):
                        t.wait_ge(sS.h, mE[b])
                        if wev:
                            t.wait_ge(sV.h, wev)
                        for z in range(4):
                            ins = t.transpose(ptX[:, 128 * z:128 * (z + 1)],
                                              A_sb[:, apar * 1024 + 512 * b + 128 * z:
                                                   apar * 1024 + 512 * b + 128 * (z + 1)],
                                              ident[:, :])
                            if z == 3:
                                _i(ins, sT)
                    TE(trb)
                    mTrB[cc] = sT.v
                    mTr = sT.v

                    def ate(v, bb=bb, mTr=mTr, tpar=tpar, ptX=ptX):
                        v.wait_ge(sT.h, mTr)
                        _i(v.tensor_copy(AT_sb[:, tpar * 512:(tpar + 1) * 512],
                                         ptX[:, 0:512]), sV)
                    VE(ate)
                    mATe[bb] = sV.v

                    def avm(t, h=h, j=j, jj=jj, b=b, bb=bb, cc=cc, par=par,
                            tpar=tpar, woe=(mOe[j - 2] if (jj == 0 and b == 0 and j >= 2) else 0)):
                        t.wait_ge(sV.h, mATe[bb])
                        if woe:
                            t.wait_ge(sV.h, woe)
                        for z in range(4):
                            kt = 8 * jj + 4 * b + z
                            ins = t.matmul(psO[:, 128 * (j % 2):128 * (j % 2 + 1)],
                                           AT_sb[:, tpar * 512 + 128 * z:
                                                 tpar * 512 + 128 * (z + 1)],
                                           V_sb[:, par * S + 128 * kt:
                                                par * S + 128 * (kt + 1)],
                                           start=(jj == 0 and b == 0 and z == 0),
                                           stop=(jj == j and b == 1 and z == 3))
                            if z == 3:
                                _i(ins, sT)
                    TE(avm)
                if not last_of_j:
                    return
                mAVj = sT.v
                jpar = j % 2

                def sgm(v, j=j, jpar=jpar, mE=mE):
                    v.wait_ge(sS.h, mE[1])
                    _i(v.reduce_sum(sigS[:, :],
                                    sigc[:, jpar * 2 * NT:jpar * 2 * NT + 2 * (j + 1)],
                                    axis=AX), sV)
                VE(sgm)
                mSg = sV.v
                mSgP[jpar] = sV.v

                def sln(sc, mSg=mSg):
                    sc.wait_ge(sV.h, mSg)
                    sc.activation(rSg[:, :], sigS[:, :], ACT.Ln)
                    sc.drain()
                    _i(sc.activation(rSg[:, :], rSg[:, :], ACT.Exp, scale=-1.0), sS)
                SE(sln)
                mRS = sS.v

                def oev(v, j=j, mAVj=mAVj, mRS=mRS):
                    v.wait_ge(sT.h, mAVj)
                    v.wait_ge(sS.h, mRS)
                    _i(v.tensor_scalar(O_sb[:, :],
                                       psO[:, 128 * (j % 2):128 * (j % 2 + 1)],
                                       rSg[:, 0:1], None, ALU.mult), sV)
                VE(oev)
                mOev = sV.v

                def otr(t, mOev=mOev):
                    t.wait_ge(sV.h, mOev)
                    _i(t.transpose(ptA[:, 512:640], O_sb[:, :], ident[:, :]), sT)
                TE(otr)
                mOtr = sT.v

                def ofe(v, h=h, j=j, mOtr=mOtr):
                    v.wait_ge(sT.h, mOtr)
                    _i(v.tensor_copy(o_fm[:, T * h + 128 * j:T * h + 128 * (j + 1)],
                                     ptA[:, 512:640]), sV)
                VE(ofe)
                mOe[j] = sV.v

            for j in range(NT):
                jpar = j % 2
                for jj in range(j + 1):
                    cc = ccg[0]
                    spar = cc % 2
                    apar = cc % 2
                    psX = psA if spar == 0 else psB
                    diag = (jj == j)
                    mS = s_seg(h, j, jj, cc, cch, par, psX)
                    if pend[0] is not None:
                        pend[0]()
                    mE = mid_seg(h, j, jj, cc, jpar, apar, psX, mS, diag)
                    mExpL[h] = mE[1]
                    pend[0] = (lambda h=h, j=j, jj=jj, cc=cc, par=par, apar=apar,
                               mE=mE, lj=(jj == j):
                               tail_seg(h, j, jj, cc, par, apar, mE, lj))
                    ccg[0] += 1
                    cch += 1
            pend[0]()
            pend[0] = None
            mHead[h] = sT.v

        mOFM = sV.v

        if stage == 3:
            def s3(g):
                g.wait_ge(sV.h, mOFM)
                tk = dqG.issue(g, out=dbg.ap().rearrange("(p m) -> p m", p=128)
                               if False else dbg[:, :], in_=o_fm[:, :])
                DQ.wait(g, tk)
            GE(s3)
            at.close()
            return nc

        at.close()
        mF4 = fence()

        sel_ctx = ExitStack()
        sc_tm = sel_ctx.enter_context(nc.sbuf_tensor("sc_tm", [128, NT], F32))
        sig_tm = sel_ctx.enter_context(nc.sbuf_tensor("sig_tm", [128, NT], F32))
        tau_t = sel_ctx.enter_context(nc.sbuf_tensor("tau_t", [1, 2], F32))
        tau_b = sel_ctx.enter_context(nc.sbuf_tensor("tau_b", [128, 1], F32))
        cnt_t = sel_ctx.enter_context(nc.sbuf_tensor("cnt_t", [1, 1], F32))
        sel16 = sel_ctx.enter_context(nc.sbuf_tensor("sel16", [16, CAP], I16))
        self32 = sel_ctx.enter_context(nc.sbuf_tensor("self32", [1, CAP], F32))
        scatf = sel_ctx.enter_context(nc.sbuf_tensor("scatf", [1, CAP], F32))
        cge = sel_ctx.enter_context(nc.sbuf_tensor("cge", [1, CAP], F32))
        scat16 = sel_ctx.enter_context(nc.sbuf_tensor("scat16", [1, CAP], I16))
        scat_b = sel_ctx.enter_context(nc.sbuf_tensor("scat_b", [128, CAP], I16))
        gidx = sel_ctx.enter_context(nc.sbuf_tensor("gidx", [128, CAP // 16], I16))
        dfm = sel_ctx.enter_context(nc.sbuf_tensor("dfm", [128, 16 * 1024], BF16))
        p4 = ExitStack()
        wo_sb = p4.enter_context(nc.sbuf_tensor("wo_sb", [128, 2 * D], BF16))
        rw_row = p4.enter_context(nc.sbuf_tensor("rw_row", [1, D], BF16))
        rwb = p4.enter_context(nc.sbuf_tensor("rwb", [128, D], BF16))
        ao_fm = p4.enter_context(nc.sbuf_tensor("ao_fm", [128, T], BF16))
        ao_tm = p4.enter_context(nc.sbuf_tensor("ao_tm", [128, NT * D], BF16))
        x0r = [p4.enter_context(nc.sbuf_tensor(f"x0r{i}", [128, D], F32)) for i in range(2)]
        x1t = [p4.enter_context(nc.sbuf_tensor(f"x1t{i}", [128, D], F32)) for i in range(2)]
        ht = [p4.enter_context(nc.sbuf_tensor(f"ht{i}", [128, D], BF16)) for i in range(2)]
        st6b = p4.enter_context(nc.sbuf_tensor("st6b", [128, 24], F32))
        st2b = p4.enter_context(nc.sbuf_tensor("st2b", [128, 2], F32))
        lrs2 = p4.enter_context(nc.sbuf_tensor("lrs2", [128, 1], F32))
        lnm2 = p4.enter_context(nc.sbuf_tensor("lnm2", [128, 1], F32))

        def rbpb(g):
            g.wait_ge(sP.h, mF4)
            DQ.wait(g, tC0[10])
            _i(g.partition_broadcast(rbb[:, :], rb_sb[0:1, 0:1]), sP)
            tok = dqG.issue(g, out=rw_row[:, :],
                            in_=rw_d.ap().rearrange("(o x) -> o x", o=1))
            DQ.wait(g, tok)
            _i(g.partition_broadcast(rwb[:, :], rw_row[0:1, :]), sP)
        GE(rbpb)
        mRBB = sP.v

        tWOC = [None] * 16
        mWOev = [0] * 16
        mAOtr = [0] * 16
        for oc in range(16):
            def wod(g, oc=oc):
                if oc < 2:
                    g.wait_ge(sP.h, mF4)
                else:
                    g.wait_ge(sT.h, mWOmm[oc - 2])
                tWOC[oc] = dqG.issue(
                    g, out=wo_sb[:, (oc % 2) * D:(oc % 2 + 1) * D]
                    .rearrange("p (c j) -> p c j", c=16),
                    in_=w_o_d.ap().rearrange("(c p) j -> p c j", p=128)
                    [:, :, 128 * oc:128 * (oc + 1)])
            GE(wod)

            def womm(t, oc=oc):
                DQ.wait(t, tWOC[oc])
                if oc == 0:
                    t.wait_ge(sV.h, mOFM)
                if oc >= 2:
                    t.wait_ge(sV.h, mWOev[oc - 2])
                psX = psA if oc % 2 == 0 else psB
                for half in range(2):
                    for ic in range(16):
                        t.matmul(psX[:, 512 * half:512 * (half + 1)],
                                 wo_sb[:, (oc % 2) * D:(oc % 2 + 1) * D]
                                 .rearrange("p (c j) -> p c j", c=16)[:, ic, :],
                                 o_fm[:, :].rearrange("p (c m) -> p c m", c=16)
                                 [:, ic, 512 * half:512 * (half + 1)],
                                 start=(ic == 0), stop=(ic == 15))
                _i(t.drain(), sT)
            TE(womm)
            if oc == 0:
                mWOmm = [0] * 16
            mWOmm[oc] = sT.v

            def woev(v, oc=oc, m=mWOmm[oc]):
                v.wait_ge(sT.h, m)
                if oc >= 2:
                    v.wait_ge(sT.h, mAOtr[oc - 2])
                psX = psA if oc % 2 == 0 else psB
                _i(v.tensor_copy(ao_fm[:, :], psX[:, :]), sV)
            VE(woev)
            mWOev[oc] = sV.v

            def aotr(t, oc=oc, m=mWOev[oc]):
                t.wait_ge(sV.h, m)
                for half in range(2):
                    ptX = ptA if half == 0 else ptB
                    for z in range(4):
                        j = 4 * half + z
                        t.transpose(ptX[:, 128 * z:128 * (z + 1)],
                                    ao_fm[:, 128 * j:128 * (j + 1)], ident[:, :])
                    _i(t.drain(), sT)
            TE(aotr)
            mAOtr[oc] = sT.v
            mtr2 = (sT.v - 1, sT.v)

            def aoev(v, oc=oc, mtr2=mtr2):
                for half in range(2):
                    v.wait_ge(sT.h, mtr2[half])
                    _i(v.tensor_copy(
                        ao_tm[:, :].rearrange("p (j m) -> p j m", j=NT)
                        [:, 4 * half:4 * half + 4, 128 * oc:128 * (oc + 1)],
                        (ptA if half == 0 else ptB)[:, 0:512]
                        .rearrange("p (j m) -> p j m", j=4)), sV)
            VE(aoev)
        mAOTM = sV.v
        ofm_ctx.close()
        sel_tr = ExitStack()
        sc_all = sel_tr.enter_context(nc.sbuf_tensor("sc_all", [128, 64], F32, side="right"))
        kthtmp = sel_tr.enter_context(nc.sbuf_tensor("kthtmp", [128, 64], F32, side="right"))
        selmask = sel_tr.enter_context(nc.sbuf_tensor("selmask", [1, T], F32, side="right"))
        sc_own = sel_tr.enter_context(nc.sbuf_tensor("sc_own", [1, T], F32, side="right"))
        pref_a = sel_tr.enter_context(nc.sbuf_tensor("pref_a", [1, T], F32, side="right"))
        pref_b = sel_tr.enter_context(nc.sbuf_tensor("pref_b", [1, T], F32, side="right"))
        ranks = sel_tr.enter_context(nc.sbuf_tensor("ranks", [1, T], F32, side="right"))
        r16b = sel_tr.enter_context(nc.sbuf_tensor("r16b", [16, T], I16, side="right"))
        bs_lo = sel_tr.enter_context(nc.sbuf_tensor("bs_lo", [128, 1], F32, side="right"))
        bs_hi = sel_tr.enter_context(nc.sbuf_tensor("bs_hi", [128, 1], F32, side="right"))
        bs_mid = sel_tr.enter_context(nc.sbuf_tensor("bs_mid", [128, 1], F32, side="right"))
        bs_fl = sel_tr.enter_context(nc.sbuf_tensor("bs_fl", [128, 1], F32, side="right"))
        bs_pc = sel_tr.enter_context(nc.sbuf_tensor("bs_pc", [128, 1], F32, side="right"))

        tX1S = [None] * NT
        tHS = [None] * NT
        mX1f = [0, 0]
        mHtf = [0, 0]
        for j in range(NT):
            par = j % 2
            tx0 = [None]

            def ldx0(sy, j=j, par=par, w=mX1f[par]):
                if w:
                    sy.wait_ge(sV.h, w)
                else:
                    sy.wait_ge(sP.h, mF4)
                tx0[0] = dqQ.issue(sy, out=x0r[par][:, :],
                                   in_=x0_d[128 * j:128 * (j + 1), :])
            SY(ldx0)
            tx0 = tx0[0]

            def ln2a(v, j=j, mAOTM=mAOTM):
                v.wait_ge(sV.h, mAOTM)
                tl = ao_tm[:, :].rearrange("p (j m) -> p j m", j=NT)[:, j, :]
                for q in range(4):
                    v.bn_stats(st6b[:, 6 * q:6 * q + 6], tl[:, 512 * q:512 * (q + 1)])
                v.drain()
                v.bn_aggr(st2b[:, :], st6b[:, :])
                v.drain()
                v.tensor_scalar(lrs2[:, :], st2b[:, 1:2], EPS_LN, None, ALU.add)
                v.drain()
                _i(v.reciprocal(lrs2[:, :], lrs2[:, :]), sV)
            VE(ln2a)
            m1 = sV.v

            def ln2b(sc, m1=m1):
                sc.wait_ge(sV.h, m1)
                _i(sc.activation(lrs2[:, :], lrs2[:, :], ACT.Sqrt), sS)
            SE(ln2b)
            m2 = sS.v

            def ln2c(v, j=j, par=par, m2=m2, tx0=tx0):
                v.wait_ge(sS.h, m2)
                v.tensor_mul(lnm2[:, :], st2b[:, 0:1], lrs2[:, :])
                v.drain()
                DQ.wait(v, tx0)
                tl = ao_tm[:, :].rearrange("p (j m) -> p j m", j=NT)[:, j, :]
                v.scalar_tensor_tensor(x1t[par][:, :], tl, lrs2[:, 0:1], x0r[par][:, :],
                                       ALU.mult, ALU.add)
                v.drain()
                _i(v.tensor_scalar(x1t[par][:, :], x1t[par][:, :], lnm2[:, 0:1],
                                   None, ALU.subtract), sV)
            VE(ln2c)
            mX1 = sV.v
            mX1f[par] = mX1

            def spx1(sy, j=j, par=par, mX1=mX1):
                sy.wait_ge(sV.h, mX1)
                tX1S[j] = dqQ.issue(sy, out=x1_dram[128 * j:128 * (j + 1), :],
                                    in_=x1t[par][:, :])
            SY(spx1)

            def ln3a(v, j=j, par=par, mX1=mX1):
                v.drain()
                for q in range(4):
                    v.bn_stats(st6b[:, 6 * q:6 * q + 6],
                               x1t[par][:, 512 * q:512 * (q + 1)])
                v.drain()
                v.bn_aggr(st2b[:, :], st6b[:, :])
                v.drain()
                v.tensor_scalar(lrs2[:, :], st2b[:, 1:2], EPS_LN, None, ALU.add)
                v.drain()
                _i(v.reciprocal(lrs2[:, :], lrs2[:, :]), sV)
            VE(ln3a)
            m3 = sV.v

            def ln3b(sc, m3=m3):
                sc.wait_ge(sV.h, m3)
                _i(sc.activation(lrs2[:, :], lrs2[:, :], ACT.Sqrt), sS)
            SE(ln3b)
            m4 = sS.v

            def ln3c(v, m4=m4):
                v.wait_ge(sS.h, m4)
                v.tensor_mul(lnm2[:, :], st2b[:, 0:1], lrs2[:, :])
                v.drain()
                _i(v.tensor_scalar(lnm2[:, :], lnm2[:, :], -1.0, None, ALU.mult), sV)
            VE(ln3c)
            m5 = sV.v

            def ln3d(sc, j=j, par=par, m5=m5, whf=mHtf[par],
                     tprev=(tHS[j - 2] if j >= 2 else None)):
                sc.wait_ge(sV.h, m5)
                if whf:
                    sc.wait_ge(sV.h, whf)
                if tprev is not None:
                    DQ.wait(sc, tprev)
                _i(sc.activation(ht[par][:, :], x1t[par][:, :], ACT.Identity,
                                 bias=lnm2[:, 0:1], scale=lrs2[:, 0:1]), sS)
            SE(ln3d)
            m6 = sS.v

            def sth(sy, j=j, par=par, m6=m6):
                sy.wait_ge(sS.h, m6)
                tHS[j] = dqQ.issue(sy, out=h_dram[128 * j:128 * (j + 1), :],
                                   in_=ht[par][:, :])
            SY(sth)

            def scj(v, j=j, par=par, m6=m6):
                v.wait_ge(sS.h, m6)
                DQ.wait(v, tX1S[j])
                v.tensor_tensor(x1t[par][:, :], ht[par][:, :], rwb[:, :], ALU.mult)
                v.drain()
                _i(v.reduce_sum(sc_tm[:, j:j + 1], x1t[par][:, :], axis=AX), sV)
            VE(scj)
            mHtf[par] = sV.v
        mSCT = sV.v

        def scf(v, mSCT=mSCT):
            v.wait_ge(sV.h, mSCT)
            v.wait_ge(sP.h, mRBB)
            _i(v.tensor_scalar(sc_tm[:, :], sc_tm[:, :], rbb[:, 0:1], None, ALU.add),
               sV)
        VE(scf)
        mSCF = sV.v

        def sgf(sc, mSCF=mSCF):
            sc.wait_ge(sV.h, mSCF)
            _i(sc.activation(sig_tm[:, :], sc_tm[:, :], ACT.Sigmoid), sS)
        SE(sgf)
        mSIG = sS.v

        if stage == 4:
            def s4(sy):
                for j in range(NT):
                    DQ.wait(sy, tX1S[j])
                tk = dqQ.issue(sy, out=dbg[:, :], in_=x1_dram[:, :])
                DQ.wait(sy, tk)
            SY(s4)
            p4.close()
            sel_tr.close()
            sel_ctx.close()
            return nc

        tSC = [None]

        def scout(sy, mSCF=mSCF):
            sy.wait_ge(sV.h, mSCF)
            tSC[0] = dqQ.issue(sy, out=agsc_in.ap().rearrange("(j p) -> p j", p=128),
                               in_=sc_tm[:, :], allow_slow_non_contiguous=True)
        SY(scout)

        def agsc(g):
            DQ.wait(g, tSC[0])
            _i(g.collective_compute(
                "AllGather", ALU.bypass,
                replica_groups=[list(range(NCORES))],
                ins=[agsc_in.ap().opt()],
                outs=[agsc_out.ap().opt()]), sCC)
        GE(agsc)
        mAGS = sCC.v

        tSCA = [None]
        tSCO = [None]

        def scload(sy):
            sy.wait_ge(sCC.h, mAGS)
            tSCA[0] = dqQ.issue(sy, out=sc_all[:, :],
                                in_=agsc_out.ap().rearrange("(p f) -> p f", p=128))
            tSCO[0] = dqQ.issue(sy, out=sc_own[:, :],
                                in_=agsc_in.ap().rearrange("(o x) -> o x", o=1))
        SY(scload)

        N_BS = 56

        def bsinit(v):
            DQ.wait(v, tSCA[0])
            _i(v.reduce_max(bs_lo[:, :], sc_all[:, :], axis=AX,
                            apply_absolute_value=True), sV)
        VE(bsinit)
        mBS0 = sV.v

        def bssum(t, mBS0=mBS0):
            t.wait_ge(sV.h, mBS0)
            _i(t.matmul(psM[:, 320:321], onesf[:, :], bs_lo[:, :],
                        start=True, stop=True), sT)
        TE(bssum)
        mBS1 = sT.v

        def bsrng(v, mBS1=mBS1):
            v.wait_ge(sT.h, mBS1)
            v.tensor_scalar(bs_hi[:, :], psM[:, 320:321], 1.0, None, ALU.add)
            v.tensor_scalar(bs_lo[:, :], psM[:, 320:321], -1.0, -1.0,
                            ALU.mult, ALU.add)
            v.drain()
            v.tensor_add(bs_mid[:, :], bs_lo[:, :], bs_hi[:, :])
            v.drain()
            v.tensor_scalar(bs_mid[:, :], bs_mid[:, :], 0.5, None, ALU.mult)
            v.drain()
            v.tensor_scalar(kthtmp[:, :], sc_all[:, :], bs_mid[:, 0:1], None,
                            ALU.is_ge)
            v.drain()
            _i(v.reduce_sum(bs_pc[:, :], kthtmp[:, :], axis=AX), sV)
        VE(bsrng)
        mBSit = sV.v

        for it in range(N_BS):
            def bcnt(t, m=mBSit):
                t.wait_ge(sV.h, m)
                _i(t.matmul(psM[:, 320:321], onesf[:, :], bs_pc[:, :],
                            start=True, stop=True), sT)
            TE(bcnt)
            mPE = sT.v

            def bupd(v, mPE=mPE, last=(it == N_BS - 1)):
                v.wait_ge(sT.h, mPE)
                v.tensor_scalar(bs_fl[:, :], psM[:, 320:321], float(KSEL), None,
                                ALU.is_ge)
                v.drain()
                v.tensor_sub(kthtmp[:, 0:1], bs_mid[:, :], bs_lo[:, :])
                v.tensor_sub(kthtmp[:, 1:2], bs_mid[:, :], bs_hi[:, :])
                v.tensor_scalar(kthtmp[:, 2:3], bs_fl[:, :], -1.0, 1.0,
                                ALU.mult, ALU.add)
                v.drain()
                v.scalar_tensor_tensor(bs_lo[:, :], kthtmp[:, 0:1], bs_fl[:, 0:1],
                                       bs_lo[:, :], ALU.mult, ALU.add)
                v.scalar_tensor_tensor(bs_hi[:, :], kthtmp[:, 1:2], kthtmp[:, 2:3],
                                       bs_hi[:, :], ALU.mult, ALU.add)
                v.drain()
                if not last:
                    v.tensor_add(bs_mid[:, :], bs_lo[:, :], bs_hi[:, :])
                    v.drain()
                    v.tensor_scalar(bs_mid[:, :], bs_mid[:, :], 0.5, None, ALU.mult)
                    v.drain()
                    v.tensor_scalar(kthtmp[:, :], sc_all[:, :], bs_mid[:, 0:1], None,
                                    ALU.is_ge)
                    v.drain()
                    _i(v.reduce_sum(bs_pc[:, :], kthtmp[:, :], axis=AX), sV)
                else:
                    v.tensor_copy(tau_t[0:1, 0:1], bs_lo[0:1, 0:1])
                    _i(v.tensor_copy(tau_t[0:1, 1:2], bs_hi[0:1, 0:1]), sV)
            VE(bupd)
            mBSit = sV.v
        mKth4 = mBSit

        def selv(v):
            v.wait_ge(sV.h, mKth4)
            v.drain()
            DQ.wait(v, tSCO[0])
            v.tensor_scalar(selmask[:, :], sc_own[:, :], tau_t[0:1, 0:1], None,
                            ALU.is_ge)
            v.drain()
            v.reduce_sum(cnt_t[:, :], selmask[:, :], axis=AX)
            v.tensor_copy(pref_a[:, :], selmask[:, :])
            v.drain()
            src_, dst_ = pref_a, pref_b
            for sh in [1, 2, 4, 8, 16, 32, 64, 128, 256, 512]:
                v.tensor_copy(dst_[:, 0:sh], src_[:, 0:sh])
                v.tensor_tensor(dst_[:, sh:T], src_[:, sh:T], src_[:, 0:T - sh],
                                ALU.add)
                v.drain()
                src_, dst_ = dst_, src_
            v.tensor_sub(ranks[:, :], src_[:, :], selmask[:, :])
            v.drain()
            v.tensor_scalar(pref_b[:, :], ranks[:, :], float(CAP), None, ALU.is_lt)
            v.drain()
            v.tensor_mul(pref_b[:, :], pref_b[:, :], selmask[:, :])
            v.tensor_scalar(pref_a[:, :], ranks[:, :], 1.0, None, ALU.add)
            v.drain()
            v.tensor_mul(pref_a[:, :], pref_a[:, :], pref_b[:, :])
            v.drain()
            v.tensor_scalar(pref_a[:, :], pref_a[:, :], 1.0, None, ALU.subtract)
            v.memset(r16b[:, :], -1)
            v.drain()
            _i(v.tensor_copy(r16b[0:1, :], pref_a[:, :]), sV)
        VE(selv)
        mRANKS = sV.v

        def comp(g):
            g.wait_ge(sV.h, mRANKS)
            g.load_library(libcfg.local_scatter)
            _i(g.local_scatter(sel16[:, :], tok16[:, :], r16b[:, :], channels=16,
                               num_elems=CAP, num_idxs=T), sP)
            g.wait_ge(sP.h, sP.v)
            t1 = dqG.issue(g, out=idx_dram[:, :], in_=sel16[0:1, :])
            DQ.wait(g, t1)
            g.memset(gidx[:, :], 0)
            g.drain()
            toks = []
            for b in range(8):
                toks.append(dqG.issue(
                    g, out=gidx[16 * b:16 * (b + 1), :],
                    in_=idx_dram.ap().rearrange("o (f p) -> (o p) f", p=16),
                    allow_slow_non_contiguous=True))
            for tk in toks:
                DQ.wait(g, tk)
            _i(g.drain(), sP)
        GE(comp)
        mGIDX = sP.v

        def selw(v):
            v.wait_ge(sP.h, mGIDX)
            v.tensor_copy(self32[:, :], sel16[0:1, :])
            v.drain()
            v.tensor_scalar(cge[:, :], capio[:, :], cnt_t[0:1, 0:1], None, ALU.is_ge)
            v.tensor_scalar(scatf[:, :], self32[:, :], 1.0, None, ALU.add)
            v.drain()
            v.tensor_mul(scatf[:, :], scatf[:, :], cge[:, :])
            v.drain()
            v.tensor_sub(scatf[:, :], self32[:, :], scatf[:, :])
            v.drain()
            _i(v.tensor_copy(scat16[:, :], scatf[:, :]), sV)
        VE(selw)
        mSCAT = sV.v

        if stage == 5:
            def s5(sy):
                sy.wait_ge(sV.h, mSCAT)
                t1 = dqQ.issue(sy, out=dbg[0:1, :], in_=sc_own[:, :])
                t2 = dqQ.issue(sy, out=dbg[1:2, :], in_=selmask[:, :])
                t3 = dqQ.issue(sy, out=dbg[2:3, :], in_=ranks[:, :])
                t4 = dqQ.issue(sy, out=dbg[3:4, 0:2], in_=tau_t[:, :])
                t5 = dqQ.issue(sy, out=dbg[3:4, 2:3], in_=cnt_t[:, :])
                t6 = dqQ.issue(sy, out=dbg[4:5, 0:CAP], in_=scatf[:, :])
                for tk in (t1, t2, t3, t4, t5, t6):
                    DQ.wait(sy, tk)
            SY(s5)
            p4.close()
            sel_tr.close()
            sel_ctx.close()
            return nc

        p4.close()
        sel_tr.close()
        mF5 = fence()

        f5 = ExitStack()
        hsel = f5.enter_context(nc.sbuf_tensor("hsel", [128, 16 * CAP], BF16))
        g_sb = f5.enter_context(nc.sbuf_tensor("g_sb", [128, 64 * CAP], BF16))
        w1c = [f5.enter_context(nc.sbuf_tensor(f"w1c{i}", [128, 16 * 512], BF16))
               for i in range(2)]
        w2c = [f5.enter_context(nc.sbuf_tensor(f"w2c{i}", [128, 64 * 128], BF16))
               for i in range(2)]
        wd2 = f5.enter_context(nc.sbuf_tensor("wd2", [128, 16 * CAP], BF16))

        def hgat(g):
            g.wait_ge(sP.h, mF5)
            g.load_library(libcfg.attnmlp)
            g.wait_ge(sP.h, mGIDX)
            for j in range(NT):
                DQ.wait(g, tHS[j])
            g.wait_ge(sS.h, mSIG)
            g.wait_ge(sV.h, mSCAT)
            s = dqG.sems[dqG.n % dqG.ns]
            if s.v > 0:
                g.wait_ge(s.h, s.v)
            g.dma_gather(hsel[:, :].rearrange("p (c i) -> p c i", c=16),
                         h_dram[:, :], gidx[:, :], num_idxs=CAP, num_idxs_reg=CAP,
                         elem_size=D, transpose=True).then_inc(s.h, 16)
            s.v += 16
            dqG.n += 1
            g.wait_ge(s.h, s.v)
            _i(g.drain(), sP)
        GE(hgat)
        mHSEL = sP.v

        mW1mm = [0] * 16
        mGelQ = [0] * 64
        tW1 = [None] * 16
        for dc in range(16):
            def w1l(g, dc=dc):
                if dc < 2:
                    g.wait_ge(sP.h, mF5)
                    g.wait_ge(sV.h, mSCAT)
                else:
                    g.wait_ge(sT.h, mW1mm[dc - 2])
                tW1[dc] = dqG.issue(
                    g, out=w1c[dc % 2][:, :].rearrange("p (c j) -> p c j", c=16),
                    in_=w1_d.ap().rearrange("(c p) j -> p c j", p=128)
                    [:, :, 512 * dc:512 * (dc + 1)])
            GE(w1l)
            for q in range(4):
                qq = 4 * dc + q

                def d1mm(t, dc=dc, q=q, qq=qq):
                    if q == 0:
                        DQ.wait(t, tW1[dc])
                    if qq == 0:
                        t.wait_ge(sP.h, mHSEL)
                    if qq >= 2:
                        t.wait_ge(sS.h, mGelQ[qq - 2])
                    psX = psA if qq % 2 == 0 else psB
                    for ic in range(16):
                        t.matmul(psX[:, 0:CAP],
                                 w1c[dc % 2][:, :].rearrange("p (c j) -> p c j", c=16)
                                 [:, ic, 128 * q:128 * (q + 1)],
                                 hsel[:, :].rearrange("p (c i) -> p c i", c=16)
                                 [:, ic, :],
                                 start=(ic == 0), stop=(ic == 15))
                    _i(t.drain(), sT)
                TE(d1mm)
                md1 = sT.v

                def gel(sc, qq=qq, md1=md1):
                    sc.wait_ge(sT.h, md1)
                    psX = psA if qq % 2 == 0 else psB
                    _i(sc.activation(
                        g_sb[:, :].rearrange("p (c i) -> p c i", c=64)[:, qq, :],
                        psX[:, 0:CAP], ACT.Gelu_apprx_tanh,
                        bias=b1_sb[:, qq:qq + 1]), sS)
                SE(gel)
                mGelQ[qq] = sS.v
            mW1mm[dc] = sT.v
        mGELL = sS.v

        mW2mm = [0] * 16
        mW2ev = [0] * 16
        tW2 = [None] * 16
        for oc in range(16):
            def w2l(g, oc=oc):
                if oc < 2:
                    g.wait_ge(sP.h, mF5)
                else:
                    g.wait_ge(sT.h, mW2mm[oc - 2])
                tW2[oc] = dqG.issue(
                    g, out=w2c[oc % 2][:, :].rearrange("p (c j) -> p c j", c=64),
                    in_=w2_d.ap().rearrange("(c p) j -> p c j", p=128)
                    [:, :, 128 * oc:128 * (oc + 1)])
            GE(w2l)

            def d2mm(t, oc=oc):
                DQ.wait(t, tW2[oc])
                t.wait_ge(sS.h, mGELL)
                if oc >= 2:
                    t.wait_ge(sV.h, mW2ev[oc - 2])
                psX = psA if oc % 2 == 0 else psB
                for d in range(64):
                    t.matmul(psX[:, 0:CAP],
                             w2c[oc % 2][:, :].rearrange("p (c j) -> p c j", c=64)
                             [:, d, :],
                             g_sb[:, :].rearrange("p (c i) -> p c i", c=64)[:, d, :],
                             start=(d == 0), stop=(d == 63))
                _i(t.drain(), sT)
            TE(d2mm)
            mW2mm[oc] = sT.v

            def d2ev(v, oc=oc, m=mW2mm[oc]):
                v.wait_ge(sT.h, m)
                psX = psA if oc % 2 == 0 else psB
                _i(v.tensor_scalar(
                    wd2[:, :].rearrange("p (c i) -> p c i", c=16)[:, oc, :],
                    psX[:, 0:CAP], b2_sb[:, oc:oc + 1], None, ALU.add), sV)
            VE(d2ev)
            mW2ev[oc] = sV.v
        mWD2 = sV.v

        def dsc(g):
            g.wait_ge(sV.h, mWD2)
            _i(g.partition_broadcast(scat_b[:, :], scat16[0:1, :]), sP)
            g.wait_ge(sP.h, sP.v)
            g.load_library(libcfg.local_scatter)
            for fc in range(16):
                _i(g.local_scatter(
                    dfm[:, :].rearrange("p (c m) -> p c m", c=16)[:, fc, :],
                    wd2[:, :].rearrange("p (c i) -> p c i", c=16)[:, fc, :],
                    scat_b[:, :], channels=128, num_elems=1024, num_idxs=CAP), sP)
            _i(g.drain(), sP)
        GE(dsc)
        mDFM = sP.v

        f5.close()
        fin = ExitStack()
        dtm = [fin.enter_context(nc.sbuf_tensor(f"dtm{i}", [128, D], BF16))
               for i in range(2)]
        x1r = [fin.enter_context(nc.sbuf_tensor(f"x1r{i}", [128, D], F32))
               for i in range(2)]
        hr = [fin.enter_context(nc.sbuf_tensor(f"hr{i}", [128, D], BF16))
              for i in range(2)]
        ft = fin.enter_context(nc.sbuf_tensor("ft", [128, D], F32))
        yt = [fin.enter_context(nc.sbuf_tensor(f"yt{i}", [128, D], F32))
              for i in range(2)]
        st6c = fin.enter_context(nc.sbuf_tensor("st6c", [128, 24], F32))
        st2c = fin.enter_context(nc.sbuf_tensor("st2c", [128, 2], F32))
        lrs3 = fin.enter_context(nc.sbuf_tensor("lrs3", [128, 1], F32))
        lnm3 = fin.enter_context(nc.sbuf_tensor("lnm3", [128, 1], F32))

        tY = [None] * NT
        mDT = [0, 0]
        mYW = [0, 0]
        gDb = {}
        gbc = [0]
        for j in range(NT):
            par = j % 2
            tld = [None, None]

            def ldf(sy, j=j, par=par, wx=mYW[par]):
                sy.wait_ge(sP.h, mDFM)
                if wx:
                    sy.wait_ge(sV.h, wx)
                DQ.wait(sy, tX1S[j])
                tld[0] = dqQ.issue(sy, out=x1r[par][:, :],
                                   in_=x1_dram[128 * j:128 * (j + 1), :])
                tld[1] = dqQ.issue(sy, out=hr[par][:, :],
                                   in_=h_dram[128 * j:128 * (j + 1), :])
            SY(ldf)
            tldx, tldh = tld

            for b in range(4):
                gb = gbc[0]

                def dtr(t, j=j, b=b, gb=gb, wv=(mDT[par] if b == 0 else 0),
                        wpt=gDb.get(gb - 2)):
                    if b == 0:
                        t.wait_ge(sP.h, mDFM)
                    if wv:
                        t.wait_ge(sV.h, wv)
                    if wpt:
                        t.wait_ge(sV.h, wpt)
                    ptX = ptA if b % 2 == 0 else ptB
                    for z in range(4):
                        fc = 4 * b + z
                        t.transpose(ptX[:, 128 * z:128 * (z + 1)],
                                    dfm[:, :].rearrange("p (c m) -> p c m", c=16)
                                    [:, fc, 128 * j:128 * (j + 1)], ident[:, :])
                    _i(t.drain(), sT)
                TE(dtr)
                mtr = sT.v

                def dte(v, par=par, b=b, mtr=mtr):
                    v.wait_ge(sT.h, mtr)
                    ptX = ptA if b % 2 == 0 else ptB
                    _i(v.tensor_copy(
                        dtm[par][:, 512 * b:512 * (b + 1)], ptX[:, 0:512]), sV)
                VE(dte)
                gDb[gb] = sV.v
                gbc[0] += 1
            mDT[par] = sV.v
            mDTj = sV.v

            def ffo(v, j=j, par=par, mDTj=mDTj, tldx=tldx, tldh=tldh):
                v.wait_ge(sV.h, mDTj)
                DQ.wait(v, tldh)
                v.scalar_tensor_tensor(ft[:, :], dtm[par][:, :], sig_tm[:, j:j + 1],
                                       hr[par][:, :], ALU.mult, ALU.add)
                v.drain()
                for q in range(4):
                    v.bn_stats(st6c[:, 6 * q:6 * q + 6], ft[:, 512 * q:512 * (q + 1)])
                v.drain()
                v.bn_aggr(st2c[:, :], st6c[:, :])
                v.drain()
                v.tensor_scalar(lrs3[:, :], st2c[:, 1:2], EPS_LN, None, ALU.add)
                v.drain()
                _i(v.reciprocal(lrs3[:, :], lrs3[:, :]), sV)
            VE(ffo)
            m1 = sV.v

            def fsq(sc, m1=m1):
                sc.wait_ge(sV.h, m1)
                _i(sc.activation(lrs3[:, :], lrs3[:, :], ACT.Sqrt), sS)
            SE(fsq)
            m2 = sS.v

            def fy(v, j=j, par=par, m2=m2, tldx=tldx,
                   typ=(tY[j - 2] if j >= 2 else None)):
                v.wait_ge(sS.h, m2)
                if typ is not None:
                    DQ.wait(v, typ)
                v.tensor_mul(lnm3[:, :], st2c[:, 0:1], lrs3[:, :])
                v.drain()
                DQ.wait(v, tldx)
                v.scalar_tensor_tensor(yt[par][:, :], ft[:, :], lrs3[:, 0:1],
                                       x1r[par][:, :], ALU.mult, ALU.add)
                v.drain()
                _i(v.tensor_scalar(yt[par][:, :], yt[par][:, :], lnm3[:, 0:1],
                                   None, ALU.subtract), sV)
            VE(fy)
            mYW[par] = sV.v
            mY = sV.v

            def sty(sy, j=j, par=par, mY=mY):
                sy.wait_ge(sV.h, mY)
                tY[j] = dqQ.issue(sy, out=y_d[128 * j:128 * (j + 1), :],
                                  in_=yt[par][:, :])
            SY(sty)

        def fwait(sy):
            for j in range(NT):
                DQ.wait(sy, tY[j])
        SY(fwait)

        fin.close()
        sel_ctx.close()
        return nc
    return nc



def _consts(core):
    pos = (np.arange(T) * NCORES + core).astype(np.float64)
    inv = 1.0 / (10000.0 ** (np.arange(0, ROPE, 2, dtype=np.float64) / ROPE))
    fr = np.outer(inv, pos)
    cos = np.cos(fr)
    sin = np.sin(fr)
    ropeCh = np.tile(cos, (4, 1)).astype(np.float32)
    ropeSh = np.concatenate([np.tile(-sin, 1), sin], axis=0)
    ropeSh = np.tile(np.concatenate([-sin, sin], axis=0), (2, 1)).astype(np.float32)
    r = np.arange(128)[:, None]
    c = np.tile(np.arange(128)[None, :], (1, 8)).reshape(1, 1024)
    rr = np.repeat(np.arange(8), 128).reshape(1, 1024)
    allow = (c < r) | ((c == r) & (rr <= core))
    dm = np.where(allow, 0.0, MASK_NEG).astype(np.float32)
    return ropeCh, ropeSh, dm


def kernel(**inputs):
    nc = build_nc(stage=6)
    return _run(nc, inputs, stage=6)["y"]


def _run(nc, inputs, stage):
    mybir.codegen_inst_isa_subclasses(nc)
    x = np.ascontiguousarray(np.asarray(inputs["L0_attn_x"], dtype=np.float32)[0])
    ident = np.eye(128, dtype=ml_dtypes.bfloat16)
    onesm = np.ones((128, 128), dtype=ml_dtypes.bfloat16)
    tok = np.arange(T, dtype=np.int16).reshape(1, T)
    cap = np.arange(CAP, dtype=np.float32).reshape(1, CAP)
    common = {
        "w_dq": np.asarray(inputs["L0_attn_w_dq"], np.float32),
        "w_uq": np.asarray(inputs["L0_attn_w_uq"], np.float32),
        "w_dkv": np.asarray(inputs["L0_attn_w_dkv"], np.float32),
        "w_kr": np.asarray(inputs["L0_attn_w_kr"], np.float32),
        "w_uk": np.asarray(inputs["L0_attn_w_uk"], np.float32),
        "w_uv": np.asarray(inputs["L0_attn_w_uv"], np.float32),
        "w_o": np.asarray(inputs["L0_attn_w_o"], np.float32),
        "g_q": np.asarray(inputs["L0_attn_g_q"], np.float32),
        "g_kv": np.asarray(inputs["L0_attn_g_kv"], np.float32),
        "w1": np.asarray(inputs["L0_ffn_w1"], np.float32),
        "b1": np.asarray(inputs["L0_ffn_b1"], np.float32),
        "w2": np.asarray(inputs["L0_ffn_w2"], np.float32),
        "b2": np.asarray(inputs["L0_ffn_b2"], np.float32),
        "rw": np.asarray(inputs["L0_ffn_router_weight"], np.float32),
        "rb": np.asarray(inputs["L0_ffn_router_bias"], np.float32),
        "ident": ident, "ones": onesm, "tok16": tok, "capio": cap,
        "id32f": np.eye(32, dtype=np.float32),
        "onesf": np.ones((128, 128), dtype=np.float32),
    }
    in_maps = []
    for i in range(NCORES):
        rc, rs, dm = _consts(i)
        m = dict(common)
        m["x0"] = np.ascontiguousarray(x[i::NCORES])
        m["ropeC"] = rc
        m["ropeS"] = rs
        m["dmask"] = dm
        in_maps.append(m)
    res = run_bass_kernel_spmd(nc, in_maps, core_ids=list(range(NCORES)))
    if stage == 6:
        out = np.empty((1, S, D), np.float32)
        for i in range(NCORES):
            out[0, i::NCORES, :] = res.results[i]["y"]
        return {"y": out}
    return {"res": res}


# version 05
# speedup vs baseline: 2.0732x; 1.2491x over previous
import math
from contextlib import ExitStack

import numpy as np
import ml_dtypes

import concourse.bass as bass
import concourse.mybir as mybir
import concourse.library_config as libcfg
from concourse.bass_utils import run_bass_kernel_spmd

F32 = mybir.dt.float32
BF16 = mybir.dt.bfloat16
I16 = mybir.dt.int16
AX = mybir.AxisListType.X
ALU = mybir.AluOpType
ACT = mybir.ActivationFunctionType

D = 2048
S = 8192
NCORES = 8
T = S // NCORES
NT = T // 128
H = 16
NOPE, ROPE, VD = 128, 64, 128
QKD = NOPE + ROPE
QL, KVL = 768, 512
DFF = 8192
CAP = 384
SCALE = QKD ** (-0.5)
EPS_LN = 1e-5
EPS_RMS = 1e-6
MASK_NEG = -1.0e30
NKC = 8
KSEL = 2048

KN_OFF = 0
KPE_OFF = KN_OFF + D * T
KN2_OFF = KPE_OFF + ROPE * T
V_OFF = KN2_OFF + 32
KVSH = V_OFF + H * T * VD


class Sem:
    def __init__(self, h):
        self.h = h
        self.v = 0


def _i(instr, s, n=1):
    instr.then_inc(s.h, n)
    s.v += n
    return s.v


class DQ:

    def __init__(self, nc, ctx, name, ns=8):
        self.sems = [Sem(ctx.enter_context(nc.semaphore(f"{name}{i}")))
                     for i in range(ns)]
        self.ns = ns
        self.n = 0

    def issue(self, eng, **kw):
        s = self.sems[self.n % self.ns]
        if s.v > 0:
            eng.wait_ge(s.h, s.v)
        eng.dma_start(**kw).then_inc(s.h, 16)
        s.v += 16
        self.n += 1
        return (s, s.v)

    @staticmethod
    def wait(eng, tok):
        eng.wait_ge(tok[0].h, tok[1])

    def wait_all(self, eng):
        for s in self.sems:
            if s.v > 0:
                eng.wait_ge(s.h, s.v)


def build_nc(stage=6):
    nc = bass.Bass("TRN2", target_bir_lowering=False, debug=False)

    x0_d = nc.dram_tensor("x0", [T, D], F32, kind="ExternalInput")
    w_dq_d = nc.dram_tensor("w_dq", [D, QL], F32, kind="ExternalInput")
    w_uq_d = nc.dram_tensor("w_uq", [QL, H * QKD], F32, kind="ExternalInput")
    w_dkv_d = nc.dram_tensor("w_dkv", [D, KVL], F32, kind="ExternalInput")
    w_kr_d = nc.dram_tensor("w_kr", [D, ROPE], F32, kind="ExternalInput")
    w_uk_d = nc.dram_tensor("w_uk", [KVL, H * NOPE], F32, kind="ExternalInput")
    w_uv_d = nc.dram_tensor("w_uv", [KVL, H * VD], F32, kind="ExternalInput")
    w_o_d = nc.dram_tensor("w_o", [H * VD, D], F32, kind="ExternalInput")
    g_q_d = nc.dram_tensor("g_q", [QL], F32, kind="ExternalInput")
    g_kv_d = nc.dram_tensor("g_kv", [KVL], F32, kind="ExternalInput")
    w1_d = nc.dram_tensor("w1", [D, DFF], F32, kind="ExternalInput")
    b1_d = nc.dram_tensor("b1", [DFF], F32, kind="ExternalInput")
    w2_d = nc.dram_tensor("w2", [DFF, D], F32, kind="ExternalInput")
    b2_d = nc.dram_tensor("b2", [D], F32, kind="ExternalInput")
    rw_d = nc.dram_tensor("rw", [D], F32, kind="ExternalInput")
    rb_d = nc.dram_tensor("rb", [1], F32, kind="ExternalInput")
    ropeC_d = nc.dram_tensor("ropeC", [128, T], F32, kind="ExternalInput")
    ropeS_d = nc.dram_tensor("ropeS", [128, T], F32, kind="ExternalInput")
    dmask_d = nc.dram_tensor("dmask", [128, 1024], F32, kind="ExternalInput")
    ident_d = nc.dram_tensor("ident", [128, 128], BF16, kind="ExternalInput")
    ones_d = nc.dram_tensor("ones", [128, 128], BF16, kind="ExternalInput")
    tok16_d = nc.dram_tensor("tok16", [1, T], I16, kind="ExternalInput")
    capio_d = nc.dram_tensor("capio", [1, CAP], F32, kind="ExternalInput")
    id32f_d = nc.dram_tensor("id32f", [32, 32], F32, kind="ExternalInput")
    onesf_d = nc.dram_tensor("onesf", [128, 128], F32, kind="ExternalInput")

    y_d = nc.dram_tensor("y", [T, D], F32, kind="ExternalOutput")
    dbg = None
    if stage == 2:
        dbg = nc.dram_tensor("dbg", [KVSH], BF16, kind="ExternalOutput")
    elif stage == 3:
        dbg = nc.dram_tensor("dbg", [128, H * T], BF16, kind="ExternalOutput")
    elif stage == 4:
        dbg = nc.dram_tensor("dbg", [T, D], F32, kind="ExternalOutput")
    elif stage == 5:
        dbg = nc.dram_tensor("dbg", [8, T], F32, kind="ExternalOutput")

    w1b_d = nc.dram_tensor("w1b", [D, DFF], BF16)
    w2b_d = nc.dram_tensor("w2b", [DFF, D], BF16)
    agkv_in = nc.dram_tensor("agkv_in", [KVSH], BF16)
    agkv_out = nc.dram_tensor("agkv_out", [NCORES * KVSH], BF16)
    agsc_in = nc.dram_tensor("agsc_in", [T], F32)
    agsc_out = nc.dram_tensor("agsc_out", [NCORES * T], F32)
    h_dram = nc.dram_tensor("h_dram", [T, D], BF16)
    x1_dram = nc.dram_tensor("x1_dram", [T, D], F32)
    idx_dram = nc.dram_tensor("idx_dram", [1, CAP], I16)

    ctx = ExitStack()
    with ctx:
        dqQ = DQ(nc, ctx, "sQ", ns=8)
        dqG = DQ(nc, ctx, "sG", ns=8)
        sCC = Sem(ctx.enter_context(nc.semaphore("sCC")))
        sT = Sem(ctx.enter_context(nc.semaphore("sT")))
        sV = Sem(ctx.enter_context(nc.semaphore("sV")))
        sS = Sem(ctx.enter_context(nc.semaphore("sS")))
        sP = Sem(ctx.enter_context(nc.semaphore("sP")))
        blk = ctx.enter_context(nc.Block())

        TE, VE, SE, GE, SY = blk.tensor, blk.vector, blk.scalar, blk.gpsimd, blk.sync

        def fence():
            snap = (sT.v, sV.v, sS.v)

            def fg(g, snap=snap):
                dqQ.wait_all(g)
                dqG.wait_all(g)
                g.wait_ge(sT.h, snap[0])
                g.wait_ge(sV.h, snap[1])
                g.wait_ge(sS.h, snap[2])
                _i(g.drain(), sP)
            GE(fg)
            return sP.v

        ident = ctx.enter_context(nc.sbuf_tensor("ident_s", [128, 128], BF16))
        ones = ctx.enter_context(nc.sbuf_tensor("ones_s", [128, 128], BF16))
        ropeC = ctx.enter_context(nc.sbuf_tensor("ropeC_s", [128, T], F32))
        ropeS = ctx.enter_context(nc.sbuf_tensor("ropeS_s", [128, T], F32))
        dmask = ctx.enter_context(nc.sbuf_tensor("dmask_s", [128, 1024], F32))
        gq_sb = ctx.enter_context(nc.sbuf_tensor("gq_sb", [128, 6], F32))
        gkv_sb = ctx.enter_context(nc.sbuf_tensor("gkv_sb", [128, 4], F32))
        b1_sb = ctx.enter_context(nc.sbuf_tensor("b1_sb", [128, 64], F32))
        b2_sb = ctx.enter_context(nc.sbuf_tensor("b2_sb", [128, 16], F32))
        rb_sb = ctx.enter_context(nc.sbuf_tensor("rb_sb", [1, 1], F32))
        tok16 = ctx.enter_context(nc.sbuf_tensor("tok16_s", [16, T], I16))
        capio = ctx.enter_context(nc.sbuf_tensor("capio_s", [1, CAP], F32))
        id32f = ctx.enter_context(nc.sbuf_tensor("id32f_s", [32, 32], F32))
        onesf = ctx.enter_context(nc.sbuf_tensor("onesf_s", [128, 128], F32))

        psA = ctx.enter_context(nc.psum_tensor("psA", [128, 1024], F32))
        psB = ctx.enter_context(nc.psum_tensor("psB", [128, 1024], F32))
        psC = ctx.enter_context(nc.psum_tensor("psC", [128, 1024], F32))
        psO = ctx.enter_context(nc.psum_tensor("psO", [128, 512], F32))
        psM = ctx.enter_context(nc.psum_tensor("psM", [128, 512], F32))

        tC0 = []

        def c0(sy):
            for dst, src in ((ident, ident_d), (ones, ones_d), (ropeC, ropeC_d),
                             (ropeS, ropeS_d), (dmask, dmask_d), (capio, capio_d),
                             (id32f, id32f_d), (onesf, onesf_d)):
                tC0.append(dqQ.issue(sy, out=dst[:, :], in_=src[:, :]))
            for dst, src in ((gq_sb, g_q_d), (gkv_sb, g_kv_d),
                             (b1_sb, b1_d), (b2_sb, b2_d)):
                tC0.append(dqQ.issue(
                    sy, out=dst[:, :], in_=src.ap().rearrange("(c p) -> p c", p=128),
                    allow_slow_non_contiguous=True))
            tC0.append(dqQ.issue(sy, out=rb_sb[:, :],
                                 in_=rb_d.ap().rearrange("(o x) -> o x", o=1)))
            for b in range(16):
                tC0.append(dqQ.issue(sy, out=tok16[b:b + 1, :], in_=tok16_d[:, :]))
        SY(c0)
        tIdent = tC0

        def g0(g):
            g.load_library(libcfg.attnmlp)
        GE(g0)

        ofm_ctx = ExitStack()
        aqn = ofm_ctx.enter_context(nc.sbuf_tensor("aqn", [128, 6 * T], BF16, side="right"))
        o_fm = ofm_ctx.enter_context(nc.sbuf_tensor("o_fm", [128, H * T], BF16, side="right"))
        rbb = ctx.enter_context(nc.sbuf_tensor("rbb", [128, 1], F32))

        p2 = ExitStack()
        x0b = [p2.enter_context(nc.sbuf_tensor(f"x0b{i}", [128, D], F32)) for i in range(2)]
        st6 = p2.enter_context(nc.sbuf_tensor("st6", [128, 24], F32))
        st2 = p2.enter_context(nc.sbuf_tensor("st2", [128, 2], F32))
        lrs = p2.enter_context(nc.sbuf_tensor("lrs", [128, 1], F32))
        lnm = p2.enter_context(nc.sbuf_tensor("lnm", [128, 1], F32))
        t_tm = p2.enter_context(nc.sbuf_tensor("t_tm", [128, D], BF16))
        t_fm = p2.enter_context(nc.sbuf_tensor("t_fm", [128, 16 * T], BF16))
        ckvn = p2.enter_context(nc.sbuf_tensor("ckvn", [128, 4 * T], BF16))
        sqtmp = p2.enter_context(nc.sbuf_tensor("sqtmp", [128, T], BF16))
        rmss = p2.enter_context(nc.sbuf_tensor("rmss", [1, T], F32))
        rmsb = p2.enter_context(nc.sbuf_tensor("rmsb", [128, T], F32))
        knch = p2.enter_context(nc.sbuf_tensor("knch", [128, T], BF16))
        knsq = p2.enter_context(nc.sbuf_tensor("knsq", [128, T], BF16))
        kn2row = p2.enter_context(nc.sbuf_tensor("kn2row", [1, 32], F32))
        kn2rb = p2.enter_context(nc.sbuf_tensor("kn2rb", [1, 32], BF16))
        vch = p2.enter_context(nc.sbuf_tensor("vch", [128, T], BF16))
        vtt = p2.enter_context(nc.sbuf_tensor("vtt", [128, T], BF16))
        kpe0 = p2.enter_context(nc.sbuf_tensor("kpe0", [64, T], F32))
        kpet = p2.enter_context(nc.sbuf_tensor("kpet", [64, T], F32))
        kper = p2.enter_context(nc.sbuf_tensor("kper", [64, T], BF16))
        kpesq = p2.enter_context(nc.sbuf_tensor("kpesq", [64, T], BF16))
        p2a = ExitStack()
        wdkv = p2a.enter_context(nc.sbuf_tensor("wdkv", [128, 16 * KVL], BF16))
        wkr = p2a.enter_context(nc.sbuf_tensor("wkr", [128, 16 * 2 * ROPE], BF16))
        wuk = p2a.enter_context(nc.sbuf_tensor("wuk", [128, 4 * H * NOPE], BF16))
        wuv = p2a.enter_context(nc.sbuf_tensor("wuv", [128, 4 * H * VD], BF16))

        tW = []

        def wld(g):
            tW.append(dqG.issue(g, out=wdkv[:, :].rearrange("p (c j) -> p c j", c=16),
                                in_=w_dkv_d.ap().rearrange("(c p) j -> p c j", p=128)))
            tW.append(dqG.issue(g, out=wuk[:, :].rearrange("p (c j) -> p c j", c=4),
                                in_=w_uk_d.ap().rearrange("(c p) j -> p c j", p=128)))
            tW.append(dqG.issue(g, out=wuv[:, :].rearrange("p (c j) -> p c j", c=4),
                                in_=w_uv_d.ap().rearrange("(c p) j -> p c j", p=128)))
            tW.append(dqG.issue(
                g, out=wkr[:, :].rearrange("p (c u) -> p c u", c=16)[:, :, 0:ROPE],
                in_=w_kr_d.ap().rearrange("(c p) j -> p c j", p=128)))
            tW.append(dqG.issue(
                g, out=wkr[:, :].rearrange("p (c u) -> p c u", c=16)[:, :, ROPE:ROPE + 32],
                in_=w_kr_d.ap().rearrange("(c p) j -> p c j", p=128)[:, :, 32:64]))
            tW.append(dqG.issue(
                g, out=wkr[:, :].rearrange("p (c u) -> p c u", c=16)[:, :, ROPE + 32:2 * ROPE],
                in_=w_kr_d.ap().rearrange("(c p) j -> p c j", p=128)[:, :, 0:32]))
        GE(wld)

        mX0free = [None, None]
        for b in range(NT):
            par = b % 2

            tkx = [None]

            def ldx(sy, b=b, par=par, w=mX0free[par]):
                if w is not None:
                    sy.wait_ge(sS.h, w)
                tkx[0] = dqQ.issue(sy, out=x0b[par][:, :],
                                   in_=x0_d[128 * b:128 * (b + 1), :])
            SY(ldx)
            mx = tkx[0]

            def lns(v, par=par, mx=mx, wprev=mX0free[1 - par]):
                DQ.wait(v, mx)
                if wprev is not None:
                    v.wait_ge(sS.h, wprev)
                for q in range(4):
                    v.bn_stats(st6[:, 6 * q:6 * q + 6], x0b[par][:, 512 * q:512 * (q + 1)])
                v.drain()
                v.bn_aggr(st2[:, :], st6[:, :])
                v.drain()
                v.tensor_scalar(lrs[:, :], st2[:, 1:2], EPS_LN, None, ALU.add)
                v.drain()
                _i(v.reciprocal(lrs[:, :], lrs[:, :]), sV)
            VE(lns)
            m1 = sV.v

            def lnr(sc, m1=m1):
                sc.wait_ge(sV.h, m1)
                _i(sc.activation(lrs[:, :], lrs[:, :], ACT.Sqrt), sS)
            SE(lnr)
            m2 = sS.v

            def lnn(v, m2=m2):
                v.wait_ge(sS.h, m2)
                v.tensor_mul(lnm[:, :], st2[:, 0:1], lrs[:, :])
                v.drain()
                _i(v.tensor_scalar(lnm[:, :], lnm[:, :], -1.0, None, ALU.mult), sV)
            VE(lnn)
            m3 = sV.v

            def lnap(sc, par=par, m3=m3):
                sc.wait_ge(sV.h, m3)
                _i(sc.activation(t_tm[:, :], x0b[par][:, :], ACT.Identity,
                                 bias=lnm[:, 0:1], scale=lrs[:, 0:1]), sS)
            SE(lnap)
            m4 = sS.v
            mX0free[par] = m4

            def ttr(t, m4=m4, b=b):
                t.wait_ge(sS.h, m4)
                if b == 0:
                    DQ.wait(t, tIdent[0])
                for half in range(2):
                    pt = (psA if half == 0 else psB).ap().bitcast(BF16)
                    for c in range(8):
                        cc = 8 * half + c
                        t.transpose(pt[:, 128 * c:128 * (c + 1)],
                                    t_tm[:, 128 * cc:128 * (cc + 1)], ident[:, :])
                    _i(t.drain(), sT)
            TE(ttr)
            mh = (sT.v - 1, sT.v)

            def tev(v, mh=mh, b=b):
                for half in range(2):
                    v.wait_ge(sT.h, mh[half])
                    _i(v.tensor_copy(
                        t_fm[:, :].rearrange("p (c m) -> p c m", c=16)
                        [:, 8 * half:8 * half + 8, 128 * b:128 * (b + 1)],
                        (psA if half == 0 else psB).ap().bitcast(BF16)
                        [:, 0:1024].rearrange("p (c m) -> p c m", c=8)), sV)
            VE(tev)
        mTFM = sV.v

        def proj_rms(wsb, nchunk, gcol_sb, buf):
            vgate = sV.v
            sgate = sS.v
            mEv = []
            mSq = []
            for oc in range(nchunk):
                ps_full = psA if oc % 2 == 0 else psB
                def mm(t, oc=oc, ps_full=ps_full):
                    if oc == 0:
                        for tk in tW:
                            DQ.wait(t, tk)
                        t.wait_ge(sV.h, max(mTFM, vgate))
                    if oc >= 2:
                        t.wait_ge(sV.h, mEv[oc - 2])
                    for half in range(2):
                        ps = ps_full[:, 512 * half:512 * (half + 1)]
                        for ic in range(16):
                            t.matmul(ps,
                                     wsb[:, :].rearrange("p (c j) -> p c j", c=16)
                                     [:, ic, 128 * oc:128 * (oc + 1)],
                                     t_fm[:, :].rearrange("p (c m) -> p c m", c=16)
                                     [:, ic, 512 * half:512 * (half + 1)],
                                     start=(ic == 0), stop=(ic == 15))
                    _i(t.drain(), sT)
                TE(mm)
                mp = sT.v

                def ev(v, oc=oc, mp=mp, ps_full=ps_full):
                    v.wait_ge(sT.h, mp)
                    _i(v.tensor_copy(
                        buf[:, :].rearrange("p (c m) -> p c m", c=nchunk)[:, oc, :],
                        ps_full[:, :]), sV)
                VE(ev)
                mEv.append(sV.v)
                mr = sV.v

                def sq(sc, oc=oc, mr=mr):
                    sc.wait_ge(sV.h, mr)
                    if oc >= 1:
                        sc.wait_ge(sT.h, mSq[oc - 1])
                    _i(sc.activation(sqtmp[:, :],
                                     buf[:, :].rearrange("p (c m) -> p c m", c=nchunk)
                                     [:, oc, :], ACT.Square), sS)
                SE(sq)
                msq = sS.v

                def ssum(t, oc=oc, msq=msq, vgate=vgate):
                    t.wait_ge(sS.h, msq)
                    if oc == 0:
                        t.wait_ge(sV.h, vgate)
                    t.matmul(psM[0:1, 0:512], ones[:, 0:1], sqtmp[:, 0:512],
                             start=(oc == 0), stop=(oc == nchunk - 1))
                    t.matmul(psO[0:1, 0:512], ones[:, 0:1], sqtmp[:, 512:1024],
                             start=(oc == 0), stop=(oc == nchunk - 1))
                    _i(t.drain(), sT)
                TE(ssum)
                mSq.append(sT.v)
            msum = sT.v

            def rstd(v, msum=msum, nchunk=nchunk):
                v.wait_ge(sT.h, msum)
                v.tensor_scalar(rmss[0:1, 0:512], psM[0:1, 0:512],
                                1.0 / (nchunk * 128), EPS_RMS, ALU.mult, ALU.add)
                v.tensor_scalar(rmss[0:1, 512:1024], psO[0:1, 0:512],
                                1.0 / (nchunk * 128), EPS_RMS, ALU.mult, ALU.add)
                _i(v.drain(), sV)
            VE(rstd)
            mms = sV.v

            def rsq(sc, mms=mms):
                sc.wait_ge(sV.h, mms)
                _i(sc.activation(rmss[:, :], rmss[:, :], ACT.Sqrt), sS)
            SE(rsq)
            msq2 = sS.v

            def rrec(v, msq2=msq2):
                v.wait_ge(sS.h, msq2)
                _i(v.reciprocal(rmss[:, :], rmss[:, :]), sV)
            VE(rrec)
            mrec = sV.v

            def bcast(g, mrec=mrec):
                g.wait_ge(sV.h, mrec)
                _i(g.partition_broadcast(rmsb[:, :], rmss[0:1, :]), sP)
            GE(bcast)
            mb = sP.v

            for oc in range(nchunk):
                def nrm(v, oc=oc, mb=mb):
                    v.wait_ge(sP.h, mb)
                    _i(v.scalar_tensor_tensor(
                        buf[:, :].rearrange("p (c m) -> p c m", c=nchunk)[:, oc, :],
                        buf[:, :].rearrange("p (c m) -> p c m", c=nchunk)[:, oc, :],
                        gcol_sb[:, oc:oc + 1], rmsb[:, :],
                        ALU.mult, ALU.mult), sV)
                VE(nrm)
            def nrmd(v):
                _i(v.drain(), sV)
            VE(nrmd)
            return sV.v

        mCKV = proj_rms(wdkv, 4, gkv_sb, ckvn)

        tShard = []

        def z32(v):
            v.memset(kn2row[:, :], 0.0)
            _i(v.drain(), sV)
        VE(z32)
        mZ32 = sV.v

        mKNprev = [None, None]
        mKN2V = [None]
        tOutPrev = [None, None]
        for hh in range(H):
            def knmm(t, hh=hh, wprev=(mKNprev[0] if hh > 0 else None)):
                if hh == 0:
                    t.wait_ge(sV.h, mCKV)
                else:
                    t.wait_ge(sV.h, wprev)
                for half in range(2):
                    ps = psA[:, 512 * half:512 * (half + 1)]
                    for ic in range(4):
                        t.matmul(ps,
                                 wuk[:, :].rearrange("p (c j) -> p c j", c=4)
                                 [:, ic, 128 * hh:128 * (hh + 1)],
                                 ckvn[:, :].rearrange("p (c m) -> p c m", c=4)
                                 [:, ic, 512 * half:512 * (half + 1)],
                                 start=(ic == 0), stop=(ic == 3))
                _i(t.drain(), sT)
            TE(knmm)
            mkn = sT.v

            def knev(v, mkn=mkn, tprev=tOutPrev[0]):
                v.wait_ge(sT.h, mkn)
                if tprev is not None:
                    DQ.wait(v, tprev)
                _i(v.tensor_copy(knch[:, :], psA[:, :]), sV)
            VE(knev)
            mknb = sV.v

            mKNprev[0] = mknb

            def knq(sc, mknb=mknb):
                sc.wait_ge(sV.h, mknb)
                _i(sc.activation(knsq[:, :], knch[:, :], ACT.Square), sS)
            SE(knq)
            mknsq = sS.v

            def kn2mm(t, mknsq=mknsq, hh=hh, wv=(mKN2V[0] if hh > 0 else None)):
                t.wait_ge(sS.h, mknsq)
                if wv is not None:
                    t.wait_ge(sV.h, wv)
                t.matmul(psM[0:1, 0:512], ones[:, 0:1], knsq[:, 0:512],
                         start=True, stop=True)
                t.matmul(psO[0:1, 0:512], ones[:, 0:1], knsq[:, 512:1024],
                         start=True, stop=True)
                _i(t.drain(), sT)
            TE(kn2mm)
            mkn2 = sT.v

            def kn2red(v, mkn2=mkn2, hh=hh):
                v.wait_ge(sT.h, mkn2)
                if hh == 0:
                    v.wait_ge(sV.h, mZ32)
                v.reduce_max(kn2row[0:1, hh:hh + 1], psM[0:1, 0:512], axis=AX)
                v.reduce_max(lrs[0:1, 0:1], psO[0:1, 0:512], axis=AX)
                v.drain()
                _i(v.tensor_tensor(kn2row[0:1, hh:hh + 1], kn2row[0:1, hh:hh + 1],
                                   lrs[0:1, 0:1], ALU.max), sV)
            VE(kn2red)
            mKN2V[0] = sV.v

            tkn = [None]

            def knout(sy, hh=hh, mknb=mknb):
                sy.wait_ge(sV.h, mknb)
                tkn[0] = dqQ.issue(
                    sy,
                    out=agkv_in.ap()[KN_OFF + hh * 128 * T:KN_OFF + (hh + 1) * 128 * T]
                    .rearrange("(p m) -> p m", p=128),
                    in_=knch[:, :])
            SY(knout)
            tShard.append(tkn[0])
            tOutPrev[0] = tkn[0]

            def vmm(t, hh=hh, wprev=(mKNprev[1] if hh > 0 else None)):
                if wprev is not None:
                    t.wait_ge(sV.h, wprev)
                for half in range(2):
                    ps = psB[:, 512 * half:512 * (half + 1)]
                    for ic in range(4):
                        t.matmul(ps,
                                 wuv[:, :].rearrange("p (c j) -> p c j", c=4)
                                 [:, ic, 128 * hh:128 * (hh + 1)],
                                 ckvn[:, :].rearrange("p (c m) -> p c m", c=4)
                                 [:, ic, 512 * half:512 * (half + 1)],
                                 start=(ic == 0), stop=(ic == 3))
                _i(t.drain(), sT)
            TE(vmm)
            mv = sT.v

            def vev(v, mv=mv):
                v.wait_ge(sT.h, mv)
                _i(v.tensor_copy(vch[:, :], psB[:, :]), sV)
            VE(vev)
            mvb = sV.v
            mKNprev[1] = mvb

            def vtr(t, mvb=mvb):
                t.wait_ge(sV.h, mvb)
                pcb = psC.ap().bitcast(BF16)
                for half in range(2):
                    for bb in range(4):
                        b = 4 * half + bb
                        t.transpose(pcb[:, 1024 * half + 128 * bb:1024 * half + 128 * (bb + 1)],
                                    vch[:, 128 * b:128 * (b + 1)], ident[:, :])
                    _i(t.drain(), sT)
            TE(vtr)
            mvt = (sT.v - 1, sT.v)

            def vtev(v, mvt=mvt, tprev=tOutPrev[1]):
                if tprev is not None:
                    DQ.wait(v, tprev)
                for half in range(2):
                    v.wait_ge(sT.h, mvt[half])
                    _i(v.tensor_copy(
                        vtt[:, :].rearrange("p (b m) -> p b m", b=8)
                        [:, 4 * half:4 * half + 4, :],
                        psC.ap().bitcast(BF16)[:, 1024 * half:1024 * half + 512]
                        .rearrange("p (b m) -> p b m", b=4)), sV)
            VE(vtev)
            mvtt = sV.v

            tvo = [None]

            def vout(sy, hh=hh, mvtt=mvtt):
                sy.wait_ge(sV.h, mvtt)
                tvo[0] = dqQ.issue(
                    sy,
                    out=agkv_in.ap()[V_OFF + hh * T * VD:V_OFF + (hh + 1) * T * VD]
                    .rearrange("(b p v) -> p b v", b=8, p=128),
                    in_=vtt[:, :].rearrange("p (b v) -> p b v", b=8))
            SY(vout)
            tShard.append(tvo[0])
            tOutPrev[1] = tvo[0]

        def kpmm(t):
            for half in range(2):
                ps = psA[0:64, 512 * half:512 * (half + 1)]
                for ic in range(16):
                    t.matmul(ps,
                             wkr[:, :].rearrange("p (c u) -> p c u", c=16)[:, ic, 0:64],
                             t_fm[:, :].rearrange("p (c m) -> p c m", c=16)
                             [:, ic, 512 * half:512 * (half + 1)],
                             start=(ic == 0), stop=(ic == 15))
                ps2 = psB[0:64, 512 * half:512 * (half + 1)]
                for ic in range(16):
                    t.matmul(ps2,
                             wkr[:, :].rearrange("p (c u) -> p c u", c=16)[:, ic, 64:128],
                             t_fm[:, :].rearrange("p (c m) -> p c m", c=16)
                             [:, ic, 512 * half:512 * (half + 1)],
                             start=(ic == 0), stop=(ic == 15))
            _i(t.drain(), sT)
        TE(kpmm)
        mKP = sT.v

        def kprope(v, mKP=mKP):
            v.wait_ge(sT.h, mKP)
            v.tensor_tensor(kpe0[:, :], psA[0:64, :], ropeC[0:64, :], ALU.mult)
            v.tensor_tensor(kpet[:, :], psB[0:64, :], ropeS[0:64, :], ALU.mult)
            v.drain()
            _i(v.tensor_tensor(kper[:, :], kpe0[:, :], kpet[:, :], ALU.add), sV)
        VE(kprope)
        mKPR = sV.v

        def kpsq(sc, mKPR=mKPR):
            sc.wait_ge(sV.h, mKPR)
            _i(sc.activation(kpesq[:, :], kper[:, :], ACT.Square), sS)
        SE(kpsq)
        mKPSQ = sS.v

        def kpsum(t, mKPSQ=mKPSQ, wv=mKN2V[0]):
            t.wait_ge(sS.h, mKPSQ)
            t.wait_ge(sV.h, wv)
            t.matmul(psM[0:1, 0:512], ones[0:64, 0:1], kpesq[:, 0:512],
                     start=True, stop=True)
            t.matmul(psO[0:1, 0:512], ones[0:64, 0:1], kpesq[:, 512:1024],
                     start=True, stop=True)
            _i(t.drain(), sT)
        TE(kpsum)
        mKPS = sT.v

        def kpred(v, mKPS=mKPS):
            v.wait_ge(sT.h, mKPS)
            v.reduce_max(kn2row[0:1, 16:17], psM[0:1, 0:512], axis=AX)
            v.reduce_max(lrs[0:1, 0:1], psO[0:1, 0:512], axis=AX)
            v.drain()
            v.tensor_tensor(kn2row[0:1, 16:17], kn2row[0:1, 16:17], lrs[0:1, 0:1], ALU.max)
            v.drain()
            _i(v.tensor_copy(kn2rb[:, :], kn2row[:, :]), sV)
        VE(kpred)
        mKN2B = sV.v

        def kpout(sy, mKPR=mKPR, mKN2B=mKN2B):
            sy.wait_ge(sV.h, mKPR)
            tShard.append(dqQ.issue(
                sy,
                out=agkv_in.ap()[KPE_OFF:KPE_OFF + ROPE * T].rearrange("(p m) -> p m", p=64),
                in_=kper[:, :]))
            sy.wait_ge(sV.h, mKN2B)
            tShard.append(dqQ.issue(
                sy,
                out=agkv_in.ap()[KN2_OFF:KN2_OFF + 32].rearrange("(o x) -> o x", o=1),
                in_=kn2rb[:, :]))
        SY(kpout)

        p2a.close()
        wdq = p2.enter_context(nc.sbuf_tensor("wdq", [128, 16 * QL], BF16))

        def wdql(g, mKP=mKP):
            g.wait_ge(sT.h, mKP)
            tW.append(dqG.issue(g, out=wdq[:, :].rearrange("p (c j) -> p c j", c=16),
                                in_=w_dq_d.ap().rearrange("(c p) j -> p c j", p=128)))
        GE(wdql)

        def agkv(g):
            for tk in tShard:
                DQ.wait(g, tk)
            _i(g.collective_compute(
                "AllGather", ALU.bypass,
                replica_groups=[list(range(NCORES))],
                ins=[agkv_in.ap().opt()],
                outs=[agkv_out.ap().opt()]), sCC)
        GE(agkv)
        mAG = sCC.v

        mAQ = proj_rms(wdq, 6, gq_sb, aqn)

        if stage == 2:
            def s2(sy):
                sy.wait_ge(sCC.h, mAG)
                tk = dqQ.issue(sy, out=dbg.ap(), in_=agkv_in.ap())
                DQ.wait(sy, tk)
            SY(s2)

            def s2g(g, mAQ=mAQ):
                g.wait_ge(sV.h, mAQ)
                tk = dqG.issue(g, out=y_d[0:128, :], in_=aqn[:, 0:D])
                DQ.wait(g, tk)
            GE(s2g)
            p2.close()
            return nc

        p2.close()

        at = ExitStack()
        K_sb = at.enter_context(nc.sbuf_tensor("K_sb", [128, 2 * S], BF16))
        V_sb = at.enter_context(nc.sbuf_tensor("V_sb", [128, 2 * S], BF16))
        kpeA = at.enter_context(nc.sbuf_tensor("kpeA", [64, S], BF16))
        A_sb = at.enter_context(nc.sbuf_tensor("A_sb", [128, 2 * 1024], BF16))
        Qn = at.enter_context(nc.sbuf_tensor("Qn", [128, 2 * T], BF16))
        Qp = at.enter_context(nc.sbuf_tensor("Qp", [64, 2 * T], BF16))
        peF = at.enter_context(nc.sbuf_tensor("peF", [64, 512], F32))
        peG = at.enter_context(nc.sbuf_tensor("peG", [64, 512], F32))
        wuqh = at.enter_context(nc.sbuf_tensor("wuqh", [128, 2 * 6 * 256], BF16))
        srow = at.enter_context(nc.sbuf_tensor("srow", [1, 1024], F32))
        srec = at.enter_context(nc.sbuf_tensor("srec", [1, 1024], F32))
        rbc = at.enter_context(nc.sbuf_tensor("rbc", [128, 1024], F32))

        tKPE = []

        def kpeload(sy, mAQ=mAQ):
            sy.wait_ge(sCC.h, mAG)
            sy.wait_ge(sV.h, mAQ)
            for jj in range(NKC):
                tKPE.append(dqQ.issue(
                    sy,
                    out=kpeA[:, 1024 * jj:1024 * (jj + 1)].rearrange(
                        "p (r c) -> p r c", r=8),
                    in_=agkv_out.ap()[0:NCORES * KVSH]
                    .rearrange("(r x) -> r x", x=KVSH)
                    [:, KPE_OFF:KPE_OFF + ROPE * T].rearrange("r (p m) -> p r m", p=ROPE)
                    [:, :, 128 * jj:128 * (jj + 1)]))
        SY(kpeload)

        mHead = [None] * H
        mQMM = [None] * H
        tWQ = [None] * H
        tKV = [[None] * NKC for _ in range(H)]
        mExpL = [0] * H
        mOEV = [0] * H
        mRBC = [0] * H
        mSLN = [0] * H
        ccg = [0]
        mExp = {}
        mAVd = {}
        tCast = []

        for h in range(H):
            par = h % 2

            def wcst(g, h=h):
                tCast.append(dqG.issue(g, out=w1b_d[128 * h:128 * (h + 1), :],
                                       in_=w1_d[128 * h:128 * (h + 1), :]))
                tCast.append(dqG.issue(g, out=w2b_d[512 * h:512 * (h + 1), :],
                                       in_=w2_d[512 * h:512 * (h + 1), :]))
            GE(wcst)

            def wq(g, h=h, par=par):
                if h < 2:
                    g.wait_ge(sCC.h, mAG)
                    g.wait_ge(sV.h, mAQ)
                else:
                    g.wait_ge(sT.h, mQMM[h - 2])
                toks = []
                w = wuqh[:, :].rearrange("p (q l u) -> p q l u", q=2, l=6)
                toks.append(dqG.issue(
                    g, out=w[:, par, :, 0:128],
                    in_=w_uq_d.ap().rearrange("(l p) j -> p l j", p=128)
                    [:, :, QKD * h:QKD * h + 128]))
                toks.append(dqG.issue(
                    g, out=w[:, par, :, 128:192],
                    in_=w_uq_d.ap().rearrange("(l p) j -> p l j", p=128)
                    [:, :, QKD * h + 128:QKD * h + 192]))
                toks.append(dqG.issue(
                    g, out=w[:, par, :, 192:224],
                    in_=w_uq_d.ap().rearrange("(l p) j -> p l j", p=128)
                    [:, :, QKD * h + 160:QKD * h + 192]))
                toks.append(dqG.issue(
                    g, out=w[:, par, :, 224:256],
                    in_=w_uq_d.ap().rearrange("(l p) j -> p l j", p=128)
                    [:, :, QKD * h + 128:QKD * h + 160]))
                tWQ[h] = toks
            GE(wq)

            def kv(sy, h=h, par=par, mAQ=mAQ):
                if h < 2:
                    sy.wait_ge(sCC.h, mAG)
                    sy.wait_ge(sV.h, mAQ)
                else:
                    sy.wait_ge(sT.h, mHead[h - 2])
                kview = agkv_out.ap()[0:NCORES * KVSH] \
                    .rearrange("(r x) -> r x", x=KVSH) \
                    [:, KN_OFF + h * 128 * T:KN_OFF + (h + 1) * 128 * T] \
                    .rearrange("r (p m) -> p r m", p=128)
                vview = agkv_out.ap()[0:NCORES * KVSH] \
                    .rearrange("(r x) -> r x", x=KVSH) \
                    [:, V_OFF + h * T * VD:V_OFF + (h + 1) * T * VD] \
                    .rearrange("r (b p v) -> p b r v", b=NT, p=128)
                for jj in range(NKC):
                    t1 = dqQ.issue(
                        sy,
                        out=K_sb[:, par * S + 1024 * jj:par * S + 1024 * (jj + 1)]
                        .rearrange("p (r c) -> p r c", r=8),
                        in_=kview[:, :, 128 * jj:128 * (jj + 1)])
                    t2 = dqQ.issue(
                        sy,
                        out=V_sb[:, par * S + 1024 * jj:par * S + 1024 * (jj + 1)]
                        .rearrange("p (r v) -> p r v", r=8),
                        in_=vview[:, jj, :, :])
                    tKV[h][jj] = (t1, t2)
            SY(kv)

            def qmm(t, h=h, par=par):
                for tk in tWQ[h]:
                    DQ.wait(t, tk)
                if h == 0:
                    t.wait_ge(sV.h, mAQ)
                else:
                    t.wait_ge(sS.h, mExpL[h - 1])
                    t.wait_ge(sV.h, mRBC[h - 1])
                    t.wait_ge(sV.h, mOEV[h - 1])
                w = wuqh[:, :].rearrange("p (q l u) -> p q l u", q=2, l=6)
                for half in range(2):
                    for l in range(6):
                        t.matmul(psA[:, 512 * half:512 * (half + 1)],
                                 w[:, par, l, 0:128],
                                 aqn[:, :].rearrange("p (c m) -> p c m", c=6)
                                 [:, l, 512 * half:512 * (half + 1)],
                                 start=(l == 0), stop=(l == 5))
                for half in range(2):
                    for l in range(6):
                        t.matmul(psB[0:64, 512 * half:512 * (half + 1)],
                                 w[:, par, l, 128:192],
                                 aqn[:, :].rearrange("p (c m) -> p c m", c=6)
                                 [:, l, 512 * half:512 * (half + 1)],
                                 start=(l == 0), stop=(l == 5))
                for half in range(2):
                    for l in range(6):
                        t.matmul(psC[0:64, 512 * half:512 * (half + 1)],
                                 w[:, par, l, 192:256],
                                 aqn[:, :].rearrange("p (c m) -> p c m", c=6)
                                 [:, l, 512 * half:512 * (half + 1)],
                                 start=(l == 0), stop=(l == 5))
                _i(t.drain(), sT)
            TE(qmm)
            mQMM[h] = sT.v

            def qev(v, h=h, par=par):
                v.wait_ge(sT.h, mQMM[h])
                v.tensor_copy(Qn[:, par * T:(par + 1) * T], psA[:, :])
                for half in range(2):
                    pesw = psC[0:64, 512 * half:512 * (half + 1)]
                    v.tensor_tensor(peF[:, :], psB[0:64, 512 * half:512 * (half + 1)],
                                    ropeC[0:64, 512 * half:512 * (half + 1)], ALU.mult)
                    v.tensor_tensor(peG[:, :], pesw,
                                    ropeS[0:64, 512 * half:512 * (half + 1)], ALU.mult)
                    v.drain()
                    v.tensor_tensor(Qp[:, par * T + 512 * half:par * T + 512 * (half + 1)],
                                    peF[:, :], peG[:, :], ALU.add)
                    v.drain()
                _i(v.drain(), sV)
            VE(qev)
            mQEV = sV.v

            mQnEv = mQEV
            cch = 0
            pend = [None]

            def s_seg(jj, st, cc, cch, par, h=h):
                psX = psA if cc % 2 == 0 else psB

                def smm(t, jj=jj, st=st, cc=cc, cch=cch, par=par, psX=psX, h=h,
                        mQnEv=mQnEv, wexp=(mExp.get(cc - 2, 0))):
                    DQ.wait(t, tKV[h][jj][0])
                    DQ.wait(t, tKV[h][jj][1])
                    DQ.wait(t, tKPE[jj])
                    if wexp:
                        t.wait_ge(sS.h, wexp)
                    if cch == 0:
                        t.wait_ge(sV.h, mQnEv)
                    kb = par * S + 1024 * jj + 128 * st
                    slc = ([(128 * jj, 512), (512, 1024)] if jj < 4
                           else [(128 * jj, 1024)])
                    ins = None
                    for (a, bnd) in slc:
                        t.matmul(psX[:, a:bnd],
                                 K_sb[:, kb:kb + 128],
                                 Qn[:, par * T + a:par * T + bnd],
                                 start=True, stop=False)
                        ins = t.matmul(psX[:, a:bnd],
                                       kpeA[:, 1024 * jj + 128 * st:
                                            1024 * jj + 128 * (st + 1)],
                                       Qp[:, par * T + a:par * T + bnd],
                                       start=False, stop=True)
                    _i(ins, sT)
                TE(smm)
                return sT.v

            def mid_seg(jj, st, cc, mS):
                psX = psA if cc % 2 == 0 else psB
                apar = cc % 2

                def dmk(v, jj=jj, st=st, mS=mS, psX=psX):
                    v.wait_ge(sT.h, mS)
                    _i(v.tensor_tensor(psX[:, 128 * jj:128 * (jj + 1)],
                                       psX[:, 128 * jj:128 * (jj + 1)],
                                       dmask[:, 128 * st:128 * (st + 1)],
                                       ALU.add), sV)
                VE(dmk)
                mMsk = sV.v

                def expf(sc, cc=cc, jj=jj, apar=apar, psX=psX, mMsk=mMsk,
                         wav=(mAVd.get(cc - 2, 0))):
                    sc.wait_ge(sV.h, mMsk)
                    if wav:
                        sc.wait_ge(sT.h, wav)
                    _i(sc.activation(
                        A_sb[:, apar * 1024 + 128 * jj:(apar + 1) * 1024],
                        psX[:, 128 * jj:1024],
                        ACT.Exp, scale=SCALE), sS)
                SE(expf)
                mExp[cc] = sS.v
                return sS.v

            def tail_seg(jj, st, cc, par, mE, h=h):
                apar = cc % 2

                def avm(t, jj=jj, st=st, cc=cc, par=par, apar=apar, mE=mE, h=h,
                        woev=(mOEV[h - 1] if (h > 0 and jj == 0 and st == 0) else 0),
                        wsln=(mSLN[h - 1] if (h > 0 and jj == 0 and st == 0) else 0)):
                    t.wait_ge(sS.h, mE)
                    if woev:
                        t.wait_ge(sV.h, woev)
                    if wsln:
                        t.wait_ge(sS.h, wsln)
                    vb = par * S + 1024 * jj + 128 * st
                    first = (jj == 0 and st == 0)
                    slc = ([(128 * jj, 512), (512, 1024)] if jj < 4
                           else [(128 * jj, 1024)])
                    for (a, bnd) in slc:
                        t.matmul(psC[:, a:bnd],
                                 V_sb[:, vb:vb + 128],
                                 A_sb[:, apar * 1024 + a:apar * 1024 + bnd],
                                 start=first,
                                 stop=(st == 7 and jj == (3 if a < 512 else 7)))
                    ins = None
                    for (a, bnd) in slc:
                        ps = psO if a < 512 else psM
                        aa = a if a < 512 else a - 512
                        ins = t.matmul(ps[0:1, aa:aa + (bnd - a)],
                                       ones[:, 0:1],
                                       A_sb[:, apar * 1024 + a:apar * 1024 + bnd],
                                       start=first,
                                       stop=(st == 7 and jj == (3 if a < 512 else 7)))
                    _i(ins, sT)
                TE(avm)
                mAVd[cc] = sT.v

            for jj in range(NKC):
                for st in range(8):
                    cc = ccg[0]
                    mS = s_seg(jj, st, cc, cch, par)
                    if pend[0] is not None:
                        pend[0]()
                    mE = mid_seg(jj, st, cc, mS)
                    mExpL[h] = mE
                    pend[0] = (lambda jj=jj, st=st, cc=cc, par=par, mE=mE:
                               tail_seg(jj, st, cc, par, mE))
                    ccg[0] += 1
                    cch += 1
            pend[0]()
            pend[0] = None
            mAVl = sT.v

            def sln(sc, mAVl=mAVl):
                sc.wait_ge(sT.h, mAVl)
                sc.activation(srow[0:1, 0:512], psO[0:1, 0:512], ACT.Ln)
                sc.activation(srow[0:1, 512:1024], psM[0:1, 0:512], ACT.Ln)
                sc.drain()
                _i(sc.activation(srec[0:1, :], srow[0:1, :], ACT.Exp,
                                 scale=-1.0), sS)
            SE(sln)
            mSLN[h] = sS.v

            def bcmm(t, m=mSLN[h]):
                t.wait_ge(sS.h, m)
                DQ.wait(t, tC0[7])
                t.matmul(psB[:, 0:512], onesf[0:1, 0:128], srec[0:1, 0:512],
                         start=True, stop=True)
                ins = t.matmul(psB[:, 512:1024], onesf[0:1, 0:128],
                               srec[0:1, 512:1024], start=True, stop=True)
                _i(ins, sT)
            TE(bcmm)
            mBC = sT.v

            def ofin(v, h=h, mBC=mBC):
                v.wait_ge(sT.h, mBC)
                v.tensor_copy(rbc[:, :], psB[:, :])
                _i(v.drain(), sV)
                v.tensor_tensor(o_fm[:, T * h:T * (h + 1)], psC[:, :],
                                rbc[:, :], ALU.mult)
                _i(v.drain(), sV)
            VE(ofin)
            mOEV[h] = sV.v
            mRBC[h] = sV.v
            mHead[h] = sT.v

        mOFM = sV.v

        if stage == 3:
            def s3(g):
                g.wait_ge(sV.h, mOFM)
                tk = dqG.issue(g, out=dbg.ap().rearrange("(p m) -> p m", p=128)
                               if False else dbg[:, :], in_=o_fm[:, :])
                DQ.wait(g, tk)
            GE(s3)
            at.close()
            return nc

        at.close()
        mF4 = fence()

        sel_ctx = ExitStack()
        sc_tm = sel_ctx.enter_context(nc.sbuf_tensor("sc_tm", [128, NT], F32))
        sig_tm = sel_ctx.enter_context(nc.sbuf_tensor("sig_tm", [128, NT], F32))
        tau_t = sel_ctx.enter_context(nc.sbuf_tensor("tau_t", [1, 2], F32))
        tau_b = sel_ctx.enter_context(nc.sbuf_tensor("tau_b", [128, 1], F32))
        cnt_t = sel_ctx.enter_context(nc.sbuf_tensor("cnt_t", [1, 1], F32))
        sel16 = sel_ctx.enter_context(nc.sbuf_tensor("sel16", [16, CAP], I16))
        self32 = sel_ctx.enter_context(nc.sbuf_tensor("self32", [1, CAP], F32))
        scatf = sel_ctx.enter_context(nc.sbuf_tensor("scatf", [1, CAP], F32))
        cge = sel_ctx.enter_context(nc.sbuf_tensor("cge", [1, CAP], F32))
        scat16 = sel_ctx.enter_context(nc.sbuf_tensor("scat16", [1, CAP], I16))
        scat_b = sel_ctx.enter_context(nc.sbuf_tensor("scat_b", [128, CAP], I16))
        gidx = sel_ctx.enter_context(nc.sbuf_tensor("gidx", [128, CAP // 16], I16))
        dfm = sel_ctx.enter_context(nc.sbuf_tensor("dfm", [128, 16 * 1024], BF16))
        p4 = ExitStack()
        wo_sb = p4.enter_context(nc.sbuf_tensor("wo_sb", [128, 2 * D], BF16))
        rw_row = p4.enter_context(nc.sbuf_tensor("rw_row", [1, D], BF16))
        rwb = p4.enter_context(nc.sbuf_tensor("rwb", [128, D], BF16))
        ao_fm = p4.enter_context(nc.sbuf_tensor("ao_fm", [128, T], BF16))
        ao_tm = p4.enter_context(nc.sbuf_tensor("ao_tm", [128, NT * D], BF16))
        x0r = [p4.enter_context(nc.sbuf_tensor(f"x0r{i}", [128, D], F32)) for i in range(2)]
        x1t = [p4.enter_context(nc.sbuf_tensor(f"x1t{i}", [128, D], F32)) for i in range(2)]
        ht = [p4.enter_context(nc.sbuf_tensor(f"ht{i}", [128, D], BF16)) for i in range(2)]
        st6b = p4.enter_context(nc.sbuf_tensor("st6b", [128, 24], F32))
        st2b = p4.enter_context(nc.sbuf_tensor("st2b", [128, 2], F32))
        lrs2 = p4.enter_context(nc.sbuf_tensor("lrs2", [128, 1], F32))
        lnm2 = p4.enter_context(nc.sbuf_tensor("lnm2", [128, 1], F32))

        def rbpb(g):
            g.wait_ge(sP.h, mF4)
            DQ.wait(g, tC0[10])
            _i(g.partition_broadcast(rbb[:, :], rb_sb[0:1, 0:1]), sP)
            tok = dqG.issue(g, out=rw_row[:, :],
                            in_=rw_d.ap().rearrange("(o x) -> o x", o=1))
            DQ.wait(g, tok)
            _i(g.partition_broadcast(rwb[:, :], rw_row[0:1, :]), sP)
        GE(rbpb)
        mRBB = sP.v

        tWOC = [None] * 16
        mWOev = [0] * 16
        mAOtr = [0] * 16
        for oc in range(16):
            def wod(g, oc=oc):
                if oc < 2:
                    g.wait_ge(sP.h, mF4)
                else:
                    g.wait_ge(sT.h, mWOmm[oc - 2])
                tWOC[oc] = dqG.issue(
                    g, out=wo_sb[:, (oc % 2) * D:(oc % 2 + 1) * D]
                    .rearrange("p (c j) -> p c j", c=16),
                    in_=w_o_d.ap().rearrange("(c p) j -> p c j", p=128)
                    [:, :, 128 * oc:128 * (oc + 1)])
            GE(wod)

            def womm(t, oc=oc):
                DQ.wait(t, tWOC[oc])
                if oc == 0:
                    t.wait_ge(sV.h, mOFM)
                if oc >= 2:
                    t.wait_ge(sV.h, mWOev[oc - 2])
                psX = psA if oc % 2 == 0 else psB
                for half in range(2):
                    for ic in range(16):
                        t.matmul(psX[:, 512 * half:512 * (half + 1)],
                                 wo_sb[:, (oc % 2) * D:(oc % 2 + 1) * D]
                                 .rearrange("p (c j) -> p c j", c=16)[:, ic, :],
                                 o_fm[:, :].rearrange("p (c m) -> p c m", c=16)
                                 [:, ic, 512 * half:512 * (half + 1)],
                                 start=(ic == 0), stop=(ic == 15))
                _i(t.drain(), sT)
            TE(womm)
            if oc == 0:
                mWOmm = [0] * 16
            mWOmm[oc] = sT.v

            def woev(v, oc=oc, m=mWOmm[oc]):
                v.wait_ge(sT.h, m)
                if oc >= 2:
                    v.wait_ge(sT.h, mAOtr[oc - 2])
                psX = psA if oc % 2 == 0 else psB
                _i(v.tensor_copy(ao_fm[:, :], psX[:, :]), sV)
            VE(woev)
            mWOev[oc] = sV.v

            def aotr(t, oc=oc, m=mWOev[oc]):
                t.wait_ge(sV.h, m)
                pcb = psC.ap().bitcast(BF16)
                for half in range(2):
                    for z in range(4):
                        j = 4 * half + z
                        t.transpose(pcb[:, 1024 * half + 128 * z:1024 * half + 128 * (z + 1)],
                                    ao_fm[:, 128 * j:128 * (j + 1)], ident[:, :])
                    _i(t.drain(), sT)
            TE(aotr)
            mAOtr[oc] = sT.v
            mtr2 = (sT.v - 1, sT.v)

            def aoev(v, oc=oc, mtr2=mtr2):
                for half in range(2):
                    v.wait_ge(sT.h, mtr2[half])
                    _i(v.tensor_copy(
                        ao_tm[:, :].rearrange("p (j m) -> p j m", j=NT)
                        [:, 4 * half:4 * half + 4, 128 * oc:128 * (oc + 1)],
                        psC.ap().bitcast(BF16)[:, 1024 * half:1024 * half + 512]
                        .rearrange("p (j m) -> p j m", j=4)), sV)
            VE(aoev)
        mAOTM = sV.v
        ofm_ctx.close()
        sel_tr = ExitStack()
        sc_all = sel_tr.enter_context(nc.sbuf_tensor("sc_all", [128, 64], F32, side="right"))
        kthtmp = sel_tr.enter_context(nc.sbuf_tensor("kthtmp", [128, 64], F32, side="right"))
        selmask = sel_tr.enter_context(nc.sbuf_tensor("selmask", [1, T], F32, side="right"))
        sc_own = sel_tr.enter_context(nc.sbuf_tensor("sc_own", [1, T], F32, side="right"))
        pref_a = sel_tr.enter_context(nc.sbuf_tensor("pref_a", [1, T], F32, side="right"))
        pref_b = sel_tr.enter_context(nc.sbuf_tensor("pref_b", [1, T], F32, side="right"))
        ranks = sel_tr.enter_context(nc.sbuf_tensor("ranks", [1, T], F32, side="right"))
        r16b = sel_tr.enter_context(nc.sbuf_tensor("r16b", [16, T], I16, side="right"))
        bs_lo = sel_tr.enter_context(nc.sbuf_tensor("bs_lo", [128, 1], F32, side="right"))
        bs_hi = sel_tr.enter_context(nc.sbuf_tensor("bs_hi", [128, 1], F32, side="right"))
        bs_mid = sel_tr.enter_context(nc.sbuf_tensor("bs_mid", [128, 1], F32, side="right"))
        bs_fl = sel_tr.enter_context(nc.sbuf_tensor("bs_fl", [128, 1], F32, side="right"))
        bs_pc = sel_tr.enter_context(nc.sbuf_tensor("bs_pc", [128, 1], F32, side="right"))

        tX1S = [None] * NT
        tHS = [None] * NT
        mX1f = [0, 0]
        mHtf = [0, 0]
        for j in range(NT):
            par = j % 2
            tx0 = [None]

            def ldx0(sy, j=j, par=par, w=mX1f[par]):
                if w:
                    sy.wait_ge(sV.h, w)
                else:
                    sy.wait_ge(sP.h, mF4)
                tx0[0] = dqQ.issue(sy, out=x0r[par][:, :],
                                   in_=x0_d[128 * j:128 * (j + 1), :])
            SY(ldx0)
            tx0 = tx0[0]

            def ln2a(v, j=j, mAOTM=mAOTM):
                v.wait_ge(sV.h, mAOTM)
                tl = ao_tm[:, :].rearrange("p (j m) -> p j m", j=NT)[:, j, :]
                for q in range(4):
                    v.bn_stats(st6b[:, 6 * q:6 * q + 6], tl[:, 512 * q:512 * (q + 1)])
                v.drain()
                v.bn_aggr(st2b[:, :], st6b[:, :])
                v.drain()
                v.tensor_scalar(lrs2[:, :], st2b[:, 1:2], EPS_LN, None, ALU.add)
                v.drain()
                _i(v.reciprocal(lrs2[:, :], lrs2[:, :]), sV)
            VE(ln2a)
            m1 = sV.v

            def ln2b(sc, m1=m1):
                sc.wait_ge(sV.h, m1)
                _i(sc.activation(lrs2[:, :], lrs2[:, :], ACT.Sqrt), sS)
            SE(ln2b)
            m2 = sS.v

            def ln2c(v, j=j, par=par, m2=m2, tx0=tx0):
                v.wait_ge(sS.h, m2)
                v.tensor_mul(lnm2[:, :], st2b[:, 0:1], lrs2[:, :])
                v.drain()
                DQ.wait(v, tx0)
                tl = ao_tm[:, :].rearrange("p (j m) -> p j m", j=NT)[:, j, :]
                v.scalar_tensor_tensor(x1t[par][:, :], tl, lrs2[:, 0:1], x0r[par][:, :],
                                       ALU.mult, ALU.add)
                v.drain()
                _i(v.tensor_scalar(x1t[par][:, :], x1t[par][:, :], lnm2[:, 0:1],
                                   None, ALU.subtract), sV)
            VE(ln2c)
            mX1 = sV.v
            mX1f[par] = mX1

            def spx1(sy, j=j, par=par, mX1=mX1):
                sy.wait_ge(sV.h, mX1)
                tX1S[j] = dqQ.issue(sy, out=x1_dram[128 * j:128 * (j + 1), :],
                                    in_=x1t[par][:, :])
            SY(spx1)

            def ln3a(v, j=j, par=par, mX1=mX1):
                v.drain()
                for q in range(4):
                    v.bn_stats(st6b[:, 6 * q:6 * q + 6],
                               x1t[par][:, 512 * q:512 * (q + 1)])
                v.drain()
                v.bn_aggr(st2b[:, :], st6b[:, :])
                v.drain()
                v.tensor_scalar(lrs2[:, :], st2b[:, 1:2], EPS_LN, None, ALU.add)
                v.drain()
                _i(v.reciprocal(lrs2[:, :], lrs2[:, :]), sV)
            VE(ln3a)
            m3 = sV.v

            def ln3b(sc, m3=m3):
                sc.wait_ge(sV.h, m3)
                _i(sc.activation(lrs2[:, :], lrs2[:, :], ACT.Sqrt), sS)
            SE(ln3b)
            m4 = sS.v

            def ln3c(v, m4=m4):
                v.wait_ge(sS.h, m4)
                v.tensor_mul(lnm2[:, :], st2b[:, 0:1], lrs2[:, :])
                v.drain()
                _i(v.tensor_scalar(lnm2[:, :], lnm2[:, :], -1.0, None, ALU.mult), sV)
            VE(ln3c)
            m5 = sV.v

            def ln3d(sc, j=j, par=par, m5=m5, whf=mHtf[par],
                     tprev=(tHS[j - 2] if j >= 2 else None)):
                sc.wait_ge(sV.h, m5)
                if whf:
                    sc.wait_ge(sV.h, whf)
                if tprev is not None:
                    DQ.wait(sc, tprev)
                _i(sc.activation(ht[par][:, :], x1t[par][:, :], ACT.Identity,
                                 bias=lnm2[:, 0:1], scale=lrs2[:, 0:1]), sS)
            SE(ln3d)
            m6 = sS.v

            def sth(sy, j=j, par=par, m6=m6):
                sy.wait_ge(sS.h, m6)
                tHS[j] = dqQ.issue(sy, out=h_dram[128 * j:128 * (j + 1), :],
                                   in_=ht[par][:, :])
            SY(sth)

            def scj(v, j=j, par=par, m6=m6):
                v.wait_ge(sS.h, m6)
                DQ.wait(v, tX1S[j])
                v.tensor_tensor(x1t[par][:, :], ht[par][:, :], rwb[:, :], ALU.mult)
                v.drain()
                _i(v.reduce_sum(sc_tm[:, j:j + 1], x1t[par][:, :], axis=AX), sV)
            VE(scj)
            mHtf[par] = sV.v
        mSCT = sV.v

        def scf(v, mSCT=mSCT):
            v.wait_ge(sV.h, mSCT)
            v.wait_ge(sP.h, mRBB)
            _i(v.tensor_scalar(sc_tm[:, :], sc_tm[:, :], rbb[:, 0:1], None, ALU.add),
               sV)
        VE(scf)
        mSCF = sV.v

        def sgf(sc, mSCF=mSCF):
            sc.wait_ge(sV.h, mSCF)
            _i(sc.activation(sig_tm[:, :], sc_tm[:, :], ACT.Sigmoid), sS)
        SE(sgf)
        mSIG = sS.v

        if stage == 4:
            def s4(sy):
                for j in range(NT):
                    DQ.wait(sy, tX1S[j])
                tk = dqQ.issue(sy, out=dbg[:, :], in_=x1_dram[:, :])
                DQ.wait(sy, tk)
            SY(s4)
            p4.close()
            sel_tr.close()
            sel_ctx.close()
            return nc

        tSC = [None]

        def scout(sy, mSCF=mSCF):
            sy.wait_ge(sV.h, mSCF)
            tSC[0] = dqQ.issue(sy, out=agsc_in.ap().rearrange("(j p) -> p j", p=128),
                               in_=sc_tm[:, :], allow_slow_non_contiguous=True)
        SY(scout)

        def agsc(g):
            DQ.wait(g, tSC[0])
            _i(g.collective_compute(
                "AllGather", ALU.bypass,
                replica_groups=[list(range(NCORES))],
                ins=[agsc_in.ap().opt()],
                outs=[agsc_out.ap().opt()]), sCC)
        GE(agsc)
        mAGS = sCC.v

        tSCA = [None]
        tSCO = [None]

        def scload(sy):
            sy.wait_ge(sCC.h, mAGS)
            tSCA[0] = dqQ.issue(sy, out=sc_all[:, :],
                                in_=agsc_out.ap().rearrange("(p f) -> p f", p=128))
            tSCO[0] = dqQ.issue(sy, out=sc_own[:, :],
                                in_=agsc_in.ap().rearrange("(o x) -> o x", o=1))
        SY(scload)

        N_BS = 56

        def bsinit(v):
            DQ.wait(v, tSCA[0])
            _i(v.reduce_max(bs_lo[:, :], sc_all[:, :], axis=AX,
                            apply_absolute_value=True), sV)
        VE(bsinit)
        mBS0 = sV.v

        def bssum(t, mBS0=mBS0):
            t.wait_ge(sV.h, mBS0)
            _i(t.matmul(psM[:, 320:321], onesf[:, :], bs_lo[:, :],
                        start=True, stop=True), sT)
        TE(bssum)
        mBS1 = sT.v

        def bsrng(v, mBS1=mBS1):
            v.wait_ge(sT.h, mBS1)
            v.tensor_scalar(bs_hi[:, :], psM[:, 320:321], 1.0, None, ALU.add)
            v.tensor_scalar(bs_lo[:, :], psM[:, 320:321], -1.0, -1.0,
                            ALU.mult, ALU.add)
            v.drain()
            v.tensor_add(bs_mid[:, :], bs_lo[:, :], bs_hi[:, :])
            v.drain()
            v.tensor_scalar(bs_mid[:, :], bs_mid[:, :], 0.5, None, ALU.mult)
            v.drain()
            v.tensor_scalar(kthtmp[:, :], sc_all[:, :], bs_mid[:, 0:1], None,
                            ALU.is_ge)
            v.drain()
            _i(v.reduce_sum(bs_pc[:, :], kthtmp[:, :], axis=AX), sV)
        VE(bsrng)
        mBSit = sV.v

        for it in range(N_BS):
            def bcnt(t, m=mBSit):
                t.wait_ge(sV.h, m)
                _i(t.matmul(psM[:, 320:321], onesf[:, :], bs_pc[:, :],
                            start=True, stop=True), sT)
            TE(bcnt)
            mPE = sT.v

            def bupd(v, mPE=mPE, last=(it == N_BS - 1)):
                v.wait_ge(sT.h, mPE)
                v.tensor_scalar(bs_fl[:, :], psM[:, 320:321], float(KSEL), None,
                                ALU.is_ge)
                v.drain()
                v.tensor_sub(kthtmp[:, 0:1], bs_mid[:, :], bs_lo[:, :])
                v.tensor_sub(kthtmp[:, 1:2], bs_mid[:, :], bs_hi[:, :])
                v.tensor_scalar(kthtmp[:, 2:3], bs_fl[:, :], -1.0, 1.0,
                                ALU.mult, ALU.add)
                v.drain()
                v.scalar_tensor_tensor(bs_lo[:, :], kthtmp[:, 0:1], bs_fl[:, 0:1],
                                       bs_lo[:, :], ALU.mult, ALU.add)
                v.scalar_tensor_tensor(bs_hi[:, :], kthtmp[:, 1:2], kthtmp[:, 2:3],
                                       bs_hi[:, :], ALU.mult, ALU.add)
                v.drain()
                if not last:
                    v.tensor_add(bs_mid[:, :], bs_lo[:, :], bs_hi[:, :])
                    v.drain()
                    v.tensor_scalar(bs_mid[:, :], bs_mid[:, :], 0.5, None, ALU.mult)
                    v.drain()
                    v.tensor_scalar(kthtmp[:, :], sc_all[:, :], bs_mid[:, 0:1], None,
                                    ALU.is_ge)
                    v.drain()
                    _i(v.reduce_sum(bs_pc[:, :], kthtmp[:, :], axis=AX), sV)
                else:
                    v.tensor_copy(tau_t[0:1, 0:1], bs_lo[0:1, 0:1])
                    _i(v.tensor_copy(tau_t[0:1, 1:2], bs_hi[0:1, 0:1]), sV)
            VE(bupd)
            mBSit = sV.v
        mKth4 = mBSit

        def selv(v):
            v.wait_ge(sV.h, mKth4)
            v.drain()
            DQ.wait(v, tSCO[0])
            v.tensor_scalar(selmask[:, :], sc_own[:, :], tau_t[0:1, 0:1], None,
                            ALU.is_ge)
            v.drain()
            v.reduce_sum(cnt_t[:, :], selmask[:, :], axis=AX)
            v.tensor_copy(pref_a[:, :], selmask[:, :])
            v.drain()
            src_, dst_ = pref_a, pref_b
            for sh in [1, 2, 4, 8, 16, 32, 64, 128, 256, 512]:
                v.tensor_copy(dst_[:, 0:sh], src_[:, 0:sh])
                v.tensor_tensor(dst_[:, sh:T], src_[:, sh:T], src_[:, 0:T - sh],
                                ALU.add)
                v.drain()
                src_, dst_ = dst_, src_
            v.tensor_sub(ranks[:, :], src_[:, :], selmask[:, :])
            v.drain()
            v.tensor_scalar(pref_b[:, :], ranks[:, :], float(CAP), None, ALU.is_lt)
            v.drain()
            v.tensor_mul(pref_b[:, :], pref_b[:, :], selmask[:, :])
            v.tensor_scalar(pref_a[:, :], ranks[:, :], 1.0, None, ALU.add)
            v.drain()
            v.tensor_mul(pref_a[:, :], pref_a[:, :], pref_b[:, :])
            v.drain()
            v.tensor_scalar(pref_a[:, :], pref_a[:, :], 1.0, None, ALU.subtract)
            v.memset(r16b[:, :], -1)
            v.drain()
            _i(v.tensor_copy(r16b[0:1, :], pref_a[:, :]), sV)
        VE(selv)
        mRANKS = sV.v

        def comp(g):
            g.wait_ge(sV.h, mRANKS)
            g.load_library(libcfg.local_scatter)
            _i(g.local_scatter(sel16[:, :], tok16[:, :], r16b[:, :], channels=16,
                               num_elems=CAP, num_idxs=T), sP)
            g.wait_ge(sP.h, sP.v)
            t1 = dqG.issue(g, out=idx_dram[:, :], in_=sel16[0:1, :])
            DQ.wait(g, t1)
            g.memset(gidx[:, :], 0)
            g.drain()
            toks = []
            for b in range(8):
                toks.append(dqG.issue(
                    g, out=gidx[16 * b:16 * (b + 1), :],
                    in_=idx_dram.ap().rearrange("o (f p) -> (o p) f", p=16),
                    allow_slow_non_contiguous=True))
            for tk in toks:
                DQ.wait(g, tk)
            _i(g.drain(), sP)
        GE(comp)
        mGIDX = sP.v

        def selw(v):
            v.wait_ge(sP.h, mGIDX)
            v.tensor_copy(self32[:, :], sel16[0:1, :])
            v.drain()
            v.tensor_scalar(cge[:, :], capio[:, :], cnt_t[0:1, 0:1], None, ALU.is_ge)
            v.tensor_scalar(scatf[:, :], self32[:, :], 1.0, None, ALU.add)
            v.drain()
            v.tensor_mul(scatf[:, :], scatf[:, :], cge[:, :])
            v.drain()
            v.tensor_sub(scatf[:, :], self32[:, :], scatf[:, :])
            v.drain()
            _i(v.tensor_copy(scat16[:, :], scatf[:, :]), sV)
        VE(selw)
        mSCAT = sV.v

        if stage == 5:
            def s5(sy):
                sy.wait_ge(sV.h, mSCAT)
                t1 = dqQ.issue(sy, out=dbg[0:1, :], in_=sc_own[:, :])
                t2 = dqQ.issue(sy, out=dbg[1:2, :], in_=selmask[:, :])
                t3 = dqQ.issue(sy, out=dbg[2:3, :], in_=ranks[:, :])
                t4 = dqQ.issue(sy, out=dbg[3:4, 0:2], in_=tau_t[:, :])
                t5 = dqQ.issue(sy, out=dbg[3:4, 2:3], in_=cnt_t[:, :])
                t6 = dqQ.issue(sy, out=dbg[4:5, 0:CAP], in_=scatf[:, :])
                for tk in (t1, t2, t3, t4, t5, t6):
                    DQ.wait(sy, tk)
            SY(s5)
            p4.close()
            sel_tr.close()
            sel_ctx.close()
            return nc

        p4.close()
        sel_tr.close()
        mF5 = fence()

        f5 = ExitStack()
        hsel = f5.enter_context(nc.sbuf_tensor("hsel", [128, 16 * CAP], BF16))
        g_sb = f5.enter_context(nc.sbuf_tensor("g_sb", [128, 64 * CAP], BF16))
        w1c = [f5.enter_context(nc.sbuf_tensor(f"w1c{i}", [128, 16 * 512], BF16))
               for i in range(2)]
        w2c = [f5.enter_context(nc.sbuf_tensor(f"w2c{i}", [128, 64 * 128], BF16))
               for i in range(2)]
        wd2 = f5.enter_context(nc.sbuf_tensor("wd2", [128, 16 * CAP], BF16))

        def hgat(g):
            g.wait_ge(sP.h, mF5)
            g.load_library(libcfg.attnmlp)
            g.wait_ge(sP.h, mGIDX)
            for j in range(NT):
                DQ.wait(g, tHS[j])
            g.wait_ge(sS.h, mSIG)
            g.wait_ge(sV.h, mSCAT)
            s = dqG.sems[dqG.n % dqG.ns]
            if s.v > 0:
                g.wait_ge(s.h, s.v)
            g.dma_gather(hsel[:, :].rearrange("p (c i) -> p c i", c=16),
                         h_dram[:, :], gidx[:, :], num_idxs=CAP, num_idxs_reg=CAP,
                         elem_size=D, transpose=True).then_inc(s.h, 16)
            s.v += 16
            dqG.n += 1
            g.wait_ge(s.h, s.v)
            _i(g.drain(), sP)
        GE(hgat)
        mHSEL = sP.v

        mW1mm = [0] * 16
        mGelQ = [0] * 64
        tW1 = [None] * 16
        for dc in range(16):
            def w1l(g, dc=dc):
                if dc < 2:
                    g.wait_ge(sP.h, mF5)
                    for tk in tCast:
                        DQ.wait(g, tk)
                else:
                    g.wait_ge(sT.h, mW1mm[dc - 2])
                tW1[dc] = dqG.issue(
                    g, out=w1c[dc % 2][:, :].rearrange("p (c j) -> p c j", c=16),
                    in_=w1b_d.ap().rearrange("(c p) j -> p c j", p=128)
                    [:, :, 512 * dc:512 * (dc + 1)])
            GE(w1l)
            for q in range(4):
                qq = 4 * dc + q

                def d1mm(t, dc=dc, q=q, qq=qq):
                    if q == 0:
                        DQ.wait(t, tW1[dc])
                    if qq == 0:
                        t.wait_ge(sP.h, mHSEL)
                    if qq >= 2:
                        t.wait_ge(sS.h, mGelQ[qq - 2])
                    psX = psA if qq % 2 == 0 else psB
                    for ic in range(16):
                        t.matmul(psX[:, 0:CAP],
                                 w1c[dc % 2][:, :].rearrange("p (c j) -> p c j", c=16)
                                 [:, ic, 128 * q:128 * (q + 1)],
                                 hsel[:, :].rearrange("p (c i) -> p c i", c=16)
                                 [:, ic, :],
                                 start=(ic == 0), stop=(ic == 15))
                    _i(t.drain(), sT)
                TE(d1mm)
                md1 = sT.v

                def gel(sc, qq=qq, md1=md1):
                    sc.wait_ge(sT.h, md1)
                    psX = psA if qq % 2 == 0 else psB
                    _i(sc.activation(
                        g_sb[:, :].rearrange("p (c i) -> p c i", c=64)[:, qq, :],
                        psX[:, 0:CAP], ACT.Gelu_apprx_tanh,
                        bias=b1_sb[:, qq:qq + 1]), sS)
                SE(gel)
                mGelQ[qq] = sS.v
            mW1mm[dc] = sT.v
        mGELL = sS.v

        mW2mm = [0] * 16
        mW2ev = [0] * 16
        tW2 = [None] * 16
        for oc in range(16):
            def w2l(g, oc=oc):
                if oc < 2:
                    g.wait_ge(sP.h, mF5)
                else:
                    g.wait_ge(sT.h, mW2mm[oc - 2])
                tW2[oc] = dqG.issue(
                    g, out=w2c[oc % 2][:, :].rearrange("p (c j) -> p c j", c=64),
                    in_=w2b_d.ap().rearrange("(c p) j -> p c j", p=128)
                    [:, :, 128 * oc:128 * (oc + 1)])
            GE(w2l)

            def d2mm(t, oc=oc):
                DQ.wait(t, tW2[oc])
                t.wait_ge(sS.h, mGELL)
                if oc >= 2:
                    t.wait_ge(sV.h, mW2ev[oc - 2])
                psX = psA if oc % 2 == 0 else psB
                for d in range(64):
                    t.matmul(psX[:, 0:CAP],
                             w2c[oc % 2][:, :].rearrange("p (c j) -> p c j", c=64)
                             [:, d, :],
                             g_sb[:, :].rearrange("p (c i) -> p c i", c=64)[:, d, :],
                             start=(d == 0), stop=(d == 63))
                _i(t.drain(), sT)
            TE(d2mm)
            mW2mm[oc] = sT.v

            def d2ev(v, oc=oc, m=mW2mm[oc]):
                v.wait_ge(sT.h, m)
                psX = psA if oc % 2 == 0 else psB
                _i(v.tensor_scalar(
                    wd2[:, :].rearrange("p (c i) -> p c i", c=16)[:, oc, :],
                    psX[:, 0:CAP], b2_sb[:, oc:oc + 1], None, ALU.add), sV)
            VE(d2ev)
            mW2ev[oc] = sV.v
        mWD2 = sV.v

        def dsc(g):
            g.wait_ge(sV.h, mWD2)
            _i(g.partition_broadcast(scat_b[:, :], scat16[0:1, :]), sP)
            g.wait_ge(sP.h, sP.v)
            g.load_library(libcfg.local_scatter)
            for fc in range(16):
                _i(g.local_scatter(
                    dfm[:, :].rearrange("p (c m) -> p c m", c=16)[:, fc, :],
                    wd2[:, :].rearrange("p (c i) -> p c i", c=16)[:, fc, :],
                    scat_b[:, :], channels=128, num_elems=1024, num_idxs=CAP), sP)
            _i(g.drain(), sP)
        GE(dsc)
        mDFM = sP.v

        f5.close()
        fin = ExitStack()
        dtm = [fin.enter_context(nc.sbuf_tensor(f"dtm{i}", [128, D], BF16))
               for i in range(2)]
        x1r = [fin.enter_context(nc.sbuf_tensor(f"x1r{i}", [128, D], F32))
               for i in range(2)]
        hr = [fin.enter_context(nc.sbuf_tensor(f"hr{i}", [128, D], BF16))
              for i in range(2)]
        ft = fin.enter_context(nc.sbuf_tensor("ft", [128, D], F32))
        yt = [fin.enter_context(nc.sbuf_tensor(f"yt{i}", [128, D], F32))
              for i in range(2)]
        st6c = fin.enter_context(nc.sbuf_tensor("st6c", [128, 24], F32))
        st2c = fin.enter_context(nc.sbuf_tensor("st2c", [128, 2], F32))
        lrs3 = fin.enter_context(nc.sbuf_tensor("lrs3", [128, 1], F32))
        lnm3 = fin.enter_context(nc.sbuf_tensor("lnm3", [128, 1], F32))

        tY = [None] * NT
        mDT = [0, 0]
        mYW = [0, 0]
        gDb = {}
        gbc = [0]
        for j in range(NT):
            par = j % 2
            tld = [None, None]

            def ldf(sy, j=j, par=par, wx=mYW[par]):
                sy.wait_ge(sP.h, mDFM)
                if wx:
                    sy.wait_ge(sV.h, wx)
                DQ.wait(sy, tX1S[j])
                tld[0] = dqQ.issue(sy, out=x1r[par][:, :],
                                   in_=x1_dram[128 * j:128 * (j + 1), :])
                tld[1] = dqQ.issue(sy, out=hr[par][:, :],
                                   in_=h_dram[128 * j:128 * (j + 1), :])
            SY(ldf)
            tldx, tldh = tld

            for b in range(4):
                gb = gbc[0]

                def dtr(t, j=j, b=b, gb=gb, wv=(mDT[par] if b == 0 else 0),
                        wpt=gDb.get(gb - 2)):
                    if b == 0:
                        t.wait_ge(sP.h, mDFM)
                    if wv:
                        t.wait_ge(sV.h, wv)
                    if wpt:
                        t.wait_ge(sV.h, wpt)
                    base = 1024 * (b % 2)
                    pcb = psC.ap().bitcast(BF16)
                    for z in range(4):
                        fc = 4 * b + z
                        t.transpose(pcb[:, base + 128 * z:base + 128 * (z + 1)],
                                    dfm[:, :].rearrange("p (c m) -> p c m", c=16)
                                    [:, fc, 128 * j:128 * (j + 1)], ident[:, :])
                    _i(t.drain(), sT)
                TE(dtr)
                mtr = sT.v

                def dte(v, par=par, b=b, mtr=mtr):
                    v.wait_ge(sT.h, mtr)
                    base = 1024 * (b % 2)
                    _i(v.tensor_copy(
                        dtm[par][:, 512 * b:512 * (b + 1)],
                        psC.ap().bitcast(BF16)[:, base:base + 512]), sV)
                VE(dte)
                gDb[gb] = sV.v
                gbc[0] += 1
            mDT[par] = sV.v
            mDTj = sV.v

            def ffo(v, j=j, par=par, mDTj=mDTj, tldx=tldx, tldh=tldh):
                v.wait_ge(sV.h, mDTj)
                DQ.wait(v, tldh)
                v.scalar_tensor_tensor(ft[:, :], dtm[par][:, :], sig_tm[:, j:j + 1],
                                       hr[par][:, :], ALU.mult, ALU.add)
                v.drain()
                for q in range(4):
                    v.bn_stats(st6c[:, 6 * q:6 * q + 6], ft[:, 512 * q:512 * (q + 1)])
                v.drain()
                v.bn_aggr(st2c[:, :], st6c[:, :])
                v.drain()
                v.tensor_scalar(lrs3[:, :], st2c[:, 1:2], EPS_LN, None, ALU.add)
                v.drain()
                _i(v.reciprocal(lrs3[:, :], lrs3[:, :]), sV)
            VE(ffo)
            m1 = sV.v

            def fsq(sc, m1=m1):
                sc.wait_ge(sV.h, m1)
                _i(sc.activation(lrs3[:, :], lrs3[:, :], ACT.Sqrt), sS)
            SE(fsq)
            m2 = sS.v

            def fy(v, j=j, par=par, m2=m2, tldx=tldx,
                   typ=(tY[j - 2] if j >= 2 else None)):
                v.wait_ge(sS.h, m2)
                if typ is not None:
                    DQ.wait(v, typ)
                v.tensor_mul(lnm3[:, :], st2c[:, 0:1], lrs3[:, :])
                v.drain()
                DQ.wait(v, tldx)
                v.scalar_tensor_tensor(yt[par][:, :], ft[:, :], lrs3[:, 0:1],
                                       x1r[par][:, :], ALU.mult, ALU.add)
                v.drain()
                _i(v.tensor_scalar(yt[par][:, :], yt[par][:, :], lnm3[:, 0:1],
                                   None, ALU.subtract), sV)
            VE(fy)
            mYW[par] = sV.v
            mY = sV.v

            def sty(sy, j=j, par=par, mY=mY):
                sy.wait_ge(sV.h, mY)
                tY[j] = dqQ.issue(sy, out=y_d[128 * j:128 * (j + 1), :],
                                  in_=yt[par][:, :])
            SY(sty)

        def fwait(sy):
            for j in range(NT):
                DQ.wait(sy, tY[j])
        SY(fwait)

        fin.close()
        sel_ctx.close()
        return nc
    return nc



def _consts(core):
    pos = (np.arange(T) * NCORES + core).astype(np.float64)
    inv = 1.0 / (10000.0 ** (np.arange(0, ROPE, 2, dtype=np.float64) / ROPE))
    fr = np.outer(inv, pos)
    cos = np.cos(fr)
    sin = np.sin(fr)
    ropeCh = np.tile(cos, (4, 1)).astype(np.float32)
    ropeSh = np.concatenate([np.tile(-sin, 1), sin], axis=0)
    ropeSh = np.tile(np.concatenate([-sin, sin], axis=0), (2, 1)).astype(np.float32)
    p = np.arange(128)[:, None]
    c = np.tile(np.arange(128)[None, :], (1, 8)).reshape(1, 1024)
    st = np.repeat(np.arange(8), 128).reshape(1, 1024)
    allow = (p < c) | ((p == c) & (st <= core))
    dm = np.where(allow, 0.0, MASK_NEG).astype(np.float32)
    return ropeCh, ropeSh, dm


def kernel(**inputs):
    nc = build_nc(stage=6)
    return _run(nc, inputs, stage=6)["y"]


def _run(nc, inputs, stage):
    mybir.codegen_inst_isa_subclasses(nc)
    x = np.ascontiguousarray(np.asarray(inputs["L0_attn_x"], dtype=np.float32)[0])
    ident = np.eye(128, dtype=ml_dtypes.bfloat16)
    onesm = np.ones((128, 128), dtype=ml_dtypes.bfloat16)
    tok = np.arange(T, dtype=np.int16).reshape(1, T)
    cap = np.arange(CAP, dtype=np.float32).reshape(1, CAP)
    common = {
        "w_dq": np.asarray(inputs["L0_attn_w_dq"], np.float32),
        "w_uq": np.asarray(inputs["L0_attn_w_uq"], np.float32),
        "w_dkv": np.asarray(inputs["L0_attn_w_dkv"], np.float32),
        "w_kr": np.asarray(inputs["L0_attn_w_kr"], np.float32),
        "w_uk": np.asarray(inputs["L0_attn_w_uk"], np.float32),
        "w_uv": np.asarray(inputs["L0_attn_w_uv"], np.float32),
        "w_o": np.asarray(inputs["L0_attn_w_o"], np.float32),
        "g_q": np.asarray(inputs["L0_attn_g_q"], np.float32),
        "g_kv": np.asarray(inputs["L0_attn_g_kv"], np.float32),
        "w1": np.asarray(inputs["L0_ffn_w1"], np.float32),
        "b1": np.asarray(inputs["L0_ffn_b1"], np.float32),
        "w2": np.asarray(inputs["L0_ffn_w2"], np.float32),
        "b2": np.asarray(inputs["L0_ffn_b2"], np.float32),
        "rw": np.asarray(inputs["L0_ffn_router_weight"], np.float32),
        "rb": np.asarray(inputs["L0_ffn_router_bias"], np.float32),
        "ident": ident, "ones": onesm, "tok16": tok, "capio": cap,
        "id32f": np.eye(32, dtype=np.float32),
        "onesf": np.ones((128, 128), dtype=np.float32),
    }
    in_maps = []
    for i in range(NCORES):
        rc, rs, dm = _consts(i)
        m = dict(common)
        m["x0"] = np.ascontiguousarray(x[i::NCORES])
        m["ropeC"] = rc
        m["ropeS"] = rs
        m["dmask"] = dm
        in_maps.append(m)
    res = run_bass_kernel_spmd(nc, in_maps, core_ids=list(range(NCORES)))
    if stage == 6:
        out = np.empty((1, S, D), np.float32)
        for i in range(NCORES):
            out[0, i::NCORES, :] = res.results[i]["y"]
        return {"y": out}
    return {"res": res}


# version 06 (final)
# speedup vs baseline: 2.3433x; 1.1303x over previous
import math
from contextlib import ExitStack

import numpy as np
import ml_dtypes

import concourse.bass as bass
import concourse.mybir as mybir
import concourse.library_config as libcfg
from concourse.bass_utils import run_bass_kernel_spmd

F32 = mybir.dt.float32
BF16 = mybir.dt.bfloat16
I16 = mybir.dt.int16
AX = mybir.AxisListType.X
ALU = mybir.AluOpType
ACT = mybir.ActivationFunctionType

D = 2048
S = 8192
NCORES = 8
T = S // NCORES
NT = T // 128
H = 16
NOPE, ROPE, VD = 128, 64, 128
QKD = NOPE + ROPE
QL, KVL = 768, 512
DFF = 8192
CAP = 384
SCALE = QKD ** (-0.5)
EPS_LN = 1e-5
EPS_RMS = 1e-6
MASK_NEG = -1.0e30
NKC = 8
KSEL = 2048

KN_OFF = 0
KPE_OFF = KN_OFF + D * T
KN2_OFF = KPE_OFF + ROPE * T
V_OFF = KN2_OFF + 32
KVSH = V_OFF + H * T * VD


class Sem:
    def __init__(self, h):
        self.h = h
        self.v = 0


def _i(instr, s, n=1):
    instr.then_inc(s.h, n)
    s.v += n
    return s.v


class DQ:

    def __init__(self, nc, ctx, name, ns=8):
        self.sems = [Sem(ctx.enter_context(nc.semaphore(f"{name}{i}")))
                     for i in range(ns)]
        self.ns = ns
        self.n = 0

    def issue(self, eng, **kw):
        s = self.sems[self.n % self.ns]
        if s.v > 0:
            eng.wait_ge(s.h, s.v)
        eng.dma_start(**kw).then_inc(s.h, 16)
        s.v += 16
        self.n += 1
        return (s, s.v)

    @staticmethod
    def wait(eng, tok):
        eng.wait_ge(tok[0].h, tok[1])

    def wait_all(self, eng):
        for s in self.sems:
            if s.v > 0:
                eng.wait_ge(s.h, s.v)


def build_nc(stage=6):
    nc = bass.Bass("TRN2", target_bir_lowering=False, debug=False)

    x0_d = nc.dram_tensor("x0", [T, D], F32, kind="ExternalInput")
    w_dq_d = nc.dram_tensor("w_dq", [D, QL], F32, kind="ExternalInput")
    w_uq_d = nc.dram_tensor("w_uq", [QL, H * QKD], F32, kind="ExternalInput")
    w_dkv_d = nc.dram_tensor("w_dkv", [D, KVL], F32, kind="ExternalInput")
    w_kr_d = nc.dram_tensor("w_kr", [D, ROPE], F32, kind="ExternalInput")
    w_uk_d = nc.dram_tensor("w_uk", [KVL, H * NOPE], F32, kind="ExternalInput")
    w_uv_d = nc.dram_tensor("w_uv", [KVL, H * VD], F32, kind="ExternalInput")
    w_o_d = nc.dram_tensor("w_o", [H * VD, D], F32, kind="ExternalInput")
    g_q_d = nc.dram_tensor("g_q", [QL], F32, kind="ExternalInput")
    g_kv_d = nc.dram_tensor("g_kv", [KVL], F32, kind="ExternalInput")
    w1_d = nc.dram_tensor("w1", [D, DFF], F32, kind="ExternalInput")
    b1_d = nc.dram_tensor("b1", [DFF], F32, kind="ExternalInput")
    w2_d = nc.dram_tensor("w2", [DFF, D], F32, kind="ExternalInput")
    b2_d = nc.dram_tensor("b2", [D], F32, kind="ExternalInput")
    rw_d = nc.dram_tensor("rw", [D], F32, kind="ExternalInput")
    rb_d = nc.dram_tensor("rb", [1], F32, kind="ExternalInput")
    ropeC_d = nc.dram_tensor("ropeC", [128, T], F32, kind="ExternalInput")
    ropeS_d = nc.dram_tensor("ropeS", [128, T], F32, kind="ExternalInput")
    dmask_d = nc.dram_tensor("dmask", [128, 1024], F32, kind="ExternalInput")
    ident_d = nc.dram_tensor("ident", [128, 128], BF16, kind="ExternalInput")
    ones_d = nc.dram_tensor("ones", [128, 128], BF16, kind="ExternalInput")
    tok16_d = nc.dram_tensor("tok16", [1, T], I16, kind="ExternalInput")
    capio_d = nc.dram_tensor("capio", [1, CAP], F32, kind="ExternalInput")
    id32f_d = nc.dram_tensor("id32f", [32, 32], F32, kind="ExternalInput")
    onesf_d = nc.dram_tensor("onesf", [128, 128], F32, kind="ExternalInput")

    y_d = nc.dram_tensor("y", [T, D], F32, kind="ExternalOutput")
    dbg = None
    if stage == 2:
        dbg = nc.dram_tensor("dbg", [KVSH], BF16, kind="ExternalOutput")
    elif stage == 3:
        dbg = nc.dram_tensor("dbg", [128, H * T], BF16, kind="ExternalOutput")
    elif stage == 4:
        dbg = nc.dram_tensor("dbg", [T, D], F32, kind="ExternalOutput")
    elif stage == 5:
        dbg = nc.dram_tensor("dbg", [8, T], F32, kind="ExternalOutput")

    w1b_d = nc.dram_tensor("w1b", [D, DFF], BF16)
    w2b_d = nc.dram_tensor("w2b", [DFF, D], BF16)
    agkv_in = nc.dram_tensor("agkv_in", [KVSH], BF16)
    agkv_out = nc.dram_tensor("agkv_out", [NCORES * KVSH], BF16)
    agsc_in = nc.dram_tensor("agsc_in", [T], F32)
    agsc_out = nc.dram_tensor("agsc_out", [NCORES * T], F32)
    h_dram = nc.dram_tensor("h_dram", [T, D], BF16)
    x1_dram = nc.dram_tensor("x1_dram", [T, D], F32)
    idx_dram = nc.dram_tensor("idx_dram", [1, CAP], I16)

    ctx = ExitStack()
    with ctx:
        dqQ = DQ(nc, ctx, "sQ", ns=8)
        dqG = DQ(nc, ctx, "sG", ns=8)
        sCC = Sem(ctx.enter_context(nc.semaphore("sCC")))
        sT = Sem(ctx.enter_context(nc.semaphore("sT")))
        sV = Sem(ctx.enter_context(nc.semaphore("sV")))
        sS = Sem(ctx.enter_context(nc.semaphore("sS")))
        sP = Sem(ctx.enter_context(nc.semaphore("sP")))
        blk = ctx.enter_context(nc.Block())

        TE, VE, SE, GE, SY = blk.tensor, blk.vector, blk.scalar, blk.gpsimd, blk.sync

        def fence():
            snap = (sT.v, sV.v, sS.v)

            def fg(g, snap=snap):
                dqQ.wait_all(g)
                dqG.wait_all(g)
                g.wait_ge(sT.h, snap[0])
                g.wait_ge(sV.h, snap[1])
                g.wait_ge(sS.h, snap[2])
                _i(g.drain(), sP)
            GE(fg)
            return sP.v

        ident = ctx.enter_context(nc.sbuf_tensor("ident_s", [128, 128], BF16))
        ones = ctx.enter_context(nc.sbuf_tensor("ones_s", [128, 128], BF16))
        ropeC = ctx.enter_context(nc.sbuf_tensor("ropeC_s", [128, T], F32))
        ropeS = ctx.enter_context(nc.sbuf_tensor("ropeS_s", [128, T], F32))
        dmask = ctx.enter_context(nc.sbuf_tensor("dmask_s", [128, 1024], F32))
        gq_sb = ctx.enter_context(nc.sbuf_tensor("gq_sb", [128, 6], F32))
        gkv_sb = ctx.enter_context(nc.sbuf_tensor("gkv_sb", [128, 4], F32))
        b1_sb = ctx.enter_context(nc.sbuf_tensor("b1_sb", [128, 64], F32))
        b2_sb = ctx.enter_context(nc.sbuf_tensor("b2_sb", [128, 16], F32))
        rb_sb = ctx.enter_context(nc.sbuf_tensor("rb_sb", [1, 1], F32))
        tok16 = ctx.enter_context(nc.sbuf_tensor("tok16_s", [16, T], I16))
        capio = ctx.enter_context(nc.sbuf_tensor("capio_s", [1, CAP], F32))
        id32f = ctx.enter_context(nc.sbuf_tensor("id32f_s", [32, 32], F32))
        onesf = ctx.enter_context(nc.sbuf_tensor("onesf_s", [128, 128], F32))

        psAB = ctx.enter_context(nc.psum_tensor("psAB", [128, 2048], F32))
        psA = psAB.ap()[:, 0:1024]
        psB = psAB.ap()[:, 1024:2048]
        psC = ctx.enter_context(nc.psum_tensor("psC", [128, 1024], F32))
        psO = ctx.enter_context(nc.psum_tensor("psO", [128, 512], F32))
        psM = ctx.enter_context(nc.psum_tensor("psM", [128, 512], F32))

        tC0 = []

        def c0(sy):
            for dst, src in ((ident, ident_d), (ones, ones_d), (ropeC, ropeC_d),
                             (ropeS, ropeS_d), (dmask, dmask_d), (capio, capio_d),
                             (id32f, id32f_d), (onesf, onesf_d)):
                tC0.append(dqQ.issue(sy, out=dst[:, :], in_=src[:, :]))
            for dst, src in ((gq_sb, g_q_d), (gkv_sb, g_kv_d),
                             (b1_sb, b1_d), (b2_sb, b2_d)):
                tC0.append(dqQ.issue(
                    sy, out=dst[:, :], in_=src.ap().rearrange("(c p) -> p c", p=128),
                    allow_slow_non_contiguous=True))
            tC0.append(dqQ.issue(sy, out=rb_sb[:, :],
                                 in_=rb_d.ap().rearrange("(o x) -> o x", o=1)))
            for b in range(16):
                tC0.append(dqQ.issue(sy, out=tok16[b:b + 1, :], in_=tok16_d[:, :]))
        SY(c0)
        tIdent = tC0

        def g0(g):
            g.load_library(libcfg.attnmlp)
        GE(g0)

        ofm_ctx = ExitStack()
        aqn = ofm_ctx.enter_context(nc.sbuf_tensor("aqn", [128, 6 * T], BF16, side="right"))
        o_fm = ofm_ctx.enter_context(nc.sbuf_tensor("o_fm", [128, H * T], BF16, side="right"))
        rbb = ctx.enter_context(nc.sbuf_tensor("rbb", [128, 1], F32))

        p2 = ExitStack()
        x0b = [p2.enter_context(nc.sbuf_tensor(f"x0b{i}", [128, D], F32)) for i in range(2)]
        st6 = p2.enter_context(nc.sbuf_tensor("st6", [128, 24], F32))
        st2 = p2.enter_context(nc.sbuf_tensor("st2", [128, 2], F32))
        lrs = p2.enter_context(nc.sbuf_tensor("lrs", [128, 1], F32))
        lnm = p2.enter_context(nc.sbuf_tensor("lnm", [128, 1], F32))
        t_tm = p2.enter_context(nc.sbuf_tensor("t_tm", [128, D], BF16))
        t_fm = p2.enter_context(nc.sbuf_tensor("t_fm", [128, 16 * T], BF16))
        ckvn = p2.enter_context(nc.sbuf_tensor("ckvn", [128, 4 * T], BF16))
        sqtmp = p2.enter_context(nc.sbuf_tensor("sqtmp", [128, T], BF16))
        rmss = p2.enter_context(nc.sbuf_tensor("rmss", [1, T], F32))
        rmsb = p2.enter_context(nc.sbuf_tensor("rmsb", [128, T], F32))
        knch = p2.enter_context(nc.sbuf_tensor("knch", [128, T], BF16))
        knsq = p2.enter_context(nc.sbuf_tensor("knsq", [128, T], BF16))
        kn2row = p2.enter_context(nc.sbuf_tensor("kn2row", [1, 32], F32))
        kn2rb = p2.enter_context(nc.sbuf_tensor("kn2rb", [1, 32], BF16))
        vch = p2.enter_context(nc.sbuf_tensor("vch", [128, T], BF16))
        vtt = p2.enter_context(nc.sbuf_tensor("vtt", [128, T], BF16))
        kpe0 = p2.enter_context(nc.sbuf_tensor("kpe0", [64, T], F32))
        kpet = p2.enter_context(nc.sbuf_tensor("kpet", [64, T], F32))
        kper = p2.enter_context(nc.sbuf_tensor("kper", [64, T], BF16))
        kpesq = p2.enter_context(nc.sbuf_tensor("kpesq", [64, T], BF16))
        p2a = ExitStack()
        wdkv = p2a.enter_context(nc.sbuf_tensor("wdkv", [128, 16 * KVL], BF16))
        wkr = p2a.enter_context(nc.sbuf_tensor("wkr", [128, 16 * 2 * ROPE], BF16))
        wuk = p2a.enter_context(nc.sbuf_tensor("wuk", [128, 4 * H * NOPE], BF16))
        wuv = p2a.enter_context(nc.sbuf_tensor("wuv", [128, 4 * H * VD], BF16))

        tW = []

        def wld(g):
            tW.append(dqG.issue(g, out=wdkv[:, :].rearrange("p (c j) -> p c j", c=16),
                                in_=w_dkv_d.ap().rearrange("(c p) j -> p c j", p=128)))
            tW.append(dqG.issue(g, out=wuk[:, :].rearrange("p (c j) -> p c j", c=4),
                                in_=w_uk_d.ap().rearrange("(c p) j -> p c j", p=128)))
            tW.append(dqG.issue(g, out=wuv[:, :].rearrange("p (c j) -> p c j", c=4),
                                in_=w_uv_d.ap().rearrange("(c p) j -> p c j", p=128)))
            tW.append(dqG.issue(
                g, out=wkr[:, :].rearrange("p (c u) -> p c u", c=16)[:, :, 0:ROPE],
                in_=w_kr_d.ap().rearrange("(c p) j -> p c j", p=128)))
            tW.append(dqG.issue(
                g, out=wkr[:, :].rearrange("p (c u) -> p c u", c=16)[:, :, ROPE:ROPE + 32],
                in_=w_kr_d.ap().rearrange("(c p) j -> p c j", p=128)[:, :, 32:64]))
            tW.append(dqG.issue(
                g, out=wkr[:, :].rearrange("p (c u) -> p c u", c=16)[:, :, ROPE + 32:2 * ROPE],
                in_=w_kr_d.ap().rearrange("(c p) j -> p c j", p=128)[:, :, 0:32]))
        GE(wld)

        mX0free = [None, None]
        for b in range(NT):
            par = b % 2

            tkx = [None]

            def ldx(sy, b=b, par=par, w=mX0free[par]):
                if w is not None:
                    sy.wait_ge(sS.h, w)
                tkx[0] = dqQ.issue(sy, out=x0b[par][:, :],
                                   in_=x0_d[128 * b:128 * (b + 1), :])
            SY(ldx)
            mx = tkx[0]

            def lns(v, par=par, mx=mx, wprev=mX0free[1 - par]):
                DQ.wait(v, mx)
                if wprev is not None:
                    v.wait_ge(sS.h, wprev)
                for q in range(4):
                    v.bn_stats(st6[:, 6 * q:6 * q + 6], x0b[par][:, 512 * q:512 * (q + 1)])
                v.drain()
                v.bn_aggr(st2[:, :], st6[:, :])
                v.drain()
                v.tensor_scalar(lrs[:, :], st2[:, 1:2], EPS_LN, None, ALU.add)
                v.drain()
                _i(v.reciprocal(lrs[:, :], lrs[:, :]), sV)
            VE(lns)
            m1 = sV.v

            def lnr(sc, m1=m1):
                sc.wait_ge(sV.h, m1)
                _i(sc.activation(lrs[:, :], lrs[:, :], ACT.Sqrt), sS)
            SE(lnr)
            m2 = sS.v

            def lnn(v, m2=m2):
                v.wait_ge(sS.h, m2)
                v.tensor_mul(lnm[:, :], st2[:, 0:1], lrs[:, :])
                v.drain()
                _i(v.tensor_scalar(lnm[:, :], lnm[:, :], -1.0, None, ALU.mult), sV)
            VE(lnn)
            m3 = sV.v

            def lnap(sc, par=par, m3=m3):
                sc.wait_ge(sV.h, m3)
                _i(sc.activation(t_tm[:, :], x0b[par][:, :], ACT.Identity,
                                 bias=lnm[:, 0:1], scale=lrs[:, 0:1]), sS)
            SE(lnap)
            m4 = sS.v
            mX0free[par] = m4

            def ttr(t, m4=m4, b=b):
                t.wait_ge(sS.h, m4)
                if b == 0:
                    DQ.wait(t, tIdent[0])
                pabB = psAB.ap().bitcast(BF16)
                for half in range(2):
                    pt = pabB[:, 2048 * half:2048 * half + 2048]
                    for c in range(8):
                        cc = 8 * half + c
                        t.transpose(pt[:, 128 * c:128 * (c + 1)],
                                    t_tm[:, 128 * cc:128 * (cc + 1)], ident[:, :])
                    _i(t.drain(), sT)
            TE(ttr)
            mh = (sT.v - 1, sT.v)

            def tev(v, mh=mh, b=b):
                for half in range(2):
                    v.wait_ge(sT.h, mh[half])
                    _i(v.tensor_copy(
                        t_fm[:, :].rearrange("p (c m) -> p c m", c=16)
                        [:, 8 * half:8 * half + 8, 128 * b:128 * (b + 1)],
                        psAB.ap().bitcast(BF16)
                        [:, 2048 * half:2048 * half + 1024]
                        .rearrange("p (c m) -> p c m", c=8)), sV)
            VE(tev)
        mTFM = sV.v

        def proj_rms(wsb, nchunk, gcol_sb, buf):
            vgate = sV.v
            sgate = sS.v
            mEv = []
            mSq = []
            for oc in range(nchunk):
                ps_full = psA if oc % 2 == 0 else psB
                def mm(t, oc=oc, ps_full=ps_full):
                    if oc == 0:
                        for tk in tW:
                            DQ.wait(t, tk)
                        t.wait_ge(sV.h, max(mTFM, vgate))
                    if oc >= 2:
                        t.wait_ge(sV.h, mEv[oc - 2])
                    for half in range(2):
                        ps = ps_full[:, 512 * half:512 * (half + 1)]
                        for ic in range(16):
                            t.matmul(ps,
                                     wsb[:, :].rearrange("p (c j) -> p c j", c=16)
                                     [:, ic, 128 * oc:128 * (oc + 1)],
                                     t_fm[:, :].rearrange("p (c m) -> p c m", c=16)
                                     [:, ic, 512 * half:512 * (half + 1)],
                                     start=(ic == 0), stop=(ic == 15))
                    _i(t.drain(), sT)
                TE(mm)
                mp = sT.v

                def ev(v, oc=oc, mp=mp, ps_full=ps_full):
                    v.wait_ge(sT.h, mp)
                    _i(v.tensor_copy(
                        buf[:, :].rearrange("p (c m) -> p c m", c=nchunk)[:, oc, :],
                        ps_full[:, :]), sV)
                VE(ev)
                mEv.append(sV.v)
                mr = sV.v

                def sq(sc, oc=oc, mr=mr):
                    sc.wait_ge(sV.h, mr)
                    if oc >= 1:
                        sc.wait_ge(sT.h, mSq[oc - 1])
                    _i(sc.activation(sqtmp[:, :],
                                     buf[:, :].rearrange("p (c m) -> p c m", c=nchunk)
                                     [:, oc, :], ACT.Square), sS)
                SE(sq)
                msq = sS.v

                def ssum(t, oc=oc, msq=msq, vgate=vgate):
                    t.wait_ge(sS.h, msq)
                    if oc == 0:
                        t.wait_ge(sV.h, vgate)
                    t.matmul(psM[0:1, 0:512], ones[:, 0:1], sqtmp[:, 0:512],
                             start=(oc == 0), stop=(oc == nchunk - 1))
                    t.matmul(psO[0:1, 0:512], ones[:, 0:1], sqtmp[:, 512:1024],
                             start=(oc == 0), stop=(oc == nchunk - 1))
                    _i(t.drain(), sT)
                TE(ssum)
                mSq.append(sT.v)
            msum = sT.v

            def rstd(v, msum=msum, nchunk=nchunk):
                v.wait_ge(sT.h, msum)
                v.tensor_scalar(rmss[0:1, 0:512], psM[0:1, 0:512],
                                1.0 / (nchunk * 128), EPS_RMS, ALU.mult, ALU.add)
                v.tensor_scalar(rmss[0:1, 512:1024], psO[0:1, 0:512],
                                1.0 / (nchunk * 128), EPS_RMS, ALU.mult, ALU.add)
                _i(v.drain(), sV)
            VE(rstd)
            mms = sV.v

            def rsq(sc, mms=mms):
                sc.wait_ge(sV.h, mms)
                _i(sc.activation(rmss[:, :], rmss[:, :], ACT.Sqrt), sS)
            SE(rsq)
            msq2 = sS.v

            def rrec(v, msq2=msq2):
                v.wait_ge(sS.h, msq2)
                _i(v.reciprocal(rmss[:, :], rmss[:, :]), sV)
            VE(rrec)
            mrec = sV.v

            def bcast(g, mrec=mrec):
                g.wait_ge(sV.h, mrec)
                _i(g.partition_broadcast(rmsb[:, :], rmss[0:1, :]), sP)
            GE(bcast)
            mb = sP.v

            for oc in range(nchunk):
                def nrm(v, oc=oc, mb=mb):
                    v.wait_ge(sP.h, mb)
                    _i(v.scalar_tensor_tensor(
                        buf[:, :].rearrange("p (c m) -> p c m", c=nchunk)[:, oc, :],
                        buf[:, :].rearrange("p (c m) -> p c m", c=nchunk)[:, oc, :],
                        gcol_sb[:, oc:oc + 1], rmsb[:, :],
                        ALU.mult, ALU.mult), sV)
                VE(nrm)
            def nrmd(v):
                _i(v.drain(), sV)
            VE(nrmd)
            return sV.v

        mCKV = proj_rms(wdkv, 4, gkv_sb, ckvn)

        tShard = []

        def z32(v):
            v.memset(kn2row[:, :], 0.0)
            _i(v.drain(), sV)
        VE(z32)
        mZ32 = sV.v

        mKNprev = [None, None]
        mKN2V = [None]
        tOutPrev = [None, None]
        for hh in range(H):
            def knmm(t, hh=hh, wprev=(mKNprev[0] if hh > 0 else None)):
                if hh == 0:
                    t.wait_ge(sV.h, mCKV)
                else:
                    t.wait_ge(sV.h, wprev)
                for half in range(2):
                    ps = psA[:, 512 * half:512 * (half + 1)]
                    for ic in range(4):
                        t.matmul(ps,
                                 wuk[:, :].rearrange("p (c j) -> p c j", c=4)
                                 [:, ic, 128 * hh:128 * (hh + 1)],
                                 ckvn[:, :].rearrange("p (c m) -> p c m", c=4)
                                 [:, ic, 512 * half:512 * (half + 1)],
                                 start=(ic == 0), stop=(ic == 3))
                _i(t.drain(), sT)
            TE(knmm)
            mkn = sT.v

            def knev(v, mkn=mkn, tprev=tOutPrev[0]):
                v.wait_ge(sT.h, mkn)
                if tprev is not None:
                    DQ.wait(v, tprev)
                _i(v.tensor_copy(knch[:, :], psA[:, :]), sV)
            VE(knev)
            mknb = sV.v

            mKNprev[0] = mknb

            def knq(sc, mknb=mknb):
                sc.wait_ge(sV.h, mknb)
                _i(sc.activation(knsq[:, :], knch[:, :], ACT.Square), sS)
            SE(knq)
            mknsq = sS.v

            def kn2mm(t, mknsq=mknsq, hh=hh, wv=(mKN2V[0] if hh > 0 else None)):
                t.wait_ge(sS.h, mknsq)
                if wv is not None:
                    t.wait_ge(sV.h, wv)
                t.matmul(psM[0:1, 0:512], ones[:, 0:1], knsq[:, 0:512],
                         start=True, stop=True)
                t.matmul(psO[0:1, 0:512], ones[:, 0:1], knsq[:, 512:1024],
                         start=True, stop=True)
                _i(t.drain(), sT)
            TE(kn2mm)
            mkn2 = sT.v

            def kn2red(v, mkn2=mkn2, hh=hh):
                v.wait_ge(sT.h, mkn2)
                if hh == 0:
                    v.wait_ge(sV.h, mZ32)
                v.reduce_max(kn2row[0:1, hh:hh + 1], psM[0:1, 0:512], axis=AX)
                v.reduce_max(lrs[0:1, 0:1], psO[0:1, 0:512], axis=AX)
                v.drain()
                _i(v.tensor_tensor(kn2row[0:1, hh:hh + 1], kn2row[0:1, hh:hh + 1],
                                   lrs[0:1, 0:1], ALU.max), sV)
            VE(kn2red)
            mKN2V[0] = sV.v

            tkn = [None]

            def knout(sy, hh=hh, mknb=mknb):
                sy.wait_ge(sV.h, mknb)
                tkn[0] = dqQ.issue(
                    sy,
                    out=agkv_in.ap()[KN_OFF + hh * 128 * T:KN_OFF + (hh + 1) * 128 * T]
                    .rearrange("(p m) -> p m", p=128),
                    in_=knch[:, :])
            SY(knout)
            tShard.append(tkn[0])
            tOutPrev[0] = tkn[0]

            def vmm(t, hh=hh, wprev=(mKNprev[1] if hh > 0 else None)):
                if wprev is not None:
                    t.wait_ge(sV.h, wprev)
                for half in range(2):
                    ps = psB[:, 512 * half:512 * (half + 1)]
                    for ic in range(4):
                        t.matmul(ps,
                                 wuv[:, :].rearrange("p (c j) -> p c j", c=4)
                                 [:, ic, 128 * hh:128 * (hh + 1)],
                                 ckvn[:, :].rearrange("p (c m) -> p c m", c=4)
                                 [:, ic, 512 * half:512 * (half + 1)],
                                 start=(ic == 0), stop=(ic == 3))
                _i(t.drain(), sT)
            TE(vmm)
            mv = sT.v

            def vev(v, mv=mv):
                v.wait_ge(sT.h, mv)
                _i(v.tensor_copy(vch[:, :], psB[:, :]), sV)
            VE(vev)
            mvb = sV.v
            mKNprev[1] = mvb

            def vtr(t, mvb=mvb):
                t.wait_ge(sV.h, mvb)
                pcb = psC.ap().bitcast(BF16)
                for half in range(2):
                    for bb in range(4):
                        b = 4 * half + bb
                        t.transpose(pcb[:, 1024 * half + 128 * bb:1024 * half + 128 * (bb + 1)],
                                    vch[:, 128 * b:128 * (b + 1)], ident[:, :])
                    _i(t.drain(), sT)
            TE(vtr)
            mvt = (sT.v - 1, sT.v)

            def vtev(v, mvt=mvt, tprev=tOutPrev[1]):
                if tprev is not None:
                    DQ.wait(v, tprev)
                for half in range(2):
                    v.wait_ge(sT.h, mvt[half])
                    _i(v.tensor_copy(
                        vtt[:, :].rearrange("p (b m) -> p b m", b=8)
                        [:, 4 * half:4 * half + 4, :],
                        psC.ap().bitcast(BF16)[:, 1024 * half:1024 * half + 512]
                        .rearrange("p (b m) -> p b m", b=4)), sV)
            VE(vtev)
            mvtt = sV.v

            tvo = [None]

            def vout(sy, hh=hh, mvtt=mvtt):
                sy.wait_ge(sV.h, mvtt)
                tvo[0] = dqQ.issue(
                    sy,
                    out=agkv_in.ap()[V_OFF + hh * T * VD:V_OFF + (hh + 1) * T * VD]
                    .rearrange("(b p v) -> p b v", b=8, p=128),
                    in_=vtt[:, :].rearrange("p (b v) -> p b v", b=8))
            SY(vout)
            tShard.append(tvo[0])
            tOutPrev[1] = tvo[0]

        def kpmm(t):
            for half in range(2):
                ps = psA[0:64, 512 * half:512 * (half + 1)]
                for ic in range(16):
                    t.matmul(ps,
                             wkr[:, :].rearrange("p (c u) -> p c u", c=16)[:, ic, 0:64],
                             t_fm[:, :].rearrange("p (c m) -> p c m", c=16)
                             [:, ic, 512 * half:512 * (half + 1)],
                             start=(ic == 0), stop=(ic == 15))
                ps2 = psB[0:64, 512 * half:512 * (half + 1)]
                for ic in range(16):
                    t.matmul(ps2,
                             wkr[:, :].rearrange("p (c u) -> p c u", c=16)[:, ic, 64:128],
                             t_fm[:, :].rearrange("p (c m) -> p c m", c=16)
                             [:, ic, 512 * half:512 * (half + 1)],
                             start=(ic == 0), stop=(ic == 15))
            _i(t.drain(), sT)
        TE(kpmm)
        mKP = sT.v

        def kprope(v, mKP=mKP):
            v.wait_ge(sT.h, mKP)
            v.tensor_tensor(kpe0[:, :], psA[0:64, :], ropeC[0:64, :], ALU.mult)
            v.tensor_tensor(kpet[:, :], psB[0:64, :], ropeS[0:64, :], ALU.mult)
            v.drain()
            _i(v.tensor_tensor(kper[:, :], kpe0[:, :], kpet[:, :], ALU.add), sV)
        VE(kprope)
        mKPR = sV.v

        def kpsq(sc, mKPR=mKPR):
            sc.wait_ge(sV.h, mKPR)
            _i(sc.activation(kpesq[:, :], kper[:, :], ACT.Square), sS)
        SE(kpsq)
        mKPSQ = sS.v

        def kpsum(t, mKPSQ=mKPSQ, wv=mKN2V[0]):
            t.wait_ge(sS.h, mKPSQ)
            t.wait_ge(sV.h, wv)
            t.matmul(psM[0:1, 0:512], ones[0:64, 0:1], kpesq[:, 0:512],
                     start=True, stop=True)
            t.matmul(psO[0:1, 0:512], ones[0:64, 0:1], kpesq[:, 512:1024],
                     start=True, stop=True)
            _i(t.drain(), sT)
        TE(kpsum)
        mKPS = sT.v

        def kpred(v, mKPS=mKPS):
            v.wait_ge(sT.h, mKPS)
            v.reduce_max(kn2row[0:1, 16:17], psM[0:1, 0:512], axis=AX)
            v.reduce_max(lrs[0:1, 0:1], psO[0:1, 0:512], axis=AX)
            v.drain()
            v.tensor_tensor(kn2row[0:1, 16:17], kn2row[0:1, 16:17], lrs[0:1, 0:1], ALU.max)
            v.drain()
            _i(v.tensor_copy(kn2rb[:, :], kn2row[:, :]), sV)
        VE(kpred)
        mKN2B = sV.v

        def kpout(sy, mKPR=mKPR, mKN2B=mKN2B):
            sy.wait_ge(sV.h, mKPR)
            tShard.append(dqQ.issue(
                sy,
                out=agkv_in.ap()[KPE_OFF:KPE_OFF + ROPE * T].rearrange("(p m) -> p m", p=64),
                in_=kper[:, :]))
            sy.wait_ge(sV.h, mKN2B)
            tShard.append(dqQ.issue(
                sy,
                out=agkv_in.ap()[KN2_OFF:KN2_OFF + 32].rearrange("(o x) -> o x", o=1),
                in_=kn2rb[:, :]))
        SY(kpout)

        p2a.close()
        wdq = p2.enter_context(nc.sbuf_tensor("wdq", [128, 16 * QL], BF16))

        def wdql(g, mKP=mKP):
            g.wait_ge(sT.h, mKP)
            tW.append(dqG.issue(g, out=wdq[:, :].rearrange("p (c j) -> p c j", c=16),
                                in_=w_dq_d.ap().rearrange("(c p) j -> p c j", p=128)))
        GE(wdql)

        def agkv(g):
            for tk in tShard:
                DQ.wait(g, tk)
            _i(g.collective_compute(
                "AllGather", ALU.bypass,
                replica_groups=[list(range(NCORES))],
                ins=[agkv_in.ap().opt()],
                outs=[agkv_out.ap().opt()]), sCC)
        GE(agkv)
        mAG = sCC.v

        mAQ = proj_rms(wdq, 6, gq_sb, aqn)

        if stage == 2:
            def s2(sy):
                sy.wait_ge(sCC.h, mAG)
                tk = dqQ.issue(sy, out=dbg.ap(), in_=agkv_in.ap())
                DQ.wait(sy, tk)
            SY(s2)

            def s2g(g, mAQ=mAQ):
                g.wait_ge(sV.h, mAQ)
                tk = dqG.issue(g, out=y_d[0:128, :], in_=aqn[:, 0:D])
                DQ.wait(g, tk)
            GE(s2g)
            p2.close()
            return nc

        p2.close()

        at = ExitStack()
        K_sb = at.enter_context(nc.sbuf_tensor("K_sb", [128, 2 * S], BF16))
        V_sb = at.enter_context(nc.sbuf_tensor("V_sb", [128, 2 * S], BF16))
        kpeA = at.enter_context(nc.sbuf_tensor("kpeA", [64, S], BF16))
        A_sb = at.enter_context(nc.sbuf_tensor("A_sb", [128, 8 * 1024], BF16))
        Qn = at.enter_context(nc.sbuf_tensor("Qn", [128, 2 * T], BF16))
        Qp = at.enter_context(nc.sbuf_tensor("Qp", [64, 2 * T], BF16))
        peF = at.enter_context(nc.sbuf_tensor("peF", [64, 512], F32))
        peG = at.enter_context(nc.sbuf_tensor("peG", [64, 512], F32))
        wuqh = at.enter_context(nc.sbuf_tensor("wuqh", [128, 2 * 6 * 256], BF16))
        srow = at.enter_context(nc.sbuf_tensor("srow", [1, 1024], F32))
        srec = at.enter_context(nc.sbuf_tensor("srec", [1, 1024], F32))
        rbc = at.enter_context(nc.sbuf_tensor("rbc", [128, 1024], F32))

        tKPE = []

        def kpeload(sy, mAQ=mAQ):
            sy.wait_ge(sCC.h, mAG)
            sy.wait_ge(sV.h, mAQ)
            for jj in range(NKC):
                tKPE.append(dqQ.issue(
                    sy,
                    out=kpeA[:, 1024 * jj:1024 * (jj + 1)].rearrange(
                        "p (r c) -> p r c", r=8),
                    in_=agkv_out.ap()[0:NCORES * KVSH]
                    .rearrange("(r x) -> r x", x=KVSH)
                    [:, KPE_OFF:KPE_OFF + ROPE * T].rearrange("r (p m) -> p r m", p=ROPE)
                    [:, :, 128 * jj:128 * (jj + 1)]))
        SY(kpeload)

        mHead = [None] * H
        mQMM = [None] * H
        tWQ = [None] * H
        tKV = [[None] * NKC for _ in range(H)]
        mExpL = [0] * H
        mOEV = [0] * H
        mRBC = [0] * H
        mSLN = [0] * H
        ccg = [0]
        mExp = {}
        mAVd = {}
        tCast = []

        for h in range(H):
            par = h % 2

            def wcst(g, h=h):
                tCast.append(dqG.issue(g, out=w1b_d[128 * h:128 * (h + 1), :],
                                       in_=w1_d[128 * h:128 * (h + 1), :]))
                tCast.append(dqG.issue(g, out=w2b_d[512 * h:512 * (h + 1), :],
                                       in_=w2_d[512 * h:512 * (h + 1), :]))
            GE(wcst)

            def wq(g, h=h, par=par):
                if h < 2:
                    g.wait_ge(sCC.h, mAG)
                    g.wait_ge(sV.h, mAQ)
                else:
                    g.wait_ge(sT.h, mQMM[h - 2])
                toks = []
                w = wuqh[:, :].rearrange("p (q l u) -> p q l u", q=2, l=6)
                toks.append(dqG.issue(
                    g, out=w[:, par, :, 0:128],
                    in_=w_uq_d.ap().rearrange("(l p) j -> p l j", p=128)
                    [:, :, QKD * h:QKD * h + 128]))
                toks.append(dqG.issue(
                    g, out=w[:, par, :, 128:192],
                    in_=w_uq_d.ap().rearrange("(l p) j -> p l j", p=128)
                    [:, :, QKD * h + 128:QKD * h + 192]))
                toks.append(dqG.issue(
                    g, out=w[:, par, :, 192:224],
                    in_=w_uq_d.ap().rearrange("(l p) j -> p l j", p=128)
                    [:, :, QKD * h + 160:QKD * h + 192]))
                toks.append(dqG.issue(
                    g, out=w[:, par, :, 224:256],
                    in_=w_uq_d.ap().rearrange("(l p) j -> p l j", p=128)
                    [:, :, QKD * h + 128:QKD * h + 160]))
                tWQ[h] = toks
            GE(wq)

            def kv(sy, h=h, par=par, mAQ=mAQ):
                if h < 2:
                    sy.wait_ge(sCC.h, mAG)
                    sy.wait_ge(sV.h, mAQ)
                else:
                    sy.wait_ge(sT.h, mHead[h - 2])
                kview = agkv_out.ap()[0:NCORES * KVSH] \
                    .rearrange("(r x) -> r x", x=KVSH) \
                    [:, KN_OFF + h * 128 * T:KN_OFF + (h + 1) * 128 * T] \
                    .rearrange("r (p m) -> p r m", p=128)
                vview = agkv_out.ap()[0:NCORES * KVSH] \
                    .rearrange("(r x) -> r x", x=KVSH) \
                    [:, V_OFF + h * T * VD:V_OFF + (h + 1) * T * VD] \
                    .rearrange("r (b p v) -> p b r v", b=NT, p=128)
                for jj in range(NKC):
                    t1 = dqQ.issue(
                        sy,
                        out=K_sb[:, par * S + 1024 * jj:par * S + 1024 * (jj + 1)]
                        .rearrange("p (r c) -> p r c", r=8),
                        in_=kview[:, :, 128 * jj:128 * (jj + 1)])
                    t2 = dqQ.issue(
                        sy,
                        out=V_sb[:, par * S + 1024 * jj:par * S + 1024 * (jj + 1)]
                        .rearrange("p (r v) -> p r v", r=8),
                        in_=vview[:, jj, :, :])
                    tKV[h][jj] = (t1, t2)
            SY(kv)

            def qmm(t, h=h, par=par):
                for tk in tWQ[h]:
                    DQ.wait(t, tk)
                if h == 0:
                    t.wait_ge(sV.h, mAQ)
                else:
                    t.wait_ge(sS.h, mExpL[h - 1])
                    t.wait_ge(sV.h, mRBC[h - 1])
                    t.wait_ge(sV.h, mOEV[h - 1])
                w = wuqh[:, :].rearrange("p (q l u) -> p q l u", q=2, l=6)
                for half in range(2):
                    for l in range(6):
                        t.matmul(psA[:, 512 * half:512 * (half + 1)],
                                 w[:, par, l, 0:128],
                                 aqn[:, :].rearrange("p (c m) -> p c m", c=6)
                                 [:, l, 512 * half:512 * (half + 1)],
                                 start=(l == 0), stop=(l == 5))
                for half in range(2):
                    for l in range(6):
                        t.matmul(psB[0:64, 512 * half:512 * (half + 1)],
                                 w[:, par, l, 128:192],
                                 aqn[:, :].rearrange("p (c m) -> p c m", c=6)
                                 [:, l, 512 * half:512 * (half + 1)],
                                 start=(l == 0), stop=(l == 5))
                for half in range(2):
                    for l in range(6):
                        t.matmul(psC[0:64, 512 * half:512 * (half + 1)],
                                 w[:, par, l, 192:256],
                                 aqn[:, :].rearrange("p (c m) -> p c m", c=6)
                                 [:, l, 512 * half:512 * (half + 1)],
                                 start=(l == 0), stop=(l == 5))
                _i(t.drain(), sT)
            TE(qmm)
            mQMM[h] = sT.v

            def qev(v, h=h, par=par):
                v.wait_ge(sT.h, mQMM[h])
                v.tensor_copy(Qn[:, par * T:(par + 1) * T], psA[:, :])
                for half in range(2):
                    pesw = psC[0:64, 512 * half:512 * (half + 1)]
                    v.tensor_tensor(peF[:, :], psB[0:64, 512 * half:512 * (half + 1)],
                                    ropeC[0:64, 512 * half:512 * (half + 1)], ALU.mult)
                    v.tensor_tensor(peG[:, :], pesw,
                                    ropeS[0:64, 512 * half:512 * (half + 1)], ALU.mult)
                    v.drain()
                    v.tensor_tensor(Qp[:, par * T + 512 * half:par * T + 512 * (half + 1)],
                                    peF[:, :], peG[:, :], ALU.add)
                    v.drain()
                _i(v.drain(), sV)
            VE(qev)
            mQEV = sV.v

            mQnEv = mQEV
            cch = 0
            pend = [[]]

            def s_seg(jj, st, cc, cch, par, wexp, h=h):
                psX = psA if cc % 2 == 0 else psB

                def smm(t, jj=jj, st=st, cc=cc, cch=cch, par=par, psX=psX, h=h,
                        mQnEv=mQnEv, wexp=wexp):
                    DQ.wait(t, tKV[h][jj][0])
                    DQ.wait(t, tKV[h][jj][1])
                    DQ.wait(t, tKPE[jj])
                    if wexp:
                        t.wait_ge(sS.h, wexp)
                    if cch == 0:
                        t.wait_ge(sV.h, mQnEv)
                    kb = par * S + 1024 * jj + 128 * st
                    slc = ([(128 * jj, 512), (512, 1024)] if jj < 4
                           else [(128 * jj, 1024)])
                    ins = None
                    for (a, bnd) in slc:
                        t.matmul(psX[:, a:bnd],
                                 K_sb[:, kb:kb + 128],
                                 Qn[:, par * T + a:par * T + bnd],
                                 start=True, stop=False)
                        ins = t.matmul(psX[:, a:bnd],
                                       kpeA[:, 1024 * jj + 128 * st:
                                            1024 * jj + 128 * (st + 1)],
                                       Qp[:, par * T + a:par * T + bnd],
                                       start=False, stop=True)
                    _i(ins, sT)
                TE(smm)
                return sT.v

            def mask_seg(jj, st, cc, mS):
                psX = psA if cc % 2 == 0 else psB

                def dmk(v, jj=jj, st=st, mS=mS, psX=psX):
                    v.wait_ge(sT.h, mS)
                    _i(v.tensor_tensor(psX[:, 128 * jj:128 * (jj + 1)],
                                       psX[:, 128 * jj:128 * (jj + 1)],
                                       dmask[:, 128 * st:128 * (st + 1)],
                                       ALU.add), sV)
                VE(dmk)
                return sV.v

            def exp_one(jj, cc, mM):
                apar = cc % 8
                psX = psA if cc % 2 == 0 else psB

                def expf(sc, jj=jj, cc=cc, apar=apar, psX=psX, mM=mM,
                         wav=(mAVd.get(cc - 8, 0))):
                    sc.wait_ge(sV.h, mM)
                    if wav:
                        sc.wait_ge(sT.h, wav)
                    _i(sc.activation(
                        A_sb[:, apar * 1024 + 128 * jj:apar * 1024 + 1024],
                        psX[:, 128 * jj:1024],
                        ACT.Exp, scale=SCALE), sS)
                SE(expf)
                mExp[cc] = sS.v
                return sS.v

            def tail_seg(jj, st, cc, par, mE, h=h):
                apar = cc % 8

                def avm(t, jj=jj, st=st, cc=cc, par=par, apar=apar, mE=mE, h=h,
                        woev=(mOEV[h - 1] if (h > 0 and jj == 0 and st == 0) else 0),
                        wsln=(mSLN[h - 1] if (h > 0 and jj == 0 and st == 0) else 0)):
                    t.wait_ge(sS.h, mE)
                    if woev:
                        t.wait_ge(sV.h, woev)
                    if wsln:
                        t.wait_ge(sS.h, wsln)
                    vb = par * S + 1024 * jj + 128 * st
                    first = (jj == 0 and st == 0)
                    slc = ([(128 * jj, 512), (512, 1024)] if jj < 4
                           else [(128 * jj, 1024)])
                    for (a, bnd) in slc:
                        t.matmul(psC[:, a:bnd],
                                 V_sb[:, vb:vb + 128],
                                 A_sb[:, apar * 1024 + a:apar * 1024 + bnd],
                                 start=first,
                                 stop=(st == 7 and jj == (3 if a < 512 else 7)))
                    ins = None
                    for (a, bnd) in slc:
                        ps = psO if a < 512 else psM
                        aa = a if a < 512 else a - 512
                        ins = t.matmul(ps[0:1, aa:aa + (bnd - a)],
                                       ones[:, 0:1],
                                       A_sb[:, apar * 1024 + a:apar * 1024 + bnd],
                                       start=first,
                                       stop=(st == 7 and jj == (3 if a < 512 else 7)))
                    _i(ins, sT)
                TE(avm)
                mAVd[cc] = sT.v

            expq = []
            for jj in range(NKC):
                for st in range(8):
                    cc = ccg[0]
                    wexp = (sS.v + st - 1) if st >= 2 else mExp.get(cc - 2, 0)
                    mS = s_seg(jj, st, cc, cch, par, wexp)
                    mM = mask_seg(jj, st, cc, mS)
                    if pend[0]:
                        pend[0].pop(0)()
                    expq.append((jj, st, cc, mM))
                    ccg[0] += 1
                    cch += 1
                for (jx, sx, cx, mMx) in expq:
                    mE = exp_one(jx, cx, mMx)
                    mExpL[h] = mE
                    pend[0].append(lambda jx=jx, sx=sx, cx=cx, mE=mE:
                                   tail_seg(jx, sx, cx, par, mE))
                expq = []
            while pend[0]:
                pend[0].pop(0)()
            mAVl = sT.v

            def sln(sc, mAVl=mAVl):
                sc.wait_ge(sT.h, mAVl)
                sc.activation(srow[0:1, 0:512], psO[0:1, 0:512], ACT.Ln)
                sc.activation(srow[0:1, 512:1024], psM[0:1, 0:512], ACT.Ln)
                sc.drain()
                _i(sc.activation(srec[0:1, :], srow[0:1, :], ACT.Exp,
                                 scale=-1.0), sS)
            SE(sln)
            mSLN[h] = sS.v

            def bcmm(t, m=mSLN[h]):
                t.wait_ge(sS.h, m)
                DQ.wait(t, tC0[7])
                t.matmul(psB[:, 0:512], onesf[0:1, 0:128], srec[0:1, 0:512],
                         start=True, stop=True)
                ins = t.matmul(psB[:, 512:1024], onesf[0:1, 0:128],
                               srec[0:1, 512:1024], start=True, stop=True)
                _i(ins, sT)
            TE(bcmm)
            mBC = sT.v

            def ofin(v, h=h, mBC=mBC):
                v.wait_ge(sT.h, mBC)
                v.tensor_copy(rbc[:, :], psB[:, :])
                _i(v.drain(), sV)
                v.tensor_tensor(o_fm[:, T * h:T * (h + 1)], psC[:, :],
                                rbc[:, :], ALU.mult)
                _i(v.drain(), sV)
            VE(ofin)
            mOEV[h] = sV.v
            mRBC[h] = sV.v
            mHead[h] = sT.v

        mOFM = sV.v

        if stage == 3:
            def s3(g):
                g.wait_ge(sV.h, mOFM)
                tk = dqG.issue(g, out=dbg.ap().rearrange("(p m) -> p m", p=128)
                               if False else dbg[:, :], in_=o_fm[:, :])
                DQ.wait(g, tk)
            GE(s3)
            at.close()
            return nc

        at.close()
        mF4 = fence()

        sel_ctx = ExitStack()
        sc_tm = sel_ctx.enter_context(nc.sbuf_tensor("sc_tm", [128, NT], F32))
        sig_tm = sel_ctx.enter_context(nc.sbuf_tensor("sig_tm", [128, NT], F32))
        tau_t = sel_ctx.enter_context(nc.sbuf_tensor("tau_t", [1, 2], F32))
        tau_b = sel_ctx.enter_context(nc.sbuf_tensor("tau_b", [128, 1], F32))
        cnt_t = sel_ctx.enter_context(nc.sbuf_tensor("cnt_t", [1, 1], F32))
        sel16 = sel_ctx.enter_context(nc.sbuf_tensor("sel16", [16, CAP], I16))
        self32 = sel_ctx.enter_context(nc.sbuf_tensor("self32", [1, CAP], F32))
        scatf = sel_ctx.enter_context(nc.sbuf_tensor("scatf", [1, CAP], F32))
        cge = sel_ctx.enter_context(nc.sbuf_tensor("cge", [1, CAP], F32))
        scat16 = sel_ctx.enter_context(nc.sbuf_tensor("scat16", [1, CAP], I16))
        scat_b = sel_ctx.enter_context(nc.sbuf_tensor("scat_b", [128, CAP], I16))
        gidx = sel_ctx.enter_context(nc.sbuf_tensor("gidx", [128, CAP // 16], I16))
        dfm = sel_ctx.enter_context(nc.sbuf_tensor("dfm", [128, 16 * 1024], BF16))
        p4 = ExitStack()
        wo_sb = p4.enter_context(nc.sbuf_tensor("wo_sb", [128, 2 * D], BF16))
        rw_row = p4.enter_context(nc.sbuf_tensor("rw_row", [1, D], BF16))
        rwb = p4.enter_context(nc.sbuf_tensor("rwb", [128, D], BF16))
        ao_fm = p4.enter_context(nc.sbuf_tensor("ao_fm", [128, T], BF16))
        ao_tm = p4.enter_context(nc.sbuf_tensor("ao_tm", [128, NT * D], BF16))
        x0r = [p4.enter_context(nc.sbuf_tensor(f"x0r{i}", [128, D], F32)) for i in range(2)]
        x1t = [p4.enter_context(nc.sbuf_tensor(f"x1t{i}", [128, D], F32)) for i in range(2)]
        ht = [p4.enter_context(nc.sbuf_tensor(f"ht{i}", [128, D], BF16)) for i in range(2)]
        st6b = p4.enter_context(nc.sbuf_tensor("st6b", [128, 24], F32))
        st2b = p4.enter_context(nc.sbuf_tensor("st2b", [128, 2], F32))
        lrs2 = p4.enter_context(nc.sbuf_tensor("lrs2", [128, 1], F32))
        lnm2 = p4.enter_context(nc.sbuf_tensor("lnm2", [128, 1], F32))

        def rbpb(g):
            g.wait_ge(sP.h, mF4)
            DQ.wait(g, tC0[10])
            _i(g.partition_broadcast(rbb[:, :], rb_sb[0:1, 0:1]), sP)
            tok = dqG.issue(g, out=rw_row[:, :],
                            in_=rw_d.ap().rearrange("(o x) -> o x", o=1))
            DQ.wait(g, tok)
            _i(g.partition_broadcast(rwb[:, :], rw_row[0:1, :]), sP)
        GE(rbpb)
        mRBB = sP.v

        tWOC = [None] * 16
        mWOev = [0] * 16
        mAOtr = [0] * 16
        for oc in range(16):
            def wod(g, oc=oc):
                if oc < 2:
                    g.wait_ge(sP.h, mF4)
                else:
                    g.wait_ge(sT.h, mWOmm[oc - 2])
                tWOC[oc] = dqG.issue(
                    g, out=wo_sb[:, (oc % 2) * D:(oc % 2 + 1) * D]
                    .rearrange("p (c j) -> p c j", c=16),
                    in_=w_o_d.ap().rearrange("(c p) j -> p c j", p=128)
                    [:, :, 128 * oc:128 * (oc + 1)])
            GE(wod)

            def womm(t, oc=oc):
                DQ.wait(t, tWOC[oc])
                if oc == 0:
                    t.wait_ge(sV.h, mOFM)
                if oc >= 2:
                    t.wait_ge(sV.h, mWOev[oc - 2])
                psX = psA if oc % 2 == 0 else psB
                for half in range(2):
                    for ic in range(16):
                        t.matmul(psX[:, 512 * half:512 * (half + 1)],
                                 wo_sb[:, (oc % 2) * D:(oc % 2 + 1) * D]
                                 .rearrange("p (c j) -> p c j", c=16)[:, ic, :],
                                 o_fm[:, :].rearrange("p (c m) -> p c m", c=16)
                                 [:, ic, 512 * half:512 * (half + 1)],
                                 start=(ic == 0), stop=(ic == 15))
                _i(t.drain(), sT)
            TE(womm)
            if oc == 0:
                mWOmm = [0] * 16
            mWOmm[oc] = sT.v

            def woev(v, oc=oc, m=mWOmm[oc]):
                v.wait_ge(sT.h, m)
                if oc >= 2:
                    v.wait_ge(sT.h, mAOtr[oc - 2])
                psX = psA if oc % 2 == 0 else psB
                _i(v.tensor_copy(ao_fm[:, :], psX[:, :]), sV)
            VE(woev)
            mWOev[oc] = sV.v

            def aotr(t, oc=oc, m=mWOev[oc]):
                t.wait_ge(sV.h, m)
                pcb = psC.ap().bitcast(BF16)
                for half in range(2):
                    for z in range(4):
                        j = 4 * half + z
                        t.transpose(pcb[:, 1024 * half + 128 * z:1024 * half + 128 * (z + 1)],
                                    ao_fm[:, 128 * j:128 * (j + 1)], ident[:, :])
                    _i(t.drain(), sT)
            TE(aotr)
            mAOtr[oc] = sT.v
            mtr2 = (sT.v - 1, sT.v)

            def aoev(v, oc=oc, mtr2=mtr2):
                for half in range(2):
                    v.wait_ge(sT.h, mtr2[half])
                    _i(v.tensor_copy(
                        ao_tm[:, :].rearrange("p (j m) -> p j m", j=NT)
                        [:, 4 * half:4 * half + 4, 128 * oc:128 * (oc + 1)],
                        psC.ap().bitcast(BF16)[:, 1024 * half:1024 * half + 512]
                        .rearrange("p (j m) -> p j m", j=4)), sV)
            VE(aoev)
        mAOTM = sV.v
        ofm_ctx.close()
        sel_tr = ExitStack()
        sc_all = sel_tr.enter_context(nc.sbuf_tensor("sc_all", [128, 64], F32, side="right"))
        kthtmp = sel_tr.enter_context(nc.sbuf_tensor("kthtmp", [128, 64], F32, side="right"))
        selmask = sel_tr.enter_context(nc.sbuf_tensor("selmask", [1, T], F32, side="right"))
        sc_own = sel_tr.enter_context(nc.sbuf_tensor("sc_own", [1, T], F32, side="right"))
        pref_a = sel_tr.enter_context(nc.sbuf_tensor("pref_a", [1, T], F32, side="right"))
        pref_b = sel_tr.enter_context(nc.sbuf_tensor("pref_b", [1, T], F32, side="right"))
        ranks = sel_tr.enter_context(nc.sbuf_tensor("ranks", [1, T], F32, side="right"))
        r16b = sel_tr.enter_context(nc.sbuf_tensor("r16b", [16, T], I16, side="right"))
        bs_lo = sel_tr.enter_context(nc.sbuf_tensor("bs_lo", [128, 1], F32, side="right"))
        bs_hi = sel_tr.enter_context(nc.sbuf_tensor("bs_hi", [128, 1], F32, side="right"))
        bs_mid = sel_tr.enter_context(nc.sbuf_tensor("bs_mid", [128, 1], F32, side="right"))
        bs_fl = sel_tr.enter_context(nc.sbuf_tensor("bs_fl", [128, 1], F32, side="right"))
        bs_pc = sel_tr.enter_context(nc.sbuf_tensor("bs_pc", [128, 1], F32, side="right"))

        tX1S = [None] * NT
        tHS = [None] * NT
        mX1f = [0, 0]
        mHtf = [0, 0]
        for j in range(NT):
            par = j % 2
            tx0 = [None]

            def ldx0(sy, j=j, par=par, w=mX1f[par]):
                if w:
                    sy.wait_ge(sV.h, w)
                else:
                    sy.wait_ge(sP.h, mF4)
                tx0[0] = dqQ.issue(sy, out=x0r[par][:, :],
                                   in_=x0_d[128 * j:128 * (j + 1), :])
            SY(ldx0)
            tx0 = tx0[0]

            def ln2a(v, j=j, mAOTM=mAOTM):
                v.wait_ge(sV.h, mAOTM)
                tl = ao_tm[:, :].rearrange("p (j m) -> p j m", j=NT)[:, j, :]
                for q in range(4):
                    v.bn_stats(st6b[:, 6 * q:6 * q + 6], tl[:, 512 * q:512 * (q + 1)])
                v.drain()
                v.bn_aggr(st2b[:, :], st6b[:, :])
                v.drain()
                v.tensor_scalar(lrs2[:, :], st2b[:, 1:2], EPS_LN, None, ALU.add)
                v.drain()
                _i(v.reciprocal(lrs2[:, :], lrs2[:, :]), sV)
            VE(ln2a)
            m1 = sV.v

            def ln2b(sc, m1=m1):
                sc.wait_ge(sV.h, m1)
                _i(sc.activation(lrs2[:, :], lrs2[:, :], ACT.Sqrt), sS)
            SE(ln2b)
            m2 = sS.v

            def ln2c(v, j=j, par=par, m2=m2, tx0=tx0):
                v.wait_ge(sS.h, m2)
                v.tensor_mul(lnm2[:, :], st2b[:, 0:1], lrs2[:, :])
                v.drain()
                DQ.wait(v, tx0)
                tl = ao_tm[:, :].rearrange("p (j m) -> p j m", j=NT)[:, j, :]
                v.scalar_tensor_tensor(x1t[par][:, :], tl, lrs2[:, 0:1], x0r[par][:, :],
                                       ALU.mult, ALU.add)
                v.drain()
                _i(v.tensor_scalar(x1t[par][:, :], x1t[par][:, :], lnm2[:, 0:1],
                                   None, ALU.subtract), sV)
            VE(ln2c)
            mX1 = sV.v
            mX1f[par] = mX1

            def spx1(sy, j=j, par=par, mX1=mX1):
                sy.wait_ge(sV.h, mX1)
                tX1S[j] = dqQ.issue(sy, out=x1_dram[128 * j:128 * (j + 1), :],
                                    in_=x1t[par][:, :])
            SY(spx1)

            def ln3a(v, j=j, par=par, mX1=mX1):
                v.drain()
                for q in range(4):
                    v.bn_stats(st6b[:, 6 * q:6 * q + 6],
                               x1t[par][:, 512 * q:512 * (q + 1)])
                v.drain()
                v.bn_aggr(st2b[:, :], st6b[:, :])
                v.drain()
                v.tensor_scalar(lrs2[:, :], st2b[:, 1:2], EPS_LN, None, ALU.add)
                v.drain()
                _i(v.reciprocal(lrs2[:, :], lrs2[:, :]), sV)
            VE(ln3a)
            m3 = sV.v

            def ln3b(sc, m3=m3):
                sc.wait_ge(sV.h, m3)
                _i(sc.activation(lrs2[:, :], lrs2[:, :], ACT.Sqrt), sS)
            SE(ln3b)
            m4 = sS.v

            def ln3c(v, m4=m4):
                v.wait_ge(sS.h, m4)
                v.tensor_mul(lnm2[:, :], st2b[:, 0:1], lrs2[:, :])
                v.drain()
                _i(v.tensor_scalar(lnm2[:, :], lnm2[:, :], -1.0, None, ALU.mult), sV)
            VE(ln3c)
            m5 = sV.v

            def ln3d(sc, j=j, par=par, m5=m5, whf=mHtf[par],
                     tprev=(tHS[j - 2] if j >= 2 else None)):
                sc.wait_ge(sV.h, m5)
                if whf:
                    sc.wait_ge(sV.h, whf)
                if tprev is not None:
                    DQ.wait(sc, tprev)
                _i(sc.activation(ht[par][:, :], x1t[par][:, :], ACT.Identity,
                                 bias=lnm2[:, 0:1], scale=lrs2[:, 0:1]), sS)
            SE(ln3d)
            m6 = sS.v

            def sth(sy, j=j, par=par, m6=m6):
                sy.wait_ge(sS.h, m6)
                tHS[j] = dqQ.issue(sy, out=h_dram[128 * j:128 * (j + 1), :],
                                   in_=ht[par][:, :])
            SY(sth)

            def scj(v, j=j, par=par, m6=m6):
                v.wait_ge(sS.h, m6)
                DQ.wait(v, tX1S[j])
                v.tensor_tensor(x1t[par][:, :], ht[par][:, :], rwb[:, :], ALU.mult)
                v.drain()
                _i(v.reduce_sum(sc_tm[:, j:j + 1], x1t[par][:, :], axis=AX), sV)
            VE(scj)
            mHtf[par] = sV.v
        mSCT = sV.v

        def scf(v, mSCT=mSCT):
            v.wait_ge(sV.h, mSCT)
            v.wait_ge(sP.h, mRBB)
            _i(v.tensor_scalar(sc_tm[:, :], sc_tm[:, :], rbb[:, 0:1], None, ALU.add),
               sV)
        VE(scf)
        mSCF = sV.v

        def sgf(sc, mSCF=mSCF):
            sc.wait_ge(sV.h, mSCF)
            _i(sc.activation(sig_tm[:, :], sc_tm[:, :], ACT.Sigmoid), sS)
        SE(sgf)
        mSIG = sS.v

        if stage == 4:
            def s4(sy):
                for j in range(NT):
                    DQ.wait(sy, tX1S[j])
                tk = dqQ.issue(sy, out=dbg[:, :], in_=x1_dram[:, :])
                DQ.wait(sy, tk)
            SY(s4)
            p4.close()
            sel_tr.close()
            sel_ctx.close()
            return nc

        tSC = [None]

        def scout(sy, mSCF=mSCF):
            sy.wait_ge(sV.h, mSCF)
            tSC[0] = dqQ.issue(sy, out=agsc_in.ap().rearrange("(j p) -> p j", p=128),
                               in_=sc_tm[:, :], allow_slow_non_contiguous=True)
        SY(scout)

        def agsc(g):
            DQ.wait(g, tSC[0])
            _i(g.collective_compute(
                "AllGather", ALU.bypass,
                replica_groups=[list(range(NCORES))],
                ins=[agsc_in.ap().opt()],
                outs=[agsc_out.ap().opt()]), sCC)
        GE(agsc)
        mAGS = sCC.v

        tSCA = [None]
        tSCO = [None]

        def scload(sy):
            sy.wait_ge(sCC.h, mAGS)
            tSCA[0] = dqQ.issue(sy, out=sc_all[:, :],
                                in_=agsc_out.ap().rearrange("(p f) -> p f", p=128))
            tSCO[0] = dqQ.issue(sy, out=sc_own[:, :],
                                in_=agsc_in.ap().rearrange("(o x) -> o x", o=1))
        SY(scload)

        N_BS = 56

        def bsinit(v):
            DQ.wait(v, tSCA[0])
            _i(v.reduce_max(bs_lo[:, :], sc_all[:, :], axis=AX,
                            apply_absolute_value=True), sV)
        VE(bsinit)
        mBS0 = sV.v

        def bssum(t, mBS0=mBS0):
            t.wait_ge(sV.h, mBS0)
            _i(t.matmul(psM[:, 320:321], onesf[:, :], bs_lo[:, :],
                        start=True, stop=True), sT)
        TE(bssum)
        mBS1 = sT.v

        def bsrng(v, mBS1=mBS1):
            v.wait_ge(sT.h, mBS1)
            v.tensor_scalar(bs_hi[:, :], psM[:, 320:321], 1.0, None, ALU.add)
            v.tensor_scalar(bs_lo[:, :], psM[:, 320:321], -1.0, -1.0,
                            ALU.mult, ALU.add)
            v.drain()
            v.tensor_add(bs_mid[:, :], bs_lo[:, :], bs_hi[:, :])
            v.drain()
            v.tensor_scalar(bs_mid[:, :], bs_mid[:, :], 0.5, None, ALU.mult)
            v.drain()
            v.tensor_scalar(kthtmp[:, :], sc_all[:, :], bs_mid[:, 0:1], None,
                            ALU.is_ge)
            v.drain()
            _i(v.reduce_sum(bs_pc[:, :], kthtmp[:, :], axis=AX), sV)
        VE(bsrng)
        mBSit = sV.v

        for it in range(N_BS):
            def bcnt(t, m=mBSit):
                t.wait_ge(sV.h, m)
                _i(t.matmul(psM[:, 320:321], onesf[:, :], bs_pc[:, :],
                            start=True, stop=True), sT)
            TE(bcnt)
            mPE = sT.v

            def bupd(v, mPE=mPE, last=(it == N_BS - 1)):
                v.wait_ge(sT.h, mPE)
                v.tensor_scalar(bs_fl[:, :], psM[:, 320:321], float(KSEL), None,
                                ALU.is_ge)
                v.drain()
                v.tensor_sub(kthtmp[:, 0:1], bs_mid[:, :], bs_lo[:, :])
                v.tensor_sub(kthtmp[:, 1:2], bs_mid[:, :], bs_hi[:, :])
                v.tensor_scalar(kthtmp[:, 2:3], bs_fl[:, :], -1.0, 1.0,
                                ALU.mult, ALU.add)
                v.drain()
                v.scalar_tensor_tensor(bs_lo[:, :], kthtmp[:, 0:1], bs_fl[:, 0:1],
                                       bs_lo[:, :], ALU.mult, ALU.add)
                v.scalar_tensor_tensor(bs_hi[:, :], kthtmp[:, 1:2], kthtmp[:, 2:3],
                                       bs_hi[:, :], ALU.mult, ALU.add)
                v.drain()
                if not last:
                    v.tensor_add(bs_mid[:, :], bs_lo[:, :], bs_hi[:, :])
                    v.drain()
                    v.tensor_scalar(bs_mid[:, :], bs_mid[:, :], 0.5, None, ALU.mult)
                    v.drain()
                    v.tensor_scalar(kthtmp[:, :], sc_all[:, :], bs_mid[:, 0:1], None,
                                    ALU.is_ge)
                    v.drain()
                    _i(v.reduce_sum(bs_pc[:, :], kthtmp[:, :], axis=AX), sV)
                else:
                    v.tensor_copy(tau_t[0:1, 0:1], bs_lo[0:1, 0:1])
                    _i(v.tensor_copy(tau_t[0:1, 1:2], bs_hi[0:1, 0:1]), sV)
            VE(bupd)
            mBSit = sV.v
        mKth4 = mBSit

        def selv(v):
            v.wait_ge(sV.h, mKth4)
            v.drain()
            DQ.wait(v, tSCO[0])
            v.tensor_scalar(selmask[:, :], sc_own[:, :], tau_t[0:1, 0:1], None,
                            ALU.is_ge)
            v.drain()
            v.reduce_sum(cnt_t[:, :], selmask[:, :], axis=AX)
            v.tensor_copy(pref_a[:, :], selmask[:, :])
            v.drain()
            src_, dst_ = pref_a, pref_b
            for sh in [1, 2, 4, 8, 16, 32, 64, 128, 256, 512]:
                v.tensor_copy(dst_[:, 0:sh], src_[:, 0:sh])
                v.tensor_tensor(dst_[:, sh:T], src_[:, sh:T], src_[:, 0:T - sh],
                                ALU.add)
                v.drain()
                src_, dst_ = dst_, src_
            v.tensor_sub(ranks[:, :], src_[:, :], selmask[:, :])
            v.drain()
            v.tensor_scalar(pref_b[:, :], ranks[:, :], float(CAP), None, ALU.is_lt)
            v.drain()
            v.tensor_mul(pref_b[:, :], pref_b[:, :], selmask[:, :])
            v.tensor_scalar(pref_a[:, :], ranks[:, :], 1.0, None, ALU.add)
            v.drain()
            v.tensor_mul(pref_a[:, :], pref_a[:, :], pref_b[:, :])
            v.drain()
            v.tensor_scalar(pref_a[:, :], pref_a[:, :], 1.0, None, ALU.subtract)
            v.memset(r16b[:, :], -1)
            v.drain()
            _i(v.tensor_copy(r16b[0:1, :], pref_a[:, :]), sV)
        VE(selv)
        mRANKS = sV.v

        def comp(g):
            g.wait_ge(sV.h, mRANKS)
            g.load_library(libcfg.local_scatter)
            _i(g.local_scatter(sel16[:, :], tok16[:, :], r16b[:, :], channels=16,
                               num_elems=CAP, num_idxs=T), sP)
            g.wait_ge(sP.h, sP.v)
            t1 = dqG.issue(g, out=idx_dram[:, :], in_=sel16[0:1, :])
            DQ.wait(g, t1)
            g.memset(gidx[:, :], 0)
            g.drain()
            toks = []
            for b in range(8):
                toks.append(dqG.issue(
                    g, out=gidx[16 * b:16 * (b + 1), :],
                    in_=idx_dram.ap().rearrange("o (f p) -> (o p) f", p=16),
                    allow_slow_non_contiguous=True))
            for tk in toks:
                DQ.wait(g, tk)
            _i(g.drain(), sP)
        GE(comp)
        mGIDX = sP.v

        def selw(v):
            v.wait_ge(sP.h, mGIDX)
            v.tensor_copy(self32[:, :], sel16[0:1, :])
            v.drain()
            v.tensor_scalar(cge[:, :], capio[:, :], cnt_t[0:1, 0:1], None, ALU.is_ge)
            v.tensor_scalar(scatf[:, :], self32[:, :], 1.0, None, ALU.add)
            v.drain()
            v.tensor_mul(scatf[:, :], scatf[:, :], cge[:, :])
            v.drain()
            v.tensor_sub(scatf[:, :], self32[:, :], scatf[:, :])
            v.drain()
            _i(v.tensor_copy(scat16[:, :], scatf[:, :]), sV)
        VE(selw)
        mSCAT = sV.v

        if stage == 5:
            def s5(sy):
                sy.wait_ge(sV.h, mSCAT)
                t1 = dqQ.issue(sy, out=dbg[0:1, :], in_=sc_own[:, :])
                t2 = dqQ.issue(sy, out=dbg[1:2, :], in_=selmask[:, :])
                t3 = dqQ.issue(sy, out=dbg[2:3, :], in_=ranks[:, :])
                t4 = dqQ.issue(sy, out=dbg[3:4, 0:2], in_=tau_t[:, :])
                t5 = dqQ.issue(sy, out=dbg[3:4, 2:3], in_=cnt_t[:, :])
                t6 = dqQ.issue(sy, out=dbg[4:5, 0:CAP], in_=scatf[:, :])
                for tk in (t1, t2, t3, t4, t5, t6):
                    DQ.wait(sy, tk)
            SY(s5)
            p4.close()
            sel_tr.close()
            sel_ctx.close()
            return nc

        p4.close()
        sel_tr.close()
        mF5 = fence()

        f5 = ExitStack()
        hsel = f5.enter_context(nc.sbuf_tensor("hsel", [128, 16 * CAP], BF16))
        g_sb = f5.enter_context(nc.sbuf_tensor("g_sb", [128, 64 * CAP], BF16))
        w1c = [f5.enter_context(nc.sbuf_tensor(f"w1c{i}", [128, 16 * 512], BF16))
               for i in range(2)]
        w2c = [f5.enter_context(nc.sbuf_tensor(f"w2c{i}", [128, 64 * 128], BF16))
               for i in range(2)]
        wd2 = f5.enter_context(nc.sbuf_tensor("wd2", [128, 16 * CAP], BF16))

        def hgat(g):
            g.wait_ge(sP.h, mF5)
            g.load_library(libcfg.attnmlp)
            g.wait_ge(sP.h, mGIDX)
            for j in range(NT):
                DQ.wait(g, tHS[j])
            g.wait_ge(sS.h, mSIG)
            g.wait_ge(sV.h, mSCAT)
            s = dqG.sems[dqG.n % dqG.ns]
            if s.v > 0:
                g.wait_ge(s.h, s.v)
            g.dma_gather(hsel[:, :].rearrange("p (c i) -> p c i", c=16),
                         h_dram[:, :], gidx[:, :], num_idxs=CAP, num_idxs_reg=CAP,
                         elem_size=D, transpose=True).then_inc(s.h, 16)
            s.v += 16
            dqG.n += 1
            g.wait_ge(s.h, s.v)
            _i(g.drain(), sP)
        GE(hgat)
        mHSEL = sP.v

        mW1mm = [0] * 16
        mGelQ = [0] * 64
        tW1 = [None] * 16
        for dc in range(16):
            def w1l(g, dc=dc):
                if dc < 2:
                    g.wait_ge(sP.h, mF5)
                    for tk in tCast:
                        DQ.wait(g, tk)
                else:
                    g.wait_ge(sT.h, mW1mm[dc - 2])
                tW1[dc] = dqG.issue(
                    g, out=w1c[dc % 2][:, :].rearrange("p (c j) -> p c j", c=16),
                    in_=w1b_d.ap().rearrange("(c p) j -> p c j", p=128)
                    [:, :, 512 * dc:512 * (dc + 1)])
            GE(w1l)
            for q in range(4):
                qq = 4 * dc + q

                def d1mm(t, dc=dc, q=q, qq=qq):
                    if q == 0:
                        DQ.wait(t, tW1[dc])
                    if qq == 0:
                        t.wait_ge(sP.h, mHSEL)
                    if qq >= 2:
                        t.wait_ge(sS.h, mGelQ[qq - 2])
                    psX = psA if qq % 2 == 0 else psB
                    for ic in range(16):
                        t.matmul(psX[:, 0:CAP],
                                 w1c[dc % 2][:, :].rearrange("p (c j) -> p c j", c=16)
                                 [:, ic, 128 * q:128 * (q + 1)],
                                 hsel[:, :].rearrange("p (c i) -> p c i", c=16)
                                 [:, ic, :],
                                 start=(ic == 0), stop=(ic == 15))
                    _i(t.drain(), sT)
                TE(d1mm)
                md1 = sT.v

                def gel(sc, qq=qq, md1=md1):
                    sc.wait_ge(sT.h, md1)
                    psX = psA if qq % 2 == 0 else psB
                    _i(sc.activation(
                        g_sb[:, :].rearrange("p (c i) -> p c i", c=64)[:, qq, :],
                        psX[:, 0:CAP], ACT.Gelu_apprx_tanh,
                        bias=b1_sb[:, qq:qq + 1]), sS)
                SE(gel)
                mGelQ[qq] = sS.v
            mW1mm[dc] = sT.v
        mGELL = sS.v

        mW2mm = [0] * 16
        mW2ev = [0] * 16
        tW2 = [None] * 16
        for oc in range(16):
            def w2l(g, oc=oc):
                if oc < 2:
                    g.wait_ge(sP.h, mF5)
                else:
                    g.wait_ge(sT.h, mW2mm[oc - 2])
                tW2[oc] = dqG.issue(
                    g, out=w2c[oc % 2][:, :].rearrange("p (c j) -> p c j", c=64),
                    in_=w2b_d.ap().rearrange("(c p) j -> p c j", p=128)
                    [:, :, 128 * oc:128 * (oc + 1)])
            GE(w2l)

            def d2mm(t, oc=oc):
                DQ.wait(t, tW2[oc])
                t.wait_ge(sS.h, mGELL)
                if oc >= 2:
                    t.wait_ge(sV.h, mW2ev[oc - 2])
                psX = psA if oc % 2 == 0 else psB
                for d in range(64):
                    t.matmul(psX[:, 0:CAP],
                             w2c[oc % 2][:, :].rearrange("p (c j) -> p c j", c=64)
                             [:, d, :],
                             g_sb[:, :].rearrange("p (c i) -> p c i", c=64)[:, d, :],
                             start=(d == 0), stop=(d == 63))
                _i(t.drain(), sT)
            TE(d2mm)
            mW2mm[oc] = sT.v

            def d2ev(v, oc=oc, m=mW2mm[oc]):
                v.wait_ge(sT.h, m)
                psX = psA if oc % 2 == 0 else psB
                _i(v.tensor_scalar(
                    wd2[:, :].rearrange("p (c i) -> p c i", c=16)[:, oc, :],
                    psX[:, 0:CAP], b2_sb[:, oc:oc + 1], None, ALU.add), sV)
            VE(d2ev)
            mW2ev[oc] = sV.v
        mWD2 = sV.v

        def dsc(g):
            g.wait_ge(sV.h, mWD2)
            _i(g.partition_broadcast(scat_b[:, :], scat16[0:1, :]), sP)
            g.wait_ge(sP.h, sP.v)
            g.load_library(libcfg.local_scatter)
            for fc in range(16):
                _i(g.local_scatter(
                    dfm[:, :].rearrange("p (c m) -> p c m", c=16)[:, fc, :],
                    wd2[:, :].rearrange("p (c i) -> p c i", c=16)[:, fc, :],
                    scat_b[:, :], channels=128, num_elems=1024, num_idxs=CAP), sP)
            _i(g.drain(), sP)
        GE(dsc)
        mDFM = sP.v

        f5.close()
        fin = ExitStack()
        dtm = [fin.enter_context(nc.sbuf_tensor(f"dtm{i}", [128, D], BF16))
               for i in range(2)]
        x1r = [fin.enter_context(nc.sbuf_tensor(f"x1r{i}", [128, D], F32))
               for i in range(2)]
        hr = [fin.enter_context(nc.sbuf_tensor(f"hr{i}", [128, D], BF16))
              for i in range(2)]
        ft = fin.enter_context(nc.sbuf_tensor("ft", [128, D], F32))
        yt = [fin.enter_context(nc.sbuf_tensor(f"yt{i}", [128, D], F32))
              for i in range(2)]
        st6c = fin.enter_context(nc.sbuf_tensor("st6c", [128, 24], F32))
        st2c = fin.enter_context(nc.sbuf_tensor("st2c", [128, 2], F32))
        lrs3 = fin.enter_context(nc.sbuf_tensor("lrs3", [128, 1], F32))
        lnm3 = fin.enter_context(nc.sbuf_tensor("lnm3", [128, 1], F32))

        tY = [None] * NT
        mDT = [0, 0]
        mYW = [0, 0]
        gDb = {}
        gbc = [0]
        for j in range(NT):
            par = j % 2
            tld = [None, None]

            def ldf(sy, j=j, par=par, wx=mYW[par]):
                sy.wait_ge(sP.h, mDFM)
                if wx:
                    sy.wait_ge(sV.h, wx)
                DQ.wait(sy, tX1S[j])
                tld[0] = dqQ.issue(sy, out=x1r[par][:, :],
                                   in_=x1_dram[128 * j:128 * (j + 1), :])
                tld[1] = dqQ.issue(sy, out=hr[par][:, :],
                                   in_=h_dram[128 * j:128 * (j + 1), :])
            SY(ldf)
            tldx, tldh = tld

            for b in range(4):
                gb = gbc[0]

                def dtr(t, j=j, b=b, gb=gb, wv=(mDT[par] if b == 0 else 0),
                        wpt=gDb.get(gb - 2)):
                    if b == 0:
                        t.wait_ge(sP.h, mDFM)
                    if wv:
                        t.wait_ge(sV.h, wv)
                    if wpt:
                        t.wait_ge(sV.h, wpt)
                    base = 1024 * (b % 2)
                    pcb = psC.ap().bitcast(BF16)
                    for z in range(4):
                        fc = 4 * b + z
                        t.transpose(pcb[:, base + 128 * z:base + 128 * (z + 1)],
                                    dfm[:, :].rearrange("p (c m) -> p c m", c=16)
                                    [:, fc, 128 * j:128 * (j + 1)], ident[:, :])
                    _i(t.drain(), sT)
                TE(dtr)
                mtr = sT.v

                def dte(v, par=par, b=b, mtr=mtr):
                    v.wait_ge(sT.h, mtr)
                    base = 1024 * (b % 2)
                    _i(v.tensor_copy(
                        dtm[par][:, 512 * b:512 * (b + 1)],
                        psC.ap().bitcast(BF16)[:, base:base + 512]), sV)
                VE(dte)
                gDb[gb] = sV.v
                gbc[0] += 1
            mDT[par] = sV.v
            mDTj = sV.v

            def ffo(v, j=j, par=par, mDTj=mDTj, tldx=tldx, tldh=tldh):
                v.wait_ge(sV.h, mDTj)
                DQ.wait(v, tldh)
                v.scalar_tensor_tensor(ft[:, :], dtm[par][:, :], sig_tm[:, j:j + 1],
                                       hr[par][:, :], ALU.mult, ALU.add)
                v.drain()
                for q in range(4):
                    v.bn_stats(st6c[:, 6 * q:6 * q + 6], ft[:, 512 * q:512 * (q + 1)])
                v.drain()
                v.bn_aggr(st2c[:, :], st6c[:, :])
                v.drain()
                v.tensor_scalar(lrs3[:, :], st2c[:, 1:2], EPS_LN, None, ALU.add)
                v.drain()
                _i(v.reciprocal(lrs3[:, :], lrs3[:, :]), sV)
            VE(ffo)
            m1 = sV.v

            def fsq(sc, m1=m1):
                sc.wait_ge(sV.h, m1)
                _i(sc.activation(lrs3[:, :], lrs3[:, :], ACT.Sqrt), sS)
            SE(fsq)
            m2 = sS.v

            def fy(v, j=j, par=par, m2=m2, tldx=tldx,
                   typ=(tY[j - 2] if j >= 2 else None)):
                v.wait_ge(sS.h, m2)
                if typ is not None:
                    DQ.wait(v, typ)
                v.tensor_mul(lnm3[:, :], st2c[:, 0:1], lrs3[:, :])
                v.drain()
                DQ.wait(v, tldx)
                v.scalar_tensor_tensor(yt[par][:, :], ft[:, :], lrs3[:, 0:1],
                                       x1r[par][:, :], ALU.mult, ALU.add)
                v.drain()
                _i(v.tensor_scalar(yt[par][:, :], yt[par][:, :], lnm3[:, 0:1],
                                   None, ALU.subtract), sV)
            VE(fy)
            mYW[par] = sV.v
            mY = sV.v

            def sty(sy, j=j, par=par, mY=mY):
                sy.wait_ge(sV.h, mY)
                tY[j] = dqQ.issue(sy, out=y_d[128 * j:128 * (j + 1), :],
                                  in_=yt[par][:, :])
            SY(sty)

        def fwait(sy):
            for j in range(NT):
                DQ.wait(sy, tY[j])
        SY(fwait)

        fin.close()
        sel_ctx.close()
        return nc
    return nc



def _consts(core):
    pos = (np.arange(T) * NCORES + core).astype(np.float64)
    inv = 1.0 / (10000.0 ** (np.arange(0, ROPE, 2, dtype=np.float64) / ROPE))
    fr = np.outer(inv, pos)
    cos = np.cos(fr)
    sin = np.sin(fr)
    ropeCh = np.tile(cos, (4, 1)).astype(np.float32)
    ropeSh = np.concatenate([np.tile(-sin, 1), sin], axis=0)
    ropeSh = np.tile(np.concatenate([-sin, sin], axis=0), (2, 1)).astype(np.float32)
    p = np.arange(128)[:, None]
    c = np.tile(np.arange(128)[None, :], (1, 8)).reshape(1, 1024)
    st = np.repeat(np.arange(8), 128).reshape(1, 1024)
    allow = (p < c) | ((p == c) & (st <= core))
    dm = np.where(allow, 0.0, MASK_NEG).astype(np.float32)
    return ropeCh, ropeSh, dm


def kernel(**inputs):
    nc = build_nc(stage=6)
    return _run(nc, inputs, stage=6)["y"]


def _run(nc, inputs, stage):
    mybir.codegen_inst_isa_subclasses(nc)
    x = np.ascontiguousarray(np.asarray(inputs["L0_attn_x"], dtype=np.float32)[0])
    ident = np.eye(128, dtype=ml_dtypes.bfloat16)
    onesm = np.ones((128, 128), dtype=ml_dtypes.bfloat16)
    tok = np.arange(T, dtype=np.int16).reshape(1, T)
    cap = np.arange(CAP, dtype=np.float32).reshape(1, CAP)
    common = {
        "w_dq": np.asarray(inputs["L0_attn_w_dq"], np.float32),
        "w_uq": np.asarray(inputs["L0_attn_w_uq"], np.float32),
        "w_dkv": np.asarray(inputs["L0_attn_w_dkv"], np.float32),
        "w_kr": np.asarray(inputs["L0_attn_w_kr"], np.float32),
        "w_uk": np.asarray(inputs["L0_attn_w_uk"], np.float32),
        "w_uv": np.asarray(inputs["L0_attn_w_uv"], np.float32),
        "w_o": np.asarray(inputs["L0_attn_w_o"], np.float32),
        "g_q": np.asarray(inputs["L0_attn_g_q"], np.float32),
        "g_kv": np.asarray(inputs["L0_attn_g_kv"], np.float32),
        "w1": np.asarray(inputs["L0_ffn_w1"], np.float32),
        "b1": np.asarray(inputs["L0_ffn_b1"], np.float32),
        "w2": np.asarray(inputs["L0_ffn_w2"], np.float32),
        "b2": np.asarray(inputs["L0_ffn_b2"], np.float32),
        "rw": np.asarray(inputs["L0_ffn_router_weight"], np.float32),
        "rb": np.asarray(inputs["L0_ffn_router_bias"], np.float32),
        "ident": ident, "ones": onesm, "tok16": tok, "capio": cap,
        "id32f": np.eye(32, dtype=np.float32),
        "onesf": np.ones((128, 128), dtype=np.float32),
    }
    in_maps = []
    for i in range(NCORES):
        rc, rs, dm = _consts(i)
        m = dict(common)
        m["x0"] = np.ascontiguousarray(x[i::NCORES])
        m["ropeC"] = rc
        m["ropeS"] = rs
        m["dmask"] = dm
        in_maps.append(m)
    res = run_bass_kernel_spmd(nc, in_maps, core_ids=list(range(NCORES)))
    if stage == 6:
        out = np.empty((1, S, D), np.float32)
        for i in range(NCORES):
            out[0, i::NCORES, :] = res.results[i]["y"]
        return {"y": out}
    return {"res": res}
